# version 01
# baseline (speedup 1.0000x reference)
import numpy as np
import ml_dtypes
from contextlib import ExitStack

import concourse.bass as bass
import concourse.mybir as mybir
from concourse.bass_utils import run_bass_kernel_spmd

F32 = mybir.dt.float32
BF16 = mybir.dt.bfloat16
I32 = mybir.dt.int32
AF = mybir.ActivationFunctionType
OP = mybir.AluOpType

B, S, D = 4, 8192, 1024
NCORES = 8
TLOC = S // 2
NCH = D // 128
TT = 512
NT = TLOC // TT
TCH = TLOC // 128
LN_EPS = 1e-5

BF = ml_dtypes.bfloat16


class Em:

    def __init__(self, nc, stack):
        self.nc = nc
        self.stack = stack
        self.g = nc.gpsimd
        self.v = nc.vector
        self.sc = nc.scalar
        self.t = nc.tensor
        self.sy = nc.sync
        self._sems = {}

    def sem(self, name):
        if name not in self._sems:
            h = self.stack.enter_context(self.nc.semaphore(name))
            self._sems[name] = [h, 0]
        return self._sems[name]

    def inc(self, ins, name, amt=1):
        s = self.sem(name)
        ins.then_inc(s[0], amt)
        s[1] += amt
        return ins

    def val(self, name):
        return self.sem(name)[1]

    def wait(self, eng, name, val=None):
        s = self.sem(name)
        v = s[1] if val is None else val
        if v > 0:
            eng.wait_ge(s[0], v)

    def barrier(self):
        self.nc.all_engine_barrier()

    def vstep(self, ins):
        self.inc(ins, "DV")
        self.wait(self.v, "DV")
        return ins

    def load_consts(self, cfg):
        if not hasattr(self, "_consts"):
            nc = self.nc
            ones = self.stack.enter_context(nc.sbuf_tensor("c_ones_sb", [128, 128], F32))
            ident = self.stack.enter_context(nc.sbuf_tensor("c_id_sb", [128, 128], F32))
            misc = self.stack.enter_context(nc.sbuf_tensor("c_misc_sb", [128, 8], F32))
            self.inc(self.sy.dma_start(out=ones[:, :], in_=cfg["c_ones"][:, :]), "LD", 16)
            self.inc(self.sy.dma_start(out=ident[:, :], in_=cfg["c_id"][:, :]), "LD", 16)
            self.inc(self.sy.dma_start(out=misc[:, :], in_=cfg["c_misc"][:, :]), "LD", 16)
            sel = None
            if cfg.get("use_sel"):
                sel = self.stack.enter_context(nc.sbuf_tensor("c_sel_sb", [8, 8 * 128], BF16))
                self.inc(self.sy.dma_start(out=sel[:, :], in_=cfg["c_sel"][:, :]), "LD", 16)
            self._consts = (ones, ident, sel, misc)
        return self._consts


def emit_ffn(em, cfg):
    nc, g, v, sc, t, sy = em.nc, em.g, em.v, em.sc, em.t, em.sy
    E, I = cfg["E"], cfg["I"]
    ICH = I // 128
    NB = cfg["rbias"].shape[0]
    x_in, x_out = cfg["x_in"], cfg["x_out"]
    xn_dram, yacc = cfg["xn_dram"], cfg["yacc"]
    c_ones, c_id, c_sel, c_misc = em.load_consts(cfg)

    tag = cfg["tag"]
    sb = lambda name, shape, dt=F32: em.stack.enter_context(
        nc.sbuf_tensor(tag + name, shape, dt))
    xt = sb("xt", [128, NCH * TT])
    sq = sb("sq", [128, 2 * TT])
    xn32 = sb("xn32", [128, 2 * TT])
    xnb = sb("xnb", [128, NCH * TT], BF16)
    srow = sb("srow", [1, 4 * TT])
    probs = sb("probs", [128, TCH * 8])
    wtok = sb("wtok", [128, TCH * 8])
    trow = sb("trow", [128, 16])
    tmp8 = sb("tmp8", [128, 4 * 8])
    wem = sb("wem", [8, TLOC], BF16)
    rbs = sb("rbs", [NB, 8])
    wg = sb("wg", [128, NCH * I], BF16)
    wu = sb("wu", [128, NCH * I], BF16)
    wd = sb("wd", [128, ICH * D], BF16)
    hid = sb("hid", [128, ICH * TT], BF16)
    wb = sb("wb", [128, TT])
    tsil = sb("tsil", [128, 2 * TT])
    ybig = sb("ybig", [128, NCH * TT])
    rws = sb("rws", [128, NCH * 8])
    PS = em.stack.enter_context(nc.psum_tensor(tag + "PS", [128, 8 * 512], F32))

    def bank(b, p0=0, pn=128, c0=0, cn=512):
        return PS[p0:p0 + pn, b * 512 + c0:b * 512 + c0 + cn]

    if True:
        rww = cfg["rw"]
        if True:
            for c in range(NCH):
                em.inc(sy.dma_start(out=rws[:, c * 8:c * 8 + E],
                                    in_=rww[c * 128:(c + 1) * 128, :]), "LD", 16)
            em.inc(sy.dma_start(out=rbs[:, 0:E], in_=cfg["rbias"][:, :]), "LD", 16)
            v.memset(wem[0:8, :], 0.0)
            em.inc(v.memset(wtok[:, :], 0.0), "DV")

            for ti in range(NT):
                tc0 = ti * TT
                em.wait(sy, "PE")
                for c in range(NCH):
                    em.inc(sy.dma_start(out=xt[:, c * TT:(c + 1) * TT],
                                        in_=x_in[c, :, tc0:tc0 + TT]), "LD", 16)
                em.wait(t, "LD")
                em.wait(sc, "LD")
                for c in range(NCH):
                    em.wait(sc, "PE")
                    em.inc(sc.activation(sq[:, (c % 2) * TT:(c % 2) * TT + TT],
                                         xt[:, c * TT:(c + 1) * TT], AF.Square), "AC")
                    em.wait(t, "AC")
                    t.matmul(bank(0, 0, 1), c_ones[:, 0:1], xt[:, c * TT:(c + 1) * TT],
                             start=(c == 0), stop=(c == NCH - 1))
                    em.inc(t.matmul(bank(1, 0, 1), c_ones[:, 0:1],
                                    sq[:, (c % 2) * TT:(c % 2) * TT + TT],
                                    start=(c == 0), stop=(c == NCH - 1)), "PE")
                r = [srow[0:1, k * TT:(k + 1) * TT] for k in range(4)]
                em.wait(v, "PE")
                em.vstep(v.tensor_scalar_mul(r[0], bank(0, 0, 1), 1.0 / D))
                em.vstep(v.tensor_scalar_mul(r[1], bank(1, 0, 1), 1.0 / D))
                em.vstep(v.tensor_mul(r[2], r[0], r[0]))
                em.vstep(v.tensor_sub(r[1], r[1], r[2]))
                em.wait(sc, "DV")
                em.inc(sc.activation(r[2], r[1], AF.Sqrt, bias=c_misc[0:1, 0:1]), "AC")
                em.wait(v, "AC")
                em.vstep(v.reciprocal(r[1], r[2]))
                em.vstep(v.tensor_mul(r[3], r[0], r[1]))
                em.wait(t, "DV")
                em.inc(t.matmul(bank(2), c_ones[0:1, :], r[1], start=True, stop=True), "PE")
                em.inc(t.matmul(bank(3), c_ones[0:1, :], r[3], start=True, stop=True), "PE")
                em.wait(v, "PE")
                for c in range(NCH):
                    sl = (c % 2)
                    x32 = xn32[:, sl * TT:(sl + 1) * TT]
                    em.wait(v, "PE")
                    v.tensor_mul(x32, xt[:, c * TT:(c + 1) * TT], bank(2))
                    em.inc(v.tensor_sub(x32, x32, bank(3)), "DV")
                    em.inc(v.tensor_copy(xnb[:, c * TT:(c + 1) * TT], x32), "DV")
                    em.wait(t, "DV")
                    for tc in range(4):
                        ins = t.matmul(bank(4 + tc, 0, 128, 0, E),
                                       x32[:, tc * 128:(tc + 1) * 128],
                                       rws[:, c * 8:c * 8 + E],
                                       start=(c == 0), stop=False)
                    em.inc(ins, "PE")
                for tc in range(4):
                    ins = t.matmul(bank(4 + tc, 0, 128, 0, E), c_ones[0:NB, :],
                                   rbs[0:NB, 0:E], start=False, stop=True)
                    em.inc(ins, "PE")
                em.wait(sc, "PE")
                for tc in range(4):
                    em.inc(sc.activation(probs[:, (ti * 4 + tc) * 8:(ti * 4 + tc) * 8 + E],
                                         bank(4 + tc, 0, 128, 0, E), AF.Sigmoid), "AC")
                em.wait(sy, "DV")
                em.inc(sy.dma_start(out=xn_dram.ap()
                                    .rearrange("c p l -> p c l")[:, :, tc0:tc0 + TT],
                                    in_=xnb[:, :].rearrange("p (c l) -> p c l", c=NCH)),
                       "ST", 16)

            em.barrier()
            for tc in range(TCH):
                pr = probs[:, tc * 8:tc * 8 + E]
                sl = tc % 2
                t8 = tmp8[:, (2 + sl) * 8:(2 + sl) * 8 + E]
                m1 = trow[:, (sl * 4):(sl * 4) + 1]
                m2 = trow[:, (sl * 4) + 1:(sl * 4) + 2]
                den = trow[:, (sl * 4) + 2:(sl * 4) + 3]
                rden = trow[:, (sl * 4) + 3:(sl * 4) + 4]
                t8b = tmp8[:, sl * 8:sl * 8 + E]
                em.vstep(v.reduce_max(m1, pr, axis=mybir.AxisListType.X))
                em.vstep(v.tensor_scalar(t8, pr, m1, None, OP.is_equal))
                em.vstep(v.scalar_tensor_tensor(t8b, t8, -1e30, pr, OP.mult, OP.add))
                em.vstep(v.reduce_max(m2, t8b, axis=mybir.AxisListType.X))
                em.vstep(v.tensor_add(den, m1, m2))
                em.vstep(v.reciprocal(rden, den))
                em.vstep(v.tensor_scalar(t8, pr, m2, None, OP.is_ge))
                wv = wtok[:, tc * 8:tc * 8 + E]
                em.vstep(v.scalar_tensor_tensor(wv, pr, rden, t8, OP.mult, OP.mult))
                em.wait(t, "DV")
                em.inc(t.transpose(bank(4, 0, 8, 0, 128), wtok[:, tc * 8:tc * 8 + 8],
                                   c_id[:, :]), "PE")
                em.wait(v, "PE")
                em.inc(v.tensor_copy(wem[0:8, tc * 128:(tc + 1) * 128],
                                     bank(4, 0, 8, 0, 128)), "DV")

            if cfg.get("dbg"):
                em.wait(sy, "DV")
                em.inc(sy.dma_start(out=cfg["dbg_probs"][:, :], in_=probs[:, :]), "ST", 16)
                em.inc(sy.dma_start(out=cfg["dbg_wtok"][:, :], in_=wtok[:, :]), "ST", 16)

            em.barrier()
            for e in range(E + 1):
                if e == 0:
                    gsrc, usrc, dsrc = cfg["sg"], cfg["su"], cfg["sd"]
                else:
                    gsrc, usrc, dsrc = (cfg["gw"][e - 1], cfg["uw"][e - 1],
                                        cfg["dw"][e - 1])
                em.wait(g, "PE")
                for kc in range(NCH):
                    em.inc(g.dma_start(out=wg[:, kc * I:(kc + 1) * I],
                                       in_=gsrc[kc * 128:(kc + 1) * 128, :]), "LW", 16)
                    em.inc(g.dma_start(out=wu[:, kc * I:(kc + 1) * I],
                                       in_=usrc[kc * 128:(kc + 1) * 128, :]), "LW", 16)
                for ic in range(ICH):
                    em.inc(g.dma_start(out=wd[:, ic * D:(ic + 1) * D],
                                       in_=dsrc[ic * 128:(ic + 1) * 128, :]), "LW", 16)
                for ti in range(NT):
                    tc0 = ti * TT
                    em.wait(sy, "PE")
                    em.inc(sy.dma_start(out=xnb[:, :].rearrange("p (c l) -> p c l", c=NCH),
                                        in_=xn_dram.ap().rearrange("c p l -> p c l")[:, :, tc0:tc0 + TT]),
                           "LD", 16)
                    if e > 0:
                        em.wait(t, "DV")
                        em.inc(t.matmul(bank(4), c_sel[:, (e - 1) * 128:e * 128],
                                        wem[0:8, tc0:tc0 + TT], start=True, stop=True),
                               "PE")
                        em.wait(v, "PE")
                        em.inc(v.tensor_copy(wb[:, :], bank(4)), "DV")
                    em.wait(t, "LW")
                    em.wait(t, "LD")
                    for ic in range(ICH):
                        em.wait(t, "AC")
                        em.wait(t, "DV")
                        for kc in range(NCH):
                            t.matmul(bank(0), wg[:, kc * I + ic * 128:kc * I + (ic + 1) * 128],
                                     xnb[:, kc * TT:(kc + 1) * TT],
                                     start=(kc == 0), stop=(kc == NCH - 1))
                        for kc in range(NCH):
                            ins = t.matmul(bank(1), wu[:, kc * I + ic * 128:kc * I + (ic + 1) * 128],
                                           xnb[:, kc * TT:(kc + 1) * TT],
                                           start=(kc == 0), stop=(kc == NCH - 1))
                        em.inc(ins, "PE")
                        em.wait(sc, "PE")
                        sl = ic % 2
                        tsl = tsil[:, sl * TT:(sl + 1) * TT]
                        em.inc(sc.activation(tsl, bank(0), AF.Silu), "AC")
                        em.wait(v, "AC")
                        if e > 0:
                            v.tensor_mul(tsl, tsl, bank(1))
                            em.inc(v.tensor_mul(hid[:, ic * TT:(ic + 1) * TT], tsl,
                                                wb[:, :]), "DV")
                        else:
                            em.inc(v.tensor_mul(hid[:, ic * TT:(ic + 1) * TT], tsl,
                                                bank(1)), "DV")
                    em.wait(t, "DV")
                    for dc in range(NCH):
                        pb = 2 + (dc % 2)
                        em.wait(t, "YA")
                        em.wait(t, "DV")
                        for ic in range(ICH):
                            ins = t.matmul(bank(pb),
                                           wd[:, ic * D + dc * 128:ic * D + (dc + 1) * 128],
                                           hid[:, ic * TT:(ic + 1) * TT],
                                           start=(ic == 0), stop=(ic == ICH - 1))
                        em.inc(ins, "PE")
                        em.wait(v, "PE")
                        em.inc(v.tensor_copy(ybig[:, dc * TT:(dc + 1) * TT], bank(pb)),
                               "DV")
                    em.wait(g, "DV")
                    em.inc(g.dma_start(out=yacc.ap().rearrange("c p l -> p c l")[:, :, tc0:tc0 + TT],
                                       in_=ybig[:, :].rearrange("p (c l) -> p c l", c=NCH),
                                       accum_op=(OP.bypass if e == 0 else OP.add)),
                           "YA", 16)

            em.barrier()
            for ti in range(NT):
                tc0 = ti * TT
                em.wait(sy, "YA")
                em.wait(sy, "DV")
                em.inc(sy.dma_start(out=xt[:, :].rearrange("p (c l) -> p c l", c=NCH),
                                    in_=x_in.ap().rearrange("c p l -> p c l")[:, :, tc0:tc0 + TT]),
                       "LD", 16)
                em.inc(sy.dma_start(out=ybig[:, :].rearrange("p (c l) -> p c l", c=NCH),
                                    in_=yacc.ap().rearrange("c p l -> p c l")[:, :, tc0:tc0 + TT]),
                       "LD", 16)
                em.wait(v, "LD")
                em.inc(v.scalar_tensor_tensor(ybig[:, :], ybig[:, :], 0.5, xt[:, :],
                                              OP.mult, OP.add), "DV")
                em.wait(sy, "DV")
                em.inc(sy.dma_start(out=x_out.ap().rearrange("c p l -> p c l")[:, :, tc0:tc0 + TT],
                                    in_=ybig[:, :].rearrange("p (c l) -> p c l", c=NCH)),
                       "ST", 16)
            em.wait(sy, "ST")



def emit_xchg_setup(em, cfg):
    nc, g = em.nc, em.g
    st = em.stack
    em.x_metas = st.enter_context(nc.sbuf_tensor("xmeta", [1, 16], I32))
    em.x_nonces = st.enter_context(nc.sbuf_tensor("xnonce", [1, 8], I32))
    em.x_fl = st.enter_context(nc.sbuf_tensor("xfl", [1, 1], I32))
    em.inc(em.sy.dma_start(out=em.x_metas[:, :], in_=cfg["meta"][:, :]), "XM", 16)
    em.inc(em.sy.dma_start(out=em.x_nonces[:, :], in_=cfg["nonce"][:, :]), "XM", 16)
    regs = {}
    for nm in ["r_off", "r_tgt", "r_flv", "r_done", "r_plc"]:
        regs[nm] = st.enter_context(g.register(nm))
    g.reg_mov(regs["r_plc"], 0)
    em.x_regs = regs
    em.x_cfg = cfg


def emit_pair_exchange(em, rnd, paysb, col0, ncols):
    g = em.g
    cfg, regs = em.x_cfg, em.x_regs
    PAYC = cfg["PAYC"]
    shm_pay, shm_fl, nonce = cfg["shm_pay"], cfg["shm_fl"], cfg["nonce"]
    em.wait(g, "XM")
    em.wait(g, "DV")
    g.reg_load(regs["r_off"], em.x_metas[0:1, 4 + rnd:5 + rnd])
    em.inc(g.dma_start(
        bass.AP(shm_pay, regs["r_off"], [[PAYC, 128], [1, ncols]]),
        paysb[:, col0:col0 + ncols]), "XP", 16)
    em.wait(g, "XP")
    g.reg_load(regs["r_off"], em.x_metas[0:1, 12 + rnd:13 + rnd])
    em.inc(g.dma_start(
        bass.AP(shm_fl, regs["r_off"], [[1, 1], [1, 1]]),
        nonce[0:1, rnd:rnd + 1]), "XP", 16)
    g.reg_load(regs["r_tgt"], em.x_nonces[0:1, rnd:rnd + 1])
    g.reg_load(regs["r_off"], em.x_metas[0:1, 8 + rnd:9 + rnd])
    g.reg_mov(regs["r_done"], 1)
    pl = em.sem("PL")
    with g.While(regs["r_done"]):
        g.dma_start(em.x_fl[0:1, 0:1],
                    bass.AP(shm_fl, regs["r_off"], [[1, 1], [1, 1]])
                    ).then_inc(pl[0], 16)
        g.reg_add(regs["r_plc"], regs["r_plc"], 16)
        g.wait_ge(pl[0], regs["r_plc"])
        g.reg_load(regs["r_flv"], em.x_fl[0:1, 0:1])
        g.reg_sub(regs["r_done"], regs["r_flv"], regs["r_tgt"])
    g.reg_load(regs["r_off"], em.x_metas[0:1, 0 + rnd:1 + rnd])
    em.inc(g.dma_start(
        em.x_pasb[:, col0:col0 + ncols],
        bass.AP(shm_pay, regs["r_off"], [[PAYC, 128], [1, ncols]])), "XD", 16)


def emit_ln_to_dram(em, cfg, xt, sq, xn32, xnb, srow, bank, c_ones, c_misc):
    v, sc, t, sy = em.v, em.sc, em.t, em.sy
    x_in, xn_dram = cfg["x_in"], cfg["xn_dram"]
    for ti in range(NT):
        tc0 = ti * TT
        em.wait(sy, "PE")
        for c in range(NCH):
            em.inc(sy.dma_start(out=xt[:, c * TT:(c + 1) * TT],
                                in_=x_in[c, :, tc0:tc0 + TT]), "LD", 16)
        em.wait(t, "LD")
        em.wait(sc, "LD")
        for c in range(NCH):
            em.wait(sc, "PE")
            em.inc(sc.activation(sq[:, (c % 2) * TT:(c % 2) * TT + TT],
                                 xt[:, c * TT:(c + 1) * TT], AF.Square), "AC")
            em.wait(t, "AC")
            t.matmul(bank(0, 0, 1), c_ones[:, 0:1], xt[:, c * TT:(c + 1) * TT],
                     start=(c == 0), stop=(c == NCH - 1))
            em.inc(t.matmul(bank(1, 0, 1), c_ones[:, 0:1],
                            sq[:, (c % 2) * TT:(c % 2) * TT + TT],
                            start=(c == 0), stop=(c == NCH - 1)), "PE")
        r = [srow[0:1, k * TT:(k + 1) * TT] for k in range(4)]
        em.wait(v, "PE")
        em.vstep(v.tensor_scalar_mul(r[0], bank(0, 0, 1), 1.0 / D))
        em.vstep(v.tensor_scalar_mul(r[1], bank(1, 0, 1), 1.0 / D))
        em.vstep(v.tensor_mul(r[2], r[0], r[0]))
        em.vstep(v.tensor_sub(r[1], r[1], r[2]))
        em.wait(sc, "DV")
        em.inc(sc.activation(r[2], r[1], AF.Sqrt, bias=c_misc[0:1, 0:1]), "AC")
        em.wait(v, "AC")
        em.vstep(v.reciprocal(r[1], r[2]))
        em.vstep(v.tensor_mul(r[3], r[0], r[1]))
        em.wait(t, "DV")
        em.inc(t.matmul(bank(2), c_ones[0:1, :], r[1], start=True, stop=True), "PE")
        em.inc(t.matmul(bank(3), c_ones[0:1, :], r[3], start=True, stop=True), "PE")
        em.wait(v, "PE")
        for c in range(NCH):
            sl = (c % 2)
            x32 = xn32[:, sl * TT:(sl + 1) * TT]
            em.wait(v, "PE")
            v.tensor_mul(x32, xt[:, c * TT:(c + 1) * TT], bank(2))
            em.inc(v.tensor_sub(x32, x32, bank(3)), "DV")
            em.inc(v.tensor_copy(xnb[:, c * TT:(c + 1) * TT], x32), "DV")
        em.wait(sy, "DV")
        em.inc(sy.dma_start(out=xn_dram.ap().rearrange("c p l -> p c l")[:, :, tc0:tc0 + TT],
                            in_=xnb[:, :].rearrange("p (c l) -> p c l", c=NCH)),
               "ST", 16)


def emit_attn0(em, cfg):
    nc, g, v, sc, t, sy = em.nc, em.g, em.v, em.sc, em.t, em.sy
    c_ones, c_id, c_sel, c_misc = em.load_consts(cfg)
    x_in, x_out, xn_dram = cfg["x_in"], cfg["x_out"], cfg["xn_dram"]
    qf_dram = cfg["qf_dram"]
    tag = cfg["tag"]
    sb = lambda name, shape, dt=F32: em.stack.enter_context(
        nc.sbuf_tensor(tag + name, shape, dt))
    EPS = 1e-6
    xt = sb("xt", [128, NCH * TT])
    sq = sb("sq", [128, 2 * TT])
    xn32 = sb("xn32", [128, 2 * TT])
    xnb = sb("xnb", [128, NCH * TT], BF16)
    srow = sb("srow", [1, 4 * TT])
    qt = sb("qt", [128, 2 * NCH * TT], BF16)
    kf = sb("kf", [128, NCH * TLOC], BF16)
    T1 = sb("T1", [16, 3 * TLOC], BF16)
    csx = sb("csx", [16, 2 * TT])
    tiny = sb("tiny", [16, 16])
    wdot = sb("wdot", [128, 2 * NCH * 16], BF16)
    gsum = sb("gsum", [128, 4 * 8])
    part = sb("part", [128, 2 * NCH * 8])
    wsb = sb("wsb", [128, NCH * D], BF16)
    ktok = sb("ktok", [128, D], BF16)
    vsc = sb("vsc", [128, D], BF16)
    sctk = sb("sctk", [128, 2 * 16])
    kvb = sb("kvb", [128, 512], BF16)
    brow16 = sb("brow16", [1, 4 * D], BF16)
    bpp = sb("bpp", [128, 4 * NCH])
    selp = sb("selp", [16, NCH * 128], BF16)
    onesb = sb("onesb", [128, 128], BF16)
    idb = sb("idb", [128, 128], BF16)
    pays = sb("pays", [128, 560])
    pasb = sb("pasb", [128, 560])
    em.x_pasb = pasb
    PS = em.stack.enter_context(nc.psum_tensor(tag + "PS", [128, 7 * 512], F32))
    PSb = em.stack.enter_context(nc.psum_tensor(tag + "PSb", [128, 8 * 128], BF16))

    def bank(b, p0=0, pn=128, c0=0, cn=512):
        return PS[p0:p0 + pn, b * 512 + c0:b * 512 + c0 + cn]

    emit_xchg_setup(em, cfg)
    em.inc(g.dma_start(out=brow16[:, :], in_=cfg["b_rows"][:, :]), "LW", 16)
    em.inc(sy.dma_start(out=bpp[:, :], in_=cfg["b_pp"][:, :]), "LD", 16)
    em.inc(sy.dma_start(out=selp[:, :], in_=cfg["c_selp"][:, :]), "LD", 16)
    em.inc(sy.dma_start(out=onesb[:, :], in_=cfg["c_onesb"][:, :]), "LD", 16)
    em.inc(sy.dma_start(out=idb[:, :], in_=cfg["c_idb"][:, :]), "LD", 16)

    def load_w(w):
        em.wait(g, "PE")
        for kc in range(NCH):
            em.inc(g.dma_start(out=wsb[:, kc * D:(kc + 1) * D],
                               in_=w[kc * 128:(kc + 1) * 128, :]), "LW", 16)

    def wslice(kc, dc):
        return wsb[:, kc * D + dc * 128:kc * D + (dc + 1) * 128]

    def load_qt(ti, sl):
        em.wait(sy, "PE")
        em.wait(sy, "DV")
        em.inc(sy.dma_start(
            out=qt[:, sl * NCH * TT:(sl + 1) * NCH * TT].rearrange("p (c l) -> p c l", c=NCH),
            in_=qf_dram.ap().rearrange("c p l -> p c l")[:, :, ti * TT:ti * TT + TT]),
            "LD", 16)
        em.wait(t, "LD")
        em.wait(v, "LD")

    def qsl(sl, c, a=0, b=TT):
        return qt[:, sl * NCH * TT + c * TT + a:sl * NCH * TT + c * TT + b]

    emit_ln_to_dram(em, cfg, xt, sq, xn32, xnb, srow, bank, c_ones, c_misc)
    em.barrier()

    for which, wname in ((0, "wq"), (1, "wk")):
        load_w(cfg[wname])
        for ti in range(NT):
            tc0 = ti * TT
            em.wait(sy, "PE")
            em.inc(sy.dma_start(out=xnb[:, :].rearrange("p (c l) -> p c l", c=NCH),
                                in_=xn_dram.ap().rearrange("c p l -> p c l")[:, :, tc0:tc0 + TT]),
                   "LD", 16)
            em.wait(t, "LD")
            em.wait(t, "LW")
            for dc in range(NCH):
                pb = dc % 4
                em.wait(t, "AC")
                for kc in range(NCH):
                    em.inc(t.matmul(bank(pb), wslice(kc, dc),
                                    xnb[:, kc * TT:(kc + 1) * TT],
                                    start=(kc == 0), stop=(kc == NCH - 1)), "PE")
                em.wait(sc, "PE")
                if which == 0:
                    dst = qt[:, dc * TT:(dc + 1) * TT]
                else:
                    dst = kf[:, dc * TLOC + tc0:dc * TLOC + tc0 + TT]
                em.inc(sc.activation(dst, bank(pb), AF.Sigmoid,
                                     bias=bpp[:, which * NCH + dc:which * NCH + dc + 1]),
                       "AC")
            if which == 0:
                em.wait(sy, "AC")
                em.inc(sy.dma_start(
                    out=qf_dram.ap().rearrange("c p l -> p c l")[:, :, tc0:tc0 + TT],
                    in_=qt[:, 0:NCH * TT].rearrange("p (c l) -> p c l", c=NCH)),
                    "ST", 16)
                em.wait(sc, "ST")
    em.barrier()

    for ti in range(NT):
        load_qt(ti, 0)
        for c in range(NCH):
            em.vstep(v.reduce_sum(part[:, (NCH + c) * 8 + ti:(NCH + c) * 8 + ti + 1],
                                  qsl(0, c), axis=mybir.AxisListType.X))
    for c in range(NCH):
        em.vstep(v.reduce_sum(pays[:, c:c + 1],
                              kf[:, c * TLOC:(c + 1) * TLOC],
                              axis=mybir.AxisListType.X))
        em.vstep(v.reduce_sum(pays[:, 8 + c:9 + c],
                              part[:, (NCH + c) * 8:(NCH + c) * 8 + 8],
                              axis=mybir.AxisListType.X))
    emit_pair_exchange(em, 0, pays, 0, 16)
    em.wait(v, "XD")
    em.vstep(v.tensor_add(gsum[:, 0:16], pays[:, 0:16], pasb[:, 0:16]))

    def build_wdot(srccol0, half):
        em.wait(v, "PE")
        em.vstep(v.memset(wdot[:, half * 128:(half + 1) * 128], 0.0))
        for c in range(NCH):
            base = half * 128 + c * 16
            em.vstep(v.tensor_scalar(wdot[0:64, base + 2 * c:base + 2 * c + 1],
                                     gsum[0:64, srccol0 + c:srccol0 + c + 1],
                                     EPS, None, OP.add))
            em.vstep(v.tensor_scalar(wdot[64:128, base + 2 * c + 1:base + 2 * c + 2],
                                     gsum[64:128, srccol0 + c:srccol0 + c + 1],
                                     EPS, None, OP.add))
        em.wait(t, "DV")
        for c in range(NCH):
            em.inc(t.matmul(bank(0, 0, 16, 0, 1),
                            wdot[:, half * 128 + c * 16:half * 128 + (c + 1) * 16],
                            onesb[:, 0:1], start=(c == 0), stop=(c == NCH - 1)), "PE")
        em.wait(v, "PE")
        em.vstep(v.tensor_scalar_mul(tiny[0:16, half:half + 1], bank(0, 0, 16, 0, 1), EPS))

    build_wdot(0, 0)
    build_wdot(8, 1)

    def trow(grp, a, b):
        return T1[0:16, grp * TLOC + a:grp * TLOC + b]

    def dot16(half, use_q, cecol, grp, act):
        for ti in range(NT):
            tc0 = ti * TT
            if use_q:
                load_qt(ti, ti % 2)
            em.wait(t, "DV")
            em.wait(t, "AC")
            for c in range(NCH):
                srcsl = qsl(ti % 2, c) if use_q else kf[:, c * TLOC + tc0:c * TLOC + tc0 + TT]
                em.inc(t.matmul(bank(1, 0, 16),
                                wdot[:, half * 128 + c * 16:half * 128 + (c + 1) * 16],
                                srcsl, start=(c == 0), stop=(c == NCH - 1)), "PE")
            em.wait(v, "PE")
            if act == "recip":
                em.vstep(v.tensor_scalar(csx[0:16, 0:TT], bank(1, 0, 16),
                                         tiny[0:16, cecol:cecol + 1], None, OP.add))
                em.vstep(v.reciprocal(trow(grp, tc0, tc0 + TT), csx[0:16, 0:TT]))
            elif act == "clipexp":
                em.vstep(v.tensor_scalar(csx[0:16, 0:TT], bank(1, 0, 16),
                                         tiny[0:16, cecol:cecol + 1], None, OP.add))
                em.vstep(v.tensor_scalar_min(csx[0:16, TT:2 * TT], csx[0:16, 0:TT], 1.0))
                em.vstep(v.tensor_scalar_max(csx[0:16, 0:TT], csx[0:16, TT:2 * TT], -1.0))
                em.wait(sc, "DV")
                em.inc(sc.activation(trow(grp, tc0, tc0 + TT), csx[0:16, 0:TT],
                                     AF.Exp), "AC")
                em.wait(v, "AC")
                em.vstep(v.reduce_sum(tiny[0:16, 8 + ti:9 + ti],
                                      trow(grp, tc0, tc0 + TT),
                                      axis=mybir.AxisListType.X))
            elif act == "sigmoid":
                em.wait(sc, "PE")
                em.inc(sc.activation(trow(grp, tc0, tc0 + TT), bank(1, 0, 16),
                                     AF.Sigmoid, bias=tiny[0:16, cecol:cecol + 1]), "AC")

    dot16(0, True, 0, 0, "recip")
    dot16(1, False, 1, 1, "recip")

    def wsum(use_q, grp, outcol):
        for ti in range(NT):
            tc0 = ti * TT
            if use_q:
                load_qt(ti, ti % 2)
            for c in range(NCH):
                em.wait(t, "DV")
                em.inc(t.matmul(bank(2), selp[:, c * 128:(c + 1) * 128],
                                trow(grp, tc0, tc0 + TT),
                                start=True, stop=True), "PE")
                em.wait(v, "PE")
                srcsl = qsl(ti % 2, c) if use_q else kf[:, c * TLOC + tc0:c * TLOC + tc0 + TT]
                em.vstep(v.tensor_mul(sq[:, (c % 2) * TT:(c % 2) * TT + TT], srcsl,
                                      bank(2)))
                em.vstep(v.reduce_sum(part[:, c * 8 + ti:c * 8 + ti + 1],
                                      sq[:, (c % 2) * TT:(c % 2) * TT + TT],
                                      axis=mybir.AxisListType.X))
        for c in range(NCH):
            em.vstep(v.reduce_sum(pays[:, outcol + c:outcol + c + 1],
                                  part[:, c * 8:c * 8 + 8],
                                  axis=mybir.AxisListType.X))

    wsum(False, 1, 16)
    wsum(True, 0, 24)
    emit_pair_exchange(em, 1, pays, 16, 16)
    em.wait(v, "XD")
    em.vstep(v.tensor_add(gsum[:, 16:32], pays[:, 16:32], pasb[:, 16:32]))

    build_wdot(16, 0)
    build_wdot(24, 1)
    dot16(0, True, 0, 1, "sigmoid")
    dot16(1, False, 1, 2, "clipexp")
    em.vstep(v.reduce_sum(pays[0:16, 32:33], tiny[0:16, 8:16],
                          axis=mybir.AxisListType.X))
    em.wait(v, "AC")
    em.vstep(v.tensor_mul(trow(0, 0, TLOC), trow(0, 0, TLOC), trow(1, 0, TLOC)))
    emit_pair_exchange(em, 2, pays, 32, 16)
    em.wait(v, "XD")
    em.vstep(v.tensor_add(tiny[0:16, 4:5], pays[0:16, 32:33], pasb[0:16, 32:33]))
    em.vstep(v.reciprocal(tiny[0:16, 5:6], tiny[0:16, 4:5]))
    em.vstep(v.tensor_scalar_mul(tiny[0:16, 6:7], tiny[0:16, 5:6], float(S)))
    em.vstep(v.tensor_scalar(trow(2, 0, TLOC), trow(2, 0, TLOC), tiny[0:16, 6:7],
                             None, OP.mult))
    em.barrier()

    load_w(cfg["wv"])
    for ti in range(NT):
        tc0 = ti * TT
        em.wait(sy, "PE")
        em.inc(sy.dma_start(out=xnb[:, :].rearrange("p (c l) -> p c l", c=NCH),
                            in_=xn_dram.ap().rearrange("c p l -> p c l")[:, :, tc0:tc0 + TT]),
               "LD", 16)
        em.wait(t, "LD")
        em.wait(t, "LW")
        for t4 in range(4):
            tc = ti * 4 + t4
            for nh in range(2):
                em.wait(t, "AC")
                em.wait(t, "DV")
                for kc in range(NCH):
                    t.matmul(bank(2 + nh), xnb[:, kc * TT + t4 * 128:kc * TT + (t4 + 1) * 128],
                             wsb[:, kc * D + nh * 512:kc * D + (nh + 1) * 512],
                             start=(kc == 0), stop=False)
                em.inc(t.matmul(bank(2 + nh), onesb[0:1, :],
                                brow16[0:1, 2 * D + nh * 512:2 * D + (nh + 1) * 512],
                                start=False, stop=True), "PE")
            em.wait(t, "DV")
            for dc in range(NCH):
                em.inc(t.transpose(PSb[:, dc * 128:(dc + 1) * 128],
                                   kf[:, dc * TLOC + tc * 128:dc * TLOC + (tc + 1) * 128],
                                   idb[:, :]), "PE")
            em.wait(v, "PE")
            em.vstep(v.tensor_copy(ktok[:, :], PSb[:, :]))
            em.wait(t, "DV")
            em.inc(t.transpose(PSb[:, 0:16],
                               trow(2, tc * 128, (tc + 1) * 128),
                               idb[0:16, 0:16]), "PE")
            em.wait(v, "PE")
            em.vstep(v.tensor_copy(sctk[:, (t4 % 2) * 16:(t4 % 2) * 16 + 16],
                                   PSb[:, 0:16]))
            for h in range(16):
                em.vstep(v.tensor_scalar(vsc[:, h * 64:(h + 1) * 64],
                                         bank(2 + h // 8, 0, 128, (h % 8) * 64, 64),
                                         sctk[:, (t4 % 2) * 16 + h:(t4 % 2) * 16 + h + 1],
                                         None, OP.mult))
            em.wait(t, "DV")
            for h in range(16):
                em.inc(t.matmul(bank(5, 64 * (h % 2), 64, (h // 2) * 64, 64),
                                ktok[:, h * 64:(h + 1) * 64], vsc[:, h * 64:(h + 1) * 64],
                                start=(tc == 0), stop=(tc == TCH - 1),
                                skip_group_check=True), "PE")
    em.wait(v, "PE")
    em.vstep(v.tensor_copy(pays[:, 48:560], bank(5)))
    emit_pair_exchange(em, 3, pays, 48, 512)
    em.wait(v, "XD")
    em.vstep(v.tensor_add(pays[:, 48:560], pays[:, 48:560], pasb[:, 48:560]))
    em.vstep(v.tensor_copy(kvb[:, :], pays[:, 48:560]))
    em.barrier()

    for ti in range(NT):
        tc0 = ti * TT
        load_qt(ti, ti % 2)
        for dc in range(NCH):
            em.wait(t, "DV")
            em.inc(t.matmul(bank(2), selp[:, dc * 128:(dc + 1) * 128],
                            trow(0, tc0, tc0 + TT), start=True, stop=True), "PE")
            em.wait(v, "PE")
            q_c = qsl(ti % 2, dc)
            em.vstep(v.tensor_mul(q_c, q_c, bank(2)))
            em.wait(t, "DV")
            em.inc(t.matmul(bank(6, 0, 64), kvb[0:64, dc * 64:(dc + 1) * 64],
                            q_c[0:64, :], start=True, stop=True), "PE")
            em.inc(t.matmul(bank(6, 64, 64), kvb[64:128, dc * 64:(dc + 1) * 64],
                            q_c[64:128, :], start=True, stop=True), "PE")
            em.wait(v, "PE")
            em.vstep(v.tensor_copy(kf[:, dc * TLOC + tc0:dc * TLOC + tc0 + TT],
                                   bank(6)))
    em.barrier()

    load_w(cfg["wo"])
    for ti in range(NT):
        tc0 = ti * TT
        em.wait(sy, "DV")
        em.wait(sy, "PE")
        for c in range(NCH):
            em.inc(sy.dma_start(out=xt[:, c * TT:(c + 1) * TT],
                                in_=x_in[c, :, tc0:tc0 + TT]), "LD", 16)
        em.wait(t, "LD")
        em.wait(t, "LW")
        for dc in range(NCH):
            pb = dc % 2
            em.wait(t, "DV")
            for kc in range(NCH):
                em.inc(t.matmul(bank(pb), wslice(kc, dc),
                                kf[:, kc * TLOC + tc0:kc * TLOC + tc0 + TT],
                                start=(kc == 0), stop=(kc == NCH - 1)), "PE")
            em.wait(v, "PE")
            em.wait(v, "ST")
            sl = dc % 2
            em.vstep(v.tensor_scalar(sq[:, sl * TT:(sl + 1) * TT], bank(pb),
                                     bpp[:, 3 * NCH + dc:3 * NCH + dc + 1], None,
                                     OP.add))
            em.vstep(v.tensor_add(sq[:, sl * TT:(sl + 1) * TT],
                                  sq[:, sl * TT:(sl + 1) * TT],
                                  xt[:, dc * TT:(dc + 1) * TT]))
            em.wait(sy, "DV")
            em.inc(sy.dma_start(out=x_out[dc, :, tc0:tc0 + TT],
                                in_=sq[:, sl * TT:(sl + 1) * TT]), "ST", 16)
    em.wait(sy, "ST")


def build_attn0(dbg=False):
    nc = bass.Bass("TRN2", target_bir_lowering=False)
    dr = {}
    dr["x_in"] = nc.dram_tensor("x_in", [NCH, 128, TLOC], F32, kind="ExternalInput")
    dr["x_out"] = nc.dram_tensor("x_out", [NCH, 128, TLOC], F32, kind="ExternalOutput")
    for nm in ["wq", "wk", "wv", "wo"]:
        dr[nm] = nc.dram_tensor(nm, [D, D], F32, kind="ExternalInput")
    dr["b_rows"] = nc.dram_tensor("b_rows", [1, 4 * D], F32, kind="ExternalInput")
    dr["b_pp"] = nc.dram_tensor("b_pp", [128, 4 * NCH], F32, kind="ExternalInput")
    dr["c_ones"] = nc.dram_tensor("c_ones", [128, 128], F32, kind="ExternalInput")
    dr["c_id"] = nc.dram_tensor("c_id", [128, 128], F32, kind="ExternalInput")
    dr["c_sel"] = nc.dram_tensor("c_sel", [8, 8 * 128], BF16, kind="ExternalInput")
    dr["c_misc"] = nc.dram_tensor("c_misc", [128, 8], F32, kind="ExternalInput")
    dr["c_selp"] = nc.dram_tensor("c_selp", [16, NCH * 128], BF16, kind="ExternalInput")
    dr["c_onesb"] = nc.dram_tensor("c_onesb", [128, 128], BF16, kind="ExternalInput")
    dr["c_idb"] = nc.dram_tensor("c_idb", [128, 128], BF16, kind="ExternalInput")
    dr["meta"] = nc.dram_tensor("meta", [1, 16], I32, kind="ExternalInput")
    dr["nonce"] = nc.dram_tensor("nonce", [1, 8], I32, kind="ExternalInput")
    dr["xn_dram"] = nc.dram_tensor("xn_scr", [NCH, 128, TLOC], BF16,
                                   kind="ExternalOutput" if dbg else "Internal")
    dr["qf_dram"] = nc.dram_tensor("qf_scr", [NCH, 128, TLOC], BF16,
                                   kind="ExternalOutput" if dbg else "Internal")
    dr["PAYC"] = 560
    dr["shm_pay"] = nc.dram_tensor("shm_pay", [2, 128, 560], F32, addr_space="Shared")
    dr["shm_fl"] = nc.dram_tensor("shm_fl", [2, 8], I32, addr_space="Shared")
    with ExitStack() as stack:
        em = Em(nc, stack)
        cfg = dict(dr)
        cfg["tag"] = "a"
        with nc.allow_low_precision(reason="bf16 operand staging is intentional"):
            emit_attn0(em, cfg)
    return nc


def xchg_host_inputs(rng, col0s=None):
    PAYC = 560
    metas, nonces = [], []
    nonce_vals = rng.integers(1 << 20, 1 << 30, size=8).astype(np.int32)
    for c in range(NCORES):
        m = c % 2
        p = 1 - m
        cols = col0s if col0s is not None else [0, 16, 32, 48]
        meta = np.zeros((1, 16), np.int32)
        for r in range(4):
            meta[0, r] = p * 128 * PAYC + cols[r]
            meta[0, 4 + r] = m * 128 * PAYC + cols[r]
            meta[0, 8 + r] = p * 8 + r
            meta[0, 12 + r] = m * 8 + r
        metas.append(meta)
        nonces.append(nonce_vals[None, :])
    return metas, nonces


def attn0_in_map(inputs, xcore, meta, nonce):
    c_ones, c_id, c_sel, c_misc = consts_f()
    selp = np.zeros((16, NCH * 128), BF)
    for c in range(NCH):
        selp[2 * c, c * 128:c * 128 + 64] = 1
        selp[2 * c + 1, c * 128 + 64:(c + 1) * 128] = 1
    onesb = np.ones((128, 128), BF)
    idb = np.eye(128, dtype=BF)
    bpp = np.zeros((128, 4 * NCH), np.float32)
    rows = np.zeros((1, 4 * D), np.float32)
    for i, nm in enumerate(["bq", "bk", "bv", "bo"]):
        b = inputs[f"L0_attn_{nm}"]
        rows[0, i * D:(i + 1) * D] = b
        bpp[:, i * NCH:(i + 1) * NCH] = b.reshape(NCH, 128).T
    return dict(
        x_in=xcore,
        wq=inputs["L0_attn_Wq"], wk=inputs["L0_attn_Wk"],
        wv=inputs["L0_attn_Wv"], wo=inputs["L0_attn_Wo"],
        b_rows=rows, b_pp=bpp,
        c_ones=c_ones, c_id=c_id, c_sel=c_sel, c_misc=c_misc,
        c_selp=selp, c_onesb=onesb, c_idb=idb, meta=meta, nonce=nonce,
    )


def run_attn0(inputs, xcores):
    nc = _get_prog("attn0", build_attn0)
    rng = np.random.default_rng()
    metas, nonces = xchg_host_inputs(rng)
    maps = [attn0_in_map(inputs, xcores[c], metas[c], nonces[c])
            for c in range(NCORES)]
    res = run_bass_kernel_spmd(nc, maps, core_ids=list(range(NCORES)))
    return [res.results[c]["x_out"] for c in range(NCORES)]



def emit_attn1(em, cfg):
    nc, g, v, sc, t, sy = em.nc, em.g, em.v, em.sc, em.t, em.sy
    c_ones, c_id, c_sel, c_misc = em.load_consts(cfg)
    x_in, x_out, xn_dram = cfg["x_in"], cfg["x_out"], cfg["xn_dram"]
    qcd, kcd, vcd, ogd = cfg["qc_dram"], cfg["kc_dram"], cfg["vc_dram"], cfg["og_dram"]
    Ed, Pd, Hd = cfg["E_dram"], cfg["P_dram"], cfg["H_dram"]
    tag = cfg["tag"]
    dbg = cfg.get("dbg")
    sb = lambda name, shape, dt=F32: em.stack.enter_context(
        nc.sbuf_tensor(tag + name, shape, dt))
    CJ = TLOC // 128
    SEG = TT + 3
    GN_EPS = 1e-5
    xt = sb("xt", [128, NCH * SEG])
    pre = xt
    sq = sb("sq", [128, 2 * TT])
    xn32 = sb("xn32", [128, 2 * TT])
    xnb = sb("xnb", [128, NCH * TT], BF16)
    srow = sb("srow", [1, 4 * TT])
    silb = sb("silb", [128, NCH * TT], BF16)
    tailp = sb("tailp", [128, NCH * 3])
    tailx = sb("tailx", [128, NCH * 3], BF16)
    wsb = sb("wsb", [128, NCH * D], BF16)
    wab = sb("wab", [128, 2 * NCH * 16], BF16)
    cvpp = sb("cvpp", [128, 3 * NCH * 4])
    bpp = sb("bpp", [128, 5 * NCH])
    abb = sb("abb", [16, 2])
    gnpp = sb("gnpp", [128, 2 * NCH])
    ghmc = sb("ghmc", [16, 128])
    abrow = sb("abrow", [16, 2 * TT])
    lab = sb("lab", [48, TT])
    strx = sb("strx", [128, 48])
    gtr = sb("gtr", [128, 16 * CJ])
    btr = sb("btr", [128, 16 * CJ])
    dtr = sb("dtr", [128, 16 * CJ])
    actr = sb("actr", [128, 16 * CJ])
    batr = sb("batr", [128, 16 * CJ])
    csx = sb("csx", [16, 2 * TT])
    outs = sb("outs", [128, NCH * TLOC], BF16)
    etst = sb("etst", [128, NCH * 128], BF16)
    ptst = sb("ptst", [128, NCH * 64], BF16)
    hst = sb("hst", [128, NCH * 64], BF16)
    qct = sb("qct", [128, NCH * 128], BF16)
    kct = sb("kct", [128, NCH * 128], BF16)
    vct = sb("vct", [128, NCH * 128], BF16)
    Dsb = sb("Dsb", [128, 128])
    esb = sb("esb", [128, 2 * 128])
    lams = sb("lams", [128, 128], BF16)
    lamu = sb("lamu", [128, 128], BF16)
    absb = sb("absb", [128, 128], BF16)
    Msb = sb("Msb", [128, 128], BF16)
    Mtsb = sb("Mtsb", [128, 128], BF16)
    wtsb = sb("wtsb", [128, 128], BF16)
    Tb = sb("Tb", [128, 2 * 128], BF16)
    Ttb = sb("Ttb", [128, 2 * 128], BF16)
    Pb = sb("Pb", [128, 2 * 128], BF16)
    Ptb = sb("Ptb", [128, 2 * 128], BF16)
    vbsb = sb("vbsb", [128, 64], BF16)
    kpsb = sb("kpsb", [128, 64], BF16)
    kbasb = sb("kbasb", [128, 64], BF16)
    uvsb = sb("uvsb", [128, 64], BF16)
    zsb = sb("zsb", [128, 64], BF16)
    qasb = sb("qasb", [128, 128], BF16)
    st32 = sb("st32", [128, 512])
    stbb = sb("stbb", [128, 512], BF16)
    eld = sb("eld", [128, NCH * 128], BF16)
    pld = sb("pld", [128, NCH * 64], BF16)
    hld = sb("hld", [128, NCH * 64], BF16)
    gnb = sb("gnb", [128, 4 * NCH])
    v16 = sb("v16", [16, 16])
    pays = sb("pays", [128, 560])
    pasb = sb("pasb", [128, 560])
    em.x_pasb = pasb
    selp = sb("selp", [16, NCH * 128], BF16)
    selpf = sb("selpf", [16, NCH * 128])
    s16f = sb("s16f", [16, 16 * 128])
    cut = sb("cut", [128, 128])
    clt = sb("clt", [128, 128])
    cmsl = sb("cmsl", [128, 128])
    cmsu = sb("cmsu", [128, 128])
    chm = sb("chm", [128, NCH * 16])
    cpat = sb("cpat", [16, 8])
    idb = sb("idb", [128, 128], BF16)
    xmk = sb("xmk", [128, 1])
    PS = em.stack.enter_context(nc.psum_tensor(tag + "PS", [128, 7 * 512], F32))
    PSb = em.stack.enter_context(nc.psum_tensor(tag + "PSb", [128, 512], BF16))

    def bank(b, p0=0, pn=128, c0=0, cn=512):
        return PS[p0:p0 + pn, b * 512 + c0:b * 512 + c0 + cn]

    emit_xchg_setup(em, cfg)
    for nm, dst in [("c_selp", selp), ("c_selpf", selpf), ("c_s16f", s16f),
                    ("c_ut", cut), ("c_lt", clt), ("c_msl", cmsl),
                    ("c_msu", cmsu), ("c_hm", chm), ("c_pat", cpat),
                    ("c_idb", idb), ("xmask", xmk), ("b_pp", bpp),
                    ("ab_bias", abb), ("gn_pp", gnpp), ("cv_pp", cvpp)]:
        em.inc(sy.dma_start(out=dst[0:dst.shape[0], :], in_=cfg[nm][:, :]), "LD", 16)
    for kc in range(NCH):
        em.inc(g.dma_start(out=wab[:, kc * 16:(kc + 1) * 16],
                           in_=cfg["wa"][kc * 128:(kc + 1) * 128, :]), "LW", 16)
        em.inc(g.dma_start(out=wab[:, 128 + kc * 16:128 + (kc + 1) * 16],
                           in_=cfg["wb_"][kc * 128:(kc + 1) * 128, :]), "LW", 16)

    def load_w(w):
        em.wait(g, "PE")
        for kc in range(NCH):
            em.inc(g.dma_start(out=wsb[:, kc * D:(kc + 1) * D],
                               in_=w[kc * 128:(kc + 1) * 128, :]), "LW", 16)

    def wslice(kc, dc):
        return wsb[:, kc * D + dc * 128:kc * D + (dc + 1) * 128]

    def load_xn(ti):
        em.wait(sy, "PE")
        em.inc(sy.dma_start(out=xnb[:, :].rearrange("p (c l) -> p c l", c=NCH),
                            in_=xn_dram.ap().rearrange("c p l -> p c l")[:, :, ti * TT:(ti + 1) * TT]),
               "LD", 16)
        em.wait(t, "LD")

    emit_ln_to_dram(em, cfg, xt, sq, xn32, xnb, srow, bank, c_ones, c_misc)
    em.wait(v, "ST")
    for c in range(NCH):
        em.vstep(v.tensor_copy(pays[:, c * 3:(c + 1) * 3],
                               xnb[:, c * TT + TT - 3:c * TT + TT]))
    emit_pair_exchange(em, 0, pays, 0, 24)
    em.wait(v, "XD")
    em.vstep(v.tensor_scalar(pays[:, 24:48], pasb[:, 0:24], xmk[:, 0:1], None,
                             OP.mult))
    em.vstep(v.tensor_copy(tailx[:, :], pays[:, 24:48]))
    em.barrier()

    for ti in range(NT):
        tc0 = ti * TT
        load_xn(ti)
        em.wait(t, "LW")
        em.wait(t, "AC")
        em.wait(t, "DV")
        for half in range(2):
            for kc in range(NCH):
                em.inc(t.matmul(bank(3 + half, 0, 16),
                                wab[:, half * 128 + kc * 16:half * 128 + (kc + 1) * 16],
                                xnb[:, kc * TT:(kc + 1) * TT],
                                start=(kc == 0), stop=(kc == NCH - 1)), "PE")
        em.wait(sc, "PE")
        em.inc(sc.activation(abrow[0:16, 0:TT], bank(3, 0, 16), AF.Sigmoid,
                             bias=abb[0:16, 0:1]), "AC")
        em.inc(sc.activation(abrow[0:16, TT:2 * TT], bank(4, 0, 16), AF.Sigmoid,
                             bias=abb[0:16, 1:2]), "AC")
        if dbg:
            em.wait(sy, "AC")
            em.inc(sy.dma_start(out=cfg["dbg_ab"][0, :, tc0:tc0 + TT],
                                in_=abrow[0:16, 0:TT]), "ST", 16)
            em.inc(sy.dma_start(out=cfg["dbg_ab"][1, :, tc0:tc0 + TT],
                                in_=abrow[0:16, TT:2 * TT]), "ST", 16)
        em.wait(sc, "AC")
        em.inc(sc.activation(lab[0:16, :], abrow[0:16, 0:TT], AF.Ln), "AC")
        em.wait(v, "AC")
        em.vstep(v.tensor_copy(lab[32:48, :], abrow[0:16, TT:2 * TT]))
        for j4 in range(4):
            j = ti * 4 + j4
            em.wait(t, "DV")
            em.wait(t, "AC")
            em.inc(t.transpose(bank(5, 0, 128, 0, 48), lab[0:48, j4 * 128:(j4 + 1) * 128],
                               c_id[0:48, 0:48]), "PE")
            em.wait(v, "PE")
            em.vstep(v.tensor_copy(strx[:, :], bank(5, 0, 128, 0, 48)))
            em.vstep(v.tensor_copy(btr[:, j * 16:(j + 1) * 16], strx[:, 32:48]))
            em.wait(t, "DV")
            em.inc(t.matmul(bank(5, 0, 128, 64, 16), cut[:, :], strx[:, 0:16],
                            start=True, stop=True), "PE")
            em.inc(t.matmul(bank(5, 0, 128, 96, 16), clt[:, :], strx[:, 0:16],
                            start=True, stop=True), "PE")
            em.wait(v, "PE")
            em.vstep(v.tensor_copy(gtr[:, j * 16:(j + 1) * 16],
                                   bank(5, 0, 128, 64, 16)))
            em.wait(sc, "DV")
            em.inc(sc.activation(dtr[:, j * 16:(j + 1) * 16],
                                 bank(5, 0, 128, 96, 16), AF.Exp), "AC")
            em.vstep(v.tensor_add(strx[:, 16:32], gtr[:, j * 16:(j + 1) * 16],
                                  bank(5, 0, 128, 96, 16)))
            em.wait(sc, "DV")
            em.wait(sc, "PE")
            em.inc(sc.activation(actr[:, j * 16:(j + 1) * 16], strx[:, 16:32],
                                 AF.Exp), "AC")
            em.inc(sc.activation(strx[:, 0:16], gtr[:, j * 16:(j + 1) * 16],
                                 AF.Exp), "AC")
            em.wait(v, "AC")
            em.vstep(v.tensor_mul(batr[:, j * 16:(j + 1) * 16],
                                  btr[:, j * 16:(j + 1) * 16], strx[:, 0:16]))
    em.barrier()

    for which, wname, dramt in ((0, "wq", qcd), (1, "wk", kcd), (2, "wv", vcd)):
        load_w(cfg[wname])
        em.wait(t, "LW")
        em.wait(t, "DV")
        for dc in range(NCH):
            for kc in range(NCH):
                em.inc(t.matmul(bank(4, 0, 128, dc * 3, 3), wslice(kc, dc),
                                tailx[:, kc * 3:(kc + 1) * 3],
                                start=(kc == 0), stop=(kc == NCH - 1)), "PE")
        em.wait(v, "PE")
        for dc in range(NCH):
            em.vstep(v.tensor_scalar(tailp[:, dc * 3:(dc + 1) * 3],
                                     bank(4, 0, 128, dc * 3, 3),
                                     bpp[:, which * NCH + dc:which * NCH + dc + 1],
                                     None, OP.add))
        em.vstep(v.tensor_scalar(tailp[:, :], tailp[:, :], xmk[:, 0:1], None,
                                 OP.mult))
        for c in range(NCH):
            em.vstep(v.tensor_copy(pre[:, c * SEG:c * SEG + 3],
                                   tailp[:, c * 3:(c + 1) * 3]))
        for ti in range(NT):
            tc0 = ti * TT
            load_xn(ti)
            em.wait(sc, "ST")
            em.wait(v, "ST")
            for dc in range(NCH):
                pb = dc % 2
                em.wait(t, "DV")
                for kc in range(NCH):
                    em.inc(t.matmul(bank(pb), wslice(kc, dc),
                                    xnb[:, kc * TT:(kc + 1) * TT],
                                    start=(kc == 0), stop=(kc == NCH - 1)), "PE")
                em.wait(v, "PE")
                em.vstep(v.tensor_scalar(pre[:, dc * SEG + 3:dc * SEG + SEG],
                                         bank(pb),
                                         bpp[:, which * NCH + dc:which * NCH + dc + 1],
                                         None, OP.add))
            for c in range(NCH):
                for j in range(4):
                    cwcol = cvpp[:, which * 32 + c * 4 + j:which * 32 + c * 4 + j + 1]
                    if j == 0:
                        em.vstep(v.tensor_scalar(sq[:, 0:TT],
                                                 pre[:, c * SEG + j:c * SEG + j + TT],
                                                 cwcol, None, OP.mult))
                    else:
                        em.vstep(v.scalar_tensor_tensor(sq[:, 0:TT],
                                                        pre[:, c * SEG + j:c * SEG + j + TT],
                                                        cwcol, sq[:, 0:TT],
                                                        OP.mult, OP.add))
                em.wait(sc, "DV")
                em.inc(sc.activation(silb[:, c * TT:(c + 1) * TT], sq[:, 0:TT],
                                     AF.Silu), "AC")
                em.wait(v, "AC")
            for c in range(NCH):
                em.vstep(v.tensor_copy(pre[:, c * SEG:c * SEG + 3],
                                       pre[:, c * SEG + SEG - 3:c * SEG + SEG]))
            if which < 2:
                em.wait(t, "AC")
                em.wait(t, "DV")
                for c in range(NCH):
                    em.wait(sc, "PE")
                    em.inc(sc.activation(sq[:, (c % 2) * TT:(c % 2) * TT + TT],
                                         silb[:, c * TT:(c + 1) * TT], AF.Square),
                           "AC")
                    em.wait(t, "AC")
                    em.inc(t.matmul(bank(2, 0, 16), chm[:, c * 16:(c + 1) * 16],
                                    sq[:, (c % 2) * TT:(c % 2) * TT + TT],
                                    start=(c == 0), stop=(c == NCH - 1)), "PE")
                em.wait(sc, "PE")
                em.inc(sc.activation(csx[0:16, 0:TT], bank(2, 0, 16), AF.Sqrt), "AC")
                em.wait(v, "AC")
                em.vstep(v.tensor_scalar(csx[0:16, TT:2 * TT], csx[0:16, 0:TT],
                                         1e-6, None, OP.add))
                em.vstep(v.reciprocal(csx[0:16, 0:TT], csx[0:16, TT:2 * TT]))
                em.vstep(v.tensor_copy(abrow[0:16, 0:TT], csx[0:16, 0:TT]))
                for c in range(NCH):
                    em.wait(t, "DV")
                    em.inc(t.matmul(bank(2), selpf[:, c * 128:(c + 1) * 128],
                                    abrow[0:16, 0:TT], start=True, stop=True), "PE")
                    em.wait(v, "PE")
                    em.vstep(v.tensor_mul(silb[:, c * TT:(c + 1) * TT],
                                          silb[:, c * TT:(c + 1) * TT], bank(2)))
            em.wait(sy, "DV")
            em.inc(sy.dma_start(
                out=dramt.ap().rearrange("c p l -> p c l")[:, :, tc0:tc0 + TT],
                in_=silb[:, :].rearrange("p (c l) -> p c l", c=NCH)), "ST", 16)
    em.barrier()

    em.vstep(v.memset(st32[:, :], 0.0))
    em.vstep(v.memset(stbb[:, :], 0.0))
    for j in range(CJ):
        em.wait(sy, "PE")
        em.wait(sy, "DV")
        for dramt, dst in ((qcd, qct), (kcd, kct), (vcd, vct)):
            em.inc(sy.dma_start(
                out=dst[:, :].rearrange("p (c l) -> p c l", c=NCH),
                in_=dramt.ap().rearrange("c p l -> p c l")[:, :, j * 128:(j + 1) * 128]),
                "LD", 16)
        em.wait(t, "LD")
        em.wait(v, "LD")
        em.wait(t, "DV")
        em.inc(t.transpose(bank(0, 0, 16, 384, 128), gtr[:, j * 16:(j + 1) * 16],
                           c_id[:, :]), "PE")
        em.wait(v, "PE")
        em.vstep(v.tensor_copy(ghmc[0:16, :], bank(0, 0, 16, 384, 128)))
        for h in range(16):
            p0 = 64 * (h % 2)
            cc = (h // 2) * 128
            QT = qct[p0:p0 + 64, cc:cc + 128]
            KT = kct[p0:p0 + 64, cc:cc + 128]
            VT = vct[p0:p0 + 64, cc:cc + 128]
            gcol = gtr[:, j * 16 + h:j * 16 + h + 1]
            bcol = btr[:, j * 16 + h:j * 16 + h + 1]
            dcol = dtr[:, j * 16 + h:j * 16 + h + 1]
            bacol = batr[:, j * 16 + h:j * 16 + h + 1]
            accol = actr[0:64, j * 16 + h:j * 16 + h + 1]
            em.wait(t, "DV")
            em.wait(t, "AC")
            em.inc(t.matmul(bank(0, 0, 128, 0, 128), KT, KT, start=True, stop=True), "PE")
            em.inc(t.matmul(bank(0, 0, 128, 128, 128), KT, QT, start=True, stop=True), "PE")
            em.inc(t.matmul(bank(0, 0, 128, 256, 128), s16f[:, h * 128:(h + 1) * 128],
                            ghmc[0:16, :], start=True, stop=True), "PE")
            em.wait(v, "PE")
            em.vstep(v.tensor_scalar(Dsb[:, :], bank(0, 0, 128, 256, 128), gcol,
                                     None, OP.subtract))
            em.vstep(v.tensor_sub(esb[:, 0:128], cmsl[:, :], Dsb[:, :]))
            em.vstep(v.tensor_add(esb[:, 128:256], cmsu[:, :], Dsb[:, :]))
            em.wait(sc, "DV")
            em.inc(sc.activation(lams[:, :], esb[:, 0:128], AF.Exp), "AC")
            em.inc(sc.activation(lamu[:, :], esb[:, 128:256], AF.Exp), "AC")
            em.inc(sc.activation(absb[:, :], bank(0, 0, 128, 256, 128), AF.Exp), "AC")
            em.wait(v, "AC")
            em.vstep(v.scalar_tensor_tensor(Msb[:, :], bank(0, 0, 128, 0, 128),
                                            bcol, lams[:, :], OP.mult, OP.mult))
            em.vstep(v.tensor_mul(wtsb[:, :], lamu[:, :], bank(0, 0, 128, 128, 128)))
            em.vstep(v.tensor_mul(qasb[:, 0:128], QT, absb[p0:p0 + 64, :])
                     if False else
                     v.tensor_mul(qasb[p0:p0 + 64, 0:128], QT, absb[p0:p0 + 64, :]))
            em.wait(t, "DV")
            em.inc(t.transpose(PSb[:, 0:128], Msb[:, :], idb[:, :]), "PE")
            em.inc(t.transpose(PSb[:, 128:192], VT, idb[p0:p0 + 64, p0:p0 + 64]), "PE")
            em.inc(t.transpose(PSb[:, 192:256], KT, idb[p0:p0 + 64, p0:p0 + 64]), "PE")
            em.wait(v, "PE")
            em.vstep(v.tensor_copy(Mtsb[:, :], PSb[:, 0:128]))
            em.vstep(v.tensor_scalar(vbsb[:, :], PSb[:, 128:192], bcol, None, OP.mult))
            em.vstep(v.tensor_scalar(kpsb[:, :], PSb[:, 192:256], dcol, None, OP.mult))
            em.vstep(v.tensor_scalar(kbasb[:, :], PSb[:, 192:256], bacol, None, OP.mult))
            em.vstep(v.tensor_sub(Tb[:, 0:128], idb[:, :], Msb[:, :]))
            em.vstep(v.tensor_sub(Ttb[:, 0:128], idb[:, :], Mtsb[:, :]))
            em.wait(t, "DV")
            em.inc(t.matmul(bank(1, 0, 128, 0, 128), Mtsb[:, :], Msb[:, :],
                            start=True, stop=True), "PE")
            em.inc(t.matmul(bank(1, 0, 128, 128, 128), Msb[:, :], Mtsb[:, :],
                            start=True, stop=True), "PE")
            em.wait(v, "PE")
            em.vstep(v.tensor_copy(Pb[:, 0:128], bank(1, 0, 128, 0, 128)))
            em.vstep(v.tensor_copy(Ptb[:, 0:128], bank(1, 0, 128, 128, 128)))
            for lev in range(1, 7):
                s0 = ((lev - 1) % 2) * 128
                s1 = (lev % 2) * 128
                em.wait(t, "DV")
                em.inc(t.matmul(bank(1, 0, 128, 0, 128), Ptb[:, s0:s0 + 128],
                                Tb[:, s0:s0 + 128], start=True, stop=True), "PE")
                em.inc(t.matmul(bank(1, 0, 128, 128, 128), Tb[:, s0:s0 + 128],
                                Ptb[:, s0:s0 + 128], start=True, stop=True), "PE")
                if lev < 6:
                    em.inc(t.matmul(bank(2, 0, 128, 0, 128), Ptb[:, s0:s0 + 128],
                                    Pb[:, s0:s0 + 128], start=True, stop=True), "PE")
                    em.inc(t.matmul(bank(2, 0, 128, 128, 128), Pb[:, s0:s0 + 128],
                                    Ptb[:, s0:s0 + 128], start=True, stop=True), "PE")
                em.wait(v, "PE")
                em.vstep(v.tensor_add(Tb[:, s1:s1 + 128], bank(1, 0, 128, 0, 128),
                                      Tb[:, s0:s0 + 128]))
                em.vstep(v.tensor_add(Ttb[:, s1:s1 + 128], bank(1, 0, 128, 128, 128),
                                      Ttb[:, s0:s0 + 128]))
                if lev < 6:
                    em.vstep(v.tensor_copy(Pb[:, s1:s1 + 128], bank(2, 0, 128, 0, 128)))
                    em.vstep(v.tensor_copy(Ptb[:, s1:s1 + 128], bank(2, 0, 128, 128, 128)))
            tts = Ttb[:, (6 % 2) * 128:(6 % 2) * 128 + 128]
            em.wait(t, "DV")
            em.inc(t.matmul(bank(1, 0, 128, 256, 64), tts, vbsb[:, :],
                            start=True, stop=True), "PE")
            em.inc(t.matmul(bank(1, 0, 128, 320, 64), tts, kbasb[:, :],
                            start=True, stop=True), "PE")
            em.wait(v, "PE")
            em.vstep(v.tensor_copy(uvsb[:, :], bank(1, 0, 128, 256, 64)))
            em.vstep(v.tensor_copy(zsb[:, :], bank(1, 0, 128, 320, 64)))
            em.wait(t, "DV")
            em.inc(t.matmul(bank(2, 0, 64, 0, 128), uvsb[:, :], wtsb[:, :],
                            start=True, stop=True), "PE")
            em.inc(t.matmul(bank(2, 0, 64, 128, 128), zsb[:, :], wtsb[:, :],
                            start=True, stop=True), "PE")
            em.inc(t.matmul(bank(2, 0, 64, 256, 64), zsb[:, :], kpsb[:, :],
                            start=True, stop=True), "PE")
            em.inc(t.matmul(bank(2, 0, 64, 320, 64), kpsb[:, :], uvsb[:, :],
                            start=True, stop=True), "PE")
            em.wait(v, "PE")
            ocol = (h // 2) * TLOC + j * 128
            em.vstep(v.tensor_copy(outs[p0:p0 + 64, ocol:ocol + 128],
                                   bank(2, 0, 64, 0, 128)))
            em.vstep(v.tensor_sub(etst[p0:p0 + 64, cc:cc + 128],
                                  qasb[p0:p0 + 64, 0:128],
                                  bank(2, 0, 64, 128, 128)))
            em.vstep(v.scalar_tensor_tensor(ptst[p0:p0 + 64, (h // 2) * 64:(h // 2) * 64 + 64],
                                            idb[0:64, 0:64], accol,
                                            bank(2, 0, 64, 256, 64),
                                            OP.mult, OP.subtract))
            em.vstep(v.tensor_copy(hst[p0:p0 + 64, (h // 2) * 64:(h // 2) * 64 + 64],
                                   bank(2, 0, 64, 320, 64)))
        em.wait(sy, "DV")
        em.inc(sy.dma_start(out=Ed[j, :, :], in_=etst[:, :]), "ST", 16)
        em.inc(sy.dma_start(out=Pd[j, :, :], in_=ptst[:, :]), "ST", 16)
        em.inc(sy.dma_start(out=Hd[j, :, :], in_=hst[:, :]), "ST", 16)
    em.barrier()

    def state_pass(correction):
        for j in range(CJ):
            em.wait(sy, "PE")
            em.inc(sy.dma_start(out=eld[:, :], in_=Ed[j, :, :]), "LD", 16)
            em.inc(sy.dma_start(out=pld[:, :], in_=Pd[j, :, :]), "LD", 16)
            if not correction:
                em.inc(sy.dma_start(out=hld[:, :], in_=Hd[j, :, :]), "LD", 16)
            em.wait(t, "LD")
            for h in range(16):
                p0 = 64 * (h % 2)
                c64 = (h // 2) * 64
                cc = (h // 2) * 128
                stb_h = stbb[p0:p0 + 64, c64:c64 + 64]
                et_h = eld[p0:p0 + 64, cc:cc + 128]
                pt_h = pld[p0:p0 + 64, c64:c64 + 64]
                em.wait(t, "DV")
                em.inc(t.matmul(bank(3, p0, 64, 0, 128), stb_h, et_h,
                                start=True, stop=True), "PE")
                em.inc(t.matmul(bank(3, p0, 64, 128, 64), pt_h, stb_h,
                                start=True, stop=True), "PE")
                em.wait(v, "PE")
                ocol = (h // 2) * TLOC + j * 128
                em.vstep(v.tensor_add(outs[p0:p0 + 64, ocol:ocol + 128],
                                      outs[p0:p0 + 64, ocol:ocol + 128],
                                      bank(3, p0, 64, 0, 128)))
                if not correction:
                    em.vstep(v.tensor_add(st32[p0:p0 + 64, c64:c64 + 64],
                                          bank(3, p0, 64, 128, 64),
                                          hld[p0:p0 + 64, c64:c64 + 64]))
                else:
                    em.vstep(v.tensor_copy(st32[p0:p0 + 64, c64:c64 + 64],
                                           bank(3, p0, 64, 128, 64)))
                em.vstep(v.tensor_copy(stbb[p0:p0 + 64, c64:c64 + 64],
                                       st32[p0:p0 + 64, c64:c64 + 64]))

    state_pass(False)
    em.vstep(v.tensor_copy(pays[:, 48:560], st32[:, 0:512]))
    emit_pair_exchange(em, 1, pays, 48, 512)
    em.wait(v, "XD")
    em.vstep(v.tensor_scalar(st32[:, 0:512], pasb[:, 48:560], xmk[:, 0:1],
                             None, OP.mult))
    em.vstep(v.tensor_copy(stbb[:, 0:512], st32[:, 0:512]))
    state_pass(True)
    em.barrier()
    if dbg:
        em.inc(sy.dma_start(out=cfg["dbg_outs"].ap().rearrange("c p l -> p c l"),
                            in_=outs[:, :].rearrange("p (c l) -> p c l", c=NCH)),
               "ST", 16)
        em.wait(sy, "ST")

    for c in range(NCH):
        em.vstep(v.reduce_sum(gnb[:, c:c + 1], outs[:, c * TLOC:(c + 1) * TLOC],
                              axis=mybir.AxisListType.X))
    for ti in range(NT):
        for c in range(NCH):
            em.wait(sc, "DV")
            em.inc(sc.activation(sq[:, (c % 2) * TT:(c % 2) * TT + TT],
                                 outs[:, c * TLOC + ti * TT:c * TLOC + ti * TT + TT],
                                 AF.Square), "AC")
            em.wait(v, "AC")
            em.vstep(v.reduce_sum(pays[:, 40 + c:40 + c + 1] if ti == 0 else xn32[:, 0:1],
                                  sq[:, (c % 2) * TT:(c % 2) * TT + TT],
                                  axis=mybir.AxisListType.X))
            if ti > 0:
                em.vstep(v.tensor_add(pays[:, 40 + c:40 + c + 1],
                                      pays[:, 40 + c:40 + c + 1], xn32[:, 0:1]))
    em.wait(t, "DV")
    for c in range(NCH):
        em.inc(t.matmul(bank(4, 0, 16, 0, 1), chm[:, c * 16:(c + 1) * 16],
                        gnb[:, c:c + 1], start=(c == 0), stop=(c == NCH - 1)), "PE")
        em.inc(t.matmul(bank(4, 0, 16, 1, 1), chm[:, c * 16:(c + 1) * 16],
                        pays[:, 40 + c:40 + c + 1], start=(c == 0), stop=(c == NCH - 1)),
               "PE")
    em.wait(v, "PE")
    em.vstep(v.tensor_copy(pays[0:16, 24:25], bank(4, 0, 16, 0, 1)))
    em.vstep(v.tensor_copy(pays[0:16, 25:26], bank(4, 0, 16, 1, 1)))
    emit_pair_exchange(em, 2, pays, 24, 16)
    em.wait(v, "XD")
    NT_TOT = float(S * 64)
    em.vstep(v.tensor_add(v16[0:16, 0:2], pays[0:16, 24:26], pasb[0:16, 24:26]))
    em.vstep(v.tensor_scalar_mul(v16[0:16, 2:3], v16[0:16, 0:1], 1.0 / NT_TOT))
    em.vstep(v.tensor_scalar_mul(v16[0:16, 3:4], v16[0:16, 1:2], 1.0 / NT_TOT))
    em.vstep(v.tensor_mul(v16[0:16, 4:5], v16[0:16, 2:3], v16[0:16, 2:3]))
    em.vstep(v.tensor_sub(v16[0:16, 5:6], v16[0:16, 3:4], v16[0:16, 4:5]))
    em.wait(sc, "DV")
    em.inc(sc.activation(v16[0:16, 6:7], v16[0:16, 5:6], AF.Sqrt,
                         bias=c_misc[0:16, 0:1]), "AC")
    em.wait(v, "AC")
    em.vstep(v.reciprocal(v16[0:16, 7:8], v16[0:16, 6:7]))
    for qi, col in ((2, 0), (7, 1)):
        em.vstep(v.tensor_scalar(v16[0:16, 8:16], cpat[0:16, :], v16[0:16, qi:qi + 1],
                                 None, OP.mult))
        em.wait(t, "DV")
        for c in range(NCH):
            em.inc(t.matmul(bank(4, 0, 128, 0, 8), selpf[:, c * 128:(c + 1) * 128],
                            v16[0:16, 8:16], start=(c == 0), stop=(c == NCH - 1)), "PE")
        em.wait(v, "PE")
        em.vstep(v.tensor_copy(gnb[:, 8 + col * 8:16 + col * 8], bank(4, 0, 128, 0, 8)))
    em.vstep(v.tensor_mul(gnb[:, 24:32], gnpp[:, 0:8], gnb[:, 16:24]))
    em.vstep(v.tensor_mul(gnb[:, 0:8], gnb[:, 8:16], gnb[:, 24:32]))
    em.vstep(v.tensor_sub(gnb[:, 8:16], gnpp[:, 8:16], gnb[:, 0:8]))
    em.barrier()

    load_w(cfg["wg"])
    for ti in range(NT):
        tc0 = ti * TT
        load_xn(ti)
        em.wait(t, "LW")
        for dc in range(NCH):
            pb = dc % 2
            em.wait(t, "AC")
            for kc in range(NCH):
                em.inc(t.matmul(bank(pb), wslice(kc, dc),
                                xnb[:, kc * TT:(kc + 1) * TT],
                                start=(kc == 0), stop=(kc == NCH - 1)), "PE")
            em.wait(sc, "PE")
            em.inc(sc.activation(silb[:, dc * TT:(dc + 1) * TT], bank(pb), AF.Silu,
                                 bias=bpp[:, 3 * NCH + dc:3 * NCH + dc + 1]), "AC")
        em.wait(sy, "AC")
        em.inc(sy.dma_start(
            out=ogd.ap().rearrange("c p l -> p c l")[:, :, tc0:tc0 + TT],
            in_=silb[:, :].rearrange("p (c l) -> p c l", c=NCH)), "ST", 16)
        em.wait(sc, "ST")
    em.barrier()

    load_w(cfg["wo"])
    for ti in range(NT):
        tc0 = ti * TT
        em.wait(sy, "PE")
        em.wait(sy, "DV")
        em.inc(sy.dma_start(out=silb[:, :].rearrange("p (c l) -> p c l", c=NCH),
                            in_=ogd.ap().rearrange("c p l -> p c l")[:, :, tc0:tc0 + TT]),
               "LD", 16)
        for c in range(NCH):
            em.inc(sy.dma_start(out=xt[:, c * TT:(c + 1) * TT],
                                in_=x_in[c, :, tc0:tc0 + TT]), "LD", 16)
        em.wait(v, "LD")
        for c in range(NCH):
            em.vstep(v.tensor_scalar(xnb[:, c * TT:(c + 1) * TT],
                                     outs[:, c * TLOC + tc0:c * TLOC + tc0 + TT],
                                     gnb[:, 24 + c:24 + c + 1],
                                     gnb[:, 8 + c:8 + c + 1],
                                     OP.mult, OP.add))
            em.vstep(v.tensor_mul(xnb[:, c * TT:(c + 1) * TT],
                                  xnb[:, c * TT:(c + 1) * TT],
                                  silb[:, c * TT:(c + 1) * TT]))
        em.wait(t, "DV")
        em.wait(t, "LW")
        for dc in range(NCH):
            pb = dc % 2
            em.wait(t, "ST")
            em.wait(t, "DV")
            for kc in range(NCH):
                em.inc(t.matmul(bank(pb), wslice(kc, dc),
                                xnb[:, kc * TT:(kc + 1) * TT],
                                start=(kc == 0), stop=(kc == NCH - 1)), "PE")
            em.wait(v, "PE")
            sl2 = dc % 2
            em.vstep(v.tensor_scalar(sq[:, sl2 * TT:(sl2 + 1) * TT], bank(pb),
                                     bpp[:, 4 * NCH + dc:4 * NCH + dc + 1], None,
                                     OP.add))
            em.vstep(v.tensor_add(sq[:, sl2 * TT:(sl2 + 1) * TT],
                                  sq[:, sl2 * TT:(sl2 + 1) * TT],
                                  xt[:, dc * TT:(dc + 1) * TT]))
            em.wait(sy, "DV")
            em.inc(sy.dma_start(out=x_out[dc, :, tc0:tc0 + TT],
                                in_=sq[:, sl2 * TT:(sl2 + 1) * TT]), "ST", 16)
    em.wait(sy, "ST")


def build_attn1(dbg=False):
    nc = bass.Bass("TRN2", target_bir_lowering=False)
    CJ = TLOC // 128
    dr = {}
    dr["x_in"] = nc.dram_tensor("x_in", [NCH, 128, TLOC], F32, kind="ExternalInput")
    dr["x_out"] = nc.dram_tensor("x_out", [NCH, 128, TLOC], F32, kind="ExternalOutput")
    for nm in ["wq", "wk", "wv", "wg", "wo"]:
        dr[nm] = nc.dram_tensor(nm, [D, D], F32, kind="ExternalInput")
    dr["wa"] = nc.dram_tensor("wa", [D, 16], F32, kind="ExternalInput")
    dr["wb_"] = nc.dram_tensor("wb_", [D, 16], F32, kind="ExternalInput")
    for nm, shape in [("b_pp", [128, 5 * NCH]), ("ab_bias", [16, 2]),
                      ("gn_pp", [128, 2 * NCH]), ("cv_pp", [128, 3 * NCH * 4]),
                      ("c_ones", [128, 128]), ("c_id", [128, 128]),
                      ("c_misc", [128, 8]), ("c_selpf", [16, NCH * 128]),
                      ("c_s16f", [16, 16 * 128]), ("c_ut", [128, 128]),
                      ("c_lt", [128, 128]), ("c_msl", [128, 128]),
                      ("c_msu", [128, 128]), ("c_hm", [128, NCH * 16]),
                      ("c_pat", [16, 8]), ("xmask", [128, 1])]:
        dr[nm] = nc.dram_tensor(nm, shape, F32, kind="ExternalInput")
    dr["c_sel"] = nc.dram_tensor("c_sel", [8, 8 * 128], BF16, kind="ExternalInput")
    dr["c_selp"] = nc.dram_tensor("c_selp", [16, NCH * 128], BF16, kind="ExternalInput")
    dr["c_idb"] = nc.dram_tensor("c_idb", [128, 128], BF16, kind="ExternalInput")
    dr["meta"] = nc.dram_tensor("meta", [1, 16], I32, kind="ExternalInput")
    dr["nonce"] = nc.dram_tensor("nonce", [1, 8], I32, kind="ExternalInput")
    kind = "ExternalOutput" if dbg else "Internal"
    dr["xn_dram"] = nc.dram_tensor("xn_scr", [NCH, 128, TLOC], BF16, kind=kind)
    dr["qc_dram"] = nc.dram_tensor("qc_scr", [NCH, 128, TLOC], BF16, kind=kind)
    dr["kc_dram"] = nc.dram_tensor("kc_scr", [NCH, 128, TLOC], BF16, kind=kind)
    dr["vc_dram"] = nc.dram_tensor("vc_scr", [NCH, 128, TLOC], BF16, kind=kind)
    dr["og_dram"] = nc.dram_tensor("og_scr", [NCH, 128, TLOC], BF16)
    dr["E_dram"] = nc.dram_tensor("E_scr", [CJ, 128, NCH * 128], BF16)
    dr["P_dram"] = nc.dram_tensor("P_scr", [CJ, 128, NCH * 64], BF16)
    dr["H_dram"] = nc.dram_tensor("H_scr", [CJ, 128, NCH * 64], BF16)
    if dbg:
        dr["dbg_ab"] = nc.dram_tensor("dbg_ab", [2, 16, TLOC], F32, kind="ExternalOutput")
        dr["dbg_outs"] = nc.dram_tensor("dbg_outs", [NCH, 128, TLOC], BF16,
                                        kind="ExternalOutput")
    dr["PAYC"] = 560
    dr["shm_pay"] = nc.dram_tensor("shm_pay", [2, 128, 560], F32, addr_space="Shared")
    dr["shm_fl"] = nc.dram_tensor("shm_fl", [2, 8], I32, addr_space="Shared")
    with ExitStack() as stack:
        em = Em(nc, stack)
        cfg = dict(dr)
        cfg["tag"] = "b"
        cfg["dbg"] = dbg
        with nc.allow_low_precision(reason="bf16 operand staging is intentional"):
            emit_attn1(em, cfg)
    return nc


def attn1_in_map(inputs, xcore, meta, nonce, core_idx):
    c_ones, c_id, c_sel, c_misc = consts_f()
    selp = np.zeros((16, NCH * 128), BF)
    selpf = np.zeros((16, NCH * 128), np.float32)
    for c in range(NCH):
        selp[2 * c, c * 128:c * 128 + 64] = 1
        selp[2 * c + 1, c * 128 + 64:(c + 1) * 128] = 1
        selpf[2 * c, c * 128:c * 128 + 64] = 1
        selpf[2 * c + 1, c * 128 + 64:(c + 1) * 128] = 1
    s16f = np.zeros((16, 16 * 128), np.float32)
    for h in range(16):
        s16f[h, h * 128:(h + 1) * 128] = 1
    idx = np.arange(128)
    cut = (idx[:, None] <= idx[None, :]).astype(np.float32)
    clt = (idx[:, None] > idx[None, :]).astype(np.float32)
    cmsl = np.where(idx[None, :] < idx[:, None], 0.0, -1e30).astype(np.float32)
    cmsu = np.where(idx[:, None] <= idx[None, :], 0.0, -1e30).astype(np.float32)
    chm = np.zeros((128, NCH * 16), np.float32)
    for c in range(NCH):
        chm[0:64, c * 16 + 2 * c] = 1
        chm[64:128, c * 16 + 2 * c + 1] = 1
    cpat = np.zeros((16, 8), np.float32)
    for h in range(16):
        cpat[h, h // 2] = 1
    idb = np.eye(128, dtype=BF)
    onesb = np.ones((128, 128), BF)
    xmask = np.full((128, 1), float(core_idx % 2), np.float32)
    bpp = np.zeros((128, 5 * NCH), np.float32)
    for i, nm in enumerate(["bq", "bk", "bv", "bg", "bo"]):
        bpp[:, i * NCH:(i + 1) * NCH] = inputs[f"L1_attn_{nm}"].reshape(NCH, 128).T
    abb = np.stack([inputs["L1_attn_ba"], inputs["L1_attn_bb"]], axis=1).astype(np.float32)
    gnpp = np.zeros((128, 2 * NCH), np.float32)
    gnpp[:, 0:NCH] = inputs["L1_attn_gn_scale"].reshape(NCH, 128).T
    gnpp[:, NCH:2 * NCH] = inputs["L1_attn_gn_bias"].reshape(NCH, 128).T
    cvpp = np.zeros((128, 3 * NCH * 4), np.float32)
    for i, nm in enumerate(["q_conv", "k_conv", "v_conv"]):
        w = inputs[f"L1_attn_{nm}"]
        cvpp[:, i * 32:(i + 1) * 32] = w.reshape(NCH, 128, 4).transpose(1, 0, 2).reshape(128, 32)
    return dict(
        x_in=xcore,
        wq=inputs["L1_attn_Wq"], wk=inputs["L1_attn_Wk"], wv=inputs["L1_attn_Wv"],
        wg=inputs["L1_attn_Wg"], wo=inputs["L1_attn_Wo"],
        wa=inputs["L1_attn_Wa"], wb_=inputs["L1_attn_Wb"],
        b_pp=bpp, ab_bias=abb, gn_pp=gnpp, cv_pp=cvpp,
        c_ones=c_ones, c_id=c_id, c_sel=c_sel, c_misc=c_misc,
        c_selp=selp, c_selpf=selpf, c_s16f=s16f, c_ut=cut, c_lt=clt,
        c_msl=cmsl, c_msu=cmsu, c_hm=chm, c_pat=cpat, c_idb=idb,
        xmask=xmask, meta=meta, nonce=nonce,
    )


def run_attn1(inputs, xcores):
    nc = _get_prog("attn1", build_attn1)
    rng = np.random.default_rng()
    metas, nonces = xchg_host_inputs(rng, col0s=[0, 48, 24, 0])
    maps = [attn1_in_map(inputs, xcores[c], metas[c], nonces[c], c)
            for c in range(NCORES)]
    res = run_bass_kernel_spmd(nc, maps, core_ids=list(range(NCORES)))
    return [res.results[c]["x_out"] for c in range(NCORES)]


def build_ffn(layer, dbg=False):
    nc = bass.Bass("TRN2", target_bir_lowering=False)
    E = 8 if layer == 0 else 7
    I = 2048 if layer == 0 else 512
    NBIAS = 1 if layer == 0 else 2
    dr = {}
    dr["x_in"] = nc.dram_tensor("x_in", [NCH, 128, TLOC], F32, kind="ExternalInput")
    dr["x_out"] = nc.dram_tensor("x_out", [NCH, 128, TLOC], F32, kind="ExternalOutput")
    dr["gw"] = nc.dram_tensor("gw", [E, D, I], F32, kind="ExternalInput")
    dr["uw"] = nc.dram_tensor("uw", [E, D, I], F32, kind="ExternalInput")
    dr["dw"] = nc.dram_tensor("dw", [E, I, D], F32, kind="ExternalInput")
    dr["sg"] = nc.dram_tensor("sg", [D, I], F32, kind="ExternalInput")
    dr["su"] = nc.dram_tensor("su", [D, I], F32, kind="ExternalInput")
    dr["sd"] = nc.dram_tensor("sd", [I, D], F32, kind="ExternalInput")
    dr["rw"] = nc.dram_tensor("rw", [D, E], F32, kind="ExternalInput")
    dr["rbias"] = nc.dram_tensor("rbias", [NBIAS, E], F32, kind="ExternalInput")
    dr["c_ones"] = nc.dram_tensor("c_ones", [128, 128], F32, kind="ExternalInput")
    dr["c_id"] = nc.dram_tensor("c_id", [128, 128], F32, kind="ExternalInput")
    dr["c_sel"] = nc.dram_tensor("c_sel", [8, 8 * 128], BF16, kind="ExternalInput")
    dr["c_misc"] = nc.dram_tensor("c_misc", [128, 8], F32, kind="ExternalInput")
    kind = "ExternalOutput" if dbg else "Internal"
    dr["xn_dram"] = nc.dram_tensor("xn_scr", [NCH, 128, TLOC], BF16, kind=kind)
    dr["yacc"] = nc.dram_tensor("yacc_scr", [NCH, 128, TLOC], F32, kind=kind)
    if dbg:
        dr["dbg_probs"] = nc.dram_tensor("dbg_probs", [128, TCH * 8], F32, kind="ExternalOutput")
        dr["dbg_wtok"] = nc.dram_tensor("dbg_wtok", [128, TCH * 8], F32, kind="ExternalOutput")
    with ExitStack() as stack:
        em = Em(nc, stack)
        cfg = dict(dr)
        cfg["E"], cfg["I"], cfg["tag"] = E, I, "f"
        cfg["dbg"] = dbg
        cfg["use_sel"] = True
        with nc.allow_low_precision(reason="bf16 operand staging is intentional"):
            emit_ffn(em, cfg)
    return nc


def shard_x(x):
    out = []
    for c in range(NCORES):
        b, h = c // 2, c % 2
        xs = x[b, h * TLOC:(h + 1) * TLOC, :]
        xf = np.ascontiguousarray(xs.T)
        out.append(np.ascontiguousarray(xf.reshape(NCH, 128, TLOC)))
    return out


def unshard_x(parts):
    x = np.empty((B, S, D), np.float32)
    for c in range(NCORES):
        b, h = c // 2, c % 2
        xf = parts[c].reshape(D, TLOC)
        x[b, h * TLOC:(h + 1) * TLOC, :] = xf.T
    return x


def consts_f():
    c_ones = np.ones((128, 128), np.float32)
    c_id = np.eye(128, dtype=np.float32)
    c_sel = np.zeros((8, 8 * 128), BF)
    for e in range(8):
        c_sel[e, e * 128:(e + 1) * 128] = 1
    c_misc = np.zeros((128, 8), np.float32)
    c_misc[:, 0] = LN_EPS
    c_misc[:, 1] = 1.0
    c_misc[:, 2] = 1e-6
    return c_ones, c_id, c_sel, c_misc


def ffn_in_map(layer, inputs, xcore):
    c_ones, c_id, c_sel, c_misc = consts_f()
    if layer == 0:
        m = dict(
            gw=inputs["L0_ffn_gate_w"], uw=inputs["L0_ffn_up_w"],
            dw=inputs["L0_ffn_down_w"],
            sg=np.ascontiguousarray(inputs["L0_ffn_shared_gate"].T),
            su=np.ascontiguousarray(inputs["L0_ffn_shared_up"].T),
            sd=np.ascontiguousarray(inputs["L0_ffn_shared_down"].T),
            rw=np.ascontiguousarray(inputs["L0_ffn_router_w"].T),
            rbias=inputs["L0_ffn_routing_bias"][None, :],
        )
    else:
        m = dict(
            gw=inputs["L1_ffn_r_gate"], uw=inputs["L1_ffn_r_up"],
            dw=inputs["L1_ffn_r_down"],
            sg=inputs["L1_ffn_sh_gate"], su=inputs["L1_ffn_sh_up"],
            sd=inputs["L1_ffn_sh_down"],
            rw=inputs["L1_ffn_w_r"],
            rbias=np.stack([inputs["L1_ffn_b_r"], inputs["L1_ffn_routing_bias"]]),
        )
    m["c_ones"], m["c_id"], m["c_sel"], m["c_misc"] = c_ones, c_id, c_sel, c_misc
    m["x_in"] = xcore
    return m


_prog_cache = {}


def _get_prog(name, builder):
    if name not in _prog_cache:
        _prog_cache[name] = builder()
    return _prog_cache[name]


def run_ffn(layer, inputs, xcores):
    nc = _get_prog(f"ffn{layer}", lambda: build_ffn(layer))
    maps = [ffn_in_map(layer, inputs, xcores[c]) for c in range(NCORES)]
    res = run_bass_kernel_spmd(nc, maps, core_ids=list(range(NCORES)))
    return [res.results[c]["x_out"] for c in range(NCORES)]


def kernel(**inputs):
    xcores = shard_x(np.asarray(inputs["L0_attn_x"]))
    xcores = run_ffn(0, inputs, xcores)
    xcores = run_attn0(inputs, xcores)
    xcores = run_ffn(0, inputs, xcores)
    xcores = run_ffn(1, inputs, xcores)
    xcores = run_attn1(inputs, xcores)
    xcores = run_ffn(1, inputs, xcores)
    return unshard_x(xcores)


# version 02
# speedup vs baseline: 1.0024x; 1.0024x over previous
import numpy as np
import ml_dtypes
from contextlib import ExitStack

import concourse.bass as bass
import concourse.mybir as mybir
from concourse.bass_utils import run_bass_kernel_spmd

F32 = mybir.dt.float32
BF16 = mybir.dt.bfloat16
I32 = mybir.dt.int32
AF = mybir.ActivationFunctionType
OP = mybir.AluOpType

B, S, D = 4, 8192, 1024
NCORES = 8
TLOC = S // 2
NCH = D // 128
TT = 512
NT = TLOC // TT
TCH = TLOC // 128
LN_EPS = 1e-5

BF = ml_dtypes.bfloat16


class Em:

    def __init__(self, nc, stack):
        self.nc = nc
        self.stack = stack
        self.g = nc.gpsimd
        self.v = nc.vector
        self.sc = nc.scalar
        self.t = nc.tensor
        self.sy = nc.sync
        self._sems = {}

    def sem(self, name):
        if name not in self._sems:
            h = self.stack.enter_context(self.nc.semaphore(name))
            self._sems[name] = [h, 0]
        return self._sems[name]

    def inc(self, ins, name, amt=1):
        s = self.sem(name)
        ins.then_inc(s[0], amt)
        s[1] += amt
        return ins

    def val(self, name):
        return self.sem(name)[1]

    def wait(self, eng, name, val=None):
        s = self.sem(name)
        v = s[1] if val is None else val
        if v > 0:
            eng.wait_ge(s[0], v)

    def barrier(self):
        self.nc.all_engine_barrier()

    def vstep(self, ins):
        self.inc(ins, "DV")
        self.wait(self.v, "DV")
        return ins

    def load_consts(self, cfg):
        if not hasattr(self, "_consts"):
            nc = self.nc
            ones = self.stack.enter_context(nc.sbuf_tensor("c_ones_sb", [128, 128], F32))
            ident = self.stack.enter_context(nc.sbuf_tensor("c_id_sb", [128, 128], F32))
            misc = self.stack.enter_context(nc.sbuf_tensor("c_misc_sb", [128, 8], F32))
            self.inc(self.sy.dma_start(out=ones[:, :], in_=cfg["c_ones"][:, :]), "LD", 16)
            self.inc(self.sy.dma_start(out=ident[:, :], in_=cfg["c_id"][:, :]), "LD", 16)
            self.inc(self.sy.dma_start(out=misc[:, :], in_=cfg["c_misc"][:, :]), "LD", 16)
            sel = None
            if cfg.get("use_sel"):
                sel = self.stack.enter_context(nc.sbuf_tensor("c_sel_sb", [8, 8 * 128], BF16))
                self.inc(self.sy.dma_start(out=sel[:, :], in_=cfg["c_sel"][:, :]), "LD", 16)
            self._consts = (ones, ident, sel, misc)
        return self._consts


def emit_ffn(em, cfg):
    nc, g, v, sc, t, sy = em.nc, em.g, em.v, em.sc, em.t, em.sy
    E, I = cfg["E"], cfg["I"]
    ICH = I // 128
    NB = cfg["rbias"].shape[0]
    x_in, x_out = cfg["x_in"], cfg["x_out"]
    xn_dram, yacc = cfg["xn_dram"], cfg["yacc"]
    c_ones, c_id, c_sel, c_misc = em.load_consts(cfg)

    tag = cfg["tag"]
    sb = lambda name, shape, dt=F32: em.stack.enter_context(
        nc.sbuf_tensor(tag + name, shape, dt))
    xt = sb("xt", [128, NCH * TT])
    sq = sb("sq", [128, 2 * TT])
    xn32 = sb("xn32", [128, 2 * TT])
    xnb = sb("xnb", [128, NCH * TT], BF16)
    srow = sb("srow", [1, 4 * TT])
    probs = sb("probs", [128, TCH * 8])
    wtok = sb("wtok", [128, TCH * 8])
    trow = sb("trow", [128, 16])
    tmp8 = sb("tmp8", [128, 4 * 8])
    wem = sb("wem", [8, TLOC], BF16)
    rbs = sb("rbs", [NB, 8])
    wg = sb("wg", [128, NCH * I], BF16)
    wu = sb("wu", [128, NCH * I], BF16)
    wd = sb("wd", [128, ICH * D], BF16)
    hid = sb("hid", [128, ICH * TT], BF16)
    wb = sb("wb", [128, TT])
    tsil = sb("tsil", [128, 2 * TT])
    ybig = sb("ybig", [128, NCH * TT])
    rws = sb("rws", [128, NCH * 8])
    PS = em.stack.enter_context(nc.psum_tensor(tag + "PS", [128, 8 * 512], F32))

    def bank(b, p0=0, pn=128, c0=0, cn=512):
        return PS[p0:p0 + pn, b * 512 + c0:b * 512 + c0 + cn]

    if True:
        rww = cfg["rw"]
        if True:
            for c in range(NCH):
                em.inc(sy.dma_start(out=rws[:, c * 8:c * 8 + E],
                                    in_=rww[c * 128:(c + 1) * 128, :]), "LD", 16)
            em.inc(sy.dma_start(out=rbs[:, 0:E], in_=cfg["rbias"][:, :]), "LD", 16)
            v.memset(wem[0:8, :], 0.0)
            em.inc(v.memset(wtok[:, :], 0.0), "DV")

            for ti in range(NT):
                tc0 = ti * TT
                em.wait(sy, "PE")
                for c in range(NCH):
                    em.inc(sy.dma_start(out=xt[:, c * TT:(c + 1) * TT],
                                        in_=x_in[c, :, tc0:tc0 + TT]), "LD", 16)
                em.wait(t, "LD")
                em.wait(sc, "LD")
                for c in range(NCH):
                    em.wait(sc, "PE")
                    em.inc(sc.activation(sq[:, (c % 2) * TT:(c % 2) * TT + TT],
                                         xt[:, c * TT:(c + 1) * TT], AF.Square), "AC")
                    em.wait(t, "AC")
                    t.matmul(bank(0, 0, 1), c_ones[:, 0:1], xt[:, c * TT:(c + 1) * TT],
                             start=(c == 0), stop=(c == NCH - 1))
                    em.inc(t.matmul(bank(1, 0, 1), c_ones[:, 0:1],
                                    sq[:, (c % 2) * TT:(c % 2) * TT + TT],
                                    start=(c == 0), stop=(c == NCH - 1)), "PE")
                r = [srow[0:1, k * TT:(k + 1) * TT] for k in range(4)]
                em.wait(v, "PE")
                em.vstep(v.tensor_scalar_mul(r[0], bank(0, 0, 1), 1.0 / D))
                em.vstep(v.tensor_scalar_mul(r[1], bank(1, 0, 1), 1.0 / D))
                em.vstep(v.tensor_mul(r[2], r[0], r[0]))
                em.vstep(v.tensor_sub(r[1], r[1], r[2]))
                em.wait(sc, "DV")
                em.inc(sc.activation(r[2], r[1], AF.Sqrt, bias=c_misc[0:1, 0:1]), "AC")
                em.wait(v, "AC")
                em.vstep(v.reciprocal(r[1], r[2]))
                em.vstep(v.tensor_mul(r[3], r[0], r[1]))
                em.wait(t, "DV")
                em.inc(t.matmul(bank(2), c_ones[0:1, :], r[1], start=True, stop=True), "PE")
                em.inc(t.matmul(bank(3), c_ones[0:1, :], r[3], start=True, stop=True), "PE")
                em.wait(v, "PE")
                for c in range(NCH):
                    sl = (c % 2)
                    x32 = xn32[:, sl * TT:(sl + 1) * TT]
                    em.wait(v, "PE")
                    v.tensor_mul(x32, xt[:, c * TT:(c + 1) * TT], bank(2))
                    em.inc(v.tensor_sub(x32, x32, bank(3)), "DV")
                    em.inc(v.tensor_copy(xnb[:, c * TT:(c + 1) * TT], x32), "DV")
                    em.wait(t, "DV")
                    for tc in range(4):
                        ins = t.matmul(bank(4 + tc, 0, 128, 0, E),
                                       x32[:, tc * 128:(tc + 1) * 128],
                                       rws[:, c * 8:c * 8 + E],
                                       start=(c == 0), stop=False)
                    em.inc(ins, "PE")
                for tc in range(4):
                    ins = t.matmul(bank(4 + tc, 0, 128, 0, E), c_ones[0:NB, :],
                                   rbs[0:NB, 0:E], start=False, stop=True)
                    em.inc(ins, "PE")
                em.wait(sc, "PE")
                for tc in range(4):
                    em.inc(sc.activation(probs[:, (ti * 4 + tc) * 8:(ti * 4 + tc) * 8 + E],
                                         bank(4 + tc, 0, 128, 0, E), AF.Sigmoid), "AC")
                em.wait(sy, "DV")
                em.inc(sy.dma_start(out=xn_dram.ap()
                                    .rearrange("c p l -> p c l")[:, :, tc0:tc0 + TT],
                                    in_=xnb[:, :].rearrange("p (c l) -> p c l", c=NCH)),
                       "ST", 16)

            em.barrier()
            for tc in range(TCH):
                pr = probs[:, tc * 8:tc * 8 + E]
                sl = tc % 2
                t8 = tmp8[:, (2 + sl) * 8:(2 + sl) * 8 + E]
                m1 = trow[:, (sl * 4):(sl * 4) + 1]
                m2 = trow[:, (sl * 4) + 1:(sl * 4) + 2]
                den = trow[:, (sl * 4) + 2:(sl * 4) + 3]
                rden = trow[:, (sl * 4) + 3:(sl * 4) + 4]
                t8b = tmp8[:, sl * 8:sl * 8 + E]
                em.vstep(v.reduce_max(m1, pr, axis=mybir.AxisListType.X))
                em.vstep(v.tensor_scalar(t8, pr, m1, None, OP.is_equal))
                em.vstep(v.scalar_tensor_tensor(t8b, t8, -1e30, pr, OP.mult, OP.add))
                em.vstep(v.reduce_max(m2, t8b, axis=mybir.AxisListType.X))
                em.vstep(v.tensor_add(den, m1, m2))
                em.vstep(v.reciprocal(rden, den))
                em.vstep(v.tensor_scalar(t8, pr, m2, None, OP.is_ge))
                wv = wtok[:, tc * 8:tc * 8 + E]
                em.vstep(v.scalar_tensor_tensor(wv, pr, rden, t8, OP.mult, OP.mult))
                em.wait(t, "DV")
                em.inc(t.transpose(bank(4, 0, 8, 0, 128), wtok[:, tc * 8:tc * 8 + 8],
                                   c_id[:, :]), "PE")
                em.wait(v, "PE")
                em.inc(v.tensor_copy(wem[0:8, tc * 128:(tc + 1) * 128],
                                     bank(4, 0, 8, 0, 128)), "DV")

            if cfg.get("dbg"):
                em.wait(sy, "DV")
                em.inc(sy.dma_start(out=cfg["dbg_probs"][:, :], in_=probs[:, :]), "ST", 16)
                em.inc(sy.dma_start(out=cfg["dbg_wtok"][:, :], in_=wtok[:, :]), "ST", 16)

            em.barrier()
            for e in range(E + 1):
                if e == 0:
                    gsrc, usrc, dsrc = cfg["sg"], cfg["su"], cfg["sd"]
                else:
                    gsrc, usrc, dsrc = (cfg["gw"][e - 1], cfg["uw"][e - 1],
                                        cfg["dw"][e - 1])
                em.wait(g, "PE")
                for kc in range(NCH):
                    em.inc(g.dma_start(out=wg[:, kc * I:(kc + 1) * I],
                                       in_=gsrc[kc * 128:(kc + 1) * 128, :]), "LW", 16)
                    em.inc(g.dma_start(out=wu[:, kc * I:(kc + 1) * I],
                                       in_=usrc[kc * 128:(kc + 1) * 128, :]), "LW", 16)
                for ic in range(ICH):
                    em.inc(g.dma_start(out=wd[:, ic * D:(ic + 1) * D],
                                       in_=dsrc[ic * 128:(ic + 1) * 128, :]), "LW", 16)
                for ti in range(NT):
                    tc0 = ti * TT
                    em.wait(sy, "PE")
                    em.inc(sy.dma_start(out=xnb[:, :].rearrange("p (c l) -> p c l", c=NCH),
                                        in_=xn_dram.ap().rearrange("c p l -> p c l")[:, :, tc0:tc0 + TT]),
                           "LD", 16)
                    if e > 0:
                        em.wait(t, "DV")
                        em.inc(t.matmul(bank(4), c_sel[:, (e - 1) * 128:e * 128],
                                        wem[0:8, tc0:tc0 + TT], start=True, stop=True),
                               "PE")
                        em.wait(v, "PE")
                        em.inc(v.tensor_copy(wb[:, :], bank(4)), "DV")
                    em.wait(t, "LW")
                    em.wait(t, "LD")
                    for ic in range(ICH):
                        em.wait(t, "AC")
                        em.wait(t, "DV")
                        for kc in range(NCH):
                            t.matmul(bank(0), wg[:, kc * I + ic * 128:kc * I + (ic + 1) * 128],
                                     xnb[:, kc * TT:(kc + 1) * TT],
                                     start=(kc == 0), stop=(kc == NCH - 1))
                        for kc in range(NCH):
                            ins = t.matmul(bank(1), wu[:, kc * I + ic * 128:kc * I + (ic + 1) * 128],
                                           xnb[:, kc * TT:(kc + 1) * TT],
                                           start=(kc == 0), stop=(kc == NCH - 1))
                        em.inc(ins, "PE")
                        em.wait(sc, "PE")
                        sl = ic % 2
                        tsl = tsil[:, sl * TT:(sl + 1) * TT]
                        em.inc(sc.activation(tsl, bank(0), AF.Silu), "AC")
                        em.wait(v, "AC")
                        if e > 0:
                            v.tensor_mul(tsl, tsl, bank(1))
                            em.inc(v.tensor_mul(hid[:, ic * TT:(ic + 1) * TT], tsl,
                                                wb[:, :]), "DV")
                        else:
                            em.inc(v.tensor_mul(hid[:, ic * TT:(ic + 1) * TT], tsl,
                                                bank(1)), "DV")
                    em.wait(t, "DV")
                    for dc in range(NCH):
                        pb = 2 + (dc % 2)
                        em.wait(t, "YA")
                        em.wait(t, "DV")
                        for ic in range(ICH):
                            ins = t.matmul(bank(pb),
                                           wd[:, ic * D + dc * 128:ic * D + (dc + 1) * 128],
                                           hid[:, ic * TT:(ic + 1) * TT],
                                           start=(ic == 0), stop=(ic == ICH - 1))
                        em.inc(ins, "PE")
                        em.wait(v, "PE")
                        em.inc(v.tensor_copy(ybig[:, dc * TT:(dc + 1) * TT], bank(pb)),
                               "DV")
                    em.wait(g, "DV")
                    em.inc(g.dma_start(out=yacc.ap().rearrange("c p l -> p c l")[:, :, tc0:tc0 + TT],
                                       in_=ybig[:, :].rearrange("p (c l) -> p c l", c=NCH),
                                       accum_op=(OP.bypass if e == 0 else OP.add)),
                           "YA", 16)

            em.barrier()
            for ti in range(NT):
                tc0 = ti * TT
                em.wait(sy, "YA")
                em.wait(sy, "DV")
                em.inc(sy.dma_start(out=xt[:, :].rearrange("p (c l) -> p c l", c=NCH),
                                    in_=x_in.ap().rearrange("c p l -> p c l")[:, :, tc0:tc0 + TT]),
                       "LD", 16)
                em.inc(sy.dma_start(out=ybig[:, :].rearrange("p (c l) -> p c l", c=NCH),
                                    in_=yacc.ap().rearrange("c p l -> p c l")[:, :, tc0:tc0 + TT]),
                       "LD", 16)
                em.wait(v, "LD")
                em.inc(v.scalar_tensor_tensor(ybig[:, :], ybig[:, :], 0.5, xt[:, :],
                                              OP.mult, OP.add), "DV")
                em.wait(sy, "DV")
                em.inc(sy.dma_start(out=x_out.ap().rearrange("c p l -> p c l")[:, :, tc0:tc0 + TT],
                                    in_=ybig[:, :].rearrange("p (c l) -> p c l", c=NCH)),
                       "ST", 16)
            em.wait(sy, "ST")



def emit_xchg_setup(em, cfg):
    nc, g = em.nc, em.g
    st = em.stack
    tag = cfg.get("tag", "")
    if not hasattr(em, "x_metas"):
        em.x_metas = st.enter_context(nc.sbuf_tensor("xmeta", [1, 32], I32))
        em.x_nonces = st.enter_context(nc.sbuf_tensor("xnonce", [1, 8], I32))
        em.x_fl = st.enter_context(nc.sbuf_tensor("xfl", [1, 2], I32))
        em.inc(em.sy.dma_start(out=em.x_metas[:, :], in_=cfg["meta"][:, :]), "XM", 16)
        em.inc(em.sy.dma_start(out=em.x_nonces[:, :], in_=cfg["nonce"][:, :]), "XM", 16)
        em.x_regs = {}
        for nm in ["r_off", "r_tgt", "r_flv", "r_done", "r_plc"]:
            em.x_regs[nm] = st.enter_context(g.register(nm))
        g.reg_mov(em.x_regs["r_plc"], 0)
        em.x_plsem = "PL"
    em.x_pax = st.enter_context(nc.sbuf_tensor(tag + "xpax", [128, 2 * 512], F32))
    em.x_xmk = st.enter_context(nc.sbuf_tensor(tag + "xxmk", [128, 1], F32))
    em.inc(em.sy.dma_start(out=em.x_xmk[:, :], in_=cfg["xmask"][:, :]), "XM", 16)
    em.x_cfg = cfg


def emit_pair_exchange(em, rnd, paysb, col0, ncols):
    g, v = em.g, em.v
    cfg, regs = em.x_cfg, em.x_regs
    rnd = rnd + cfg.get("rbase", 0)
    PAYC = cfg["PAYC"]
    shm_pay, shm_fl, nonce = cfg["shm_pay"], cfg["shm_fl"], cfg["nonce"]
    em.wait(g, "XM")
    em.wait(g, "DV")
    g.reg_load(regs["r_off"], em.x_metas[0:1, 8 + rnd:9 + rnd])
    em.inc(g.dma_start(
        bass.AP(shm_pay, regs["r_off"], [[PAYC, 128], [1, ncols]]),
        paysb[:, col0:col0 + ncols]), "XP", 16)
    em.wait(g, "XP")
    g.reg_load(regs["r_off"], em.x_metas[0:1, 24 + rnd:25 + rnd])
    em.inc(g.dma_start(
        bass.AP(shm_fl, regs["r_off"], [[1, 1], [1, 1]]),
        nonce[0:1, rnd:rnd + 1]), "XP", 16)
    g.reg_load(regs["r_tgt"], em.x_nonces[0:1, rnd:rnd + 1])
    g.reg_mov(regs["r_done"], 1)
    pl = em.sem(em.x_plsem)
    with g.While(regs["r_done"]):
        g.dma_start(em.x_fl[0:1, 0:2], shm_fl[0:2, rnd:rnd + 1],
                    allow_slow_non_contiguous=True).then_inc(pl[0], 16)
        g.reg_add(regs["r_plc"], regs["r_plc"], 16)
        g.wait_ge(pl[0], regs["r_plc"])
        g.reg_load(regs["r_flv"], em.x_fl[0:1, 0:1])
        g.reg_sub(regs["r_flv"], regs["r_flv"], regs["r_tgt"])
        g.reg_load(regs["r_done"], em.x_fl[0:1, 1:2])
        g.reg_sub(regs["r_done"], regs["r_done"], regs["r_tgt"])
        g.reg_alu(regs["r_done"], regs["r_done"], regs["r_flv"],
                  op=OP.bitwise_or)
    pax = em.x_pax
    em.inc(g.dma_start(
        pax[:, 0:2 * ncols].rearrange("p (s c) -> p s c", s=2),
        shm_pay.ap().rearrange("s p c -> p s c")[:, :, col0:col0 + ncols]),
        "XD", 16)
    em.wait(v, "XD")
    dst = em.x_pasb[:, col0:col0 + ncols]
    em.vstep(v.tensor_sub(dst, pax[:, 0:ncols], pax[:, ncols:2 * ncols]))
    em.vstep(v.scalar_tensor_tensor(dst, dst, em.x_xmk[:, 0:1],
                                    pax[:, ncols:2 * ncols], OP.mult, OP.add))


def emit_ln_to_dram(em, cfg, xt, sq, xn32, xnb, srow, bank, c_ones, c_misc):
    v, sc, t, sy = em.v, em.sc, em.t, em.sy
    x_in, xn_dram = cfg["x_in"], cfg["xn_dram"]
    for ti in range(NT):
        tc0 = ti * TT
        em.wait(sy, "PE")
        for c in range(NCH):
            em.inc(sy.dma_start(out=xt[:, c * TT:(c + 1) * TT],
                                in_=x_in[c, :, tc0:tc0 + TT]), "LD", 16)
        em.wait(t, "LD")
        em.wait(sc, "LD")
        for c in range(NCH):
            em.wait(sc, "PE")
            em.inc(sc.activation(sq[:, (c % 2) * TT:(c % 2) * TT + TT],
                                 xt[:, c * TT:(c + 1) * TT], AF.Square), "AC")
            em.wait(t, "AC")
            t.matmul(bank(0, 0, 1), c_ones[:, 0:1], xt[:, c * TT:(c + 1) * TT],
                     start=(c == 0), stop=(c == NCH - 1))
            em.inc(t.matmul(bank(1, 0, 1), c_ones[:, 0:1],
                            sq[:, (c % 2) * TT:(c % 2) * TT + TT],
                            start=(c == 0), stop=(c == NCH - 1)), "PE")
        r = [srow[0:1, k * TT:(k + 1) * TT] for k in range(4)]
        em.wait(v, "PE")
        em.vstep(v.tensor_scalar_mul(r[0], bank(0, 0, 1), 1.0 / D))
        em.vstep(v.tensor_scalar_mul(r[1], bank(1, 0, 1), 1.0 / D))
        em.vstep(v.tensor_mul(r[2], r[0], r[0]))
        em.vstep(v.tensor_sub(r[1], r[1], r[2]))
        em.wait(sc, "DV")
        em.inc(sc.activation(r[2], r[1], AF.Sqrt, bias=c_misc[0:1, 0:1]), "AC")
        em.wait(v, "AC")
        em.vstep(v.reciprocal(r[1], r[2]))
        em.vstep(v.tensor_mul(r[3], r[0], r[1]))
        em.wait(t, "DV")
        em.inc(t.matmul(bank(2), c_ones[0:1, :], r[1], start=True, stop=True), "PE")
        em.inc(t.matmul(bank(3), c_ones[0:1, :], r[3], start=True, stop=True), "PE")
        em.wait(v, "PE")
        for c in range(NCH):
            sl = (c % 2)
            x32 = xn32[:, sl * TT:(sl + 1) * TT]
            em.wait(v, "PE")
            v.tensor_mul(x32, xt[:, c * TT:(c + 1) * TT], bank(2))
            em.inc(v.tensor_sub(x32, x32, bank(3)), "DV")
            em.inc(v.tensor_copy(xnb[:, c * TT:(c + 1) * TT], x32), "DV")
        em.wait(sy, "DV")
        em.inc(sy.dma_start(out=xn_dram.ap().rearrange("c p l -> p c l")[:, :, tc0:tc0 + TT],
                            in_=xnb[:, :].rearrange("p (c l) -> p c l", c=NCH)),
               "ST", 16)


def emit_attn0(em, cfg):
    nc, g, v, sc, t, sy = em.nc, em.g, em.v, em.sc, em.t, em.sy
    c_ones, c_id, c_sel, c_misc = em.load_consts(cfg)
    x_in, x_out, xn_dram = cfg["x_in"], cfg["x_out"], cfg["xn_dram"]
    qf_dram = cfg["qf_dram"]
    tag = cfg["tag"]
    sb = lambda name, shape, dt=F32: em.stack.enter_context(
        nc.sbuf_tensor(tag + name, shape, dt))
    EPS = 1e-6
    xt = sb("xt", [128, NCH * TT])
    sq = sb("sq", [128, 2 * TT])
    xn32 = sb("xn32", [128, 2 * TT])
    xnb = sb("xnb", [128, NCH * TT], BF16)
    srow = sb("srow", [1, 4 * TT])
    qt = sb("qt", [128, 2 * NCH * TT], BF16)
    kf = sb("kf", [128, NCH * TLOC], BF16)
    T1 = sb("T1", [16, 3 * TLOC], BF16)
    csx = sb("csx", [16, 2 * TT])
    tiny = sb("tiny", [16, 16])
    wdot = sb("wdot", [128, 2 * NCH * 16], BF16)
    gsum = sb("gsum", [128, 4 * 8])
    part = sb("part", [128, 2 * NCH * 8])
    wsb = sb("wsb", [128, NCH * D], BF16)
    ktok = sb("ktok", [128, D], BF16)
    vsc = sb("vsc", [128, D], BF16)
    sctk = sb("sctk", [128, 2 * 16])
    kvb = sb("kvb", [128, 512], BF16)
    brow16 = sb("brow16", [1, 4 * D], BF16)
    bpp = sb("bpp", [128, 4 * NCH])
    selp = sb("selp", [16, NCH * 128], BF16)
    onesb = sb("onesb", [128, 128], BF16)
    idb = sb("idb", [128, 128], BF16)
    pays = sb("pays", [128, 560])
    pasb = sb("pasb", [128, 560])
    em.x_pasb = pasb
    PS = em.stack.enter_context(nc.psum_tensor(tag + "PS", [128, 7 * 512], F32))
    PSb = em.stack.enter_context(nc.psum_tensor(tag + "PSb", [128, 8 * 128], BF16))

    def bank(b, p0=0, pn=128, c0=0, cn=512):
        return PS[p0:p0 + pn, b * 512 + c0:b * 512 + c0 + cn]

    emit_xchg_setup(em, cfg)
    em.inc(g.dma_start(out=brow16[:, :], in_=cfg["b_rows"][:, :]), "LW", 16)
    em.inc(sy.dma_start(out=bpp[:, :], in_=cfg["b_pp"][:, :]), "LD", 16)
    em.inc(sy.dma_start(out=selp[:, :], in_=cfg["c_selp"][:, :]), "LD", 16)
    em.inc(sy.dma_start(out=onesb[:, :], in_=cfg["c_onesb"][:, :]), "LD", 16)
    em.inc(sy.dma_start(out=idb[:, :], in_=cfg["c_idb"][:, :]), "LD", 16)

    def load_w(w):
        em.wait(g, "PE")
        for kc in range(NCH):
            em.inc(g.dma_start(out=wsb[:, kc * D:(kc + 1) * D],
                               in_=w[kc * 128:(kc + 1) * 128, :]), "LW", 16)

    def wslice(kc, dc):
        return wsb[:, kc * D + dc * 128:kc * D + (dc + 1) * 128]

    def load_qt(ti, sl):
        em.wait(sy, "PE")
        em.wait(sy, "DV")
        em.inc(sy.dma_start(
            out=qt[:, sl * NCH * TT:(sl + 1) * NCH * TT].rearrange("p (c l) -> p c l", c=NCH),
            in_=qf_dram.ap().rearrange("c p l -> p c l")[:, :, ti * TT:ti * TT + TT]),
            "LD", 16)
        em.wait(t, "LD")
        em.wait(v, "LD")

    def qsl(sl, c, a=0, b=TT):
        return qt[:, sl * NCH * TT + c * TT + a:sl * NCH * TT + c * TT + b]

    emit_ln_to_dram(em, cfg, xt, sq, xn32, xnb, srow, bank, c_ones, c_misc)
    em.barrier()

    for which, wname in ((0, "wq"), (1, "wk")):
        load_w(cfg[wname])
        for ti in range(NT):
            tc0 = ti * TT
            em.wait(sy, "PE")
            em.inc(sy.dma_start(out=xnb[:, :].rearrange("p (c l) -> p c l", c=NCH),
                                in_=xn_dram.ap().rearrange("c p l -> p c l")[:, :, tc0:tc0 + TT]),
                   "LD", 16)
            em.wait(t, "LD")
            em.wait(t, "LW")
            for dc in range(NCH):
                pb = dc % 4
                em.wait(t, "AC")
                for kc in range(NCH):
                    em.inc(t.matmul(bank(pb), wslice(kc, dc),
                                    xnb[:, kc * TT:(kc + 1) * TT],
                                    start=(kc == 0), stop=(kc == NCH - 1)), "PE")
                em.wait(sc, "PE")
                if which == 0:
                    dst = qt[:, dc * TT:(dc + 1) * TT]
                else:
                    dst = kf[:, dc * TLOC + tc0:dc * TLOC + tc0 + TT]
                em.inc(sc.activation(dst, bank(pb), AF.Sigmoid,
                                     bias=bpp[:, which * NCH + dc:which * NCH + dc + 1]),
                       "AC")
            if which == 0:
                em.wait(sy, "AC")
                em.inc(sy.dma_start(
                    out=qf_dram.ap().rearrange("c p l -> p c l")[:, :, tc0:tc0 + TT],
                    in_=qt[:, 0:NCH * TT].rearrange("p (c l) -> p c l", c=NCH)),
                    "ST", 16)
                em.wait(sc, "ST")
    em.barrier()

    for ti in range(NT):
        load_qt(ti, 0)
        for c in range(NCH):
            em.vstep(v.reduce_sum(part[:, (NCH + c) * 8 + ti:(NCH + c) * 8 + ti + 1],
                                  qsl(0, c), axis=mybir.AxisListType.X))
    for c in range(NCH):
        em.vstep(v.reduce_sum(pays[:, c:c + 1],
                              kf[:, c * TLOC:(c + 1) * TLOC],
                              axis=mybir.AxisListType.X))
        em.vstep(v.reduce_sum(pays[:, 8 + c:9 + c],
                              part[:, (NCH + c) * 8:(NCH + c) * 8 + 8],
                              axis=mybir.AxisListType.X))
    emit_pair_exchange(em, 0, pays, 0, 16)
    em.wait(v, "XD")
    em.vstep(v.tensor_add(gsum[:, 0:16], pays[:, 0:16], pasb[:, 0:16]))

    def build_wdot(srccol0, half):
        em.wait(v, "PE")
        em.vstep(v.memset(wdot[:, half * 128:(half + 1) * 128], 0.0))
        for c in range(NCH):
            base = half * 128 + c * 16
            em.vstep(v.tensor_scalar(wdot[0:64, base + 2 * c:base + 2 * c + 1],
                                     gsum[0:64, srccol0 + c:srccol0 + c + 1],
                                     EPS, None, OP.add))
            em.vstep(v.tensor_scalar(wdot[64:128, base + 2 * c + 1:base + 2 * c + 2],
                                     gsum[64:128, srccol0 + c:srccol0 + c + 1],
                                     EPS, None, OP.add))
        em.wait(t, "DV")
        for c in range(NCH):
            em.inc(t.matmul(bank(0, 0, 16, 0, 1),
                            wdot[:, half * 128 + c * 16:half * 128 + (c + 1) * 16],
                            onesb[:, 0:1], start=(c == 0), stop=(c == NCH - 1)), "PE")
        em.wait(v, "PE")
        em.vstep(v.tensor_scalar_mul(tiny[0:16, half:half + 1], bank(0, 0, 16, 0, 1), EPS))

    build_wdot(0, 0)
    build_wdot(8, 1)

    def trow(grp, a, b):
        return T1[0:16, grp * TLOC + a:grp * TLOC + b]

    def dot16(half, use_q, cecol, grp, act):
        for ti in range(NT):
            tc0 = ti * TT
            if use_q:
                load_qt(ti, ti % 2)
            em.wait(t, "DV")
            em.wait(t, "AC")
            for c in range(NCH):
                srcsl = qsl(ti % 2, c) if use_q else kf[:, c * TLOC + tc0:c * TLOC + tc0 + TT]
                em.inc(t.matmul(bank(1, 0, 16),
                                wdot[:, half * 128 + c * 16:half * 128 + (c + 1) * 16],
                                srcsl, start=(c == 0), stop=(c == NCH - 1)), "PE")
            em.wait(v, "PE")
            if act == "recip":
                em.vstep(v.tensor_scalar(csx[0:16, 0:TT], bank(1, 0, 16),
                                         tiny[0:16, cecol:cecol + 1], None, OP.add))
                em.vstep(v.reciprocal(trow(grp, tc0, tc0 + TT), csx[0:16, 0:TT]))
            elif act == "clipexp":
                em.vstep(v.tensor_scalar(csx[0:16, 0:TT], bank(1, 0, 16),
                                         tiny[0:16, cecol:cecol + 1], None, OP.add))
                em.vstep(v.tensor_scalar_min(csx[0:16, TT:2 * TT], csx[0:16, 0:TT], 1.0))
                em.vstep(v.tensor_scalar_max(csx[0:16, 0:TT], csx[0:16, TT:2 * TT], -1.0))
                em.wait(sc, "DV")
                em.inc(sc.activation(trow(grp, tc0, tc0 + TT), csx[0:16, 0:TT],
                                     AF.Exp), "AC")
                em.wait(v, "AC")
                em.vstep(v.reduce_sum(tiny[0:16, 8 + ti:9 + ti],
                                      trow(grp, tc0, tc0 + TT),
                                      axis=mybir.AxisListType.X))
            elif act == "sigmoid":
                em.wait(sc, "PE")
                em.inc(sc.activation(trow(grp, tc0, tc0 + TT), bank(1, 0, 16),
                                     AF.Sigmoid, bias=tiny[0:16, cecol:cecol + 1]), "AC")

    dot16(0, True, 0, 0, "recip")
    dot16(1, False, 1, 1, "recip")

    def wsum(use_q, grp, outcol):
        for ti in range(NT):
            tc0 = ti * TT
            if use_q:
                load_qt(ti, ti % 2)
            for c in range(NCH):
                em.wait(t, "DV")
                em.inc(t.matmul(bank(2), selp[:, c * 128:(c + 1) * 128],
                                trow(grp, tc0, tc0 + TT),
                                start=True, stop=True), "PE")
                em.wait(v, "PE")
                srcsl = qsl(ti % 2, c) if use_q else kf[:, c * TLOC + tc0:c * TLOC + tc0 + TT]
                em.vstep(v.tensor_mul(sq[:, (c % 2) * TT:(c % 2) * TT + TT], srcsl,
                                      bank(2)))
                em.vstep(v.reduce_sum(part[:, c * 8 + ti:c * 8 + ti + 1],
                                      sq[:, (c % 2) * TT:(c % 2) * TT + TT],
                                      axis=mybir.AxisListType.X))
        for c in range(NCH):
            em.vstep(v.reduce_sum(pays[:, outcol + c:outcol + c + 1],
                                  part[:, c * 8:c * 8 + 8],
                                  axis=mybir.AxisListType.X))

    wsum(False, 1, 16)
    wsum(True, 0, 24)
    emit_pair_exchange(em, 1, pays, 16, 16)
    em.wait(v, "XD")
    em.vstep(v.tensor_add(gsum[:, 16:32], pays[:, 16:32], pasb[:, 16:32]))

    build_wdot(16, 0)
    build_wdot(24, 1)
    dot16(0, True, 0, 1, "sigmoid")
    dot16(1, False, 1, 2, "clipexp")
    em.vstep(v.reduce_sum(pays[0:16, 32:33], tiny[0:16, 8:16],
                          axis=mybir.AxisListType.X))
    em.wait(v, "AC")
    em.vstep(v.tensor_mul(trow(0, 0, TLOC), trow(0, 0, TLOC), trow(1, 0, TLOC)))
    emit_pair_exchange(em, 2, pays, 32, 16)
    em.wait(v, "XD")
    em.vstep(v.tensor_add(tiny[0:16, 4:5], pays[0:16, 32:33], pasb[0:16, 32:33]))
    em.vstep(v.reciprocal(tiny[0:16, 5:6], tiny[0:16, 4:5]))
    em.vstep(v.tensor_scalar_mul(tiny[0:16, 6:7], tiny[0:16, 5:6], float(S)))
    em.vstep(v.tensor_scalar(trow(2, 0, TLOC), trow(2, 0, TLOC), tiny[0:16, 6:7],
                             None, OP.mult))
    em.barrier()

    load_w(cfg["wv"])
    for ti in range(NT):
        tc0 = ti * TT
        em.wait(sy, "PE")
        em.inc(sy.dma_start(out=xnb[:, :].rearrange("p (c l) -> p c l", c=NCH),
                            in_=xn_dram.ap().rearrange("c p l -> p c l")[:, :, tc0:tc0 + TT]),
               "LD", 16)
        em.wait(t, "LD")
        em.wait(t, "LW")
        for t4 in range(4):
            tc = ti * 4 + t4
            for nh in range(2):
                em.wait(t, "AC")
                em.wait(t, "DV")
                for kc in range(NCH):
                    t.matmul(bank(2 + nh), xnb[:, kc * TT + t4 * 128:kc * TT + (t4 + 1) * 128],
                             wsb[:, kc * D + nh * 512:kc * D + (nh + 1) * 512],
                             start=(kc == 0), stop=False)
                em.inc(t.matmul(bank(2 + nh), onesb[0:1, :],
                                brow16[0:1, 2 * D + nh * 512:2 * D + (nh + 1) * 512],
                                start=False, stop=True), "PE")
            em.wait(t, "DV")
            for dc in range(NCH):
                em.inc(t.transpose(PSb[:, dc * 128:(dc + 1) * 128],
                                   kf[:, dc * TLOC + tc * 128:dc * TLOC + (tc + 1) * 128],
                                   idb[:, :]), "PE")
            em.wait(v, "PE")
            em.vstep(v.tensor_copy(ktok[:, :], PSb[:, :]))
            em.wait(t, "DV")
            em.inc(t.transpose(PSb[:, 0:16],
                               trow(2, tc * 128, (tc + 1) * 128),
                               idb[0:16, 0:16]), "PE")
            em.wait(v, "PE")
            em.vstep(v.tensor_copy(sctk[:, (t4 % 2) * 16:(t4 % 2) * 16 + 16],
                                   PSb[:, 0:16]))
            for h in range(16):
                em.vstep(v.tensor_scalar(vsc[:, h * 64:(h + 1) * 64],
                                         bank(2 + h // 8, 0, 128, (h % 8) * 64, 64),
                                         sctk[:, (t4 % 2) * 16 + h:(t4 % 2) * 16 + h + 1],
                                         None, OP.mult))
            em.wait(t, "DV")
            for h in range(16):
                em.inc(t.matmul(bank(5, 64 * (h % 2), 64, (h // 2) * 64, 64),
                                ktok[:, h * 64:(h + 1) * 64], vsc[:, h * 64:(h + 1) * 64],
                                start=(tc == 0), stop=(tc == TCH - 1),
                                skip_group_check=True), "PE")
    em.wait(v, "PE")
    em.vstep(v.tensor_copy(pays[:, 48:560], bank(5)))
    emit_pair_exchange(em, 3, pays, 48, 512)
    em.wait(v, "XD")
    em.vstep(v.tensor_add(pays[:, 48:560], pays[:, 48:560], pasb[:, 48:560]))
    em.vstep(v.tensor_copy(kvb[:, :], pays[:, 48:560]))
    em.barrier()

    for ti in range(NT):
        tc0 = ti * TT
        load_qt(ti, ti % 2)
        for dc in range(NCH):
            em.wait(t, "DV")
            em.inc(t.matmul(bank(2), selp[:, dc * 128:(dc + 1) * 128],
                            trow(0, tc0, tc0 + TT), start=True, stop=True), "PE")
            em.wait(v, "PE")
            q_c = qsl(ti % 2, dc)
            em.vstep(v.tensor_mul(q_c, q_c, bank(2)))
            em.wait(t, "DV")
            em.inc(t.matmul(bank(6, 0, 64), kvb[0:64, dc * 64:(dc + 1) * 64],
                            q_c[0:64, :], start=True, stop=True), "PE")
            em.inc(t.matmul(bank(6, 64, 64), kvb[64:128, dc * 64:(dc + 1) * 64],
                            q_c[64:128, :], start=True, stop=True), "PE")
            em.wait(v, "PE")
            em.vstep(v.tensor_copy(kf[:, dc * TLOC + tc0:dc * TLOC + tc0 + TT],
                                   bank(6)))
    em.barrier()

    load_w(cfg["wo"])
    for ti in range(NT):
        tc0 = ti * TT
        em.wait(sy, "DV")
        em.wait(sy, "PE")
        for c in range(NCH):
            em.inc(sy.dma_start(out=xt[:, c * TT:(c + 1) * TT],
                                in_=x_in[c, :, tc0:tc0 + TT]), "LD", 16)
        em.wait(t, "LD")
        em.wait(t, "LW")
        for dc in range(NCH):
            pb = dc % 2
            em.wait(t, "DV")
            for kc in range(NCH):
                em.inc(t.matmul(bank(pb), wslice(kc, dc),
                                kf[:, kc * TLOC + tc0:kc * TLOC + tc0 + TT],
                                start=(kc == 0), stop=(kc == NCH - 1)), "PE")
            em.wait(v, "PE")
            em.wait(v, "ST")
            sl = dc % 2
            em.vstep(v.tensor_scalar(sq[:, sl * TT:(sl + 1) * TT], bank(pb),
                                     bpp[:, 3 * NCH + dc:3 * NCH + dc + 1], None,
                                     OP.add))
            em.vstep(v.tensor_add(sq[:, sl * TT:(sl + 1) * TT],
                                  sq[:, sl * TT:(sl + 1) * TT],
                                  xt[:, dc * TT:(dc + 1) * TT]))
            em.wait(sy, "DV")
            em.inc(sy.dma_start(out=x_out[dc, :, tc0:tc0 + TT],
                                in_=sq[:, sl * TT:(sl + 1) * TT]), "ST", 16)
    em.wait(sy, "ST")


def build_attn0(dbg=False):
    nc = bass.Bass("TRN2", target_bir_lowering=False)
    dr = {}
    dr["x_in"] = nc.dram_tensor("x_in", [NCH, 128, TLOC], F32, kind="ExternalInput")
    dr["x_out"] = nc.dram_tensor("x_out", [NCH, 128, TLOC], F32, kind="ExternalOutput")
    for nm in ["wq", "wk", "wv", "wo"]:
        dr[nm] = nc.dram_tensor(nm, [D, D], F32, kind="ExternalInput")
    dr["b_rows"] = nc.dram_tensor("b_rows", [1, 4 * D], F32, kind="ExternalInput")
    dr["b_pp"] = nc.dram_tensor("b_pp", [128, 4 * NCH], F32, kind="ExternalInput")
    dr["c_ones"] = nc.dram_tensor("c_ones", [128, 128], F32, kind="ExternalInput")
    dr["c_id"] = nc.dram_tensor("c_id", [128, 128], F32, kind="ExternalInput")
    dr["c_sel"] = nc.dram_tensor("c_sel", [8, 8 * 128], BF16, kind="ExternalInput")
    dr["c_misc"] = nc.dram_tensor("c_misc", [128, 8], F32, kind="ExternalInput")
    dr["c_selp"] = nc.dram_tensor("c_selp", [16, NCH * 128], BF16, kind="ExternalInput")
    dr["c_onesb"] = nc.dram_tensor("c_onesb", [128, 128], BF16, kind="ExternalInput")
    dr["c_idb"] = nc.dram_tensor("c_idb", [128, 128], BF16, kind="ExternalInput")
    dr["xmask"] = nc.dram_tensor("xmask", [128, 1], F32, kind="ExternalInput")
    dr["meta"] = nc.dram_tensor("meta", [1, 32], I32, kind="ExternalInput")
    dr["nonce"] = nc.dram_tensor("nonce", [1, 8], I32, kind="ExternalInput")
    dr["xn_dram"] = nc.dram_tensor("xn_scr", [NCH, 128, TLOC], BF16,
                                   kind="ExternalOutput" if dbg else "Internal")
    dr["qf_dram"] = nc.dram_tensor("qf_scr", [NCH, 128, TLOC], BF16,
                                   kind="ExternalOutput" if dbg else "Internal")
    dr["PAYC"] = 560
    dr["shm_pay"] = nc.dram_tensor("shm_pay", [2, 128, 560], F32, addr_space="Shared")
    dr["shm_fl"] = nc.dram_tensor("shm_fl", [2, 8], I32, addr_space="Shared")
    with ExitStack() as stack:
        em = Em(nc, stack)
        cfg = dict(dr)
        cfg["tag"] = "a"
        with nc.allow_low_precision(reason="bf16 operand staging is intentional"):
            emit_attn0(em, cfg)
    return nc


def xchg_host_inputs(rng, col0s=None):
    PAYC = 560
    metas, nonces = [], []
    nonce_vals = rng.integers(1 << 20, 1 << 30, size=8).astype(np.int32)
    if col0s is None:
        col0s = [0, 16, 32, 48]
    col0s = list(col0s) + [0] * (8 - len(col0s))
    for c in range(NCORES):
        m = c % 2
        p = 1 - m
        meta = np.zeros((1, 32), np.int32)
        for r in range(8):
            meta[0, r] = p * 128 * PAYC + col0s[r]
            meta[0, 8 + r] = m * 128 * PAYC + col0s[r]
            meta[0, 16 + r] = p * 8 + r
            meta[0, 24 + r] = m * 8 + r
        metas.append(meta)
        nonces.append(nonce_vals[None, :])
    return metas, nonces


def attn0_in_map(inputs, xcore, meta, nonce, core_idx=0):
    c_ones, c_id, c_sel, c_misc = consts_f()
    selp = np.zeros((16, NCH * 128), BF)
    for c in range(NCH):
        selp[2 * c, c * 128:c * 128 + 64] = 1
        selp[2 * c + 1, c * 128 + 64:(c + 1) * 128] = 1
    onesb = np.ones((128, 128), BF)
    idb = np.eye(128, dtype=BF)
    bpp = np.zeros((128, 4 * NCH), np.float32)
    rows = np.zeros((1, 4 * D), np.float32)
    for i, nm in enumerate(["bq", "bk", "bv", "bo"]):
        b = inputs[f"L0_attn_{nm}"]
        rows[0, i * D:(i + 1) * D] = b
        bpp[:, i * NCH:(i + 1) * NCH] = b.reshape(NCH, 128).T
    return dict(
        x_in=xcore,
        wq=inputs["L0_attn_Wq"], wk=inputs["L0_attn_Wk"],
        wv=inputs["L0_attn_Wv"], wo=inputs["L0_attn_Wo"],
        b_rows=rows, b_pp=bpp,
        c_ones=c_ones, c_id=c_id, c_sel=c_sel, c_misc=c_misc,
        c_selp=selp, c_onesb=onesb, c_idb=idb, meta=meta, nonce=nonce,
        xmask=np.full((128, 1), float(core_idx % 2), np.float32),
    )


def run_attn0(inputs, xcores):
    nc = _get_prog("attn0", build_attn0)
    rng = np.random.default_rng()
    metas, nonces = xchg_host_inputs(rng)
    maps = [attn0_in_map(inputs, xcores[c], metas[c], nonces[c], c)
            for c in range(NCORES)]
    res = run_bass_kernel_spmd(nc, maps, core_ids=list(range(NCORES)))
    return [res.results[c]["x_out"] for c in range(NCORES)]



def emit_attn1(em, cfg):
    nc, g, v, sc, t, sy = em.nc, em.g, em.v, em.sc, em.t, em.sy
    c_ones, c_id, c_sel, c_misc = em.load_consts(cfg)
    x_in, x_out, xn_dram = cfg["x_in"], cfg["x_out"], cfg["xn_dram"]
    qcd, kcd, vcd, ogd = cfg["qc_dram"], cfg["kc_dram"], cfg["vc_dram"], cfg["og_dram"]
    Ed, Pd, Hd = cfg["E_dram"], cfg["P_dram"], cfg["H_dram"]
    tag = cfg["tag"]
    dbg = cfg.get("dbg")
    sb = lambda name, shape, dt=F32: em.stack.enter_context(
        nc.sbuf_tensor(tag + name, shape, dt))
    CJ = TLOC // 128
    SEG = TT + 3
    GN_EPS = 1e-5
    xt = sb("xt", [128, NCH * SEG])
    pre = xt
    sq = sb("sq", [128, 2 * TT])
    xn32 = sb("xn32", [128, 2 * TT])
    xnb = sb("xnb", [128, NCH * TT], BF16)
    srow = sb("srow", [1, 4 * TT])
    silb = sb("silb", [128, NCH * TT], BF16)
    tailp = sb("tailp", [128, NCH * 3])
    tailx = sb("tailx", [128, NCH * 3], BF16)
    wsb = sb("wsb", [128, NCH * D], BF16)
    wab = sb("wab", [128, 2 * NCH * 16], BF16)
    cvpp = sb("cvpp", [128, 3 * NCH * 4])
    bpp = sb("bpp", [128, 5 * NCH])
    abb = sb("abb", [16, 2])
    gnpp = sb("gnpp", [128, 2 * NCH])
    ghmc = sb("ghmc", [16, 128])
    abrow = sb("abrow", [16, 2 * TT])
    lab = sb("lab", [48, TT])
    strx = sb("strx", [128, 48])
    gtr = sb("gtr", [128, 16 * CJ])
    btr = sb("btr", [128, 16 * CJ])
    dtr = sb("dtr", [128, 16 * CJ])
    actr = sb("actr", [128, 16 * CJ])
    batr = sb("batr", [128, 16 * CJ])
    csx = sb("csx", [16, 2 * TT])
    outs = sb("outs", [128, NCH * TLOC], BF16)
    etst = sb("etst", [128, NCH * 128], BF16)
    ptst = sb("ptst", [128, NCH * 64], BF16)
    hst = sb("hst", [128, NCH * 64], BF16)
    qct = sb("qct", [128, NCH * 128], BF16)
    kct = sb("kct", [128, NCH * 128], BF16)
    vct = sb("vct", [128, NCH * 128], BF16)
    Dsb = sb("Dsb", [128, 128])
    esb = sb("esb", [128, 2 * 128])
    lams = sb("lams", [128, 128], BF16)
    lamu = sb("lamu", [128, 128], BF16)
    absb = sb("absb", [128, 128], BF16)
    Msb = sb("Msb", [128, 128], BF16)
    Mtsb = sb("Mtsb", [128, 128], BF16)
    wtsb = sb("wtsb", [128, 128], BF16)
    Tb = sb("Tb", [128, 2 * 128], BF16)
    Ttb = sb("Ttb", [128, 2 * 128], BF16)
    Pb = sb("Pb", [128, 2 * 128], BF16)
    Ptb = sb("Ptb", [128, 2 * 128], BF16)
    vbsb = sb("vbsb", [128, 64], BF16)
    kpsb = sb("kpsb", [128, 64], BF16)
    kbasb = sb("kbasb", [128, 64], BF16)
    uvsb = sb("uvsb", [128, 64], BF16)
    zsb = sb("zsb", [128, 64], BF16)
    qasb = sb("qasb", [128, 128], BF16)
    st32 = sb("st32", [128, 512])
    stbb = sb("stbb", [128, 512], BF16)
    eld = sb("eld", [128, NCH * 128], BF16)
    pld = sb("pld", [128, NCH * 64], BF16)
    hld = sb("hld", [128, NCH * 64], BF16)
    gnb = sb("gnb", [128, 4 * NCH])
    v16 = sb("v16", [16, 16])
    pays = sb("pays", [128, 560])
    pasb = sb("pasb", [128, 560])
    em.x_pasb = pasb
    selp = sb("selp", [16, NCH * 128], BF16)
    selpf = sb("selpf", [16, NCH * 128])
    s16f = sb("s16f", [16, 16 * 128])
    cut = sb("cut", [128, 128])
    clt = sb("clt", [128, 128])
    cmsl = sb("cmsl", [128, 128])
    cmsu = sb("cmsu", [128, 128])
    chm = sb("chm", [128, NCH * 16])
    cpat = sb("cpat", [16, 8])
    idb = sb("idb", [128, 128], BF16)
    xmk = sb("xmk", [128, 1])
    PS = em.stack.enter_context(nc.psum_tensor(tag + "PS", [128, 7 * 512], F32))
    PSb = em.stack.enter_context(nc.psum_tensor(tag + "PSb", [128, 512], BF16))

    def bank(b, p0=0, pn=128, c0=0, cn=512):
        return PS[p0:p0 + pn, b * 512 + c0:b * 512 + c0 + cn]

    emit_xchg_setup(em, cfg)
    for nm, dst in [("c_selp", selp), ("c_selpf", selpf), ("c_s16f", s16f),
                    ("c_ut", cut), ("c_lt", clt), ("c_msl", cmsl),
                    ("c_msu", cmsu), ("c_hm", chm), ("c_pat", cpat),
                    ("c_idb", idb), ("xmask", xmk), ("b_pp", bpp),
                    ("ab_bias", abb), ("gn_pp", gnpp), ("cv_pp", cvpp)]:
        em.inc(sy.dma_start(out=dst[0:dst.shape[0], :], in_=cfg[nm][:, :]), "LD", 16)
    for kc in range(NCH):
        em.inc(g.dma_start(out=wab[:, kc * 16:(kc + 1) * 16],
                           in_=cfg["wa"][kc * 128:(kc + 1) * 128, :]), "LW", 16)
        em.inc(g.dma_start(out=wab[:, 128 + kc * 16:128 + (kc + 1) * 16],
                           in_=cfg["wb_"][kc * 128:(kc + 1) * 128, :]), "LW", 16)

    def load_w(w):
        em.wait(g, "PE")
        for kc in range(NCH):
            em.inc(g.dma_start(out=wsb[:, kc * D:(kc + 1) * D],
                               in_=w[kc * 128:(kc + 1) * 128, :]), "LW", 16)

    def wslice(kc, dc):
        return wsb[:, kc * D + dc * 128:kc * D + (dc + 1) * 128]

    def load_xn(ti):
        em.wait(sy, "PE")
        em.inc(sy.dma_start(out=xnb[:, :].rearrange("p (c l) -> p c l", c=NCH),
                            in_=xn_dram.ap().rearrange("c p l -> p c l")[:, :, ti * TT:(ti + 1) * TT]),
               "LD", 16)
        em.wait(t, "LD")

    emit_ln_to_dram(em, cfg, xt, sq, xn32, xnb, srow, bank, c_ones, c_misc)
    em.wait(v, "ST")
    for c in range(NCH):
        em.vstep(v.tensor_copy(pays[:, c * 3:(c + 1) * 3],
                               xnb[:, c * TT + TT - 3:c * TT + TT]))
    emit_pair_exchange(em, 0, pays, 0, 24)
    em.wait(v, "XD")
    em.vstep(v.tensor_scalar(pays[:, 24:48], pasb[:, 0:24], xmk[:, 0:1], None,
                             OP.mult))
    em.vstep(v.tensor_copy(tailx[:, :], pays[:, 24:48]))
    em.barrier()

    for ti in range(NT):
        tc0 = ti * TT
        load_xn(ti)
        em.wait(t, "LW")
        em.wait(t, "AC")
        em.wait(t, "DV")
        for half in range(2):
            for kc in range(NCH):
                em.inc(t.matmul(bank(3 + half, 0, 16),
                                wab[:, half * 128 + kc * 16:half * 128 + (kc + 1) * 16],
                                xnb[:, kc * TT:(kc + 1) * TT],
                                start=(kc == 0), stop=(kc == NCH - 1)), "PE")
        em.wait(sc, "PE")
        em.inc(sc.activation(abrow[0:16, 0:TT], bank(3, 0, 16), AF.Sigmoid,
                             bias=abb[0:16, 0:1]), "AC")
        em.inc(sc.activation(abrow[0:16, TT:2 * TT], bank(4, 0, 16), AF.Sigmoid,
                             bias=abb[0:16, 1:2]), "AC")
        if dbg:
            em.wait(sy, "AC")
            em.inc(sy.dma_start(out=cfg["dbg_ab"][0, :, tc0:tc0 + TT],
                                in_=abrow[0:16, 0:TT]), "ST", 16)
            em.inc(sy.dma_start(out=cfg["dbg_ab"][1, :, tc0:tc0 + TT],
                                in_=abrow[0:16, TT:2 * TT]), "ST", 16)
        em.wait(sc, "AC")
        em.inc(sc.activation(lab[0:16, :], abrow[0:16, 0:TT], AF.Ln), "AC")
        em.wait(v, "AC")
        em.vstep(v.tensor_copy(lab[32:48, :], abrow[0:16, TT:2 * TT]))
        for j4 in range(4):
            j = ti * 4 + j4
            em.wait(t, "DV")
            em.wait(t, "AC")
            em.inc(t.transpose(bank(5, 0, 128, 0, 48), lab[0:48, j4 * 128:(j4 + 1) * 128],
                               c_id[0:48, 0:48]), "PE")
            em.wait(v, "PE")
            em.vstep(v.tensor_copy(strx[:, :], bank(5, 0, 128, 0, 48)))
            em.vstep(v.tensor_copy(btr[:, j * 16:(j + 1) * 16], strx[:, 32:48]))
            em.wait(t, "DV")
            em.inc(t.matmul(bank(5, 0, 128, 64, 16), cut[:, :], strx[:, 0:16],
                            start=True, stop=True), "PE")
            em.inc(t.matmul(bank(5, 0, 128, 96, 16), clt[:, :], strx[:, 0:16],
                            start=True, stop=True), "PE")
            em.wait(v, "PE")
            em.vstep(v.tensor_copy(gtr[:, j * 16:(j + 1) * 16],
                                   bank(5, 0, 128, 64, 16)))
            em.wait(sc, "DV")
            em.inc(sc.activation(dtr[:, j * 16:(j + 1) * 16],
                                 bank(5, 0, 128, 96, 16), AF.Exp), "AC")
            em.vstep(v.tensor_add(strx[:, 16:32], gtr[:, j * 16:(j + 1) * 16],
                                  bank(5, 0, 128, 96, 16)))
            em.wait(sc, "DV")
            em.wait(sc, "PE")
            em.inc(sc.activation(actr[:, j * 16:(j + 1) * 16], strx[:, 16:32],
                                 AF.Exp), "AC")
            em.inc(sc.activation(strx[:, 0:16], gtr[:, j * 16:(j + 1) * 16],
                                 AF.Exp), "AC")
            em.wait(v, "AC")
            em.vstep(v.tensor_mul(batr[:, j * 16:(j + 1) * 16],
                                  btr[:, j * 16:(j + 1) * 16], strx[:, 0:16]))
    em.barrier()

    for which, wname, dramt in ((0, "wq", qcd), (1, "wk", kcd), (2, "wv", vcd)):
        load_w(cfg[wname])
        em.wait(t, "LW")
        em.wait(t, "DV")
        for dc in range(NCH):
            for kc in range(NCH):
                em.inc(t.matmul(bank(4, 0, 128, dc * 3, 3), wslice(kc, dc),
                                tailx[:, kc * 3:(kc + 1) * 3],
                                start=(kc == 0), stop=(kc == NCH - 1)), "PE")
        em.wait(v, "PE")
        for dc in range(NCH):
            em.vstep(v.tensor_scalar(tailp[:, dc * 3:(dc + 1) * 3],
                                     bank(4, 0, 128, dc * 3, 3),
                                     bpp[:, which * NCH + dc:which * NCH + dc + 1],
                                     None, OP.add))
        em.vstep(v.tensor_scalar(tailp[:, :], tailp[:, :], xmk[:, 0:1], None,
                                 OP.mult))
        for c in range(NCH):
            em.vstep(v.tensor_copy(pre[:, c * SEG:c * SEG + 3],
                                   tailp[:, c * 3:(c + 1) * 3]))
        for ti in range(NT):
            tc0 = ti * TT
            load_xn(ti)
            em.wait(sc, "ST")
            em.wait(v, "ST")
            for dc in range(NCH):
                pb = dc % 2
                em.wait(t, "DV")
                for kc in range(NCH):
                    em.inc(t.matmul(bank(pb), wslice(kc, dc),
                                    xnb[:, kc * TT:(kc + 1) * TT],
                                    start=(kc == 0), stop=(kc == NCH - 1)), "PE")
                em.wait(v, "PE")
                em.vstep(v.tensor_scalar(pre[:, dc * SEG + 3:dc * SEG + SEG],
                                         bank(pb),
                                         bpp[:, which * NCH + dc:which * NCH + dc + 1],
                                         None, OP.add))
            for c in range(NCH):
                for j in range(4):
                    cwcol = cvpp[:, which * 32 + c * 4 + j:which * 32 + c * 4 + j + 1]
                    if j == 0:
                        em.vstep(v.tensor_scalar(sq[:, 0:TT],
                                                 pre[:, c * SEG + j:c * SEG + j + TT],
                                                 cwcol, None, OP.mult))
                    else:
                        em.vstep(v.scalar_tensor_tensor(sq[:, 0:TT],
                                                        pre[:, c * SEG + j:c * SEG + j + TT],
                                                        cwcol, sq[:, 0:TT],
                                                        OP.mult, OP.add))
                em.wait(sc, "DV")
                em.inc(sc.activation(silb[:, c * TT:(c + 1) * TT], sq[:, 0:TT],
                                     AF.Silu), "AC")
                em.wait(v, "AC")
            for c in range(NCH):
                em.vstep(v.tensor_copy(pre[:, c * SEG:c * SEG + 3],
                                       pre[:, c * SEG + SEG - 3:c * SEG + SEG]))
            if which < 2:
                em.wait(t, "AC")
                em.wait(t, "DV")
                for c in range(NCH):
                    em.wait(sc, "PE")
                    em.inc(sc.activation(sq[:, (c % 2) * TT:(c % 2) * TT + TT],
                                         silb[:, c * TT:(c + 1) * TT], AF.Square),
                           "AC")
                    em.wait(t, "AC")
                    em.inc(t.matmul(bank(2, 0, 16), chm[:, c * 16:(c + 1) * 16],
                                    sq[:, (c % 2) * TT:(c % 2) * TT + TT],
                                    start=(c == 0), stop=(c == NCH - 1)), "PE")
                em.wait(sc, "PE")
                em.inc(sc.activation(csx[0:16, 0:TT], bank(2, 0, 16), AF.Sqrt), "AC")
                em.wait(v, "AC")
                em.vstep(v.tensor_scalar(csx[0:16, TT:2 * TT], csx[0:16, 0:TT],
                                         1e-6, None, OP.add))
                em.vstep(v.reciprocal(csx[0:16, 0:TT], csx[0:16, TT:2 * TT]))
                em.vstep(v.tensor_copy(abrow[0:16, 0:TT], csx[0:16, 0:TT]))
                for c in range(NCH):
                    em.wait(t, "DV")
                    em.inc(t.matmul(bank(2), selpf[:, c * 128:(c + 1) * 128],
                                    abrow[0:16, 0:TT], start=True, stop=True), "PE")
                    em.wait(v, "PE")
                    em.vstep(v.tensor_mul(silb[:, c * TT:(c + 1) * TT],
                                          silb[:, c * TT:(c + 1) * TT], bank(2)))
            em.wait(sy, "DV")
            em.inc(sy.dma_start(
                out=dramt.ap().rearrange("c p l -> p c l")[:, :, tc0:tc0 + TT],
                in_=silb[:, :].rearrange("p (c l) -> p c l", c=NCH)), "ST", 16)
    em.barrier()

    em.vstep(v.memset(st32[:, :], 0.0))
    em.vstep(v.memset(stbb[:, :], 0.0))
    for j in range(CJ):
        em.wait(sy, "PE")
        em.wait(sy, "DV")
        for dramt, dst in ((qcd, qct), (kcd, kct), (vcd, vct)):
            em.inc(sy.dma_start(
                out=dst[:, :].rearrange("p (c l) -> p c l", c=NCH),
                in_=dramt.ap().rearrange("c p l -> p c l")[:, :, j * 128:(j + 1) * 128]),
                "LD", 16)
        em.wait(t, "LD")
        em.wait(v, "LD")
        em.wait(t, "DV")
        em.inc(t.transpose(bank(0, 0, 16, 384, 128), gtr[:, j * 16:(j + 1) * 16],
                           c_id[:, :]), "PE")
        em.wait(v, "PE")
        em.vstep(v.tensor_copy(ghmc[0:16, :], bank(0, 0, 16, 384, 128)))
        for h in range(16):
            p0 = 64 * (h % 2)
            cc = (h // 2) * 128
            QT = qct[p0:p0 + 64, cc:cc + 128]
            KT = kct[p0:p0 + 64, cc:cc + 128]
            VT = vct[p0:p0 + 64, cc:cc + 128]
            gcol = gtr[:, j * 16 + h:j * 16 + h + 1]
            bcol = btr[:, j * 16 + h:j * 16 + h + 1]
            dcol = dtr[:, j * 16 + h:j * 16 + h + 1]
            bacol = batr[:, j * 16 + h:j * 16 + h + 1]
            accol = actr[0:64, j * 16 + h:j * 16 + h + 1]
            em.wait(t, "DV")
            em.wait(t, "AC")
            em.inc(t.matmul(bank(0, 0, 128, 0, 128), KT, KT, start=True, stop=True), "PE")
            em.inc(t.matmul(bank(0, 0, 128, 128, 128), KT, QT, start=True, stop=True), "PE")
            em.inc(t.matmul(bank(0, 0, 128, 256, 128), s16f[:, h * 128:(h + 1) * 128],
                            ghmc[0:16, :], start=True, stop=True), "PE")
            em.wait(v, "PE")
            em.vstep(v.tensor_scalar(Dsb[:, :], bank(0, 0, 128, 256, 128), gcol,
                                     None, OP.subtract))
            em.vstep(v.tensor_sub(esb[:, 0:128], cmsl[:, :], Dsb[:, :]))
            em.vstep(v.tensor_add(esb[:, 128:256], cmsu[:, :], Dsb[:, :]))
            em.wait(sc, "DV")
            em.inc(sc.activation(lams[:, :], esb[:, 0:128], AF.Exp), "AC")
            em.inc(sc.activation(lamu[:, :], esb[:, 128:256], AF.Exp), "AC")
            em.inc(sc.activation(absb[:, :], bank(0, 0, 128, 256, 128), AF.Exp), "AC")
            em.wait(v, "AC")
            em.vstep(v.scalar_tensor_tensor(Msb[:, :], bank(0, 0, 128, 0, 128),
                                            bcol, lams[:, :], OP.mult, OP.mult))
            em.vstep(v.tensor_mul(wtsb[:, :], lamu[:, :], bank(0, 0, 128, 128, 128)))
            em.vstep(v.tensor_mul(qasb[:, 0:128], QT, absb[p0:p0 + 64, :])
                     if False else
                     v.tensor_mul(qasb[p0:p0 + 64, 0:128], QT, absb[p0:p0 + 64, :]))
            em.wait(t, "DV")
            em.inc(t.transpose(PSb[:, 0:128], Msb[:, :], idb[:, :]), "PE")
            em.inc(t.transpose(PSb[:, 128:192], VT, idb[p0:p0 + 64, p0:p0 + 64]), "PE")
            em.inc(t.transpose(PSb[:, 192:256], KT, idb[p0:p0 + 64, p0:p0 + 64]), "PE")
            em.wait(v, "PE")
            em.vstep(v.tensor_copy(Mtsb[:, :], PSb[:, 0:128]))
            em.vstep(v.tensor_scalar(vbsb[:, :], PSb[:, 128:192], bcol, None, OP.mult))
            em.vstep(v.tensor_scalar(kpsb[:, :], PSb[:, 192:256], dcol, None, OP.mult))
            em.vstep(v.tensor_scalar(kbasb[:, :], PSb[:, 192:256], bacol, None, OP.mult))
            em.vstep(v.tensor_sub(Tb[:, 0:128], idb[:, :], Msb[:, :]))
            em.vstep(v.tensor_sub(Ttb[:, 0:128], idb[:, :], Mtsb[:, :]))
            em.wait(t, "DV")
            em.inc(t.matmul(bank(1, 0, 128, 0, 128), Mtsb[:, :], Msb[:, :],
                            start=True, stop=True), "PE")
            em.inc(t.matmul(bank(1, 0, 128, 128, 128), Msb[:, :], Mtsb[:, :],
                            start=True, stop=True), "PE")
            em.wait(v, "PE")
            em.vstep(v.tensor_copy(Pb[:, 0:128], bank(1, 0, 128, 0, 128)))
            em.vstep(v.tensor_copy(Ptb[:, 0:128], bank(1, 0, 128, 128, 128)))
            for lev in range(1, 7):
                s0 = ((lev - 1) % 2) * 128
                s1 = (lev % 2) * 128
                em.wait(t, "DV")
                em.inc(t.matmul(bank(1, 0, 128, 0, 128), Ptb[:, s0:s0 + 128],
                                Tb[:, s0:s0 + 128], start=True, stop=True), "PE")
                em.inc(t.matmul(bank(1, 0, 128, 128, 128), Tb[:, s0:s0 + 128],
                                Ptb[:, s0:s0 + 128], start=True, stop=True), "PE")
                if lev < 6:
                    em.inc(t.matmul(bank(2, 0, 128, 0, 128), Ptb[:, s0:s0 + 128],
                                    Pb[:, s0:s0 + 128], start=True, stop=True), "PE")
                    em.inc(t.matmul(bank(2, 0, 128, 128, 128), Pb[:, s0:s0 + 128],
                                    Ptb[:, s0:s0 + 128], start=True, stop=True), "PE")
                em.wait(v, "PE")
                em.vstep(v.tensor_add(Tb[:, s1:s1 + 128], bank(1, 0, 128, 0, 128),
                                      Tb[:, s0:s0 + 128]))
                em.vstep(v.tensor_add(Ttb[:, s1:s1 + 128], bank(1, 0, 128, 128, 128),
                                      Ttb[:, s0:s0 + 128]))
                if lev < 6:
                    em.vstep(v.tensor_copy(Pb[:, s1:s1 + 128], bank(2, 0, 128, 0, 128)))
                    em.vstep(v.tensor_copy(Ptb[:, s1:s1 + 128], bank(2, 0, 128, 128, 128)))
            tts = Ttb[:, (6 % 2) * 128:(6 % 2) * 128 + 128]
            em.wait(t, "DV")
            em.inc(t.matmul(bank(1, 0, 128, 256, 64), tts, vbsb[:, :],
                            start=True, stop=True), "PE")
            em.inc(t.matmul(bank(1, 0, 128, 320, 64), tts, kbasb[:, :],
                            start=True, stop=True), "PE")
            em.wait(v, "PE")
            em.vstep(v.tensor_copy(uvsb[:, :], bank(1, 0, 128, 256, 64)))
            em.vstep(v.tensor_copy(zsb[:, :], bank(1, 0, 128, 320, 64)))
            em.wait(t, "DV")
            em.inc(t.matmul(bank(2, 0, 64, 0, 128), uvsb[:, :], wtsb[:, :],
                            start=True, stop=True), "PE")
            em.inc(t.matmul(bank(2, 0, 64, 128, 128), zsb[:, :], wtsb[:, :],
                            start=True, stop=True), "PE")
            em.inc(t.matmul(bank(2, 0, 64, 256, 64), zsb[:, :], kpsb[:, :],
                            start=True, stop=True), "PE")
            em.inc(t.matmul(bank(2, 0, 64, 320, 64), kpsb[:, :], uvsb[:, :],
                            start=True, stop=True), "PE")
            em.wait(v, "PE")
            ocol = (h // 2) * TLOC + j * 128
            em.vstep(v.tensor_copy(outs[p0:p0 + 64, ocol:ocol + 128],
                                   bank(2, 0, 64, 0, 128)))
            em.vstep(v.tensor_sub(etst[p0:p0 + 64, cc:cc + 128],
                                  qasb[p0:p0 + 64, 0:128],
                                  bank(2, 0, 64, 128, 128)))
            em.vstep(v.scalar_tensor_tensor(ptst[p0:p0 + 64, (h // 2) * 64:(h // 2) * 64 + 64],
                                            idb[0:64, 0:64], accol,
                                            bank(2, 0, 64, 256, 64),
                                            OP.mult, OP.subtract))
            em.vstep(v.tensor_copy(hst[p0:p0 + 64, (h // 2) * 64:(h // 2) * 64 + 64],
                                   bank(2, 0, 64, 320, 64)))
        em.wait(sy, "DV")
        em.inc(sy.dma_start(out=Ed[j, :, :], in_=etst[:, :]), "ST", 16)
        em.inc(sy.dma_start(out=Pd[j, :, :], in_=ptst[:, :]), "ST", 16)
        em.inc(sy.dma_start(out=Hd[j, :, :], in_=hst[:, :]), "ST", 16)
    em.barrier()

    def state_pass(correction):
        for j in range(CJ):
            em.wait(sy, "PE")
            em.inc(sy.dma_start(out=eld[:, :], in_=Ed[j, :, :]), "LD", 16)
            em.inc(sy.dma_start(out=pld[:, :], in_=Pd[j, :, :]), "LD", 16)
            if not correction:
                em.inc(sy.dma_start(out=hld[:, :], in_=Hd[j, :, :]), "LD", 16)
            em.wait(t, "LD")
            for h in range(16):
                p0 = 64 * (h % 2)
                c64 = (h // 2) * 64
                cc = (h // 2) * 128
                stb_h = stbb[p0:p0 + 64, c64:c64 + 64]
                et_h = eld[p0:p0 + 64, cc:cc + 128]
                pt_h = pld[p0:p0 + 64, c64:c64 + 64]
                em.wait(t, "DV")
                em.inc(t.matmul(bank(3, p0, 64, 0, 128), stb_h, et_h,
                                start=True, stop=True), "PE")
                em.inc(t.matmul(bank(3, p0, 64, 128, 64), pt_h, stb_h,
                                start=True, stop=True), "PE")
                em.wait(v, "PE")
                ocol = (h // 2) * TLOC + j * 128
                em.vstep(v.tensor_add(outs[p0:p0 + 64, ocol:ocol + 128],
                                      outs[p0:p0 + 64, ocol:ocol + 128],
                                      bank(3, p0, 64, 0, 128)))
                if not correction:
                    em.vstep(v.tensor_add(st32[p0:p0 + 64, c64:c64 + 64],
                                          bank(3, p0, 64, 128, 64),
                                          hld[p0:p0 + 64, c64:c64 + 64]))
                else:
                    em.vstep(v.tensor_copy(st32[p0:p0 + 64, c64:c64 + 64],
                                           bank(3, p0, 64, 128, 64)))
                em.vstep(v.tensor_copy(stbb[p0:p0 + 64, c64:c64 + 64],
                                       st32[p0:p0 + 64, c64:c64 + 64]))

    state_pass(False)
    em.vstep(v.tensor_copy(pays[:, 48:560], st32[:, 0:512]))
    emit_pair_exchange(em, 1, pays, 48, 512)
    em.wait(v, "XD")
    em.vstep(v.tensor_scalar(st32[:, 0:512], pasb[:, 48:560], xmk[:, 0:1],
                             None, OP.mult))
    em.vstep(v.tensor_copy(stbb[:, 0:512], st32[:, 0:512]))
    state_pass(True)
    em.barrier()
    if dbg:
        em.inc(sy.dma_start(out=cfg["dbg_outs"].ap().rearrange("c p l -> p c l"),
                            in_=outs[:, :].rearrange("p (c l) -> p c l", c=NCH)),
               "ST", 16)
        em.wait(sy, "ST")

    for c in range(NCH):
        em.vstep(v.reduce_sum(gnb[:, c:c + 1], outs[:, c * TLOC:(c + 1) * TLOC],
                              axis=mybir.AxisListType.X))
    for ti in range(NT):
        for c in range(NCH):
            em.wait(sc, "DV")
            em.inc(sc.activation(sq[:, (c % 2) * TT:(c % 2) * TT + TT],
                                 outs[:, c * TLOC + ti * TT:c * TLOC + ti * TT + TT],
                                 AF.Square), "AC")
            em.wait(v, "AC")
            em.vstep(v.reduce_sum(pays[:, 40 + c:40 + c + 1] if ti == 0 else xn32[:, 0:1],
                                  sq[:, (c % 2) * TT:(c % 2) * TT + TT],
                                  axis=mybir.AxisListType.X))
            if ti > 0:
                em.vstep(v.tensor_add(pays[:, 40 + c:40 + c + 1],
                                      pays[:, 40 + c:40 + c + 1], xn32[:, 0:1]))
    em.wait(t, "DV")
    for c in range(NCH):
        em.inc(t.matmul(bank(4, 0, 16, 0, 1), chm[:, c * 16:(c + 1) * 16],
                        gnb[:, c:c + 1], start=(c == 0), stop=(c == NCH - 1)), "PE")
        em.inc(t.matmul(bank(4, 0, 16, 1, 1), chm[:, c * 16:(c + 1) * 16],
                        pays[:, 40 + c:40 + c + 1], start=(c == 0), stop=(c == NCH - 1)),
               "PE")
    em.wait(v, "PE")
    em.vstep(v.tensor_copy(pays[0:16, 24:25], bank(4, 0, 16, 0, 1)))
    em.vstep(v.tensor_copy(pays[0:16, 25:26], bank(4, 0, 16, 1, 1)))
    emit_pair_exchange(em, 2, pays, 24, 16)
    em.wait(v, "XD")
    NT_TOT = float(S * 64)
    em.vstep(v.tensor_add(v16[0:16, 0:2], pays[0:16, 24:26], pasb[0:16, 24:26]))
    em.vstep(v.tensor_scalar_mul(v16[0:16, 2:3], v16[0:16, 0:1], 1.0 / NT_TOT))
    em.vstep(v.tensor_scalar_mul(v16[0:16, 3:4], v16[0:16, 1:2], 1.0 / NT_TOT))
    em.vstep(v.tensor_mul(v16[0:16, 4:5], v16[0:16, 2:3], v16[0:16, 2:3]))
    em.vstep(v.tensor_sub(v16[0:16, 5:6], v16[0:16, 3:4], v16[0:16, 4:5]))
    em.wait(sc, "DV")
    em.inc(sc.activation(v16[0:16, 6:7], v16[0:16, 5:6], AF.Sqrt,
                         bias=c_misc[0:16, 0:1]), "AC")
    em.wait(v, "AC")
    em.vstep(v.reciprocal(v16[0:16, 7:8], v16[0:16, 6:7]))
    for qi, col in ((2, 0), (7, 1)):
        em.vstep(v.tensor_scalar(v16[0:16, 8:16], cpat[0:16, :], v16[0:16, qi:qi + 1],
                                 None, OP.mult))
        em.wait(t, "DV")
        for c in range(NCH):
            em.inc(t.matmul(bank(4, 0, 128, 0, 8), selpf[:, c * 128:(c + 1) * 128],
                            v16[0:16, 8:16], start=(c == 0), stop=(c == NCH - 1)), "PE")
        em.wait(v, "PE")
        em.vstep(v.tensor_copy(gnb[:, 8 + col * 8:16 + col * 8], bank(4, 0, 128, 0, 8)))
    em.vstep(v.tensor_mul(gnb[:, 24:32], gnpp[:, 0:8], gnb[:, 16:24]))
    em.vstep(v.tensor_mul(gnb[:, 0:8], gnb[:, 8:16], gnb[:, 24:32]))
    em.vstep(v.tensor_sub(gnb[:, 8:16], gnpp[:, 8:16], gnb[:, 0:8]))
    em.barrier()

    load_w(cfg["wg"])
    for ti in range(NT):
        tc0 = ti * TT
        load_xn(ti)
        em.wait(t, "LW")
        for dc in range(NCH):
            pb = dc % 2
            em.wait(t, "AC")
            for kc in range(NCH):
                em.inc(t.matmul(bank(pb), wslice(kc, dc),
                                xnb[:, kc * TT:(kc + 1) * TT],
                                start=(kc == 0), stop=(kc == NCH - 1)), "PE")
            em.wait(sc, "PE")
            em.inc(sc.activation(silb[:, dc * TT:(dc + 1) * TT], bank(pb), AF.Silu,
                                 bias=bpp[:, 3 * NCH + dc:3 * NCH + dc + 1]), "AC")
        em.wait(sy, "AC")
        em.inc(sy.dma_start(
            out=ogd.ap().rearrange("c p l -> p c l")[:, :, tc0:tc0 + TT],
            in_=silb[:, :].rearrange("p (c l) -> p c l", c=NCH)), "ST", 16)
        em.wait(sc, "ST")
    em.barrier()

    load_w(cfg["wo"])
    for ti in range(NT):
        tc0 = ti * TT
        em.wait(sy, "PE")
        em.wait(sy, "DV")
        em.inc(sy.dma_start(out=silb[:, :].rearrange("p (c l) -> p c l", c=NCH),
                            in_=ogd.ap().rearrange("c p l -> p c l")[:, :, tc0:tc0 + TT]),
               "LD", 16)
        for c in range(NCH):
            em.inc(sy.dma_start(out=xt[:, c * TT:(c + 1) * TT],
                                in_=x_in[c, :, tc0:tc0 + TT]), "LD", 16)
        em.wait(v, "LD")
        for c in range(NCH):
            em.vstep(v.tensor_scalar(xnb[:, c * TT:(c + 1) * TT],
                                     outs[:, c * TLOC + tc0:c * TLOC + tc0 + TT],
                                     gnb[:, 24 + c:24 + c + 1],
                                     gnb[:, 8 + c:8 + c + 1],
                                     OP.mult, OP.add))
            em.vstep(v.tensor_mul(xnb[:, c * TT:(c + 1) * TT],
                                  xnb[:, c * TT:(c + 1) * TT],
                                  silb[:, c * TT:(c + 1) * TT]))
        em.wait(t, "DV")
        em.wait(t, "LW")
        for dc in range(NCH):
            pb = dc % 2
            em.wait(t, "ST")
            em.wait(t, "DV")
            for kc in range(NCH):
                em.inc(t.matmul(bank(pb), wslice(kc, dc),
                                xnb[:, kc * TT:(kc + 1) * TT],
                                start=(kc == 0), stop=(kc == NCH - 1)), "PE")
            em.wait(v, "PE")
            sl2 = dc % 2
            em.vstep(v.tensor_scalar(sq[:, sl2 * TT:(sl2 + 1) * TT], bank(pb),
                                     bpp[:, 4 * NCH + dc:4 * NCH + dc + 1], None,
                                     OP.add))
            em.vstep(v.tensor_add(sq[:, sl2 * TT:(sl2 + 1) * TT],
                                  sq[:, sl2 * TT:(sl2 + 1) * TT],
                                  xt[:, dc * TT:(dc + 1) * TT]))
            em.wait(sy, "DV")
            em.inc(sy.dma_start(out=x_out[dc, :, tc0:tc0 + TT],
                                in_=sq[:, sl2 * TT:(sl2 + 1) * TT]), "ST", 16)
    em.wait(sy, "ST")


def build_attn1(dbg=False):
    nc = bass.Bass("TRN2", target_bir_lowering=False)
    CJ = TLOC // 128
    dr = {}
    dr["x_in"] = nc.dram_tensor("x_in", [NCH, 128, TLOC], F32, kind="ExternalInput")
    dr["x_out"] = nc.dram_tensor("x_out", [NCH, 128, TLOC], F32, kind="ExternalOutput")
    for nm in ["wq", "wk", "wv", "wg", "wo"]:
        dr[nm] = nc.dram_tensor(nm, [D, D], F32, kind="ExternalInput")
    dr["wa"] = nc.dram_tensor("wa", [D, 16], F32, kind="ExternalInput")
    dr["wb_"] = nc.dram_tensor("wb_", [D, 16], F32, kind="ExternalInput")
    for nm, shape in [("b_pp", [128, 5 * NCH]), ("ab_bias", [16, 2]),
                      ("gn_pp", [128, 2 * NCH]), ("cv_pp", [128, 3 * NCH * 4]),
                      ("c_ones", [128, 128]), ("c_id", [128, 128]),
                      ("c_misc", [128, 8]), ("c_selpf", [16, NCH * 128]),
                      ("c_s16f", [16, 16 * 128]), ("c_ut", [128, 128]),
                      ("c_lt", [128, 128]), ("c_msl", [128, 128]),
                      ("c_msu", [128, 128]), ("c_hm", [128, NCH * 16]),
                      ("c_pat", [16, 8]), ("xmask", [128, 1])]:
        dr[nm] = nc.dram_tensor(nm, shape, F32, kind="ExternalInput")
    dr["c_sel"] = nc.dram_tensor("c_sel", [8, 8 * 128], BF16, kind="ExternalInput")
    dr["c_selp"] = nc.dram_tensor("c_selp", [16, NCH * 128], BF16, kind="ExternalInput")
    dr["c_idb"] = nc.dram_tensor("c_idb", [128, 128], BF16, kind="ExternalInput")
    dr["meta"] = nc.dram_tensor("meta", [1, 32], I32, kind="ExternalInput")
    dr["nonce"] = nc.dram_tensor("nonce", [1, 8], I32, kind="ExternalInput")
    kind = "ExternalOutput" if dbg else "Internal"
    dr["xn_dram"] = nc.dram_tensor("xn_scr", [NCH, 128, TLOC], BF16, kind=kind)
    dr["qc_dram"] = nc.dram_tensor("qc_scr", [NCH, 128, TLOC], BF16, kind=kind)
    dr["kc_dram"] = nc.dram_tensor("kc_scr", [NCH, 128, TLOC], BF16, kind=kind)
    dr["vc_dram"] = nc.dram_tensor("vc_scr", [NCH, 128, TLOC], BF16, kind=kind)
    dr["og_dram"] = nc.dram_tensor("og_scr", [NCH, 128, TLOC], BF16)
    dr["E_dram"] = nc.dram_tensor("E_scr", [CJ, 128, NCH * 128], BF16)
    dr["P_dram"] = nc.dram_tensor("P_scr", [CJ, 128, NCH * 64], BF16)
    dr["H_dram"] = nc.dram_tensor("H_scr", [CJ, 128, NCH * 64], BF16)
    if dbg:
        dr["dbg_ab"] = nc.dram_tensor("dbg_ab", [2, 16, TLOC], F32, kind="ExternalOutput")
        dr["dbg_outs"] = nc.dram_tensor("dbg_outs", [NCH, 128, TLOC], BF16,
                                        kind="ExternalOutput")
    dr["PAYC"] = 560
    dr["shm_pay"] = nc.dram_tensor("shm_pay", [2, 128, 560], F32, addr_space="Shared")
    dr["shm_fl"] = nc.dram_tensor("shm_fl", [2, 8], I32, addr_space="Shared")
    with ExitStack() as stack:
        em = Em(nc, stack)
        cfg = dict(dr)
        cfg["tag"] = "b"
        cfg["dbg"] = dbg
        with nc.allow_low_precision(reason="bf16 operand staging is intentional"):
            emit_attn1(em, cfg)
    return nc


def attn1_in_map(inputs, xcore, meta, nonce, core_idx):
    c_ones, c_id, c_sel, c_misc = consts_f()
    selp = np.zeros((16, NCH * 128), BF)
    selpf = np.zeros((16, NCH * 128), np.float32)
    for c in range(NCH):
        selp[2 * c, c * 128:c * 128 + 64] = 1
        selp[2 * c + 1, c * 128 + 64:(c + 1) * 128] = 1
        selpf[2 * c, c * 128:c * 128 + 64] = 1
        selpf[2 * c + 1, c * 128 + 64:(c + 1) * 128] = 1
    s16f = np.zeros((16, 16 * 128), np.float32)
    for h in range(16):
        s16f[h, h * 128:(h + 1) * 128] = 1
    idx = np.arange(128)
    cut = (idx[:, None] <= idx[None, :]).astype(np.float32)
    clt = (idx[:, None] > idx[None, :]).astype(np.float32)
    cmsl = np.where(idx[None, :] < idx[:, None], 0.0, -1e30).astype(np.float32)
    cmsu = np.where(idx[:, None] <= idx[None, :], 0.0, -1e30).astype(np.float32)
    chm = np.zeros((128, NCH * 16), np.float32)
    for c in range(NCH):
        chm[0:64, c * 16 + 2 * c] = 1
        chm[64:128, c * 16 + 2 * c + 1] = 1
    cpat = np.zeros((16, 8), np.float32)
    for h in range(16):
        cpat[h, h // 2] = 1
    idb = np.eye(128, dtype=BF)
    onesb = np.ones((128, 128), BF)
    xmask = np.full((128, 1), float(core_idx % 2), np.float32)
    bpp = np.zeros((128, 5 * NCH), np.float32)
    for i, nm in enumerate(["bq", "bk", "bv", "bg", "bo"]):
        bpp[:, i * NCH:(i + 1) * NCH] = inputs[f"L1_attn_{nm}"].reshape(NCH, 128).T
    abb = np.stack([inputs["L1_attn_ba"], inputs["L1_attn_bb"]], axis=1).astype(np.float32)
    gnpp = np.zeros((128, 2 * NCH), np.float32)
    gnpp[:, 0:NCH] = inputs["L1_attn_gn_scale"].reshape(NCH, 128).T
    gnpp[:, NCH:2 * NCH] = inputs["L1_attn_gn_bias"].reshape(NCH, 128).T
    cvpp = np.zeros((128, 3 * NCH * 4), np.float32)
    for i, nm in enumerate(["q_conv", "k_conv", "v_conv"]):
        w = inputs[f"L1_attn_{nm}"]
        cvpp[:, i * 32:(i + 1) * 32] = w.reshape(NCH, 128, 4).transpose(1, 0, 2).reshape(128, 32)
    return dict(
        x_in=xcore,
        wq=inputs["L1_attn_Wq"], wk=inputs["L1_attn_Wk"], wv=inputs["L1_attn_Wv"],
        wg=inputs["L1_attn_Wg"], wo=inputs["L1_attn_Wo"],
        wa=inputs["L1_attn_Wa"], wb_=inputs["L1_attn_Wb"],
        b_pp=bpp, ab_bias=abb, gn_pp=gnpp, cv_pp=cvpp,
        c_ones=c_ones, c_id=c_id, c_sel=c_sel, c_misc=c_misc,
        c_selp=selp, c_selpf=selpf, c_s16f=s16f, c_ut=cut, c_lt=clt,
        c_msl=cmsl, c_msu=cmsu, c_hm=chm, c_pat=cpat, c_idb=idb,
        xmask=xmask, meta=meta, nonce=nonce,
    )


def run_attn1(inputs, xcores):
    nc = _get_prog("attn1", build_attn1)
    rng = np.random.default_rng()
    metas, nonces = xchg_host_inputs(rng, col0s=[0, 48, 24, 0])
    maps = [attn1_in_map(inputs, xcores[c], metas[c], nonces[c], c)
            for c in range(NCORES)]
    res = run_bass_kernel_spmd(nc, maps, core_ids=list(range(NCORES)))
    return [res.results[c]["x_out"] for c in range(NCORES)]


def build_ffn(layer, dbg=False):
    nc = bass.Bass("TRN2", target_bir_lowering=False)
    E = 8 if layer == 0 else 7
    I = 2048 if layer == 0 else 512
    NBIAS = 1 if layer == 0 else 2
    dr = {}
    dr["x_in"] = nc.dram_tensor("x_in", [NCH, 128, TLOC], F32, kind="ExternalInput")
    dr["x_out"] = nc.dram_tensor("x_out", [NCH, 128, TLOC], F32, kind="ExternalOutput")
    dr["gw"] = nc.dram_tensor("gw", [E, D, I], F32, kind="ExternalInput")
    dr["uw"] = nc.dram_tensor("uw", [E, D, I], F32, kind="ExternalInput")
    dr["dw"] = nc.dram_tensor("dw", [E, I, D], F32, kind="ExternalInput")
    dr["sg"] = nc.dram_tensor("sg", [D, I], F32, kind="ExternalInput")
    dr["su"] = nc.dram_tensor("su", [D, I], F32, kind="ExternalInput")
    dr["sd"] = nc.dram_tensor("sd", [I, D], F32, kind="ExternalInput")
    dr["rw"] = nc.dram_tensor("rw", [D, E], F32, kind="ExternalInput")
    dr["rbias"] = nc.dram_tensor("rbias", [NBIAS, E], F32, kind="ExternalInput")
    dr["c_ones"] = nc.dram_tensor("c_ones", [128, 128], F32, kind="ExternalInput")
    dr["c_id"] = nc.dram_tensor("c_id", [128, 128], F32, kind="ExternalInput")
    dr["c_sel"] = nc.dram_tensor("c_sel", [8, 8 * 128], BF16, kind="ExternalInput")
    dr["c_misc"] = nc.dram_tensor("c_misc", [128, 8], F32, kind="ExternalInput")
    kind = "ExternalOutput" if dbg else "Internal"
    dr["xn_dram"] = nc.dram_tensor("xn_scr", [NCH, 128, TLOC], BF16, kind=kind)
    dr["yacc"] = nc.dram_tensor("yacc_scr", [NCH, 128, TLOC], F32, kind=kind)
    if dbg:
        dr["dbg_probs"] = nc.dram_tensor("dbg_probs", [128, TCH * 8], F32, kind="ExternalOutput")
        dr["dbg_wtok"] = nc.dram_tensor("dbg_wtok", [128, TCH * 8], F32, kind="ExternalOutput")
    with ExitStack() as stack:
        em = Em(nc, stack)
        cfg = dict(dr)
        cfg["E"], cfg["I"], cfg["tag"] = E, I, "f"
        cfg["dbg"] = dbg
        cfg["use_sel"] = True
        with nc.allow_low_precision(reason="bf16 operand staging is intentional"):
            emit_ffn(em, cfg)
    return nc


def shard_x(x):
    out = []
    for c in range(NCORES):
        b, h = c // 2, c % 2
        xs = x[b, h * TLOC:(h + 1) * TLOC, :]
        xf = np.ascontiguousarray(xs.T)
        out.append(np.ascontiguousarray(xf.reshape(NCH, 128, TLOC)))
    return out


def unshard_x(parts):
    x = np.empty((B, S, D), np.float32)
    for c in range(NCORES):
        b, h = c // 2, c % 2
        xf = parts[c].reshape(D, TLOC)
        x[b, h * TLOC:(h + 1) * TLOC, :] = xf.T
    return x


def consts_f():
    c_ones = np.ones((128, 128), np.float32)
    c_id = np.eye(128, dtype=np.float32)
    c_sel = np.zeros((8, 8 * 128), BF)
    for e in range(8):
        c_sel[e, e * 128:(e + 1) * 128] = 1
    c_misc = np.zeros((128, 8), np.float32)
    c_misc[:, 0] = LN_EPS
    c_misc[:, 1] = 1.0
    c_misc[:, 2] = 1e-6
    return c_ones, c_id, c_sel, c_misc


def ffn_in_map(layer, inputs, xcore):
    c_ones, c_id, c_sel, c_misc = consts_f()
    if layer == 0:
        m = dict(
            gw=inputs["L0_ffn_gate_w"], uw=inputs["L0_ffn_up_w"],
            dw=inputs["L0_ffn_down_w"],
            sg=np.ascontiguousarray(inputs["L0_ffn_shared_gate"].T),
            su=np.ascontiguousarray(inputs["L0_ffn_shared_up"].T),
            sd=np.ascontiguousarray(inputs["L0_ffn_shared_down"].T),
            rw=np.ascontiguousarray(inputs["L0_ffn_router_w"].T),
            rbias=inputs["L0_ffn_routing_bias"][None, :],
        )
    else:
        m = dict(
            gw=inputs["L1_ffn_r_gate"], uw=inputs["L1_ffn_r_up"],
            dw=inputs["L1_ffn_r_down"],
            sg=inputs["L1_ffn_sh_gate"], su=inputs["L1_ffn_sh_up"],
            sd=inputs["L1_ffn_sh_down"],
            rw=inputs["L1_ffn_w_r"],
            rbias=np.stack([inputs["L1_ffn_b_r"], inputs["L1_ffn_routing_bias"]]),
        )
    m["c_ones"], m["c_id"], m["c_sel"], m["c_misc"] = c_ones, c_id, c_sel, c_misc
    m["x_in"] = xcore
    return m


_prog_cache = {}


def _get_prog(name, builder):
    if name not in _prog_cache:
        _prog_cache[name] = builder()
    return _prog_cache[name]


def run_ffn(layer, inputs, xcores):
    nc = _get_prog(f"ffn{layer}", lambda: build_ffn(layer))
    maps = [ffn_in_map(layer, inputs, xcores[c]) for c in range(NCORES)]
    res = run_bass_kernel_spmd(nc, maps, core_ids=list(range(NCORES)))
    return [res.results[c]["x_out"] for c in range(NCORES)]



def build_fused(dbg=False):
    nc = bass.Bass("TRN2", target_bir_lowering=False)
    CJ = TLOC // 128
    dr = {}
    dr["x0"] = nc.dram_tensor("x0", [NCH, 128, TLOC], F32, kind="ExternalInput")
    xs = [dr["x0"]]
    for i in range(1, 6):
        xs.append(nc.dram_tensor(f"xi{i}", [NCH, 128, TLOC], F32))
    xs.append(nc.dram_tensor("x_out", [NCH, 128, TLOC], F32, kind="ExternalOutput"))
    for nm, shape, dt in [("c_ones", [128, 128], F32), ("c_id", [128, 128], F32),
                          ("c_misc", [128, 8], F32), ("c_sel", [8, 8 * 128], BF16),
                          ("c_selp", [16, NCH * 128], BF16),
                          ("c_idb", [128, 128], BF16),
                          ("c_onesb", [128, 128], BF16),
                          ("c_selpf", [16, NCH * 128], F32),
                          ("c_s16f", [16, 16 * 128], F32),
                          ("c_ut", [128, 128], F32), ("c_lt", [128, 128], F32),
                          ("c_msl", [128, 128], F32), ("c_msu", [128, 128], F32),
                          ("c_hm", [128, NCH * 16], F32), ("c_pat", [16, 8], F32),
                          ("xmask", [128, 1], F32),
                          ("meta", [1, 32], I32), ("nonce", [1, 8], I32)]:
        dr[nm] = nc.dram_tensor(nm, shape, dt, kind="ExternalInput")
    dr["shm_pay"] = nc.dram_tensor("shm_pay", [2, 128, 560], F32, addr_space="Shared")
    dr["shm_fl"] = nc.dram_tensor("shm_fl", [2, 8], I32, addr_space="Shared")
    for L, E, I in ((0, 8, 2048), (1, 7, 512)):
        NB = 1 if L == 0 else 2
        dr[f"f{L}_gw"] = nc.dram_tensor(f"f{L}_gw", [E, D, I], F32, kind="ExternalInput")
        dr[f"f{L}_uw"] = nc.dram_tensor(f"f{L}_uw", [E, D, I], F32, kind="ExternalInput")
        dr[f"f{L}_dw"] = nc.dram_tensor(f"f{L}_dw", [E, I, D], F32, kind="ExternalInput")
        dr[f"f{L}_sg"] = nc.dram_tensor(f"f{L}_sg", [D, I], F32, kind="ExternalInput")
        dr[f"f{L}_su"] = nc.dram_tensor(f"f{L}_su", [D, I], F32, kind="ExternalInput")
        dr[f"f{L}_sd"] = nc.dram_tensor(f"f{L}_sd", [I, D], F32, kind="ExternalInput")
        dr[f"f{L}_rw"] = nc.dram_tensor(f"f{L}_rw", [D, E], F32, kind="ExternalInput")
        dr[f"f{L}_rb"] = nc.dram_tensor(f"f{L}_rb", [NB, E], F32, kind="ExternalInput")
    for nm in ["wq", "wk", "wv", "wo"]:
        dr[f"a0_{nm}"] = nc.dram_tensor(f"a0_{nm}", [D, D], F32, kind="ExternalInput")
    dr["a0_b_rows"] = nc.dram_tensor("a0_b_rows", [1, 4 * D], F32, kind="ExternalInput")
    dr["a0_b_pp"] = nc.dram_tensor("a0_b_pp", [128, 4 * NCH], F32, kind="ExternalInput")
    for nm in ["wq", "wk", "wv", "wg", "wo"]:
        dr[f"a1_{nm}"] = nc.dram_tensor(f"a1_{nm}", [D, D], F32, kind="ExternalInput")
    dr["a1_wa"] = nc.dram_tensor("a1_wa", [D, 16], F32, kind="ExternalInput")
    dr["a1_wb_"] = nc.dram_tensor("a1_wb_", [D, 16], F32, kind="ExternalInput")
    dr["a1_b_pp"] = nc.dram_tensor("a1_b_pp", [128, 5 * NCH], F32, kind="ExternalInput")
    dr["a1_ab_bias"] = nc.dram_tensor("a1_ab_bias", [16, 2], F32, kind="ExternalInput")
    dr["a1_gn_pp"] = nc.dram_tensor("a1_gn_pp", [128, 2 * NCH], F32, kind="ExternalInput")
    dr["a1_cv_pp"] = nc.dram_tensor("a1_cv_pp", [128, 3 * NCH * 4], F32, kind="ExternalInput")
    scr = {}
    for i in range(6):
        scr[f"xn{i}"] = nc.dram_tensor(f"xns{i}", [NCH, 128, TLOC], BF16)
    for i in (0, 2, 3, 5):
        scr[f"yacc{i}"] = nc.dram_tensor(f"yacc{i}", [NCH, 128, TLOC], F32)
    scr["qf0"] = nc.dram_tensor("qf0", [NCH, 128, TLOC], BF16)
    for nm in ["qc", "kc", "vc", "og"]:
        scr[nm] = nc.dram_tensor(nm + "s", [NCH, 128, TLOC], BF16)
    scr["E"] = nc.dram_tensor("E_scr", [CJ, 128, NCH * 128], BF16)
    scr["P"] = nc.dram_tensor("P_scr", [CJ, 128, NCH * 64], BF16)
    scr["H"] = nc.dram_tensor("H_scr", [CJ, 128, NCH * 64], BF16)

    with ExitStack() as outer:
        em = Em(nc, outer)
        base = {k: dr[k] for k in dr}
        base["PAYC"] = 560
        base["use_sel"] = True
        em.load_consts(base)
        base["tag"] = ""
        emit_xchg_setup(em, base)

        def sub(i, emitfn, cfg):
            cfg = dict(base, **cfg)
            cfg["x_in"] = xs[i]
            cfg["x_out"] = xs[i + 1]
            cfg["tag"] = f"s{i}_"
            with ExitStack() as inner:
                saved = em.stack
                em.stack = inner
                with nc.allow_low_precision(reason="bf16 staging intentional"):
                    emitfn(em, cfg)
                em.stack = saved
            em.barrier()

        def fcfg(L, i):
            return dict(gw=dr[f"f{L}_gw"], uw=dr[f"f{L}_uw"], dw=dr[f"f{L}_dw"],
                        sg=dr[f"f{L}_sg"], su=dr[f"f{L}_su"], sd=dr[f"f{L}_sd"],
                        rw=dr[f"f{L}_rw"], rbias=dr[f"f{L}_rb"],
                        E=(8 if L == 0 else 7), I=(2048 if L == 0 else 512),
                        xn_dram=scr[f"xn{i}"], yacc=scr[f"yacc{i}"])

        sub(0, emit_ffn, fcfg(0, 0))
        sub(1, emit_attn0, dict(wq=dr["a0_wq"], wk=dr["a0_wk"], wv=dr["a0_wv"],
                                wo=dr["a0_wo"], b_rows=dr["a0_b_rows"],
                                b_pp=dr["a0_b_pp"], xn_dram=scr["xn1"],
                                qf_dram=scr["qf0"], rbase=0))
        sub(2, emit_ffn, fcfg(0, 2))
        sub(3, emit_ffn, fcfg(1, 3))
        sub(4, emit_attn1, dict(wq=dr["a1_wq"], wk=dr["a1_wk"], wv=dr["a1_wv"],
                                wg=dr["a1_wg"], wo=dr["a1_wo"], wa=dr["a1_wa"],
                                wb_=dr["a1_wb_"], b_pp=dr["a1_b_pp"],
                                ab_bias=dr["a1_ab_bias"], gn_pp=dr["a1_gn_pp"],
                                cv_pp=dr["a1_cv_pp"], xn_dram=scr["xn4"],
                                qc_dram=scr["qc"], kc_dram=scr["kc"],
                                vc_dram=scr["vc"], og_dram=scr["og"],
                                E_dram=scr["E"], P_dram=scr["P"], H_dram=scr["H"],
                                rbase=4))
        sub(5, emit_ffn, fcfg(1, 5))
    return nc


def fused_in_maps(inputs):
    c_ones, c_id, c_sel, c_misc = consts_f()
    rng = np.random.default_rng()
    col0s = [0, 16, 32, 48, 0, 48, 24, 0]
    metas, nonces = xchg_host_inputs(rng, col0s=col0s)
    xcores = shard_x(np.asarray(inputs["L0_attn_x"]))
    maps = []
    for c in range(NCORES):
        a0 = attn0_in_map(inputs, None, metas[c], nonces[c], c)
        a1 = attn1_in_map(inputs, None, metas[c], nonces[c], c)
        m = dict(x0=xcores[c], c_ones=c_ones, c_id=c_id, c_misc=c_misc,
                 c_sel=a0["c_sel"], c_selp=a1["c_selp"], c_idb=a1["c_idb"],
                 c_onesb=a0["c_onesb"], c_selpf=a1["c_selpf"],
                 c_s16f=a1["c_s16f"], c_ut=a1["c_ut"], c_lt=a1["c_lt"],
                 c_msl=a1["c_msl"], c_msu=a1["c_msu"], c_hm=a1["c_hm"],
                 c_pat=a1["c_pat"], xmask=a1["xmask"],
                 meta=metas[c], nonce=nonces[c])
        for L in (0, 1):
            fm = ffn_in_map(L, inputs, None)
            for k in ["gw", "uw", "dw", "sg", "su", "sd", "rw", "rbias"]:
                m[f"f{L}_{'rb' if k == 'rbias' else k}"] = fm[k]
        for nm in ["wq", "wk", "wv", "wo"]:
            m[f"a0_{nm}"] = a0[nm]
        m["a0_b_rows"] = a0["b_rows"]
        m["a0_b_pp"] = a0["b_pp"]
        for nm in ["wq", "wk", "wv", "wg", "wo", "wa", "wb_"]:
            m[f"a1_{nm}"] = a1[nm]
        m["a1_b_pp"] = a1["b_pp"]
        m["a1_ab_bias"] = a1["ab_bias"]
        m["a1_gn_pp"] = a1["gn_pp"]
        m["a1_cv_pp"] = a1["cv_pp"]
        maps.append(m)
    return maps


def kernel(**inputs):
    nc = _get_prog("fused", build_fused)
    maps = fused_in_maps(inputs)
    res = run_bass_kernel_spmd(nc, maps, core_ids=list(range(NCORES)))
    return unshard_x([res.results[c]["x_out"] for c in range(NCORES)])


def kernel_unfused(**inputs):
    xcores = shard_x(np.asarray(inputs["L0_attn_x"]))
    xcores = run_ffn(0, inputs, xcores)
    xcores = run_attn0(inputs, xcores)
    xcores = run_ffn(0, inputs, xcores)
    xcores = run_ffn(1, inputs, xcores)
    xcores = run_attn1(inputs, xcores)
    xcores = run_ffn(1, inputs, xcores)
    return unshard_x(xcores)





# version 08 (final)
# speedup vs baseline: 1.2079x; 1.2051x over previous
import numpy as np
import ml_dtypes
from contextlib import ExitStack

import concourse.bass as bass
import concourse.mybir as mybir
from concourse.bass_utils import run_bass_kernel_spmd

F32 = mybir.dt.float32
BF16 = mybir.dt.bfloat16
I32 = mybir.dt.int32
AF = mybir.ActivationFunctionType
OP = mybir.AluOpType

B, S, D = 4, 8192, 1024
NCORES = 8
TLOC = S // 2
NCH = D // 128
TT = 512
NT = TLOC // TT
TCH = TLOC // 128
LN_EPS = 1e-5

BF = ml_dtypes.bfloat16


class Em:

    def __init__(self, nc, stack):
        self.nc = nc
        self.stack = stack
        self.g = nc.gpsimd
        self.v = nc.vector
        self.sc = nc.scalar
        self.t = nc.tensor
        self.sy = nc.sync
        self._sems = {}

    def sem(self, name):
        if name not in self._sems:
            h = self.stack.enter_context(self.nc.semaphore(name))
            self._sems[name] = [h, 0]
        return self._sems[name]

    def inc(self, ins, name, amt=1):
        s = self.sem(name)
        ins.then_inc(s[0], amt)
        s[1] += amt
        return ins

    def val(self, name):
        return self.sem(name)[1]

    def wait(self, eng, name, val=None):
        s = self.sem(name)
        v = s[1] if val is None else val
        if v > 0:
            eng.wait_ge(s[0], v)

    def barrier(self):
        self.nc.all_engine_barrier()

    def vstep(self, ins):
        self.inc(ins, "DV")
        self.wait(self.v, "DV")
        return ins

    def load_consts(self, cfg):
        if not hasattr(self, "_consts"):
            nc = self.nc
            ones = self.stack.enter_context(nc.sbuf_tensor("c_ones_sb", [128, 128], F32))
            ident = self.stack.enter_context(nc.sbuf_tensor("c_id_sb", [128, 128], F32))
            misc = self.stack.enter_context(nc.sbuf_tensor("c_misc_sb", [128, 8], F32))
            self.inc(self.sy.dma_start(out=ones[:, :], in_=cfg["c_ones"][:, :]), "LD", 16)
            self.inc(self.sy.dma_start(out=ident[:, :], in_=cfg["c_id"][:, :]), "LD", 16)
            self.inc(self.sy.dma_start(out=misc[:, :], in_=cfg["c_misc"][:, :]), "LD", 16)
            sel = None
            if cfg.get("use_sel"):
                sel = self.stack.enter_context(nc.sbuf_tensor("c_sel_sb", [8, 8 * 128], BF16))
                self.inc(self.sy.dma_start(out=sel[:, :], in_=cfg["c_sel"][:, :]), "LD", 16)
            self._consts = (ones, ident, sel, misc)
        return self._consts


def emit_ffn(em, cfg):
    nc, g, v, sc, t, sy = em.nc, em.g, em.v, em.sc, em.t, em.sy
    E, I = cfg["E"], cfg["I"]
    ICH = I // 128
    NB = cfg["rbias"].shape[0]
    x_in, x_out = cfg["x_in"], cfg["x_out"]
    xn_dram, yacc = cfg["xn_dram"], cfg["yacc"]
    c_ones, c_id, c_sel, c_misc = em.load_consts(cfg)

    tag = cfg["tag"]
    sb = lambda name, shape, dt=F32: em.stack.enter_context(
        nc.sbuf_tensor(tag + name, shape, dt))
    xt = sb("xt", [128, NCH * TT])
    sq = sb("sq", [128, 2 * TT])
    xn32 = sb("xn32", [128, 2 * TT])
    xnb = sb("xnb", [128, NCH * TT], BF16)
    srow = sb("srow", [1, 4 * TT])
    probs = sb("probs", [128, TCH * 8])
    wtok = sb("wtok", [128, TCH * 8])
    trow = sb("trow", [128, 16])
    tmp8 = sb("tmp8", [128, 4 * 8])
    wem = sb("wem", [8, TLOC], BF16)
    rbs = sb("rbs", [NB, 8])
    wg = sb("wg", [128, NCH * I], BF16)
    wu = sb("wu", [128, NCH * I], BF16)
    wd = sb("wd", [128, ICH * D], BF16)
    hid = sb("hid", [128, ICH * TT], BF16)
    wb = sb("wb", [128, TT])
    xw = sb("xw", [128, NCH * TT], BF16)
    tsil = sb("tsil", [128, 2 * TT])
    ya_freed = [0]
    ybig = sb("ybig", [128, NCH * TT])
    rws = sb("rws", [128, NCH * 8])
    PS = em.stack.enter_context(nc.psum_tensor(tag + "PS", [128, 8 * 512], F32))

    def bank(b, p0=0, pn=128, c0=0, cn=512):
        return PS[p0:p0 + pn, b * 512 + c0:b * 512 + c0 + cn]

    if True:
        rww = cfg["rw"]
        if True:
            for c in range(NCH):
                em.inc(sy.dma_start(out=rws[:, c * 8:c * 8 + E],
                                    in_=rww[c * 128:(c + 1) * 128, :]), "LD", 16)
            em.inc(sy.dma_start(out=rbs[:, 0:E], in_=cfg["rbias"][:, :]), "LD", 16)
            v.memset(wem[0:8, :], 0.0)
            em.inc(v.memset(wtok[:, :], 0.0), "DV")

            for ti in range(NT):
                tc0 = ti * TT
                em.wait(sy, "PE")
                for c in range(NCH):
                    em.inc(sy.dma_start(out=xt[:, c * TT:(c + 1) * TT],
                                        in_=x_in[c, :, tc0:tc0 + TT]), "LD", 16)
                em.wait(t, "LD")
                em.wait(sc, "LD")
                for c in range(NCH):
                    em.wait(sc, "PE")
                    em.inc(sc.activation(sq[:, (c % 2) * TT:(c % 2) * TT + TT],
                                         xt[:, c * TT:(c + 1) * TT], AF.Square), "AC")
                    em.wait(t, "AC")
                    t.matmul(bank(0, 0, 1), c_ones[:, 0:1], xt[:, c * TT:(c + 1) * TT],
                             start=(c == 0), stop=(c == NCH - 1))
                    em.inc(t.matmul(bank(1, 0, 1), c_ones[:, 0:1],
                                    sq[:, (c % 2) * TT:(c % 2) * TT + TT],
                                    start=(c == 0), stop=(c == NCH - 1)), "PE")
                r = [srow[0:1, k * TT:(k + 1) * TT] for k in range(4)]
                em.wait(v, "PE")
                em.vstep(v.tensor_scalar_mul(r[0], bank(0, 0, 1), 1.0 / D))
                em.vstep(v.tensor_scalar_mul(r[1], bank(1, 0, 1), 1.0 / D))
                em.vstep(v.tensor_mul(r[2], r[0], r[0]))
                em.vstep(v.tensor_sub(r[1], r[1], r[2]))
                em.wait(sc, "DV")
                em.inc(sc.activation(r[2], r[1], AF.Sqrt, bias=c_misc[0:1, 0:1]), "AC")
                em.wait(v, "AC")
                em.vstep(v.reciprocal(r[1], r[2]))
                em.vstep(v.tensor_mul(r[3], r[0], r[1]))
                em.wait(t, "DV")
                em.inc(t.matmul(bank(2), c_ones[0:1, :], r[1], start=True, stop=True), "PE")
                em.inc(t.matmul(bank(3), c_ones[0:1, :], r[3], start=True, stop=True), "PE")
                em.wait(v, "PE")
                for c in range(NCH):
                    sl = (c % 2)
                    x32 = xn32[:, sl * TT:(sl + 1) * TT]
                    em.wait(v, "PE")
                    v.tensor_mul(x32, xt[:, c * TT:(c + 1) * TT], bank(2))
                    em.inc(v.tensor_sub(x32, x32, bank(3)), "DV")
                    em.inc(v.tensor_copy(xnb[:, c * TT:(c + 1) * TT], x32), "DV")
                    em.wait(t, "DV")
                    for tc in range(4):
                        ins = t.matmul(bank(4 + tc, 0, 128, 0, E),
                                       x32[:, tc * 128:(tc + 1) * 128],
                                       rws[:, c * 8:c * 8 + E],
                                       start=(c == 0), stop=False)
                    em.inc(ins, "PE")
                for tc in range(4):
                    ins = t.matmul(bank(4 + tc, 0, 128, 0, E), c_ones[0:NB, :],
                                   rbs[0:NB, 0:E], start=False, stop=True)
                    em.inc(ins, "PE")
                em.wait(sc, "PE")
                for tc in range(4):
                    em.inc(sc.activation(probs[:, (ti * 4 + tc) * 8:(ti * 4 + tc) * 8 + E],
                                         bank(4 + tc, 0, 128, 0, E), AF.Sigmoid), "AC")
                em.wait(sy, "DV")
                em.inc(sy.dma_start(out=xn_dram.ap()
                                    .rearrange("c p l -> p c l")[:, :, tc0:tc0 + TT],
                                    in_=xnb[:, :].rearrange("p (c l) -> p c l", c=NCH)),
                       "ST", 16)

            em.wait(sy, "ST")

            em.barrier()
            for tc in range(TCH):
                pr = probs[:, tc * 8:tc * 8 + E]
                sl = tc % 2
                t8 = tmp8[:, (2 + sl) * 8:(2 + sl) * 8 + E]
                m1 = trow[:, (sl * 4):(sl * 4) + 1]
                m2 = trow[:, (sl * 4) + 1:(sl * 4) + 2]
                den = trow[:, (sl * 4) + 2:(sl * 4) + 3]
                rden = trow[:, (sl * 4) + 3:(sl * 4) + 4]
                t8b = tmp8[:, sl * 8:sl * 8 + E]
                em.vstep(v.reduce_max(m1, pr, axis=mybir.AxisListType.X))
                em.vstep(v.tensor_scalar(t8, pr, m1, None, OP.is_equal))
                em.vstep(v.scalar_tensor_tensor(t8b, t8, -1e30, pr, OP.mult, OP.add))
                em.vstep(v.reduce_max(m2, t8b, axis=mybir.AxisListType.X))
                em.vstep(v.tensor_add(den, m1, m2))
                em.vstep(v.reciprocal(rden, den))
                em.vstep(v.tensor_scalar(t8, pr, m2, None, OP.is_ge))
                wv = wtok[:, tc * 8:tc * 8 + E]
                em.vstep(v.scalar_tensor_tensor(wv, pr, rden, t8, OP.mult, OP.mult))
                em.wait(t, "DV")
                em.inc(t.transpose(bank(4, 0, 8, 0, 128), wtok[:, tc * 8:tc * 8 + 8],
                                   c_id[:, :]), "PE")
                em.wait(v, "PE")
                em.inc(v.tensor_copy(wem[0:8, tc * 128:(tc + 1) * 128],
                                     bank(4, 0, 8, 0, 128)), "DV")

            if cfg.get("dbg"):
                em.wait(sy, "DV")
                em.inc(sy.dma_start(out=cfg["dbg_probs"][:, :], in_=probs[:, :]), "ST", 16)
                em.inc(sy.dma_start(out=cfg["dbg_wtok"][:, :], in_=wtok[:, :]), "ST", 16)

            em.barrier()
            freed = {0: 0, 5: 0, 1: 0, 6: 0, 2: 0, 3: 0}
            fsem = {0: "AC", 5: "AC", 1: "DV", 6: "DV", 2: "XX", 3: "XX"}
            for e in range(E + 1):
                if e == 0:
                    gsrc, usrc, dsrc = cfg["sg"], cfg["su"], cfg["sd"]
                else:
                    gsrc, usrc, dsrc = (cfg["gw"][e - 1], cfg["uw"][e - 1],
                                        cfg["dw"][e - 1])
                em.wait(g, "PE")
                for kc in range(NCH):
                    em.inc(g.dma_start(out=wg[:, kc * I:(kc + 1) * I],
                                       in_=gsrc[kc * 128:(kc + 1) * 128, :]), "LW", 16)
                    em.inc(g.dma_start(out=wu[:, kc * I:(kc + 1) * I],
                                       in_=usrc[kc * 128:(kc + 1) * 128, :]), "LW", 16)
                for ic in range(ICH):
                    em.inc(g.dma_start(out=wd[:, ic * D:(ic + 1) * D],
                                       in_=dsrc[ic * 128:(ic + 1) * 128, :]), "LW", 16)
                for ti in range(NT):
                    tc0 = ti * TT
                    em.wait(sy, "PE")
                    em.inc(sy.dma_start(out=xnb[:, :].rearrange("p (c l) -> p c l", c=NCH),
                                        in_=xn_dram.ap().rearrange("c p l -> p c l")[:, :, tc0:tc0 + TT]),
                           "LD", 16)
                    if e > 0:
                        em.wait(t, "DV")
                        em.inc(t.matmul(bank(4), c_sel[:, (e - 1) * 128:e * 128],
                                        wem[0:8, tc0:tc0 + TT], start=True, stop=True),
                               "PE")
                        em.wait(v, "PE")
                        em.inc(v.tensor_copy(wb[:, :], bank(4)), "DV")
                        em.wait(v, "LD")
                        em.wait(v, "PE")
                        for c in range(NCH):
                            em.inc(v.tensor_mul(xw[:, c * TT:(c + 1) * TT],
                                                xnb[:, c * TT:(c + 1) * TT],
                                                wb[:, :]), "DV")
                        usrc_sb = xw
                    else:
                        usrc_sb = xnb
                    em.wait(t, "LW")
                    em.wait(t, "LD")
                    em.wait(t, "DV")
                    for ic in range(ICH):
                        gb = 0 if ic % 2 == 0 else 5
                        ub = 1 if ic % 2 == 0 else 6
                        em.wait(t, fsem[gb], freed[gb])
                        for kc in range(NCH):
                            t.matmul(bank(gb), wg[:, kc * I + ic * 128:kc * I + (ic + 1) * 128],
                                     xnb[:, kc * TT:(kc + 1) * TT],
                                     start=(kc == 0), stop=(kc == NCH - 1))
                        em.wait(t, fsem[ub], freed[ub])
                        for kc in range(NCH):
                            ins = t.matmul(bank(ub), wu[:, kc * I + ic * 128:kc * I + (ic + 1) * 128],
                                           usrc_sb[:, kc * TT:(kc + 1) * TT],
                                           start=(kc == 0), stop=(kc == NCH - 1))
                        em.inc(ins, "PE")
                        em.wait(sc, "PE")
                        sl = ic % 2
                        tsl = tsil[:, sl * TT:(sl + 1) * TT]
                        em.inc(sc.activation(tsl, bank(gb), AF.Silu), "AC")
                        freed[gb] = em.val("AC")
                        em.wait(v, "AC")
                        em.inc(v.tensor_mul(hid[:, ic * TT:(ic + 1) * TT], tsl,
                                            bank(ub)), "DV")
                        freed[ub] = em.val("DV")
                    em.wait(t, "DV")
                    for dc in range(NCH):
                        pb = 2 + (dc % 2)
                        em.wait(t, fsem[pb], freed[pb])
                        for ic in range(ICH):
                            ins = t.matmul(bank(pb),
                                           wd[:, ic * D + dc * 128:ic * D + (dc + 1) * 128],
                                           hid[:, ic * TT:(ic + 1) * TT],
                                           start=(ic == 0), stop=(ic == ICH - 1))
                        em.inc(ins, "PE")
                        dst = ybig[:, dc * TT:(dc + 1) * TT]
                        em.wait(v, "PE")
                        em.wait(v, "YA", ya_freed[0])
                        em.inc(v.tensor_copy(dst, bank(pb)), "DV")
                        fsem[pb] = "DV"
                        freed[pb] = em.val("DV")
                    em.wait(g, "DV")
                    em.wait(g, "AC")
                    em.inc(g.dma_start(out=yacc.ap().rearrange("c p l -> p c l")[:, :, tc0:tc0 + TT],
                                       in_=ybig[:, :].rearrange("p (c l) -> p c l", c=NCH),
                                       accum_op=(OP.bypass if e == 0 else OP.add)),
                           "YA", 16)
                    ya_freed[0] = em.val("YA")

            em.barrier()
            for ti in range(NT):
                tc0 = ti * TT
                em.wait(sy, "YA")
                em.wait(sy, "DV")
                em.inc(sy.dma_start(out=xt[:, :].rearrange("p (c l) -> p c l", c=NCH),
                                    in_=x_in.ap().rearrange("c p l -> p c l")[:, :, tc0:tc0 + TT]),
                       "LD", 16)
                em.inc(sy.dma_start(out=ybig[:, :].rearrange("p (c l) -> p c l", c=NCH),
                                    in_=yacc.ap().rearrange("c p l -> p c l")[:, :, tc0:tc0 + TT]),
                       "LD", 16)
                em.wait(v, "LD")
                em.inc(v.scalar_tensor_tensor(ybig[:, :], ybig[:, :], 0.5, xt[:, :],
                                              OP.mult, OP.add), "DV")
                em.wait(sy, "DV")
                em.inc(sy.dma_start(out=x_out.ap().rearrange("c p l -> p c l")[:, :, tc0:tc0 + TT],
                                    in_=ybig[:, :].rearrange("p (c l) -> p c l", c=NCH)),
                       "ST", 16)
            em.wait(sy, "ST")



def emit_xchg_setup(em, cfg):
    nc, g = em.nc, em.g
    st = em.stack
    tag = cfg.get("tag", "")
    if not hasattr(em, "x_metas"):
        em.x_metas = st.enter_context(nc.sbuf_tensor("xmeta", [1, 32], I32))
        em.x_nonces = st.enter_context(nc.sbuf_tensor("xnonce", [1, 8], I32))
        em.x_fl = st.enter_context(nc.sbuf_tensor("xfl", [1, 2], I32))
        em.inc(em.sy.dma_start(out=em.x_metas[:, :], in_=cfg["meta"][:, :]), "XM", 16)
        em.inc(em.sy.dma_start(out=em.x_nonces[:, :], in_=cfg["nonce"][:, :]), "XM", 16)
        em.x_regs = {}
        for nm in ["r_off", "r_tgt", "r_flv", "r_done", "r_plc"]:
            em.x_regs[nm] = st.enter_context(g.register(nm))
        g.reg_mov(em.x_regs["r_plc"], 0)
        em.x_plsem = "PL"
    em.x_pax = st.enter_context(nc.sbuf_tensor(tag + "xpax", [128, 2 * 512], F32))
    em.x_xmk = st.enter_context(nc.sbuf_tensor(tag + "xxmk", [128, 1], F32))
    em.inc(em.sy.dma_start(out=em.x_xmk[:, :], in_=cfg["xmask"][:, :]), "XM", 16)
    em.x_cfg = cfg


def emit_pair_exchange(em, rnd, paysb, col0, ncols):
    g, v = em.g, em.v
    cfg, regs = em.x_cfg, em.x_regs
    rnd = rnd + cfg.get("rbase", 0)
    PAYC = cfg["PAYC"]
    shm_pay, shm_fl, nonce = cfg["shm_pay"], cfg["shm_fl"], cfg["nonce"]
    em.wait(g, "XM")
    em.wait(g, "DV")
    g.reg_load(regs["r_off"], em.x_metas[0:1, 8 + rnd:9 + rnd])
    em.inc(g.dma_start(
        bass.AP(shm_pay, regs["r_off"], [[PAYC, 128], [1, ncols]]),
        paysb[:, col0:col0 + ncols]), "XP", 16)
    em.wait(g, "XP")
    g.reg_load(regs["r_off"], em.x_metas[0:1, 24 + rnd:25 + rnd])
    em.inc(g.dma_start(
        bass.AP(shm_fl, regs["r_off"], [[1, 1], [1, 1]]),
        nonce[0:1, rnd:rnd + 1]), "XP", 16)
    g.reg_load(regs["r_tgt"], em.x_nonces[0:1, rnd:rnd + 1])
    g.reg_mov(regs["r_done"], 1)
    pl = em.sem(em.x_plsem)
    with g.While(regs["r_done"]):
        g.dma_start(em.x_fl[0:1, 0:2], shm_fl[0:2, rnd:rnd + 1],
                    allow_slow_non_contiguous=True).then_inc(pl[0], 16)
        g.reg_add(regs["r_plc"], regs["r_plc"], 16)
        g.wait_ge(pl[0], regs["r_plc"])
        g.reg_load(regs["r_flv"], em.x_fl[0:1, 0:1])
        g.reg_sub(regs["r_flv"], regs["r_flv"], regs["r_tgt"])
        g.reg_load(regs["r_done"], em.x_fl[0:1, 1:2])
        g.reg_sub(regs["r_done"], regs["r_done"], regs["r_tgt"])
        g.reg_alu(regs["r_done"], regs["r_done"], regs["r_flv"],
                  op=OP.bitwise_or)
    pax = em.x_pax
    em.inc(g.dma_start(
        pax[:, 0:2 * ncols].rearrange("p (s c) -> p s c", s=2),
        shm_pay.ap().rearrange("s p c -> p s c")[:, :, col0:col0 + ncols]),
        "XD", 16)
    em.wait(v, "XD")
    dst = em.x_pasb[:, col0:col0 + ncols]
    em.vstep(v.tensor_sub(dst, pax[:, 0:ncols], pax[:, ncols:2 * ncols]))
    em.vstep(v.scalar_tensor_tensor(dst, dst, em.x_xmk[:, 0:1],
                                    pax[:, ncols:2 * ncols], OP.mult, OP.add))


def emit_ln_to_dram(em, cfg, xt, sq, xn32, xnb, srow, bank, c_ones, c_misc):
    v, sc, t, sy = em.v, em.sc, em.t, em.sy
    x_in, xn_dram = cfg["x_in"], cfg["xn_dram"]
    for ti in range(NT):
        tc0 = ti * TT
        em.wait(sy, "PE")
        for c in range(NCH):
            em.inc(sy.dma_start(out=xt[:, c * TT:(c + 1) * TT],
                                in_=x_in[c, :, tc0:tc0 + TT]), "LD", 16)
        em.wait(t, "LD")
        em.wait(sc, "LD")
        for c in range(NCH):
            em.wait(sc, "PE")
            em.inc(sc.activation(sq[:, (c % 2) * TT:(c % 2) * TT + TT],
                                 xt[:, c * TT:(c + 1) * TT], AF.Square), "AC")
            em.wait(t, "AC")
            t.matmul(bank(0, 0, 1), c_ones[:, 0:1], xt[:, c * TT:(c + 1) * TT],
                     start=(c == 0), stop=(c == NCH - 1))
            em.inc(t.matmul(bank(1, 0, 1), c_ones[:, 0:1],
                            sq[:, (c % 2) * TT:(c % 2) * TT + TT],
                            start=(c == 0), stop=(c == NCH - 1)), "PE")
        r = [srow[0:1, k * TT:(k + 1) * TT] for k in range(4)]
        em.wait(v, "PE")
        em.vstep(v.tensor_scalar_mul(r[0], bank(0, 0, 1), 1.0 / D))
        em.vstep(v.tensor_scalar_mul(r[1], bank(1, 0, 1), 1.0 / D))
        em.vstep(v.tensor_mul(r[2], r[0], r[0]))
        em.vstep(v.tensor_sub(r[1], r[1], r[2]))
        em.wait(sc, "DV")
        em.inc(sc.activation(r[2], r[1], AF.Sqrt, bias=c_misc[0:1, 0:1]), "AC")
        em.wait(v, "AC")
        em.vstep(v.reciprocal(r[1], r[2]))
        em.vstep(v.tensor_mul(r[3], r[0], r[1]))
        em.wait(t, "DV")
        em.inc(t.matmul(bank(2), c_ones[0:1, :], r[1], start=True, stop=True), "PE")
        em.inc(t.matmul(bank(3), c_ones[0:1, :], r[3], start=True, stop=True), "PE")
        em.wait(v, "PE")
        for c in range(NCH):
            sl = (c % 2)
            x32 = xn32[:, sl * TT:(sl + 1) * TT]
            em.wait(v, "PE")
            v.tensor_mul(x32, xt[:, c * TT:(c + 1) * TT], bank(2))
            em.inc(v.tensor_sub(x32, x32, bank(3)), "DV")
            em.inc(v.tensor_copy(xnb[:, c * TT:(c + 1) * TT], x32), "DV")
        em.wait(sy, "DV")
        em.inc(sy.dma_start(out=xn_dram.ap().rearrange("c p l -> p c l")[:, :, tc0:tc0 + TT],
                            in_=xnb[:, :].rearrange("p (c l) -> p c l", c=NCH)),
               "ST", 16)


def emit_attn0(em, cfg):
    nc, g, v, sc, t, sy = em.nc, em.g, em.v, em.sc, em.t, em.sy
    c_ones, c_id, c_sel, c_misc = em.load_consts(cfg)
    x_in, x_out, xn_dram = cfg["x_in"], cfg["x_out"], cfg["xn_dram"]
    qf_dram = cfg["qf_dram"]
    tag = cfg["tag"]
    sb = lambda name, shape, dt=F32: em.stack.enter_context(
        nc.sbuf_tensor(tag + name, shape, dt))
    EPS = 1e-6
    xt = sb("xt", [128, NCH * TT])
    sq = sb("sq", [128, 2 * TT])
    xn32 = sb("xn32", [128, 2 * TT])
    xnb = sb("xnb", [128, NCH * TT], BF16)
    srow = sb("srow", [1, 4 * TT])
    qt = sb("qt", [128, 2 * NCH * TT], BF16)
    kf = sb("kf", [128, NCH * TLOC], BF16)
    T1 = sb("T1", [16, 3 * TLOC], BF16)
    csx = sb("csx", [16, 2 * TT])
    tiny = sb("tiny", [16, 16])
    wdot = sb("wdot", [128, 2 * NCH * 16], BF16)
    gsum = sb("gsum", [128, 4 * 8])
    part = sb("part", [128, 2 * NCH * 8])
    wsb = sb("wsb", [128, NCH * D], BF16)
    ktok = sb("ktok", [128, D], BF16)
    vsc = sb("vsc", [128, D], BF16)
    sctk = sb("sctk", [128, 2 * 16])
    kvb = sb("kvb", [128, 512], BF16)
    brow16 = sb("brow16", [1, 4 * D], BF16)
    bpp = sb("bpp", [128, 4 * NCH])
    selp = sb("selp", [16, NCH * 128], BF16)
    onesb = sb("onesb", [128, 128], BF16)
    idb = sb("idb", [128, 128], BF16)
    pays = sb("pays", [128, 560])
    pasb = sb("pasb", [128, 560])
    em.x_pasb = pasb
    PS = em.stack.enter_context(nc.psum_tensor(tag + "PS", [128, 7 * 512], F32))
    PSb = em.stack.enter_context(nc.psum_tensor(tag + "PSb", [128, 8 * 128], BF16))

    def bank(b, p0=0, pn=128, c0=0, cn=512):
        return PS[p0:p0 + pn, b * 512 + c0:b * 512 + c0 + cn]

    emit_xchg_setup(em, cfg)
    em.inc(g.dma_start(out=brow16[:, :], in_=cfg["b_rows"][:, :]), "LW", 16)
    em.inc(sy.dma_start(out=bpp[:, :], in_=cfg["b_pp"][:, :]), "LD", 16)
    em.inc(sy.dma_start(out=selp[:, :], in_=cfg["c_selp"][:, :]), "LD", 16)
    em.inc(sy.dma_start(out=onesb[:, :], in_=cfg["c_onesb"][:, :]), "LD", 16)
    em.inc(sy.dma_start(out=idb[:, :], in_=cfg["c_idb"][:, :]), "LD", 16)

    def load_w(w):
        em.wait(g, "PE")
        for kc in range(NCH):
            em.inc(g.dma_start(out=wsb[:, kc * D:(kc + 1) * D],
                               in_=w[kc * 128:(kc + 1) * 128, :]), "LW", 16)

    def wslice(kc, dc):
        return wsb[:, kc * D + dc * 128:kc * D + (dc + 1) * 128]

    def load_qt(ti, sl):
        em.wait(sy, "PE")
        em.wait(sy, "DV")
        em.inc(sy.dma_start(
            out=qt[:, sl * NCH * TT:(sl + 1) * NCH * TT].rearrange("p (c l) -> p c l", c=NCH),
            in_=qf_dram.ap().rearrange("c p l -> p c l")[:, :, ti * TT:ti * TT + TT]),
            "LD", 16)
        em.wait(t, "LD")
        em.wait(v, "LD")

    def qsl(sl, c, a=0, b=TT):
        return qt[:, sl * NCH * TT + c * TT + a:sl * NCH * TT + c * TT + b]

    emit_ln_to_dram(em, cfg, xt, sq, xn32, xnb, srow, bank, c_ones, c_misc)
    em.barrier()

    for which, wname in ((0, "wq"), (1, "wk")):
        load_w(cfg[wname])
        for ti in range(NT):
            tc0 = ti * TT
            em.wait(sy, "PE")
            em.inc(sy.dma_start(out=xnb[:, :].rearrange("p (c l) -> p c l", c=NCH),
                                in_=xn_dram.ap().rearrange("c p l -> p c l")[:, :, tc0:tc0 + TT]),
                   "LD", 16)
            em.wait(t, "LD")
            em.wait(t, "LW")
            for dc in range(NCH):
                pb = dc % 4
                em.wait(t, "AC")
                for kc in range(NCH):
                    em.inc(t.matmul(bank(pb), wslice(kc, dc),
                                    xnb[:, kc * TT:(kc + 1) * TT],
                                    start=(kc == 0), stop=(kc == NCH - 1)), "PE")
                em.wait(sc, "PE")
                if which == 0:
                    dst = qt[:, dc * TT:(dc + 1) * TT]
                else:
                    dst = kf[:, dc * TLOC + tc0:dc * TLOC + tc0 + TT]
                em.inc(sc.activation(dst, bank(pb), AF.Sigmoid,
                                     bias=bpp[:, which * NCH + dc:which * NCH + dc + 1]),
                       "AC")
            if which == 0:
                em.wait(sy, "AC")
                em.inc(sy.dma_start(
                    out=qf_dram.ap().rearrange("c p l -> p c l")[:, :, tc0:tc0 + TT],
                    in_=qt[:, 0:NCH * TT].rearrange("p (c l) -> p c l", c=NCH)),
                    "ST", 16)
                em.wait(sc, "ST")
    em.barrier()

    for ti in range(NT):
        load_qt(ti, 0)
        for c in range(NCH):
            em.vstep(v.reduce_sum(part[:, (NCH + c) * 8 + ti:(NCH + c) * 8 + ti + 1],
                                  qsl(0, c), axis=mybir.AxisListType.X))
    for c in range(NCH):
        em.vstep(v.reduce_sum(pays[:, c:c + 1],
                              kf[:, c * TLOC:(c + 1) * TLOC],
                              axis=mybir.AxisListType.X))
        em.vstep(v.reduce_sum(pays[:, 8 + c:9 + c],
                              part[:, (NCH + c) * 8:(NCH + c) * 8 + 8],
                              axis=mybir.AxisListType.X))
    emit_pair_exchange(em, 0, pays, 0, 16)
    em.wait(v, "XD")
    em.vstep(v.tensor_add(gsum[:, 0:16], pays[:, 0:16], pasb[:, 0:16]))

    def build_wdot(srccol0, half):
        em.wait(v, "PE")
        em.vstep(v.memset(wdot[:, half * 128:(half + 1) * 128], 0.0))
        for c in range(NCH):
            base = half * 128 + c * 16
            em.vstep(v.tensor_scalar(wdot[0:64, base + 2 * c:base + 2 * c + 1],
                                     gsum[0:64, srccol0 + c:srccol0 + c + 1],
                                     EPS, None, OP.add))
            em.vstep(v.tensor_scalar(wdot[64:128, base + 2 * c + 1:base + 2 * c + 2],
                                     gsum[64:128, srccol0 + c:srccol0 + c + 1],
                                     EPS, None, OP.add))
        em.wait(t, "DV")
        for c in range(NCH):
            em.inc(t.matmul(bank(0, 0, 16, 0, 1),
                            wdot[:, half * 128 + c * 16:half * 128 + (c + 1) * 16],
                            onesb[:, 0:1], start=(c == 0), stop=(c == NCH - 1)), "PE")
        em.wait(v, "PE")
        em.vstep(v.tensor_scalar_mul(tiny[0:16, half:half + 1], bank(0, 0, 16, 0, 1), EPS))

    build_wdot(0, 0)
    build_wdot(8, 1)

    def trow(grp, a, b):
        return T1[0:16, grp * TLOC + a:grp * TLOC + b]

    def dot16(half, use_q, cecol, grp, act):
        for ti in range(NT):
            tc0 = ti * TT
            if use_q:
                load_qt(ti, ti % 2)
            em.wait(t, "DV")
            em.wait(t, "AC")
            for c in range(NCH):
                srcsl = qsl(ti % 2, c) if use_q else kf[:, c * TLOC + tc0:c * TLOC + tc0 + TT]
                em.inc(t.matmul(bank(1, 0, 16),
                                wdot[:, half * 128 + c * 16:half * 128 + (c + 1) * 16],
                                srcsl, start=(c == 0), stop=(c == NCH - 1)), "PE")
            em.wait(v, "PE")
            if act == "recip":
                em.vstep(v.tensor_scalar(csx[0:16, 0:TT], bank(1, 0, 16),
                                         tiny[0:16, cecol:cecol + 1], None, OP.add))
                em.vstep(v.reciprocal(trow(grp, tc0, tc0 + TT), csx[0:16, 0:TT]))
            elif act == "clipexp":
                em.vstep(v.tensor_scalar(csx[0:16, 0:TT], bank(1, 0, 16),
                                         tiny[0:16, cecol:cecol + 1], None, OP.add))
                em.vstep(v.tensor_scalar_min(csx[0:16, TT:2 * TT], csx[0:16, 0:TT], 1.0))
                em.vstep(v.tensor_scalar_max(csx[0:16, 0:TT], csx[0:16, TT:2 * TT], -1.0))
                em.wait(sc, "DV")
                em.inc(sc.activation(trow(grp, tc0, tc0 + TT), csx[0:16, 0:TT],
                                     AF.Exp), "AC")
                em.wait(v, "AC")
                em.vstep(v.reduce_sum(tiny[0:16, 8 + ti:9 + ti],
                                      trow(grp, tc0, tc0 + TT),
                                      axis=mybir.AxisListType.X))
            elif act == "sigmoid":
                em.wait(sc, "PE")
                em.inc(sc.activation(trow(grp, tc0, tc0 + TT), bank(1, 0, 16),
                                     AF.Sigmoid, bias=tiny[0:16, cecol:cecol + 1]), "AC")

    dot16(0, True, 0, 0, "recip")
    dot16(1, False, 1, 1, "recip")

    def wsum(use_q, grp, outcol):
        for ti in range(NT):
            tc0 = ti * TT
            if use_q:
                load_qt(ti, ti % 2)
            for c in range(NCH):
                em.wait(t, "DV")
                em.inc(t.matmul(bank(2), selp[:, c * 128:(c + 1) * 128],
                                trow(grp, tc0, tc0 + TT),
                                start=True, stop=True), "PE")
                em.wait(v, "PE")
                srcsl = qsl(ti % 2, c) if use_q else kf[:, c * TLOC + tc0:c * TLOC + tc0 + TT]
                em.vstep(v.tensor_mul(sq[:, (c % 2) * TT:(c % 2) * TT + TT], srcsl,
                                      bank(2)))
                em.vstep(v.reduce_sum(part[:, c * 8 + ti:c * 8 + ti + 1],
                                      sq[:, (c % 2) * TT:(c % 2) * TT + TT],
                                      axis=mybir.AxisListType.X))
        for c in range(NCH):
            em.vstep(v.reduce_sum(pays[:, outcol + c:outcol + c + 1],
                                  part[:, c * 8:c * 8 + 8],
                                  axis=mybir.AxisListType.X))

    wsum(False, 1, 16)
    wsum(True, 0, 24)
    emit_pair_exchange(em, 1, pays, 16, 16)
    em.wait(v, "XD")
    em.vstep(v.tensor_add(gsum[:, 16:32], pays[:, 16:32], pasb[:, 16:32]))

    build_wdot(16, 0)
    build_wdot(24, 1)
    dot16(0, True, 0, 1, "sigmoid")
    dot16(1, False, 1, 2, "clipexp")
    em.vstep(v.reduce_sum(pays[0:16, 32:33], tiny[0:16, 8:16],
                          axis=mybir.AxisListType.X))
    em.wait(v, "AC")
    em.vstep(v.tensor_mul(trow(0, 0, TLOC), trow(0, 0, TLOC), trow(1, 0, TLOC)))
    emit_pair_exchange(em, 2, pays, 32, 16)
    em.wait(v, "XD")
    em.vstep(v.tensor_add(tiny[0:16, 4:5], pays[0:16, 32:33], pasb[0:16, 32:33]))
    em.vstep(v.reciprocal(tiny[0:16, 5:6], tiny[0:16, 4:5]))
    em.vstep(v.tensor_scalar_mul(tiny[0:16, 6:7], tiny[0:16, 5:6], float(S)))
    em.vstep(v.tensor_scalar(trow(2, 0, TLOC), trow(2, 0, TLOC), tiny[0:16, 6:7],
                             None, OP.mult))
    em.barrier()

    load_w(cfg["wv"])
    for ti in range(NT):
        tc0 = ti * TT
        em.wait(sy, "PE")
        em.inc(sy.dma_start(out=xnb[:, :].rearrange("p (c l) -> p c l", c=NCH),
                            in_=xn_dram.ap().rearrange("c p l -> p c l")[:, :, tc0:tc0 + TT]),
               "LD", 16)
        em.wait(t, "LD")
        em.wait(t, "LW")
        for t4 in range(4):
            tc = ti * 4 + t4
            for nh in range(2):
                em.wait(t, "AC")
                em.wait(t, "DV")
                for kc in range(NCH):
                    t.matmul(bank(2 + nh), xnb[:, kc * TT + t4 * 128:kc * TT + (t4 + 1) * 128],
                             wsb[:, kc * D + nh * 512:kc * D + (nh + 1) * 512],
                             start=(kc == 0), stop=False)
                em.inc(t.matmul(bank(2 + nh), onesb[0:1, :],
                                brow16[0:1, 2 * D + nh * 512:2 * D + (nh + 1) * 512],
                                start=False, stop=True), "PE")
            em.wait(t, "DV")
            for dc in range(NCH):
                em.inc(t.transpose(PSb[:, dc * 128:(dc + 1) * 128],
                                   kf[:, dc * TLOC + tc * 128:dc * TLOC + (tc + 1) * 128],
                                   idb[:, :]), "PE")
            em.wait(v, "PE")
            em.vstep(v.tensor_copy(ktok[:, :], PSb[:, :]))
            em.wait(t, "DV")
            em.inc(t.transpose(PSb[:, 0:16],
                               trow(2, tc * 128, (tc + 1) * 128),
                               idb[0:16, 0:16]), "PE")
            em.wait(v, "PE")
            em.vstep(v.tensor_copy(sctk[:, (t4 % 2) * 16:(t4 % 2) * 16 + 16],
                                   PSb[:, 0:16]))
            for h in range(16):
                em.vstep(v.tensor_scalar(vsc[:, h * 64:(h + 1) * 64],
                                         bank(2 + h // 8, 0, 128, (h % 8) * 64, 64),
                                         sctk[:, (t4 % 2) * 16 + h:(t4 % 2) * 16 + h + 1],
                                         None, OP.mult))
            em.wait(t, "DV")
            for h in range(16):
                em.inc(t.matmul(bank(5, 64 * (h % 2), 64, (h // 2) * 64, 64),
                                ktok[:, h * 64:(h + 1) * 64], vsc[:, h * 64:(h + 1) * 64],
                                start=(tc == 0), stop=(tc == TCH - 1),
                                skip_group_check=True), "PE")
    em.wait(v, "PE")
    em.vstep(v.tensor_copy(pays[:, 48:560], bank(5)))
    emit_pair_exchange(em, 3, pays, 48, 512)
    em.wait(v, "XD")
    em.vstep(v.tensor_add(pays[:, 48:560], pays[:, 48:560], pasb[:, 48:560]))
    em.vstep(v.tensor_copy(kvb[:, :], pays[:, 48:560]))
    em.barrier()

    for ti in range(NT):
        tc0 = ti * TT
        load_qt(ti, ti % 2)
        for dc in range(NCH):
            em.wait(t, "DV")
            em.inc(t.matmul(bank(2), selp[:, dc * 128:(dc + 1) * 128],
                            trow(0, tc0, tc0 + TT), start=True, stop=True), "PE")
            em.wait(v, "PE")
            q_c = qsl(ti % 2, dc)
            em.vstep(v.tensor_mul(q_c, q_c, bank(2)))
            em.wait(t, "DV")
            em.inc(t.matmul(bank(6, 0, 64), kvb[0:64, dc * 64:(dc + 1) * 64],
                            q_c[0:64, :], start=True, stop=True), "PE")
            em.inc(t.matmul(bank(6, 64, 64), kvb[64:128, dc * 64:(dc + 1) * 64],
                            q_c[64:128, :], start=True, stop=True), "PE")
            em.wait(v, "PE")
            em.vstep(v.tensor_copy(kf[:, dc * TLOC + tc0:dc * TLOC + tc0 + TT],
                                   bank(6)))
    em.barrier()

    load_w(cfg["wo"])
    for ti in range(NT):
        tc0 = ti * TT
        em.wait(sy, "DV")
        em.wait(sy, "PE")
        for c in range(NCH):
            em.inc(sy.dma_start(out=xt[:, c * TT:(c + 1) * TT],
                                in_=x_in[c, :, tc0:tc0 + TT]), "LD", 16)
        em.wait(t, "LD")
        em.wait(t, "LW")
        for dc in range(NCH):
            pb = dc % 2
            em.wait(t, "DV")
            for kc in range(NCH):
                em.inc(t.matmul(bank(pb), wslice(kc, dc),
                                kf[:, kc * TLOC + tc0:kc * TLOC + tc0 + TT],
                                start=(kc == 0), stop=(kc == NCH - 1)), "PE")
            em.wait(v, "PE")
            em.wait(v, "ST")
            sl = dc % 2
            em.vstep(v.tensor_scalar(sq[:, sl * TT:(sl + 1) * TT], bank(pb),
                                     bpp[:, 3 * NCH + dc:3 * NCH + dc + 1], None,
                                     OP.add))
            em.vstep(v.tensor_add(sq[:, sl * TT:(sl + 1) * TT],
                                  sq[:, sl * TT:(sl + 1) * TT],
                                  xt[:, dc * TT:(dc + 1) * TT]))
            em.wait(sy, "DV")
            em.inc(sy.dma_start(out=x_out[dc, :, tc0:tc0 + TT],
                                in_=sq[:, sl * TT:(sl + 1) * TT]), "ST", 16)
    em.wait(sy, "ST")


def build_attn0(dbg=False):
    nc = bass.Bass("TRN2", target_bir_lowering=False)
    dr = {}
    dr["x_in"] = nc.dram_tensor("x_in", [NCH, 128, TLOC], F32, kind="ExternalInput")
    dr["x_out"] = nc.dram_tensor("x_out", [NCH, 128, TLOC], F32, kind="ExternalOutput")
    for nm in ["wq", "wk", "wv", "wo"]:
        dr[nm] = nc.dram_tensor(nm, [D, D], F32, kind="ExternalInput")
    dr["b_rows"] = nc.dram_tensor("b_rows", [1, 4 * D], F32, kind="ExternalInput")
    dr["b_pp"] = nc.dram_tensor("b_pp", [128, 4 * NCH], F32, kind="ExternalInput")
    dr["c_ones"] = nc.dram_tensor("c_ones", [128, 128], F32, kind="ExternalInput")
    dr["c_id"] = nc.dram_tensor("c_id", [128, 128], F32, kind="ExternalInput")
    dr["c_sel"] = nc.dram_tensor("c_sel", [8, 8 * 128], BF16, kind="ExternalInput")
    dr["c_misc"] = nc.dram_tensor("c_misc", [128, 8], F32, kind="ExternalInput")
    dr["c_selp"] = nc.dram_tensor("c_selp", [16, NCH * 128], BF16, kind="ExternalInput")
    dr["c_onesb"] = nc.dram_tensor("c_onesb", [128, 128], BF16, kind="ExternalInput")
    dr["c_idb"] = nc.dram_tensor("c_idb", [128, 128], BF16, kind="ExternalInput")
    dr["xmask"] = nc.dram_tensor("xmask", [128, 1], F32, kind="ExternalInput")
    dr["meta"] = nc.dram_tensor("meta", [1, 32], I32, kind="ExternalInput")
    dr["nonce"] = nc.dram_tensor("nonce", [1, 8], I32, kind="ExternalInput")
    dr["xn_dram"] = nc.dram_tensor("xn_scr", [NCH, 128, TLOC], BF16,
                                   kind="ExternalOutput" if dbg else "Internal")
    dr["qf_dram"] = nc.dram_tensor("qf_scr", [NCH, 128, TLOC], BF16,
                                   kind="ExternalOutput" if dbg else "Internal")
    dr["PAYC"] = 560
    dr["shm_pay"] = nc.dram_tensor("shm_pay", [2, 128, 560], F32, addr_space="Shared")
    dr["shm_fl"] = nc.dram_tensor("shm_fl", [2, 8], I32, addr_space="Shared")
    with ExitStack() as stack:
        em = Em(nc, stack)
        cfg = dict(dr)
        cfg["tag"] = "a"
        with nc.allow_low_precision(reason="bf16 operand staging is intentional"):
            emit_attn0(em, cfg)
    return nc


def xchg_host_inputs(rng, col0s=None):
    PAYC = 560
    metas, nonces = [], []
    nonce_vals = rng.integers(1 << 20, 1 << 30, size=8).astype(np.int32)
    if col0s is None:
        col0s = [0, 16, 32, 48]
    col0s = list(col0s) + [0] * (8 - len(col0s))
    for c in range(NCORES):
        m = c % 2
        p = 1 - m
        meta = np.zeros((1, 32), np.int32)
        for r in range(8):
            meta[0, r] = p * 128 * PAYC + col0s[r]
            meta[0, 8 + r] = m * 128 * PAYC + col0s[r]
            meta[0, 16 + r] = p * 8 + r
            meta[0, 24 + r] = m * 8 + r
        metas.append(meta)
        nonces.append(nonce_vals[None, :])
    return metas, nonces


def attn0_in_map(inputs, xcore, meta, nonce, core_idx=0):
    c_ones, c_id, c_sel, c_misc = consts_f()
    selp = np.zeros((16, NCH * 128), BF)
    for c in range(NCH):
        selp[2 * c, c * 128:c * 128 + 64] = 1
        selp[2 * c + 1, c * 128 + 64:(c + 1) * 128] = 1
    onesb = np.ones((128, 128), BF)
    idb = np.eye(128, dtype=BF)
    bpp = np.zeros((128, 4 * NCH), np.float32)
    rows = np.zeros((1, 4 * D), np.float32)
    for i, nm in enumerate(["bq", "bk", "bv", "bo"]):
        b = inputs[f"L0_attn_{nm}"]
        rows[0, i * D:(i + 1) * D] = b
        bpp[:, i * NCH:(i + 1) * NCH] = b.reshape(NCH, 128).T
    return dict(
        x_in=xcore,
        wq=inputs["L0_attn_Wq"], wk=inputs["L0_attn_Wk"],
        wv=inputs["L0_attn_Wv"], wo=inputs["L0_attn_Wo"],
        b_rows=rows, b_pp=bpp,
        c_ones=c_ones, c_id=c_id, c_sel=c_sel, c_misc=c_misc,
        c_selp=selp, c_onesb=onesb, c_idb=idb, meta=meta, nonce=nonce,
        xmask=np.full((128, 1), float(core_idx % 2), np.float32),
    )


def run_attn0(inputs, xcores):
    nc = _get_prog("attn0", build_attn0)
    rng = np.random.default_rng()
    metas, nonces = xchg_host_inputs(rng)
    maps = [attn0_in_map(inputs, xcores[c], metas[c], nonces[c], c)
            for c in range(NCORES)]
    res = run_bass_kernel_spmd(nc, maps, core_ids=list(range(NCORES)))
    return [res.results[c]["x_out"] for c in range(NCORES)]



def emit_attn1(em, cfg):
    nc, g, v, sc, t, sy = em.nc, em.g, em.v, em.sc, em.t, em.sy
    c_ones, c_id, c_sel, c_misc = em.load_consts(cfg)
    x_in, x_out, xn_dram = cfg["x_in"], cfg["x_out"], cfg["xn_dram"]
    qcd, kcd, vcd, ogd = cfg["qc_dram"], cfg["kc_dram"], cfg["vc_dram"], cfg["og_dram"]
    Ed, Pd, Hd = cfg["E_dram"], cfg["P_dram"], cfg["H_dram"]
    tag = cfg["tag"]
    dbg = cfg.get("dbg")
    sb = lambda name, shape, dt=F32: em.stack.enter_context(
        nc.sbuf_tensor(tag + name, shape, dt))
    CJ = TLOC // 128
    SEG = TT + 3
    GN_EPS = 1e-5
    xt = sb("xt", [128, NCH * SEG])
    pre = xt
    sq = sb("sq", [128, 2 * TT])
    xn32 = sb("xn32", [128, 2 * TT])
    xnb = sb("xnb", [128, NCH * TT], BF16)
    srow = sb("srow", [1, 4 * TT])
    silb = sb("silb", [128, NCH * TT], BF16)
    tailp = sb("tailp", [128, NCH * 3])
    tailx = sb("tailx", [128, NCH * 3], BF16)
    wsb = sb("wsb", [128, NCH * D], BF16)
    wab = sb("wab", [128, 2 * NCH * 16], BF16)
    cvpp = sb("cvpp", [128, 3 * NCH * 4])
    bpp = sb("bpp", [128, 5 * NCH])
    abb = sb("abb", [16, 2])
    gnpp = sb("gnpp", [128, 2 * NCH])
    ghmc = sb("ghmc", [16, 128])
    abrow = sb("abrow", [16, 2 * TT])
    lab = sb("lab", [48, TT])
    strx = sb("strx", [128, 48])
    gtr = sb("gtr", [128, 16 * CJ])
    btr = sb("btr", [128, 16 * CJ])
    dtr = sb("dtr", [128, 16 * CJ])
    actr = sb("actr", [128, 16 * CJ])
    batr = sb("batr", [128, 16 * CJ])
    csx = sb("csx", [16, 2 * TT])
    outs = sb("outs", [128, NCH * TLOC], BF16)
    etst = sb("etst", [128, NCH * 128], BF16)
    ptst = sb("ptst", [128, NCH * 64], BF16)
    hst = sb("hst", [128, NCH * 64], BF16)
    qct = sb("qct", [128, NCH * 128], BF16)
    kct = sb("kct", [128, NCH * 128], BF16)
    vct = sb("vct", [128, NCH * 128], BF16)
    Dsb = sb("Dsb", [128, 128])
    esb = sb("esb", [128, 2 * 128])
    lams = sb("lams", [128, 128], BF16)
    lamu = sb("lamu", [128, 128], BF16)
    absb = sb("absb", [128, 128], BF16)
    Msb = sb("Msb", [128, 128], BF16)
    Mtsb = sb("Mtsb", [128, 128], BF16)
    wtsb = sb("wtsb", [128, 128], BF16)
    Tb = sb("Tb", [128, 2 * 128], BF16)
    Ttb = sb("Ttb", [128, 2 * 128], BF16)
    Pb = sb("Pb", [128, 2 * 128], BF16)
    Ptb = sb("Ptb", [128, 2 * 128], BF16)
    vbsb = sb("vbsb", [128, 64], BF16)
    kpsb = sb("kpsb", [128, 64], BF16)
    kbasb = sb("kbasb", [128, 64], BF16)
    uvsb = sb("uvsb", [128, 64], BF16)
    zsb = sb("zsb", [128, 64], BF16)
    qasb = sb("qasb", [128, 128], BF16)
    st32 = sb("st32", [128, 512])
    stbb = sb("stbb", [128, 512], BF16)
    eld = sb("eld", [128, NCH * 128], BF16)
    pld = sb("pld", [128, NCH * 64], BF16)
    hld = sb("hld", [128, NCH * 64], BF16)
    gnb = sb("gnb", [128, 4 * NCH])
    v16 = sb("v16", [16, 16])
    pays = sb("pays", [128, 560])
    pasb = sb("pasb", [128, 560])
    em.x_pasb = pasb
    selp = sb("selp", [16, NCH * 128], BF16)
    selpf = sb("selpf", [16, NCH * 128])
    s16f = sb("s16f", [16, 16 * 128])
    cut = sb("cut", [128, 128])
    clt = sb("clt", [128, 128])
    cmsl = sb("cmsl", [128, 128])
    cmsu = sb("cmsu", [128, 128])
    chm = sb("chm", [128, NCH * 16])
    cpat = sb("cpat", [16, 8])
    idb = sb("idb", [128, 128], BF16)
    xmk = sb("xmk", [128, 1])
    PS = em.stack.enter_context(nc.psum_tensor(tag + "PS", [128, 7 * 512], F32))
    PSb = em.stack.enter_context(nc.psum_tensor(tag + "PSb", [128, 512], BF16))

    def bank(b, p0=0, pn=128, c0=0, cn=512):
        return PS[p0:p0 + pn, b * 512 + c0:b * 512 + c0 + cn]

    emit_xchg_setup(em, cfg)
    for nm, dst in [("c_selp", selp), ("c_selpf", selpf), ("c_s16f", s16f),
                    ("c_ut", cut), ("c_lt", clt), ("c_msl", cmsl),
                    ("c_msu", cmsu), ("c_hm", chm), ("c_pat", cpat),
                    ("c_idb", idb), ("xmask", xmk), ("b_pp", bpp),
                    ("ab_bias", abb), ("gn_pp", gnpp), ("cv_pp", cvpp)]:
        em.inc(sy.dma_start(out=dst[0:dst.shape[0], :], in_=cfg[nm][:, :]), "LD", 16)
    for kc in range(NCH):
        em.inc(g.dma_start(out=wab[:, kc * 16:(kc + 1) * 16],
                           in_=cfg["wa"][kc * 128:(kc + 1) * 128, :]), "LW", 16)
        em.inc(g.dma_start(out=wab[:, 128 + kc * 16:128 + (kc + 1) * 16],
                           in_=cfg["wb_"][kc * 128:(kc + 1) * 128, :]), "LW", 16)

    def load_w(w):
        em.wait(g, "PE")
        for kc in range(NCH):
            em.inc(g.dma_start(out=wsb[:, kc * D:(kc + 1) * D],
                               in_=w[kc * 128:(kc + 1) * 128, :]), "LW", 16)

    def wslice(kc, dc):
        return wsb[:, kc * D + dc * 128:kc * D + (dc + 1) * 128]

    def load_xn(ti):
        em.wait(sy, "PE")
        em.inc(sy.dma_start(out=xnb[:, :].rearrange("p (c l) -> p c l", c=NCH),
                            in_=xn_dram.ap().rearrange("c p l -> p c l")[:, :, ti * TT:(ti + 1) * TT]),
               "LD", 16)
        em.wait(t, "LD")

    emit_ln_to_dram(em, cfg, xt, sq, xn32, xnb, srow, bank, c_ones, c_misc)
    em.wait(v, "ST")
    for c in range(NCH):
        em.vstep(v.tensor_copy(pays[:, c * 3:(c + 1) * 3],
                               xnb[:, c * TT + TT - 3:c * TT + TT]))
    emit_pair_exchange(em, 0, pays, 0, 24)
    em.wait(v, "XD")
    em.vstep(v.tensor_scalar(pays[:, 24:48], pasb[:, 0:24], xmk[:, 0:1], None,
                             OP.mult))
    em.vstep(v.tensor_copy(tailx[:, :], pays[:, 24:48]))
    em.barrier()

    for ti in range(NT):
        tc0 = ti * TT
        load_xn(ti)
        em.wait(t, "LW")
        em.wait(t, "AC")
        em.wait(t, "DV")
        for half in range(2):
            for kc in range(NCH):
                em.inc(t.matmul(bank(3 + half, 0, 16),
                                wab[:, half * 128 + kc * 16:half * 128 + (kc + 1) * 16],
                                xnb[:, kc * TT:(kc + 1) * TT],
                                start=(kc == 0), stop=(kc == NCH - 1)), "PE")
        em.wait(sc, "PE")
        em.inc(sc.activation(abrow[0:16, 0:TT], bank(3, 0, 16), AF.Sigmoid,
                             bias=abb[0:16, 0:1]), "AC")
        em.inc(sc.activation(abrow[0:16, TT:2 * TT], bank(4, 0, 16), AF.Sigmoid,
                             bias=abb[0:16, 1:2]), "AC")
        if dbg:
            em.wait(sy, "AC")
            em.inc(sy.dma_start(out=cfg["dbg_ab"][0, :, tc0:tc0 + TT],
                                in_=abrow[0:16, 0:TT]), "ST", 16)
            em.inc(sy.dma_start(out=cfg["dbg_ab"][1, :, tc0:tc0 + TT],
                                in_=abrow[0:16, TT:2 * TT]), "ST", 16)
        em.wait(sc, "AC")
        em.inc(sc.activation(lab[0:16, :], abrow[0:16, 0:TT], AF.Ln), "AC")
        em.wait(v, "AC")
        em.vstep(v.tensor_copy(lab[32:48, :], abrow[0:16, TT:2 * TT]))
        for j4 in range(4):
            j = ti * 4 + j4
            em.wait(t, "DV")
            em.wait(t, "AC")
            em.inc(t.transpose(bank(5, 0, 128, 0, 48), lab[0:48, j4 * 128:(j4 + 1) * 128],
                               c_id[0:48, 0:48]), "PE")
            em.wait(v, "PE")
            em.vstep(v.tensor_copy(strx[:, :], bank(5, 0, 128, 0, 48)))
            em.vstep(v.tensor_copy(btr[:, j * 16:(j + 1) * 16], strx[:, 32:48]))
            em.wait(t, "DV")
            em.inc(t.matmul(bank(5, 0, 128, 64, 16), cut[:, :], strx[:, 0:16],
                            start=True, stop=True), "PE")
            em.inc(t.matmul(bank(5, 0, 128, 96, 16), clt[:, :], strx[:, 0:16],
                            start=True, stop=True), "PE")
            em.wait(v, "PE")
            em.vstep(v.tensor_copy(gtr[:, j * 16:(j + 1) * 16],
                                   bank(5, 0, 128, 64, 16)))
            em.wait(sc, "DV")
            em.inc(sc.activation(dtr[:, j * 16:(j + 1) * 16],
                                 bank(5, 0, 128, 96, 16), AF.Exp), "AC")
            em.vstep(v.tensor_add(strx[:, 16:32], gtr[:, j * 16:(j + 1) * 16],
                                  bank(5, 0, 128, 96, 16)))
            em.wait(sc, "DV")
            em.wait(sc, "PE")
            em.inc(sc.activation(actr[:, j * 16:(j + 1) * 16], strx[:, 16:32],
                                 AF.Exp), "AC")
            em.inc(sc.activation(strx[:, 0:16], gtr[:, j * 16:(j + 1) * 16],
                                 AF.Exp), "AC")
            em.wait(v, "AC")
            em.vstep(v.tensor_mul(batr[:, j * 16:(j + 1) * 16],
                                  btr[:, j * 16:(j + 1) * 16], strx[:, 0:16]))
    em.barrier()

    for which, wname, dramt in ((0, "wq", qcd), (1, "wk", kcd), (2, "wv", vcd)):
        load_w(cfg[wname])
        em.wait(t, "LW")
        em.wait(t, "DV")
        for dc in range(NCH):
            for kc in range(NCH):
                em.inc(t.matmul(bank(4, 0, 128, dc * 3, 3), wslice(kc, dc),
                                tailx[:, kc * 3:(kc + 1) * 3],
                                start=(kc == 0), stop=(kc == NCH - 1)), "PE")
        em.wait(v, "PE")
        for dc in range(NCH):
            em.vstep(v.tensor_scalar(tailp[:, dc * 3:(dc + 1) * 3],
                                     bank(4, 0, 128, dc * 3, 3),
                                     bpp[:, which * NCH + dc:which * NCH + dc + 1],
                                     None, OP.add))
        em.vstep(v.tensor_scalar(tailp[:, :], tailp[:, :], xmk[:, 0:1], None,
                                 OP.mult))
        for c in range(NCH):
            em.vstep(v.tensor_copy(pre[:, c * SEG:c * SEG + 3],
                                   tailp[:, c * 3:(c + 1) * 3]))
        for ti in range(NT):
            tc0 = ti * TT
            load_xn(ti)
            em.wait(sc, "ST")
            em.wait(v, "ST")
            for dc in range(NCH):
                pb = dc % 2
                em.wait(t, "DV")
                for kc in range(NCH):
                    em.inc(t.matmul(bank(pb), wslice(kc, dc),
                                    xnb[:, kc * TT:(kc + 1) * TT],
                                    start=(kc == 0), stop=(kc == NCH - 1)), "PE")
                em.wait(v, "PE")
                em.vstep(v.tensor_scalar(pre[:, dc * SEG + 3:dc * SEG + SEG],
                                         bank(pb),
                                         bpp[:, which * NCH + dc:which * NCH + dc + 1],
                                         None, OP.add))
            for c in range(NCH):
                for j in range(4):
                    cwcol = cvpp[:, which * 32 + c * 4 + j:which * 32 + c * 4 + j + 1]
                    if j == 0:
                        em.vstep(v.tensor_scalar(sq[:, 0:TT],
                                                 pre[:, c * SEG + j:c * SEG + j + TT],
                                                 cwcol, None, OP.mult))
                    else:
                        em.vstep(v.scalar_tensor_tensor(sq[:, 0:TT],
                                                        pre[:, c * SEG + j:c * SEG + j + TT],
                                                        cwcol, sq[:, 0:TT],
                                                        OP.mult, OP.add))
                em.wait(sc, "DV")
                em.inc(sc.activation(silb[:, c * TT:(c + 1) * TT], sq[:, 0:TT],
                                     AF.Silu), "AC")
                em.wait(v, "AC")
            for c in range(NCH):
                em.vstep(v.tensor_copy(pre[:, c * SEG:c * SEG + 3],
                                       pre[:, c * SEG + SEG - 3:c * SEG + SEG]))
            if which < 2:
                em.wait(t, "AC")
                em.wait(t, "DV")
                for c in range(NCH):
                    em.wait(sc, "PE")
                    em.inc(sc.activation(sq[:, (c % 2) * TT:(c % 2) * TT + TT],
                                         silb[:, c * TT:(c + 1) * TT], AF.Square),
                           "AC")
                    em.wait(t, "AC")
                    em.inc(t.matmul(bank(2, 0, 16), chm[:, c * 16:(c + 1) * 16],
                                    sq[:, (c % 2) * TT:(c % 2) * TT + TT],
                                    start=(c == 0), stop=(c == NCH - 1)), "PE")
                em.wait(sc, "PE")
                em.inc(sc.activation(csx[0:16, 0:TT], bank(2, 0, 16), AF.Sqrt), "AC")
                em.wait(v, "AC")
                em.vstep(v.tensor_scalar(csx[0:16, TT:2 * TT], csx[0:16, 0:TT],
                                         1e-6, None, OP.add))
                em.vstep(v.reciprocal(csx[0:16, 0:TT], csx[0:16, TT:2 * TT]))
                em.vstep(v.tensor_copy(abrow[0:16, 0:TT], csx[0:16, 0:TT]))
                for c in range(NCH):
                    em.wait(t, "DV")
                    em.inc(t.matmul(bank(2), selpf[:, c * 128:(c + 1) * 128],
                                    abrow[0:16, 0:TT], start=True, stop=True), "PE")
                    em.wait(v, "PE")
                    em.vstep(v.tensor_mul(silb[:, c * TT:(c + 1) * TT],
                                          silb[:, c * TT:(c + 1) * TT], bank(2)))
            em.wait(sy, "DV")
            em.inc(sy.dma_start(
                out=dramt.ap().rearrange("c p l -> p c l")[:, :, tc0:tc0 + TT],
                in_=silb[:, :].rearrange("p (c l) -> p c l", c=NCH)), "ST", 16)
    em.barrier()

    em.vstep(v.memset(st32[:, :], 0.0))
    em.vstep(v.memset(stbb[:, :], 0.0))
    for j in range(CJ):
        em.wait(sy, "PE")
        em.wait(sy, "DV")
        for dramt, dst in ((qcd, qct), (kcd, kct), (vcd, vct)):
            em.inc(sy.dma_start(
                out=dst[:, :].rearrange("p (c l) -> p c l", c=NCH),
                in_=dramt.ap().rearrange("c p l -> p c l")[:, :, j * 128:(j + 1) * 128]),
                "LD", 16)
        em.wait(t, "LD")
        em.wait(v, "LD")
        em.wait(t, "DV")
        em.inc(t.transpose(bank(0, 0, 16, 384, 128), gtr[:, j * 16:(j + 1) * 16],
                           c_id[:, :]), "PE")
        em.wait(v, "PE")
        em.vstep(v.tensor_copy(ghmc[0:16, :], bank(0, 0, 16, 384, 128)))
        for h in range(16):
            p0 = 64 * (h % 2)
            cc = (h // 2) * 128
            QT = qct[p0:p0 + 64, cc:cc + 128]
            KT = kct[p0:p0 + 64, cc:cc + 128]
            VT = vct[p0:p0 + 64, cc:cc + 128]
            gcol = gtr[:, j * 16 + h:j * 16 + h + 1]
            bcol = btr[:, j * 16 + h:j * 16 + h + 1]
            dcol = dtr[:, j * 16 + h:j * 16 + h + 1]
            bacol = batr[:, j * 16 + h:j * 16 + h + 1]
            accol = actr[0:64, j * 16 + h:j * 16 + h + 1]
            em.wait(t, "DV")
            em.wait(t, "AC")
            em.inc(t.matmul(bank(0, 0, 128, 0, 128), KT, KT, start=True, stop=True), "PE")
            em.inc(t.matmul(bank(0, 0, 128, 128, 128), KT, QT, start=True, stop=True), "PE")
            em.inc(t.matmul(bank(0, 0, 128, 256, 128), s16f[:, h * 128:(h + 1) * 128],
                            ghmc[0:16, :], start=True, stop=True), "PE")
            em.wait(v, "PE")
            em.vstep(v.tensor_scalar(Dsb[:, :], bank(0, 0, 128, 256, 128), gcol,
                                     None, OP.subtract))
            em.vstep(v.tensor_sub(esb[:, 0:128], cmsl[:, :], Dsb[:, :]))
            em.vstep(v.tensor_add(esb[:, 128:256], cmsu[:, :], Dsb[:, :]))
            em.wait(sc, "DV")
            em.inc(sc.activation(lams[:, :], esb[:, 0:128], AF.Exp), "AC")
            em.inc(sc.activation(lamu[:, :], esb[:, 128:256], AF.Exp), "AC")
            em.inc(sc.activation(absb[:, :], bank(0, 0, 128, 256, 128), AF.Exp), "AC")
            em.wait(v, "AC")
            em.vstep(v.scalar_tensor_tensor(Msb[:, :], bank(0, 0, 128, 0, 128),
                                            bcol, lams[:, :], OP.mult, OP.mult))
            em.vstep(v.tensor_mul(wtsb[:, :], lamu[:, :], bank(0, 0, 128, 128, 128)))
            em.vstep(v.tensor_mul(qasb[:, 0:128], QT, absb[p0:p0 + 64, :])
                     if False else
                     v.tensor_mul(qasb[p0:p0 + 64, 0:128], QT, absb[p0:p0 + 64, :]))
            em.wait(t, "DV")
            em.inc(t.transpose(PSb[:, 0:128], Msb[:, :], idb[:, :]), "PE")
            em.inc(t.transpose(PSb[:, 128:192], VT, idb[p0:p0 + 64, p0:p0 + 64]), "PE")
            em.inc(t.transpose(PSb[:, 192:256], KT, idb[p0:p0 + 64, p0:p0 + 64]), "PE")
            em.wait(v, "PE")
            em.vstep(v.tensor_copy(Mtsb[:, :], PSb[:, 0:128]))
            em.vstep(v.tensor_scalar(vbsb[:, :], PSb[:, 128:192], bcol, None, OP.mult))
            em.vstep(v.tensor_scalar(kpsb[:, :], PSb[:, 192:256], dcol, None, OP.mult))
            em.vstep(v.tensor_scalar(kbasb[:, :], PSb[:, 192:256], bacol, None, OP.mult))
            em.vstep(v.tensor_sub(Tb[:, 0:128], idb[:, :], Msb[:, :]))
            em.vstep(v.tensor_sub(Ttb[:, 0:128], idb[:, :], Mtsb[:, :]))
            em.wait(t, "DV")
            em.inc(t.matmul(bank(1, 0, 128, 0, 128), Mtsb[:, :], Msb[:, :],
                            start=True, stop=True), "PE")
            em.inc(t.matmul(bank(1, 0, 128, 128, 128), Msb[:, :], Mtsb[:, :],
                            start=True, stop=True), "PE")
            em.wait(v, "PE")
            em.vstep(v.tensor_copy(Pb[:, 0:128], bank(1, 0, 128, 0, 128)))
            em.vstep(v.tensor_copy(Ptb[:, 0:128], bank(1, 0, 128, 128, 128)))
            for lev in range(1, 7):
                s0 = ((lev - 1) % 2) * 128
                s1 = (lev % 2) * 128
                em.wait(t, "DV")
                em.inc(t.matmul(bank(1, 0, 128, 0, 128), Ptb[:, s0:s0 + 128],
                                Tb[:, s0:s0 + 128], start=True, stop=True), "PE")
                em.inc(t.matmul(bank(1, 0, 128, 128, 128), Tb[:, s0:s0 + 128],
                                Ptb[:, s0:s0 + 128], start=True, stop=True), "PE")
                if lev < 6:
                    em.inc(t.matmul(bank(2, 0, 128, 0, 128), Ptb[:, s0:s0 + 128],
                                    Pb[:, s0:s0 + 128], start=True, stop=True), "PE")
                    em.inc(t.matmul(bank(2, 0, 128, 128, 128), Pb[:, s0:s0 + 128],
                                    Ptb[:, s0:s0 + 128], start=True, stop=True), "PE")
                em.wait(v, "PE")
                em.vstep(v.tensor_add(Tb[:, s1:s1 + 128], bank(1, 0, 128, 0, 128),
                                      Tb[:, s0:s0 + 128]))
                em.vstep(v.tensor_add(Ttb[:, s1:s1 + 128], bank(1, 0, 128, 128, 128),
                                      Ttb[:, s0:s0 + 128]))
                if lev < 6:
                    em.vstep(v.tensor_copy(Pb[:, s1:s1 + 128], bank(2, 0, 128, 0, 128)))
                    em.vstep(v.tensor_copy(Ptb[:, s1:s1 + 128], bank(2, 0, 128, 128, 128)))
            tts = Ttb[:, (6 % 2) * 128:(6 % 2) * 128 + 128]
            em.wait(t, "DV")
            em.inc(t.matmul(bank(1, 0, 128, 256, 64), tts, vbsb[:, :],
                            start=True, stop=True), "PE")
            em.inc(t.matmul(bank(1, 0, 128, 320, 64), tts, kbasb[:, :],
                            start=True, stop=True), "PE")
            em.wait(v, "PE")
            em.vstep(v.tensor_copy(uvsb[:, :], bank(1, 0, 128, 256, 64)))
            em.vstep(v.tensor_copy(zsb[:, :], bank(1, 0, 128, 320, 64)))
            em.wait(t, "DV")
            em.inc(t.matmul(bank(2, 0, 64, 0, 128), uvsb[:, :], wtsb[:, :],
                            start=True, stop=True), "PE")
            em.inc(t.matmul(bank(2, 0, 64, 128, 128), zsb[:, :], wtsb[:, :],
                            start=True, stop=True), "PE")
            em.inc(t.matmul(bank(2, 0, 64, 256, 64), zsb[:, :], kpsb[:, :],
                            start=True, stop=True), "PE")
            em.inc(t.matmul(bank(2, 0, 64, 320, 64), kpsb[:, :], uvsb[:, :],
                            start=True, stop=True), "PE")
            em.wait(v, "PE")
            ocol = (h // 2) * TLOC + j * 128
            em.vstep(v.tensor_copy(outs[p0:p0 + 64, ocol:ocol + 128],
                                   bank(2, 0, 64, 0, 128)))
            em.vstep(v.tensor_sub(etst[p0:p0 + 64, cc:cc + 128],
                                  qasb[p0:p0 + 64, 0:128],
                                  bank(2, 0, 64, 128, 128)))
            em.vstep(v.scalar_tensor_tensor(ptst[p0:p0 + 64, (h // 2) * 64:(h // 2) * 64 + 64],
                                            idb[0:64, 0:64], accol,
                                            bank(2, 0, 64, 256, 64),
                                            OP.mult, OP.subtract))
            em.vstep(v.tensor_copy(hst[p0:p0 + 64, (h // 2) * 64:(h // 2) * 64 + 64],
                                   bank(2, 0, 64, 320, 64)))
        em.wait(sy, "DV")
        em.inc(sy.dma_start(out=Ed[j, :, :], in_=etst[:, :]), "ST", 16)
        em.inc(sy.dma_start(out=Pd[j, :, :], in_=ptst[:, :]), "ST", 16)
        em.inc(sy.dma_start(out=Hd[j, :, :], in_=hst[:, :]), "ST", 16)
    em.barrier()

    def state_pass(correction):
        for j in range(CJ):
            em.wait(sy, "PE")
            em.inc(sy.dma_start(out=eld[:, :], in_=Ed[j, :, :]), "LD", 16)
            em.inc(sy.dma_start(out=pld[:, :], in_=Pd[j, :, :]), "LD", 16)
            if not correction:
                em.inc(sy.dma_start(out=hld[:, :], in_=Hd[j, :, :]), "LD", 16)
            em.wait(t, "LD")
            for h in range(16):
                p0 = 64 * (h % 2)
                c64 = (h // 2) * 64
                cc = (h // 2) * 128
                stb_h = stbb[p0:p0 + 64, c64:c64 + 64]
                et_h = eld[p0:p0 + 64, cc:cc + 128]
                pt_h = pld[p0:p0 + 64, c64:c64 + 64]
                em.wait(t, "DV")
                em.inc(t.matmul(bank(3, p0, 64, 0, 128), stb_h, et_h,
                                start=True, stop=True), "PE")
                em.inc(t.matmul(bank(3, p0, 64, 128, 64), pt_h, stb_h,
                                start=True, stop=True), "PE")
                em.wait(v, "PE")
                ocol = (h // 2) * TLOC + j * 128
                em.vstep(v.tensor_add(outs[p0:p0 + 64, ocol:ocol + 128],
                                      outs[p0:p0 + 64, ocol:ocol + 128],
                                      bank(3, p0, 64, 0, 128)))
                if not correction:
                    em.vstep(v.tensor_add(st32[p0:p0 + 64, c64:c64 + 64],
                                          bank(3, p0, 64, 128, 64),
                                          hld[p0:p0 + 64, c64:c64 + 64]))
                else:
                    em.vstep(v.tensor_copy(st32[p0:p0 + 64, c64:c64 + 64],
                                           bank(3, p0, 64, 128, 64)))
                em.vstep(v.tensor_copy(stbb[p0:p0 + 64, c64:c64 + 64],
                                       st32[p0:p0 + 64, c64:c64 + 64]))

    state_pass(False)
    em.vstep(v.tensor_copy(pays[:, 48:560], st32[:, 0:512]))
    emit_pair_exchange(em, 1, pays, 48, 512)
    em.wait(v, "XD")
    em.vstep(v.tensor_scalar(st32[:, 0:512], pasb[:, 48:560], xmk[:, 0:1],
                             None, OP.mult))
    em.vstep(v.tensor_copy(stbb[:, 0:512], st32[:, 0:512]))
    state_pass(True)
    em.barrier()
    if dbg:
        em.inc(sy.dma_start(out=cfg["dbg_outs"].ap().rearrange("c p l -> p c l"),
                            in_=outs[:, :].rearrange("p (c l) -> p c l", c=NCH)),
               "ST", 16)
        em.wait(sy, "ST")

    for c in range(NCH):
        em.vstep(v.reduce_sum(gnb[:, c:c + 1], outs[:, c * TLOC:(c + 1) * TLOC],
                              axis=mybir.AxisListType.X))
    for ti in range(NT):
        for c in range(NCH):
            em.wait(sc, "DV")
            em.inc(sc.activation(sq[:, (c % 2) * TT:(c % 2) * TT + TT],
                                 outs[:, c * TLOC + ti * TT:c * TLOC + ti * TT + TT],
                                 AF.Square), "AC")
            em.wait(v, "AC")
            em.vstep(v.reduce_sum(pays[:, 40 + c:40 + c + 1] if ti == 0 else xn32[:, 0:1],
                                  sq[:, (c % 2) * TT:(c % 2) * TT + TT],
                                  axis=mybir.AxisListType.X))
            if ti > 0:
                em.vstep(v.tensor_add(pays[:, 40 + c:40 + c + 1],
                                      pays[:, 40 + c:40 + c + 1], xn32[:, 0:1]))
    em.wait(t, "DV")
    for c in range(NCH):
        em.inc(t.matmul(bank(4, 0, 16, 0, 1), chm[:, c * 16:(c + 1) * 16],
                        gnb[:, c:c + 1], start=(c == 0), stop=(c == NCH - 1)), "PE")
        em.inc(t.matmul(bank(4, 0, 16, 1, 1), chm[:, c * 16:(c + 1) * 16],
                        pays[:, 40 + c:40 + c + 1], start=(c == 0), stop=(c == NCH - 1)),
               "PE")
    em.wait(v, "PE")
    em.vstep(v.tensor_copy(pays[0:16, 24:25], bank(4, 0, 16, 0, 1)))
    em.vstep(v.tensor_copy(pays[0:16, 25:26], bank(4, 0, 16, 1, 1)))
    emit_pair_exchange(em, 2, pays, 24, 16)
    em.wait(v, "XD")
    NT_TOT = float(S * 64)
    em.vstep(v.tensor_add(v16[0:16, 0:2], pays[0:16, 24:26], pasb[0:16, 24:26]))
    em.vstep(v.tensor_scalar_mul(v16[0:16, 2:3], v16[0:16, 0:1], 1.0 / NT_TOT))
    em.vstep(v.tensor_scalar_mul(v16[0:16, 3:4], v16[0:16, 1:2], 1.0 / NT_TOT))
    em.vstep(v.tensor_mul(v16[0:16, 4:5], v16[0:16, 2:3], v16[0:16, 2:3]))
    em.vstep(v.tensor_sub(v16[0:16, 5:6], v16[0:16, 3:4], v16[0:16, 4:5]))
    em.wait(sc, "DV")
    em.inc(sc.activation(v16[0:16, 6:7], v16[0:16, 5:6], AF.Sqrt,
                         bias=c_misc[0:16, 0:1]), "AC")
    em.wait(v, "AC")
    em.vstep(v.reciprocal(v16[0:16, 7:8], v16[0:16, 6:7]))
    for qi, col in ((2, 0), (7, 1)):
        em.vstep(v.tensor_scalar(v16[0:16, 8:16], cpat[0:16, :], v16[0:16, qi:qi + 1],
                                 None, OP.mult))
        em.wait(t, "DV")
        for c in range(NCH):
            em.inc(t.matmul(bank(4, 0, 128, 0, 8), selpf[:, c * 128:(c + 1) * 128],
                            v16[0:16, 8:16], start=(c == 0), stop=(c == NCH - 1)), "PE")
        em.wait(v, "PE")
        em.vstep(v.tensor_copy(gnb[:, 8 + col * 8:16 + col * 8], bank(4, 0, 128, 0, 8)))
    em.vstep(v.tensor_mul(gnb[:, 24:32], gnpp[:, 0:8], gnb[:, 16:24]))
    em.vstep(v.tensor_mul(gnb[:, 0:8], gnb[:, 8:16], gnb[:, 24:32]))
    em.vstep(v.tensor_sub(gnb[:, 8:16], gnpp[:, 8:16], gnb[:, 0:8]))
    em.barrier()

    load_w(cfg["wg"])
    for ti in range(NT):
        tc0 = ti * TT
        load_xn(ti)
        em.wait(t, "LW")
        for dc in range(NCH):
            pb = dc % 2
            em.wait(t, "AC")
            for kc in range(NCH):
                em.inc(t.matmul(bank(pb), wslice(kc, dc),
                                xnb[:, kc * TT:(kc + 1) * TT],
                                start=(kc == 0), stop=(kc == NCH - 1)), "PE")
            em.wait(sc, "PE")
            em.inc(sc.activation(silb[:, dc * TT:(dc + 1) * TT], bank(pb), AF.Silu,
                                 bias=bpp[:, 3 * NCH + dc:3 * NCH + dc + 1]), "AC")
        em.wait(sy, "AC")
        em.inc(sy.dma_start(
            out=ogd.ap().rearrange("c p l -> p c l")[:, :, tc0:tc0 + TT],
            in_=silb[:, :].rearrange("p (c l) -> p c l", c=NCH)), "ST", 16)
        em.wait(sc, "ST")
    em.barrier()

    load_w(cfg["wo"])
    for ti in range(NT):
        tc0 = ti * TT
        em.wait(sy, "PE")
        em.wait(sy, "DV")
        em.inc(sy.dma_start(out=silb[:, :].rearrange("p (c l) -> p c l", c=NCH),
                            in_=ogd.ap().rearrange("c p l -> p c l")[:, :, tc0:tc0 + TT]),
               "LD", 16)
        for c in range(NCH):
            em.inc(sy.dma_start(out=xt[:, c * TT:(c + 1) * TT],
                                in_=x_in[c, :, tc0:tc0 + TT]), "LD", 16)
        em.wait(v, "LD")
        for c in range(NCH):
            em.vstep(v.tensor_scalar(xnb[:, c * TT:(c + 1) * TT],
                                     outs[:, c * TLOC + tc0:c * TLOC + tc0 + TT],
                                     gnb[:, 24 + c:24 + c + 1],
                                     gnb[:, 8 + c:8 + c + 1],
                                     OP.mult, OP.add))
            em.vstep(v.tensor_mul(xnb[:, c * TT:(c + 1) * TT],
                                  xnb[:, c * TT:(c + 1) * TT],
                                  silb[:, c * TT:(c + 1) * TT]))
        em.wait(t, "DV")
        em.wait(t, "LW")
        for dc in range(NCH):
            pb = dc % 2
            em.wait(t, "ST")
            em.wait(t, "DV")
            for kc in range(NCH):
                em.inc(t.matmul(bank(pb), wslice(kc, dc),
                                xnb[:, kc * TT:(kc + 1) * TT],
                                start=(kc == 0), stop=(kc == NCH - 1)), "PE")
            em.wait(v, "PE")
            sl2 = dc % 2
            em.vstep(v.tensor_scalar(sq[:, sl2 * TT:(sl2 + 1) * TT], bank(pb),
                                     bpp[:, 4 * NCH + dc:4 * NCH + dc + 1], None,
                                     OP.add))
            em.vstep(v.tensor_add(sq[:, sl2 * TT:(sl2 + 1) * TT],
                                  sq[:, sl2 * TT:(sl2 + 1) * TT],
                                  xt[:, dc * TT:(dc + 1) * TT]))
            em.wait(sy, "DV")
            em.inc(sy.dma_start(out=x_out[dc, :, tc0:tc0 + TT],
                                in_=sq[:, sl2 * TT:(sl2 + 1) * TT]), "ST", 16)
    em.wait(sy, "ST")


def build_attn1(dbg=False):
    nc = bass.Bass("TRN2", target_bir_lowering=False)
    CJ = TLOC // 128
    dr = {}
    dr["x_in"] = nc.dram_tensor("x_in", [NCH, 128, TLOC], F32, kind="ExternalInput")
    dr["x_out"] = nc.dram_tensor("x_out", [NCH, 128, TLOC], F32, kind="ExternalOutput")
    for nm in ["wq", "wk", "wv", "wg", "wo"]:
        dr[nm] = nc.dram_tensor(nm, [D, D], F32, kind="ExternalInput")
    dr["wa"] = nc.dram_tensor("wa", [D, 16], F32, kind="ExternalInput")
    dr["wb_"] = nc.dram_tensor("wb_", [D, 16], F32, kind="ExternalInput")
    for nm, shape in [("b_pp", [128, 5 * NCH]), ("ab_bias", [16, 2]),
                      ("gn_pp", [128, 2 * NCH]), ("cv_pp", [128, 3 * NCH * 4]),
                      ("c_ones", [128, 128]), ("c_id", [128, 128]),
                      ("c_misc", [128, 8]), ("c_selpf", [16, NCH * 128]),
                      ("c_s16f", [16, 16 * 128]), ("c_ut", [128, 128]),
                      ("c_lt", [128, 128]), ("c_msl", [128, 128]),
                      ("c_msu", [128, 128]), ("c_hm", [128, NCH * 16]),
                      ("c_pat", [16, 8]), ("xmask", [128, 1])]:
        dr[nm] = nc.dram_tensor(nm, shape, F32, kind="ExternalInput")
    dr["c_sel"] = nc.dram_tensor("c_sel", [8, 8 * 128], BF16, kind="ExternalInput")
    dr["c_selp"] = nc.dram_tensor("c_selp", [16, NCH * 128], BF16, kind="ExternalInput")
    dr["c_idb"] = nc.dram_tensor("c_idb", [128, 128], BF16, kind="ExternalInput")
    dr["meta"] = nc.dram_tensor("meta", [1, 32], I32, kind="ExternalInput")
    dr["nonce"] = nc.dram_tensor("nonce", [1, 8], I32, kind="ExternalInput")
    kind = "ExternalOutput" if dbg else "Internal"
    dr["xn_dram"] = nc.dram_tensor("xn_scr", [NCH, 128, TLOC], BF16, kind=kind)
    dr["qc_dram"] = nc.dram_tensor("qc_scr", [NCH, 128, TLOC], BF16, kind=kind)
    dr["kc_dram"] = nc.dram_tensor("kc_scr", [NCH, 128, TLOC], BF16, kind=kind)
    dr["vc_dram"] = nc.dram_tensor("vc_scr", [NCH, 128, TLOC], BF16, kind=kind)
    dr["og_dram"] = nc.dram_tensor("og_scr", [NCH, 128, TLOC], BF16)
    dr["E_dram"] = nc.dram_tensor("E_scr", [CJ, 128, NCH * 128], BF16)
    dr["P_dram"] = nc.dram_tensor("P_scr", [CJ, 128, NCH * 64], BF16)
    dr["H_dram"] = nc.dram_tensor("H_scr", [CJ, 128, NCH * 64], BF16)
    if dbg:
        dr["dbg_ab"] = nc.dram_tensor("dbg_ab", [2, 16, TLOC], F32, kind="ExternalOutput")
        dr["dbg_outs"] = nc.dram_tensor("dbg_outs", [NCH, 128, TLOC], BF16,
                                        kind="ExternalOutput")
    dr["PAYC"] = 560
    dr["shm_pay"] = nc.dram_tensor("shm_pay", [2, 128, 560], F32, addr_space="Shared")
    dr["shm_fl"] = nc.dram_tensor("shm_fl", [2, 8], I32, addr_space="Shared")
    with ExitStack() as stack:
        em = Em(nc, stack)
        cfg = dict(dr)
        cfg["tag"] = "b"
        cfg["dbg"] = dbg
        with nc.allow_low_precision(reason="bf16 operand staging is intentional"):
            emit_attn1(em, cfg)
    return nc


def attn1_in_map(inputs, xcore, meta, nonce, core_idx):
    c_ones, c_id, c_sel, c_misc = consts_f()
    selp = np.zeros((16, NCH * 128), BF)
    selpf = np.zeros((16, NCH * 128), np.float32)
    for c in range(NCH):
        selp[2 * c, c * 128:c * 128 + 64] = 1
        selp[2 * c + 1, c * 128 + 64:(c + 1) * 128] = 1
        selpf[2 * c, c * 128:c * 128 + 64] = 1
        selpf[2 * c + 1, c * 128 + 64:(c + 1) * 128] = 1
    s16f = np.zeros((16, 16 * 128), np.float32)
    for h in range(16):
        s16f[h, h * 128:(h + 1) * 128] = 1
    idx = np.arange(128)
    cut = (idx[:, None] <= idx[None, :]).astype(np.float32)
    clt = (idx[:, None] > idx[None, :]).astype(np.float32)
    cmsl = np.where(idx[None, :] < idx[:, None], 0.0, -1e30).astype(np.float32)
    cmsu = np.where(idx[:, None] <= idx[None, :], 0.0, -1e30).astype(np.float32)
    chm = np.zeros((128, NCH * 16), np.float32)
    for c in range(NCH):
        chm[0:64, c * 16 + 2 * c] = 1
        chm[64:128, c * 16 + 2 * c + 1] = 1
    cpat = np.zeros((16, 8), np.float32)
    for h in range(16):
        cpat[h, h // 2] = 1
    idb = np.eye(128, dtype=BF)
    onesb = np.ones((128, 128), BF)
    xmask = np.full((128, 1), float(core_idx % 2), np.float32)
    bpp = np.zeros((128, 5 * NCH), np.float32)
    for i, nm in enumerate(["bq", "bk", "bv", "bg", "bo"]):
        bpp[:, i * NCH:(i + 1) * NCH] = inputs[f"L1_attn_{nm}"].reshape(NCH, 128).T
    abb = np.stack([inputs["L1_attn_ba"], inputs["L1_attn_bb"]], axis=1).astype(np.float32)
    gnpp = np.zeros((128, 2 * NCH), np.float32)
    gnpp[:, 0:NCH] = inputs["L1_attn_gn_scale"].reshape(NCH, 128).T
    gnpp[:, NCH:2 * NCH] = inputs["L1_attn_gn_bias"].reshape(NCH, 128).T
    cvpp = np.zeros((128, 3 * NCH * 4), np.float32)
    for i, nm in enumerate(["q_conv", "k_conv", "v_conv"]):
        w = inputs[f"L1_attn_{nm}"]
        cvpp[:, i * 32:(i + 1) * 32] = w.reshape(NCH, 128, 4).transpose(1, 0, 2).reshape(128, 32)
    return dict(
        x_in=xcore,
        wq=inputs["L1_attn_Wq"], wk=inputs["L1_attn_Wk"], wv=inputs["L1_attn_Wv"],
        wg=inputs["L1_attn_Wg"], wo=inputs["L1_attn_Wo"],
        wa=inputs["L1_attn_Wa"], wb_=inputs["L1_attn_Wb"],
        b_pp=bpp, ab_bias=abb, gn_pp=gnpp, cv_pp=cvpp,
        c_ones=c_ones, c_id=c_id, c_sel=c_sel, c_misc=c_misc,
        c_selp=selp, c_selpf=selpf, c_s16f=s16f, c_ut=cut, c_lt=clt,
        c_msl=cmsl, c_msu=cmsu, c_hm=chm, c_pat=cpat, c_idb=idb,
        xmask=xmask, meta=meta, nonce=nonce,
    )


def run_attn1(inputs, xcores):
    nc = _get_prog("attn1", build_attn1)
    rng = np.random.default_rng()
    metas, nonces = xchg_host_inputs(rng, col0s=[0, 48, 24, 0])
    maps = [attn1_in_map(inputs, xcores[c], metas[c], nonces[c], c)
            for c in range(NCORES)]
    res = run_bass_kernel_spmd(nc, maps, core_ids=list(range(NCORES)))
    return [res.results[c]["x_out"] for c in range(NCORES)]


def build_ffn(layer, dbg=False):
    nc = bass.Bass("TRN2", target_bir_lowering=False)
    E = 8 if layer == 0 else 7
    I = 2048 if layer == 0 else 512
    NBIAS = 1 if layer == 0 else 2
    dr = {}
    dr["x_in"] = nc.dram_tensor("x_in", [NCH, 128, TLOC], F32, kind="ExternalInput")
    dr["x_out"] = nc.dram_tensor("x_out", [NCH, 128, TLOC], F32, kind="ExternalOutput")
    dr["gw"] = nc.dram_tensor("gw", [E, D, I], F32, kind="ExternalInput")
    dr["uw"] = nc.dram_tensor("uw", [E, D, I], F32, kind="ExternalInput")
    dr["dw"] = nc.dram_tensor("dw", [E, I, D], F32, kind="ExternalInput")
    dr["sg"] = nc.dram_tensor("sg", [D, I], F32, kind="ExternalInput")
    dr["su"] = nc.dram_tensor("su", [D, I], F32, kind="ExternalInput")
    dr["sd"] = nc.dram_tensor("sd", [I, D], F32, kind="ExternalInput")
    dr["rw"] = nc.dram_tensor("rw", [D, E], F32, kind="ExternalInput")
    dr["rbias"] = nc.dram_tensor("rbias", [NBIAS, E], F32, kind="ExternalInput")
    dr["c_ones"] = nc.dram_tensor("c_ones", [128, 128], F32, kind="ExternalInput")
    dr["c_id"] = nc.dram_tensor("c_id", [128, 128], F32, kind="ExternalInput")
    dr["c_sel"] = nc.dram_tensor("c_sel", [8, 8 * 128], BF16, kind="ExternalInput")
    dr["c_misc"] = nc.dram_tensor("c_misc", [128, 8], F32, kind="ExternalInput")
    kind = "ExternalOutput" if dbg else "Internal"
    dr["xn_dram"] = nc.dram_tensor("xn_scr", [NCH, 128, TLOC], BF16, kind=kind)
    dr["yacc"] = nc.dram_tensor("yacc_scr", [NCH, 128, TLOC], F32, kind=kind)
    if dbg:
        dr["dbg_probs"] = nc.dram_tensor("dbg_probs", [128, TCH * 8], F32, kind="ExternalOutput")
        dr["dbg_wtok"] = nc.dram_tensor("dbg_wtok", [128, TCH * 8], F32, kind="ExternalOutput")
    with ExitStack() as stack:
        em = Em(nc, stack)
        cfg = dict(dr)
        cfg["E"], cfg["I"], cfg["tag"] = E, I, "f"
        cfg["dbg"] = dbg
        cfg["use_sel"] = True
        with nc.allow_low_precision(reason="bf16 operand staging is intentional"):
            emit_ffn(em, cfg)
    return nc


def shard_x(x):
    out = []
    for c in range(NCORES):
        b, h = c // 2, c % 2
        xs = x[b, h * TLOC:(h + 1) * TLOC, :]
        xf = np.ascontiguousarray(xs.T)
        out.append(np.ascontiguousarray(xf.reshape(NCH, 128, TLOC)))
    return out


def unshard_x(parts):
    x = np.empty((B, S, D), np.float32)
    for c in range(NCORES):
        b, h = c // 2, c % 2
        xf = parts[c].reshape(D, TLOC)
        x[b, h * TLOC:(h + 1) * TLOC, :] = xf.T
    return x


def consts_f():
    c_ones = np.ones((128, 128), np.float32)
    c_id = np.eye(128, dtype=np.float32)
    c_sel = np.zeros((8, 8 * 128), BF)
    for e in range(8):
        c_sel[e, e * 128:(e + 1) * 128] = 1
    c_misc = np.zeros((128, 8), np.float32)
    c_misc[:, 0] = LN_EPS
    c_misc[:, 1] = 1.0
    c_misc[:, 2] = 1e-6
    return c_ones, c_id, c_sel, c_misc


def ffn_in_map(layer, inputs, xcore):
    c_ones, c_id, c_sel, c_misc = consts_f()
    if layer == 0:
        m = dict(
            gw=inputs["L0_ffn_gate_w"], uw=inputs["L0_ffn_up_w"],
            dw=inputs["L0_ffn_down_w"],
            sg=np.ascontiguousarray(inputs["L0_ffn_shared_gate"].T),
            su=np.ascontiguousarray(inputs["L0_ffn_shared_up"].T),
            sd=np.ascontiguousarray(inputs["L0_ffn_shared_down"].T),
            rw=np.ascontiguousarray(inputs["L0_ffn_router_w"].T),
            rbias=inputs["L0_ffn_routing_bias"][None, :],
        )
    else:
        m = dict(
            gw=inputs["L1_ffn_r_gate"], uw=inputs["L1_ffn_r_up"],
            dw=inputs["L1_ffn_r_down"],
            sg=inputs["L1_ffn_sh_gate"], su=inputs["L1_ffn_sh_up"],
            sd=inputs["L1_ffn_sh_down"],
            rw=inputs["L1_ffn_w_r"],
            rbias=np.stack([inputs["L1_ffn_b_r"], inputs["L1_ffn_routing_bias"]]),
        )
    m["c_ones"], m["c_id"], m["c_sel"], m["c_misc"] = c_ones, c_id, c_sel, c_misc
    m["x_in"] = xcore
    return m


_prog_cache = {}


def _get_prog(name, builder):
    if name not in _prog_cache:
        _prog_cache[name] = builder()
    return _prog_cache[name]


def run_ffn(layer, inputs, xcores):
    nc = _get_prog(f"ffn{layer}", lambda: build_ffn(layer))
    maps = [ffn_in_map(layer, inputs, xcores[c]) for c in range(NCORES)]
    res = run_bass_kernel_spmd(nc, maps, core_ids=list(range(NCORES)))
    return [res.results[c]["x_out"] for c in range(NCORES)]



def build_fused(dbg=False):
    nc = bass.Bass("TRN2", target_bir_lowering=False)
    CJ = TLOC // 128
    dr = {}
    dr["x0"] = nc.dram_tensor("x0", [NCH, 128, TLOC], F32, kind="ExternalInput")
    xs = [dr["x0"]]
    for i in range(1, 6):
        xs.append(nc.dram_tensor(f"xi{i}", [NCH, 128, TLOC], F32))
    xs.append(nc.dram_tensor("x_out", [NCH, 128, TLOC], F32, kind="ExternalOutput"))
    for nm, shape, dt in [("c_ones", [128, 128], F32), ("c_id", [128, 128], F32),
                          ("c_misc", [128, 8], F32), ("c_sel", [8, 8 * 128], BF16),
                          ("c_selp", [16, NCH * 128], BF16),
                          ("c_idb", [128, 128], BF16),
                          ("c_onesb", [128, 128], BF16),
                          ("c_selpf", [16, NCH * 128], F32),
                          ("c_s16f", [16, 16 * 128], F32),
                          ("c_ut", [128, 128], F32), ("c_lt", [128, 128], F32),
                          ("c_msl", [128, 128], F32), ("c_msu", [128, 128], F32),
                          ("c_hm", [128, NCH * 16], F32), ("c_pat", [16, 8], F32),
                          ("xmask", [128, 1], F32),
                          ("meta", [1, 32], I32), ("nonce", [1, 8], I32)]:
        dr[nm] = nc.dram_tensor(nm, shape, dt, kind="ExternalInput")
    dr["shm_pay"] = nc.dram_tensor("shm_pay", [2, 128, 560], F32, addr_space="Shared")
    dr["shm_fl"] = nc.dram_tensor("shm_fl", [2, 8], I32, addr_space="Shared")
    for L, E, I in ((0, 8, 2048), (1, 7, 512)):
        NB = 1 if L == 0 else 2
        dr[f"f{L}_gw"] = nc.dram_tensor(f"f{L}_gw", [E, D, I], F32, kind="ExternalInput")
        dr[f"f{L}_uw"] = nc.dram_tensor(f"f{L}_uw", [E, D, I], F32, kind="ExternalInput")
        dr[f"f{L}_dw"] = nc.dram_tensor(f"f{L}_dw", [E, I, D], F32, kind="ExternalInput")
        dr[f"f{L}_sg"] = nc.dram_tensor(f"f{L}_sg", [D, I], F32, kind="ExternalInput")
        dr[f"f{L}_su"] = nc.dram_tensor(f"f{L}_su", [D, I], F32, kind="ExternalInput")
        dr[f"f{L}_sd"] = nc.dram_tensor(f"f{L}_sd", [I, D], F32, kind="ExternalInput")
        dr[f"f{L}_rw"] = nc.dram_tensor(f"f{L}_rw", [D, E], F32, kind="ExternalInput")
        dr[f"f{L}_rb"] = nc.dram_tensor(f"f{L}_rb", [NB, E], F32, kind="ExternalInput")
    for nm in ["wq", "wk", "wv", "wo"]:
        dr[f"a0_{nm}"] = nc.dram_tensor(f"a0_{nm}", [D, D], F32, kind="ExternalInput")
    dr["a0_b_rows"] = nc.dram_tensor("a0_b_rows", [1, 4 * D], F32, kind="ExternalInput")
    dr["a0_b_pp"] = nc.dram_tensor("a0_b_pp", [128, 4 * NCH], F32, kind="ExternalInput")
    for nm in ["wq", "wk", "wv", "wg", "wo"]:
        dr[f"a1_{nm}"] = nc.dram_tensor(f"a1_{nm}", [D, D], F32, kind="ExternalInput")
    dr["a1_wa"] = nc.dram_tensor("a1_wa", [D, 16], F32, kind="ExternalInput")
    dr["a1_wb_"] = nc.dram_tensor("a1_wb_", [D, 16], F32, kind="ExternalInput")
    dr["a1_b_pp"] = nc.dram_tensor("a1_b_pp", [128, 5 * NCH], F32, kind="ExternalInput")
    dr["a1_ab_bias"] = nc.dram_tensor("a1_ab_bias", [16, 2], F32, kind="ExternalInput")
    dr["a1_gn_pp"] = nc.dram_tensor("a1_gn_pp", [128, 2 * NCH], F32, kind="ExternalInput")
    dr["a1_cv_pp"] = nc.dram_tensor("a1_cv_pp", [128, 3 * NCH * 4], F32, kind="ExternalInput")
    scr = {}
    for i in range(6):
        scr[f"xn{i}"] = nc.dram_tensor(f"xns{i}", [NCH, 128, TLOC], BF16)
    for i in (0, 2, 3, 5):
        scr[f"yacc{i}"] = nc.dram_tensor(f"yacc{i}", [NCH, 128, TLOC], F32)
    scr["qf0"] = nc.dram_tensor("qf0", [NCH, 128, TLOC], BF16)
    for nm in ["qc", "kc", "vc", "og"]:
        scr[nm] = nc.dram_tensor(nm + "s", [NCH, 128, TLOC], BF16)
    scr["E"] = nc.dram_tensor("E_scr", [CJ, 128, NCH * 128], BF16)
    scr["P"] = nc.dram_tensor("P_scr", [CJ, 128, NCH * 64], BF16)
    scr["H"] = nc.dram_tensor("H_scr", [CJ, 128, NCH * 64], BF16)

    with ExitStack() as outer:
        em = Em(nc, outer)
        base = {k: dr[k] for k in dr}
        base["PAYC"] = 560
        base["use_sel"] = True
        em.load_consts(base)
        base["tag"] = ""
        emit_xchg_setup(em, base)

        def sub(i, emitfn, cfg):
            cfg = dict(base, **cfg)
            cfg["x_in"] = xs[i]
            cfg["x_out"] = xs[i + 1]
            cfg["tag"] = f"s{i}_"
            with ExitStack() as inner:
                saved = em.stack
                em.stack = inner
                with nc.allow_low_precision(reason="bf16 staging intentional"):
                    emitfn(em, cfg)
                em.stack = saved
            em.barrier()

        def fcfg(L, i):
            return dict(gw=dr[f"f{L}_gw"], uw=dr[f"f{L}_uw"], dw=dr[f"f{L}_dw"],
                        sg=dr[f"f{L}_sg"], su=dr[f"f{L}_su"], sd=dr[f"f{L}_sd"],
                        rw=dr[f"f{L}_rw"], rbias=dr[f"f{L}_rb"],
                        E=(8 if L == 0 else 7), I=(2048 if L == 0 else 512),
                        xn_dram=scr[f"xn{i}"], yacc=scr[f"yacc{i}"])

        sub(0, emit_ffn, fcfg(0, 0))
        sub(1, emit_attn0, dict(wq=dr["a0_wq"], wk=dr["a0_wk"], wv=dr["a0_wv"],
                                wo=dr["a0_wo"], b_rows=dr["a0_b_rows"],
                                b_pp=dr["a0_b_pp"], xn_dram=scr["xn1"],
                                qf_dram=scr["qf0"], rbase=0))
        sub(2, emit_ffn, fcfg(0, 2))
        sub(3, emit_ffn, fcfg(1, 3))
        sub(4, emit_attn1, dict(wq=dr["a1_wq"], wk=dr["a1_wk"], wv=dr["a1_wv"],
                                wg=dr["a1_wg"], wo=dr["a1_wo"], wa=dr["a1_wa"],
                                wb_=dr["a1_wb_"], b_pp=dr["a1_b_pp"],
                                ab_bias=dr["a1_ab_bias"], gn_pp=dr["a1_gn_pp"],
                                cv_pp=dr["a1_cv_pp"], xn_dram=scr["xn4"],
                                qc_dram=scr["qc"], kc_dram=scr["kc"],
                                vc_dram=scr["vc"], og_dram=scr["og"],
                                E_dram=scr["E"], P_dram=scr["P"], H_dram=scr["H"],
                                rbase=4))
        sub(5, emit_ffn, fcfg(1, 5))
    return nc


def fused_in_maps(inputs):
    c_ones, c_id, c_sel, c_misc = consts_f()
    rng = np.random.default_rng()
    col0s = [0, 16, 32, 48, 0, 48, 24, 0]
    metas, nonces = xchg_host_inputs(rng, col0s=col0s)
    xcores = shard_x(np.asarray(inputs["L0_attn_x"]))
    maps = []
    for c in range(NCORES):
        a0 = attn0_in_map(inputs, None, metas[c], nonces[c], c)
        a1 = attn1_in_map(inputs, None, metas[c], nonces[c], c)
        m = dict(x0=xcores[c], c_ones=c_ones, c_id=c_id, c_misc=c_misc,
                 c_sel=a0["c_sel"], c_selp=a1["c_selp"], c_idb=a1["c_idb"],
                 c_onesb=a0["c_onesb"], c_selpf=a1["c_selpf"],
                 c_s16f=a1["c_s16f"], c_ut=a1["c_ut"], c_lt=a1["c_lt"],
                 c_msl=a1["c_msl"], c_msu=a1["c_msu"], c_hm=a1["c_hm"],
                 c_pat=a1["c_pat"], xmask=a1["xmask"],
                 meta=metas[c], nonce=nonces[c])
        for L in (0, 1):
            fm = ffn_in_map(L, inputs, None)
            for k in ["gw", "uw", "dw", "sg", "su", "sd", "rw", "rbias"]:
                m[f"f{L}_{'rb' if k == 'rbias' else k}"] = fm[k]
        for nm in ["wq", "wk", "wv", "wo"]:
            m[f"a0_{nm}"] = a0[nm]
        m["a0_b_rows"] = a0["b_rows"]
        m["a0_b_pp"] = a0["b_pp"]
        for nm in ["wq", "wk", "wv", "wg", "wo", "wa", "wb_"]:
            m[f"a1_{nm}"] = a1[nm]
        m["a1_b_pp"] = a1["b_pp"]
        m["a1_ab_bias"] = a1["ab_bias"]
        m["a1_gn_pp"] = a1["gn_pp"]
        m["a1_cv_pp"] = a1["cv_pp"]
        maps.append(m)
    return maps


def kernel(**inputs):
    nc = _get_prog("fused", build_fused)
    maps = fused_in_maps(inputs)
    res = run_bass_kernel_spmd(nc, maps, core_ids=list(range(NCORES)))
    return unshard_x([res.results[c]["x_out"] for c in range(NCORES)])


def kernel_unfused(**inputs):
    xcores = shard_x(np.asarray(inputs["L0_attn_x"]))
    xcores = run_ffn(0, inputs, xcores)
    xcores = run_attn0(inputs, xcores)
    xcores = run_ffn(0, inputs, xcores)
    xcores = run_ffn(1, inputs, xcores)
    xcores = run_attn1(inputs, xcores)
    xcores = run_ffn(1, inputs, xcores)
    return unshard_x(xcores)





# version 09
# speedup vs baseline: 1.2090x; 1.0009x over previous
import numpy as np
import ml_dtypes
from contextlib import ExitStack

import concourse.bass as bass
import concourse.mybir as mybir
from concourse.bass_utils import run_bass_kernel_spmd

F32 = mybir.dt.float32
BF16 = mybir.dt.bfloat16
I32 = mybir.dt.int32
AF = mybir.ActivationFunctionType
OP = mybir.AluOpType

B, S, D = 4, 8192, 1024
NCORES = 8
TLOC = S // 2
NCH = D // 128
TT = 512
NT = TLOC // TT
TCH = TLOC // 128
LN_EPS = 1e-5

BF = ml_dtypes.bfloat16


class Em:

    def __init__(self, nc, stack):
        self.nc = nc
        self.stack = stack
        self.g = nc.gpsimd
        self.v = nc.vector
        self.sc = nc.scalar
        self.t = nc.tensor
        self.sy = nc.sync
        self._sems = {}

    def sem(self, name):
        if name not in self._sems:
            h = self.stack.enter_context(self.nc.semaphore(name))
            self._sems[name] = [h, 0]
        return self._sems[name]

    def inc(self, ins, name, amt=1):
        s = self.sem(name)
        ins.then_inc(s[0], amt)
        s[1] += amt
        return ins

    def val(self, name):
        return self.sem(name)[1]

    def wait(self, eng, name, val=None):
        s = self.sem(name)
        v = s[1] if val is None else val
        if v > 0:
            eng.wait_ge(s[0], v)

    def barrier(self):
        self.nc.all_engine_barrier()

    def vstep(self, ins):
        self.inc(ins, "DV")
        self.wait(self.v, "DV")
        return ins

    def load_consts(self, cfg):
        if not hasattr(self, "_consts"):
            nc = self.nc
            ones = self.stack.enter_context(nc.sbuf_tensor("c_ones_sb", [128, 128], F32))
            ident = self.stack.enter_context(nc.sbuf_tensor("c_id_sb", [128, 128], F32))
            misc = self.stack.enter_context(nc.sbuf_tensor("c_misc_sb", [128, 8], F32))
            self.inc(self.sy.dma_start(out=ones[:, :], in_=cfg["c_ones"][:, :]), "LD", 16)
            self.inc(self.sy.dma_start(out=ident[:, :], in_=cfg["c_id"][:, :]), "LD", 16)
            self.inc(self.sy.dma_start(out=misc[:, :], in_=cfg["c_misc"][:, :]), "LD", 16)
            sel = None
            if cfg.get("use_sel"):
                sel = self.stack.enter_context(nc.sbuf_tensor("c_sel_sb", [8, 8 * 128], BF16))
                self.inc(self.sy.dma_start(out=sel[:, :], in_=cfg["c_sel"][:, :]), "LD", 16)
            self._consts = (ones, ident, sel, misc)
        return self._consts


def emit_ffn(em, cfg):
    nc, g, v, sc, t, sy = em.nc, em.g, em.v, em.sc, em.t, em.sy
    E, I = cfg["E"], cfg["I"]
    ICH = I // 128
    NB = cfg["rbias"].shape[0]
    x_in, x_out = cfg["x_in"], cfg["x_out"]
    xn_dram, yacc = cfg["xn_dram"], cfg["yacc"]
    c_ones, c_id, c_sel, c_misc = em.load_consts(cfg)

    tag = cfg["tag"]
    sb = lambda name, shape, dt=F32: em.stack.enter_context(
        nc.sbuf_tensor(tag + name, shape, dt))
    xt = sb("xt", [128, NCH * TT])
    sq = sb("sq", [128, 2 * TT])
    xn32 = sb("xn32", [128, 2 * TT])
    xnb = sb("xnb", [128, NCH * TT], BF16)
    srow = sb("srow", [1, 4 * TT])
    probs = sb("probs", [128, TCH * 8])
    wtok = sb("wtok", [128, TCH * 8])
    trow = sb("trow", [128, 16])
    tmp8 = sb("tmp8", [128, 4 * 8])
    wem = sb("wem", [8, TLOC], BF16)
    rbs = sb("rbs", [NB, 8])
    wg = sb("wg", [128, NCH * I], BF16)
    wu = sb("wu", [128, NCH * I], BF16)
    wd = sb("wd", [128, ICH * D], BF16)
    hid = sb("hid", [128, ICH * TT], BF16)
    wb = sb("wb", [128, TT])
    xw = sb("xw", [128, NCH * TT], BF16)
    tsil = sb("tsil", [128, 2 * TT])
    ya_freed = [0]
    ybig = sb("ybig", [128, NCH * TT])
    rws = sb("rws", [128, NCH * 8])
    PS = em.stack.enter_context(nc.psum_tensor(tag + "PS", [128, 8 * 512], F32))

    def bank(b, p0=0, pn=128, c0=0, cn=512):
        return PS[p0:p0 + pn, b * 512 + c0:b * 512 + c0 + cn]

    if True:
        rww = cfg["rw"]
        if True:
            for c in range(NCH):
                em.inc(sy.dma_start(out=rws[:, c * 8:c * 8 + E],
                                    in_=rww[c * 128:(c + 1) * 128, :]), "LD", 16)
            em.inc(sy.dma_start(out=rbs[:, 0:E], in_=cfg["rbias"][:, :]), "LD", 16)
            v.memset(wem[0:8, :], 0.0)
            em.inc(v.memset(wtok[:, :], 0.0), "DV")

            for ti in range(NT):
                tc0 = ti * TT
                em.wait(sy, "PE")
                for c in range(NCH):
                    em.inc(sy.dma_start(out=xt[:, c * TT:(c + 1) * TT],
                                        in_=x_in[c, :, tc0:tc0 + TT]), "LD", 16)
                em.wait(t, "LD")
                em.wait(sc, "LD")
                for c in range(NCH):
                    em.wait(sc, "PE")
                    em.inc(sc.activation(sq[:, (c % 2) * TT:(c % 2) * TT + TT],
                                         xt[:, c * TT:(c + 1) * TT], AF.Square), "AC")
                    em.wait(t, "AC")
                    t.matmul(bank(0, 0, 1), c_ones[:, 0:1], xt[:, c * TT:(c + 1) * TT],
                             start=(c == 0), stop=(c == NCH - 1))
                    em.inc(t.matmul(bank(1, 0, 1), c_ones[:, 0:1],
                                    sq[:, (c % 2) * TT:(c % 2) * TT + TT],
                                    start=(c == 0), stop=(c == NCH - 1)), "PE")
                r = [srow[0:1, k * TT:(k + 1) * TT] for k in range(4)]
                em.wait(v, "PE")
                em.vstep(v.tensor_scalar_mul(r[0], bank(0, 0, 1), 1.0 / D))
                em.vstep(v.tensor_scalar_mul(r[1], bank(1, 0, 1), 1.0 / D))
                em.vstep(v.tensor_mul(r[2], r[0], r[0]))
                em.vstep(v.tensor_sub(r[1], r[1], r[2]))
                em.wait(sc, "DV")
                em.inc(sc.activation(r[2], r[1], AF.Sqrt, bias=c_misc[0:1, 0:1]), "AC")
                em.wait(v, "AC")
                em.vstep(v.reciprocal(r[1], r[2]))
                em.vstep(v.tensor_mul(r[3], r[0], r[1]))
                em.wait(t, "DV")
                em.inc(t.matmul(bank(2), c_ones[0:1, :], r[1], start=True, stop=True), "PE")
                em.inc(t.matmul(bank(3), c_ones[0:1, :], r[3], start=True, stop=True), "PE")
                em.wait(v, "PE")
                for c in range(NCH):
                    sl = (c % 2)
                    x32 = xn32[:, sl * TT:(sl + 1) * TT]
                    em.wait(v, "PE")
                    v.tensor_mul(x32, xt[:, c * TT:(c + 1) * TT], bank(2))
                    em.inc(v.tensor_sub(x32, x32, bank(3)), "DV")
                    em.inc(v.tensor_copy(xnb[:, c * TT:(c + 1) * TT], x32), "DV")
                    em.wait(t, "DV")
                    for tc in range(4):
                        ins = t.matmul(bank(4 + tc, 0, 128, 0, E),
                                       x32[:, tc * 128:(tc + 1) * 128],
                                       rws[:, c * 8:c * 8 + E],
                                       start=(c == 0), stop=False)
                    em.inc(ins, "PE")
                for tc in range(4):
                    ins = t.matmul(bank(4 + tc, 0, 128, 0, E), c_ones[0:NB, :],
                                   rbs[0:NB, 0:E], start=False, stop=True)
                    em.inc(ins, "PE")
                em.wait(sc, "PE")
                for tc in range(4):
                    em.inc(sc.activation(probs[:, (ti * 4 + tc) * 8:(ti * 4 + tc) * 8 + E],
                                         bank(4 + tc, 0, 128, 0, E), AF.Sigmoid), "AC")
                em.wait(sy, "DV")
                em.inc(sy.dma_start(out=xn_dram.ap()
                                    .rearrange("c p l -> p c l")[:, :, tc0:tc0 + TT],
                                    in_=xnb[:, :].rearrange("p (c l) -> p c l", c=NCH)),
                       "ST", 16)

            em.wait(sy, "ST")

            em.barrier()
            for tc in range(TCH):
                pr = probs[:, tc * 8:tc * 8 + E]
                sl = tc % 2
                t8 = tmp8[:, (2 + sl) * 8:(2 + sl) * 8 + E]
                m1 = trow[:, (sl * 4):(sl * 4) + 1]
                m2 = trow[:, (sl * 4) + 1:(sl * 4) + 2]
                den = trow[:, (sl * 4) + 2:(sl * 4) + 3]
                rden = trow[:, (sl * 4) + 3:(sl * 4) + 4]
                t8b = tmp8[:, sl * 8:sl * 8 + E]
                em.vstep(v.reduce_max(m1, pr, axis=mybir.AxisListType.X))
                em.vstep(v.tensor_scalar(t8, pr, m1, None, OP.is_equal))
                em.vstep(v.scalar_tensor_tensor(t8b, t8, -1e30, pr, OP.mult, OP.add))
                em.vstep(v.reduce_max(m2, t8b, axis=mybir.AxisListType.X))
                em.vstep(v.tensor_add(den, m1, m2))
                em.vstep(v.reciprocal(rden, den))
                em.vstep(v.tensor_scalar(t8, pr, m2, None, OP.is_ge))
                wv = wtok[:, tc * 8:tc * 8 + E]
                em.vstep(v.scalar_tensor_tensor(wv, pr, rden, t8, OP.mult, OP.mult))
                em.wait(t, "DV")
                em.inc(t.transpose(bank(4, 0, 8, 0, 128), wtok[:, tc * 8:tc * 8 + 8],
                                   c_id[:, :]), "PE")
                em.wait(v, "PE")
                em.inc(v.tensor_copy(wem[0:8, tc * 128:(tc + 1) * 128],
                                     bank(4, 0, 8, 0, 128)), "DV")

            if cfg.get("dbg"):
                em.wait(sy, "DV")
                em.inc(sy.dma_start(out=cfg["dbg_probs"][:, :], in_=probs[:, :]), "ST", 16)
                em.inc(sy.dma_start(out=cfg["dbg_wtok"][:, :], in_=wtok[:, :]), "ST", 16)

            em.barrier()
            freed = {0: 0, 5: 0, 1: 0, 6: 0, 2: 0, 3: 0}
            fsem = {0: "AC", 5: "AC", 1: "DV", 6: "DV", 2: "XX", 3: "XX"}
            for e in range(E + 1):
                if e == 0:
                    gsrc, usrc, dsrc = cfg["sg"], cfg["su"], cfg["sd"]
                else:
                    gsrc, usrc, dsrc = (cfg["gw"][e - 1], cfg["uw"][e - 1],
                                        cfg["dw"][e - 1])
                em.wait(g, "PE")
                for kc in range(NCH):
                    em.inc(g.dma_start(out=wg[:, kc * I:(kc + 1) * I],
                                       in_=gsrc[kc * 128:(kc + 1) * 128, :]), "LW", 16)
                    em.inc(g.dma_start(out=wu[:, kc * I:(kc + 1) * I],
                                       in_=usrc[kc * 128:(kc + 1) * 128, :]), "LW", 16)
                for ic in range(ICH):
                    em.inc(g.dma_start(out=wd[:, ic * D:(ic + 1) * D],
                                       in_=dsrc[ic * 128:(ic + 1) * 128, :]), "LW", 16)
                for ti in range(NT):
                    tc0 = ti * TT
                    em.wait(sy, "PE")
                    em.inc(sy.dma_start(out=xnb[:, :].rearrange("p (c l) -> p c l", c=NCH),
                                        in_=xn_dram.ap().rearrange("c p l -> p c l")[:, :, tc0:tc0 + TT]),
                           "LD", 16)
                    if e > 0:
                        em.wait(t, "DV")
                        em.inc(t.matmul(bank(4), c_sel[:, (e - 1) * 128:e * 128],
                                        wem[0:8, tc0:tc0 + TT], start=True, stop=True),
                               "PE")
                        em.wait(v, "PE")
                        em.inc(v.tensor_copy(wb[:, :], bank(4)), "DV")
                        em.wait(v, "LD")
                        em.wait(v, "PE")
                        for c in range(NCH):
                            em.inc(v.tensor_mul(xw[:, c * TT:(c + 1) * TT],
                                                xnb[:, c * TT:(c + 1) * TT],
                                                wb[:, :]), "DV")
                        usrc_sb = xw
                    else:
                        usrc_sb = xnb
                    em.wait(t, "LW")
                    em.wait(t, "LD")
                    em.wait(t, "DV")
                    for ic in range(ICH):
                        gb = 0 if ic % 2 == 0 else 5
                        ub = 1 if ic % 2 == 0 else 6
                        em.wait(t, fsem[gb], freed[gb])
                        for kc in range(NCH):
                            t.matmul(bank(gb), wg[:, kc * I + ic * 128:kc * I + (ic + 1) * 128],
                                     xnb[:, kc * TT:(kc + 1) * TT],
                                     start=(kc == 0), stop=(kc == NCH - 1))
                        em.wait(t, fsem[ub], freed[ub])
                        for kc in range(NCH):
                            ins = t.matmul(bank(ub), wu[:, kc * I + ic * 128:kc * I + (ic + 1) * 128],
                                           usrc_sb[:, kc * TT:(kc + 1) * TT],
                                           start=(kc == 0), stop=(kc == NCH - 1))
                        em.inc(ins, "PE")
                        em.wait(sc, "PE")
                        sl = ic % 2
                        tsl = tsil[:, sl * TT:(sl + 1) * TT]
                        em.inc(sc.activation(tsl, bank(gb), AF.Silu), "AC")
                        freed[gb] = em.val("AC")
                        em.wait(v, "AC")
                        em.inc(v.tensor_mul(hid[:, ic * TT:(ic + 1) * TT], tsl,
                                            bank(ub)), "DV")
                        freed[ub] = em.val("DV")
                    em.wait(t, "DV")
                    for dc in range(NCH):
                        pb = 2 + (dc % 2)
                        em.wait(t, fsem[pb], freed[pb])
                        for ic in range(ICH):
                            ins = t.matmul(bank(pb),
                                           wd[:, ic * D + dc * 128:ic * D + (dc + 1) * 128],
                                           hid[:, ic * TT:(ic + 1) * TT],
                                           start=(ic == 0), stop=(ic == ICH - 1))
                        em.inc(ins, "PE")
                        dst = ybig[:, dc * TT:(dc + 1) * TT]
                        em.wait(v, "PE")
                        em.wait(v, "YA", ya_freed[0])
                        em.inc(v.tensor_copy(dst, bank(pb)), "DV")
                        fsem[pb] = "DV"
                        freed[pb] = em.val("DV")
                    em.wait(g, "DV")
                    em.wait(g, "AC")
                    em.inc(g.dma_start(out=yacc.ap().rearrange("c p l -> p c l")[:, :, tc0:tc0 + TT],
                                       in_=ybig[:, :].rearrange("p (c l) -> p c l", c=NCH),
                                       accum_op=(OP.bypass if e == 0 else OP.add)),
                           "YA", 16)
                    ya_freed[0] = em.val("YA")

            em.barrier()
            for ti in range(NT):
                tc0 = ti * TT
                em.wait(sy, "YA")
                em.wait(sy, "DV")
                em.inc(sy.dma_start(out=xt[:, :].rearrange("p (c l) -> p c l", c=NCH),
                                    in_=x_in.ap().rearrange("c p l -> p c l")[:, :, tc0:tc0 + TT]),
                       "LD", 16)
                em.inc(sy.dma_start(out=ybig[:, :].rearrange("p (c l) -> p c l", c=NCH),
                                    in_=yacc.ap().rearrange("c p l -> p c l")[:, :, tc0:tc0 + TT]),
                       "LD", 16)
                em.wait(v, "LD")
                em.inc(v.scalar_tensor_tensor(ybig[:, :], ybig[:, :], 0.5, xt[:, :],
                                              OP.mult, OP.add), "DV")
                em.wait(sy, "DV")
                em.inc(sy.dma_start(out=x_out.ap().rearrange("c p l -> p c l")[:, :, tc0:tc0 + TT],
                                    in_=ybig[:, :].rearrange("p (c l) -> p c l", c=NCH)),
                       "ST", 16)
            em.wait(sy, "ST")



def emit_xchg_setup(em, cfg):
    nc, g = em.nc, em.g
    st = em.stack
    tag = cfg.get("tag", "")
    if not hasattr(em, "x_metas"):
        em.x_metas = st.enter_context(nc.sbuf_tensor("xmeta", [1, 32], I32))
        em.x_nonces = st.enter_context(nc.sbuf_tensor("xnonce", [1, 8], I32))
        em.x_fl = st.enter_context(nc.sbuf_tensor("xfl", [1, 2], I32))
        em.inc(em.sy.dma_start(out=em.x_metas[:, :], in_=cfg["meta"][:, :]), "XM", 16)
        em.inc(em.sy.dma_start(out=em.x_nonces[:, :], in_=cfg["nonce"][:, :]), "XM", 16)
        em.x_regs = {}
        for nm in ["r_off", "r_tgt", "r_flv", "r_done", "r_plc"]:
            em.x_regs[nm] = st.enter_context(g.register(nm))
        g.reg_mov(em.x_regs["r_plc"], 0)
        em.x_plsem = "PL"
    em.x_pax = st.enter_context(nc.sbuf_tensor(tag + "xpax", [128, 2 * 512], F32))
    em.x_xmk = st.enter_context(nc.sbuf_tensor(tag + "xxmk", [128, 1], F32))
    em.inc(em.sy.dma_start(out=em.x_xmk[:, :], in_=cfg["xmask"][:, :]), "XM", 16)
    em.x_cfg = cfg


def emit_pair_exchange(em, rnd, paysb, col0, ncols):
    g, v = em.g, em.v
    cfg, regs = em.x_cfg, em.x_regs
    rnd = rnd + cfg.get("rbase", 0)
    PAYC = cfg["PAYC"]
    shm_pay, shm_fl, nonce = cfg["shm_pay"], cfg["shm_fl"], cfg["nonce"]
    em.wait(g, "XM")
    em.wait(g, "DV")
    g.reg_load(regs["r_off"], em.x_metas[0:1, 8 + rnd:9 + rnd])
    em.inc(g.dma_start(
        bass.AP(shm_pay, regs["r_off"], [[PAYC, 128], [1, ncols]]),
        paysb[:, col0:col0 + ncols]), "XP", 16)
    em.wait(g, "XP")
    g.reg_load(regs["r_off"], em.x_metas[0:1, 24 + rnd:25 + rnd])
    em.inc(g.dma_start(
        bass.AP(shm_fl, regs["r_off"], [[1, 1], [1, 1]]),
        nonce[0:1, rnd:rnd + 1]), "XP", 16)
    g.reg_load(regs["r_tgt"], em.x_nonces[0:1, rnd:rnd + 1])
    g.reg_mov(regs["r_done"], 1)
    pl = em.sem(em.x_plsem)
    with g.While(regs["r_done"]):
        g.dma_start(em.x_fl[0:1, 0:2], shm_fl[0:2, rnd:rnd + 1],
                    allow_slow_non_contiguous=True).then_inc(pl[0], 16)
        g.reg_add(regs["r_plc"], regs["r_plc"], 16)
        g.wait_ge(pl[0], regs["r_plc"])
        g.reg_load(regs["r_flv"], em.x_fl[0:1, 0:1])
        g.reg_sub(regs["r_flv"], regs["r_flv"], regs["r_tgt"])
        g.reg_load(regs["r_done"], em.x_fl[0:1, 1:2])
        g.reg_sub(regs["r_done"], regs["r_done"], regs["r_tgt"])
        g.reg_alu(regs["r_done"], regs["r_done"], regs["r_flv"],
                  op=OP.bitwise_or)
    pax = em.x_pax
    em.inc(g.dma_start(
        pax[:, 0:2 * ncols].rearrange("p (s c) -> p s c", s=2),
        shm_pay.ap().rearrange("s p c -> p s c")[:, :, col0:col0 + ncols]),
        "XD", 16)
    em.wait(v, "XD")
    dst = em.x_pasb[:, col0:col0 + ncols]
    em.vstep(v.tensor_sub(dst, pax[:, 0:ncols], pax[:, ncols:2 * ncols]))
    em.vstep(v.scalar_tensor_tensor(dst, dst, em.x_xmk[:, 0:1],
                                    pax[:, ncols:2 * ncols], OP.mult, OP.add))


def emit_ln_to_dram(em, cfg, xt, sq, xn32, xnb, srow, bank, c_ones, c_misc):
    v, sc, t, sy = em.v, em.sc, em.t, em.sy
    x_in, xn_dram = cfg["x_in"], cfg["xn_dram"]
    for ti in range(NT):
        tc0 = ti * TT
        em.wait(sy, "PE")
        for c in range(NCH):
            em.inc(sy.dma_start(out=xt[:, c * TT:(c + 1) * TT],
                                in_=x_in[c, :, tc0:tc0 + TT]), "LD", 16)
        em.wait(t, "LD")
        em.wait(sc, "LD")
        for c in range(NCH):
            em.wait(sc, "PE")
            em.inc(sc.activation(sq[:, (c % 2) * TT:(c % 2) * TT + TT],
                                 xt[:, c * TT:(c + 1) * TT], AF.Square), "AC")
            em.wait(t, "AC")
            t.matmul(bank(0, 0, 1), c_ones[:, 0:1], xt[:, c * TT:(c + 1) * TT],
                     start=(c == 0), stop=(c == NCH - 1))
            em.inc(t.matmul(bank(1, 0, 1), c_ones[:, 0:1],
                            sq[:, (c % 2) * TT:(c % 2) * TT + TT],
                            start=(c == 0), stop=(c == NCH - 1)), "PE")
        r = [srow[0:1, k * TT:(k + 1) * TT] for k in range(4)]
        em.wait(v, "PE")
        em.vstep(v.tensor_scalar_mul(r[0], bank(0, 0, 1), 1.0 / D))
        em.vstep(v.tensor_scalar_mul(r[1], bank(1, 0, 1), 1.0 / D))
        em.vstep(v.tensor_mul(r[2], r[0], r[0]))
        em.vstep(v.tensor_sub(r[1], r[1], r[2]))
        em.wait(sc, "DV")
        em.inc(sc.activation(r[2], r[1], AF.Sqrt, bias=c_misc[0:1, 0:1]), "AC")
        em.wait(v, "AC")
        em.vstep(v.reciprocal(r[1], r[2]))
        em.vstep(v.tensor_mul(r[3], r[0], r[1]))
        em.wait(t, "DV")
        em.inc(t.matmul(bank(2), c_ones[0:1, :], r[1], start=True, stop=True), "PE")
        em.inc(t.matmul(bank(3), c_ones[0:1, :], r[3], start=True, stop=True), "PE")
        em.wait(v, "PE")
        for c in range(NCH):
            sl = (c % 2)
            x32 = xn32[:, sl * TT:(sl + 1) * TT]
            em.wait(v, "PE")
            v.tensor_mul(x32, xt[:, c * TT:(c + 1) * TT], bank(2))
            em.inc(v.tensor_sub(x32, x32, bank(3)), "DV")
            em.inc(v.tensor_copy(xnb[:, c * TT:(c + 1) * TT], x32), "DV")
        em.wait(sy, "DV")
        em.inc(sy.dma_start(out=xn_dram.ap().rearrange("c p l -> p c l")[:, :, tc0:tc0 + TT],
                            in_=xnb[:, :].rearrange("p (c l) -> p c l", c=NCH)),
               "ST", 16)


def emit_attn0(em, cfg):
    nc, g, v, sc, t, sy = em.nc, em.g, em.v, em.sc, em.t, em.sy
    c_ones, c_id, c_sel, c_misc = em.load_consts(cfg)
    x_in, x_out, xn_dram = cfg["x_in"], cfg["x_out"], cfg["xn_dram"]
    qf_dram = cfg["qf_dram"]
    tag = cfg["tag"]
    sb = lambda name, shape, dt=F32: em.stack.enter_context(
        nc.sbuf_tensor(tag + name, shape, dt))
    EPS = 1e-6
    xt = sb("xt", [128, NCH * TT])
    sq = sb("sq", [128, 2 * TT])
    xn32 = sb("xn32", [128, 2 * TT])
    xnb = sb("xnb", [128, NCH * TT], BF16)
    srow = sb("srow", [1, 4 * TT])
    qt = sb("qt", [128, 2 * NCH * TT], BF16)
    kf = sb("kf", [128, NCH * TLOC], BF16)
    T1 = sb("T1", [16, 3 * TLOC], BF16)
    csx = sb("csx", [16, 2 * TT])
    tiny = sb("tiny", [16, 16])
    wdot = sb("wdot", [128, 2 * NCH * 16], BF16)
    gsum = sb("gsum", [128, 4 * 8])
    part = sb("part", [128, 2 * NCH * 8])
    wsb = sb("wsb", [128, NCH * D], BF16)
    ktok = sb("ktok", [128, D], BF16)
    vsc = sb("vsc", [128, D], BF16)
    sctk = sb("sctk", [128, 2 * 16])
    kvb = sb("kvb", [128, 512], BF16)
    brow16 = sb("brow16", [1, 4 * D], BF16)
    bpp = sb("bpp", [128, 4 * NCH])
    selp = sb("selp", [16, NCH * 128], BF16)
    onesb = sb("onesb", [128, 128], BF16)
    idb = sb("idb", [128, 128], BF16)
    pays = sb("pays", [128, 560])
    pasb = sb("pasb", [128, 560])
    em.x_pasb = pasb
    PS = em.stack.enter_context(nc.psum_tensor(tag + "PS", [128, 7 * 512], F32))
    PSb = em.stack.enter_context(nc.psum_tensor(tag + "PSb", [128, 8 * 128], BF16))

    def bank(b, p0=0, pn=128, c0=0, cn=512):
        return PS[p0:p0 + pn, b * 512 + c0:b * 512 + c0 + cn]

    emit_xchg_setup(em, cfg)
    em.inc(g.dma_start(out=brow16[:, :], in_=cfg["b_rows"][:, :]), "LW", 16)
    em.inc(sy.dma_start(out=bpp[:, :], in_=cfg["b_pp"][:, :]), "LD", 16)
    em.inc(sy.dma_start(out=selp[:, :], in_=cfg["c_selp"][:, :]), "LD", 16)
    em.inc(sy.dma_start(out=onesb[:, :], in_=cfg["c_onesb"][:, :]), "LD", 16)
    em.inc(sy.dma_start(out=idb[:, :], in_=cfg["c_idb"][:, :]), "LD", 16)

    def load_w(w):
        em.wait(g, "PE")
        for kc in range(NCH):
            em.inc(g.dma_start(out=wsb[:, kc * D:(kc + 1) * D],
                               in_=w[kc * 128:(kc + 1) * 128, :]), "LW", 16)

    def wslice(kc, dc):
        return wsb[:, kc * D + dc * 128:kc * D + (dc + 1) * 128]

    def load_qt(ti, sl):
        em.wait(sy, "PE")
        em.wait(sy, "DV")
        em.inc(sy.dma_start(
            out=qt[:, sl * NCH * TT:(sl + 1) * NCH * TT].rearrange("p (c l) -> p c l", c=NCH),
            in_=qf_dram.ap().rearrange("c p l -> p c l")[:, :, ti * TT:ti * TT + TT]),
            "LD", 16)
        em.wait(t, "LD")
        em.wait(v, "LD")

    def qsl(sl, c, a=0, b=TT):
        return qt[:, sl * NCH * TT + c * TT + a:sl * NCH * TT + c * TT + b]

    emit_ln_to_dram(em, cfg, xt, sq, xn32, xnb, srow, bank, c_ones, c_misc)
    em.barrier()

    pfree = {0: 0, 1: 0, 2: 0, 3: 0}
    for which, wname in ((0, "wq"), (1, "wk")):
        load_w(cfg[wname])
        for ti in range(NT):
            tc0 = ti * TT
            em.wait(sy, "PE")
            em.inc(sy.dma_start(out=xnb[:, :].rearrange("p (c l) -> p c l", c=NCH),
                                in_=xn_dram.ap().rearrange("c p l -> p c l")[:, :, tc0:tc0 + TT]),
                   "LD", 16)
            em.wait(t, "LD")
            em.wait(t, "LW")
            for dc in range(NCH):
                pb = dc % 4
                em.wait(t, "AC", pfree[pb])
                for kc in range(NCH):
                    em.inc(t.matmul(bank(pb), wslice(kc, dc),
                                    xnb[:, kc * TT:(kc + 1) * TT],
                                    start=(kc == 0), stop=(kc == NCH - 1)), "PE")
                em.wait(sc, "PE")
                if which == 0:
                    dst = qt[:, dc * TT:(dc + 1) * TT]
                else:
                    dst = kf[:, dc * TLOC + tc0:dc * TLOC + tc0 + TT]
                em.inc(sc.activation(dst, bank(pb), AF.Sigmoid,
                                     bias=bpp[:, which * NCH + dc:which * NCH + dc + 1]),
                       "AC")
                pfree[pb] = em.val("AC")
            if which == 0:
                em.wait(sy, "AC")
                em.inc(sy.dma_start(
                    out=qf_dram.ap().rearrange("c p l -> p c l")[:, :, tc0:tc0 + TT],
                    in_=qt[:, 0:NCH * TT].rearrange("p (c l) -> p c l", c=NCH)),
                    "ST", 16)
                em.wait(sc, "ST")
    em.wait(sy, "ST")
    em.barrier()

    for ti in range(NT):
        load_qt(ti, 0)
        for c in range(NCH):
            em.vstep(v.reduce_sum(part[:, (NCH + c) * 8 + ti:(NCH + c) * 8 + ti + 1],
                                  qsl(0, c), axis=mybir.AxisListType.X))
    for c in range(NCH):
        em.vstep(v.reduce_sum(pays[:, c:c + 1],
                              kf[:, c * TLOC:(c + 1) * TLOC],
                              axis=mybir.AxisListType.X))
        em.vstep(v.reduce_sum(pays[:, 8 + c:9 + c],
                              part[:, (NCH + c) * 8:(NCH + c) * 8 + 8],
                              axis=mybir.AxisListType.X))
    emit_pair_exchange(em, 0, pays, 0, 16)
    em.wait(v, "XD")
    em.vstep(v.tensor_add(gsum[:, 0:16], pays[:, 0:16], pasb[:, 0:16]))

    def build_wdot(srccol0, half):
        em.wait(v, "PE")
        em.vstep(v.memset(wdot[:, half * 128:(half + 1) * 128], 0.0))
        for c in range(NCH):
            base = half * 128 + c * 16
            em.vstep(v.tensor_scalar(wdot[0:64, base + 2 * c:base + 2 * c + 1],
                                     gsum[0:64, srccol0 + c:srccol0 + c + 1],
                                     EPS, None, OP.add))
            em.vstep(v.tensor_scalar(wdot[64:128, base + 2 * c + 1:base + 2 * c + 2],
                                     gsum[64:128, srccol0 + c:srccol0 + c + 1],
                                     EPS, None, OP.add))
        em.wait(t, "DV")
        for c in range(NCH):
            em.inc(t.matmul(bank(0, 0, 16, 0, 1),
                            wdot[:, half * 128 + c * 16:half * 128 + (c + 1) * 16],
                            onesb[:, 0:1], start=(c == 0), stop=(c == NCH - 1)), "PE")
        em.wait(v, "PE")
        em.vstep(v.tensor_scalar_mul(tiny[0:16, half:half + 1], bank(0, 0, 16, 0, 1), EPS))

    build_wdot(0, 0)
    build_wdot(8, 1)

    def trow(grp, a, b):
        return T1[0:16, grp * TLOC + a:grp * TLOC + b]

    def dot16(half, use_q, cecol, grp, act):
        for ti in range(NT):
            tc0 = ti * TT
            if use_q:
                load_qt(ti, ti % 2)
            em.wait(t, "DV")
            em.wait(t, "AC")
            for c in range(NCH):
                srcsl = qsl(ti % 2, c) if use_q else kf[:, c * TLOC + tc0:c * TLOC + tc0 + TT]
                em.inc(t.matmul(bank(1, 0, 16),
                                wdot[:, half * 128 + c * 16:half * 128 + (c + 1) * 16],
                                srcsl, start=(c == 0), stop=(c == NCH - 1)), "PE")
            em.wait(v, "PE")
            if act == "recip":
                em.vstep(v.tensor_scalar(csx[0:16, 0:TT], bank(1, 0, 16),
                                         tiny[0:16, cecol:cecol + 1], None, OP.add))
                em.vstep(v.reciprocal(trow(grp, tc0, tc0 + TT), csx[0:16, 0:TT]))
            elif act == "clipexp":
                em.vstep(v.tensor_scalar(csx[0:16, 0:TT], bank(1, 0, 16),
                                         tiny[0:16, cecol:cecol + 1], None, OP.add))
                em.vstep(v.tensor_scalar_min(csx[0:16, TT:2 * TT], csx[0:16, 0:TT], 1.0))
                em.vstep(v.tensor_scalar_max(csx[0:16, 0:TT], csx[0:16, TT:2 * TT], -1.0))
                em.wait(sc, "DV")
                em.inc(sc.activation(trow(grp, tc0, tc0 + TT), csx[0:16, 0:TT],
                                     AF.Exp), "AC")
                em.wait(v, "AC")
                em.vstep(v.reduce_sum(tiny[0:16, 8 + ti:9 + ti],
                                      trow(grp, tc0, tc0 + TT),
                                      axis=mybir.AxisListType.X))
            elif act == "sigmoid":
                em.wait(sc, "PE")
                em.inc(sc.activation(trow(grp, tc0, tc0 + TT), bank(1, 0, 16),
                                     AF.Sigmoid, bias=tiny[0:16, cecol:cecol + 1]), "AC")

    dot16(0, True, 0, 0, "recip")
    dot16(1, False, 1, 1, "recip")

    def wsum(use_q, grp, outcol):
        for ti in range(NT):
            tc0 = ti * TT
            if use_q:
                load_qt(ti, ti % 2)
            for c in range(NCH):
                em.wait(t, "DV")
                em.inc(t.matmul(bank(2), selp[:, c * 128:(c + 1) * 128],
                                trow(grp, tc0, tc0 + TT),
                                start=True, stop=True), "PE")
                em.wait(v, "PE")
                srcsl = qsl(ti % 2, c) if use_q else kf[:, c * TLOC + tc0:c * TLOC + tc0 + TT]
                em.vstep(v.tensor_mul(sq[:, (c % 2) * TT:(c % 2) * TT + TT], srcsl,
                                      bank(2)))
                em.vstep(v.reduce_sum(part[:, c * 8 + ti:c * 8 + ti + 1],
                                      sq[:, (c % 2) * TT:(c % 2) * TT + TT],
                                      axis=mybir.AxisListType.X))
        for c in range(NCH):
            em.vstep(v.reduce_sum(pays[:, outcol + c:outcol + c + 1],
                                  part[:, c * 8:c * 8 + 8],
                                  axis=mybir.AxisListType.X))

    wsum(False, 1, 16)
    wsum(True, 0, 24)
    emit_pair_exchange(em, 1, pays, 16, 16)
    em.wait(v, "XD")
    em.vstep(v.tensor_add(gsum[:, 16:32], pays[:, 16:32], pasb[:, 16:32]))

    build_wdot(16, 0)
    build_wdot(24, 1)
    dot16(0, True, 0, 1, "sigmoid")
    dot16(1, False, 1, 2, "clipexp")
    em.vstep(v.reduce_sum(pays[0:16, 32:33], tiny[0:16, 8:16],
                          axis=mybir.AxisListType.X))
    em.wait(v, "AC")
    em.vstep(v.tensor_mul(trow(0, 0, TLOC), trow(0, 0, TLOC), trow(1, 0, TLOC)))
    emit_pair_exchange(em, 2, pays, 32, 16)
    em.wait(v, "XD")
    em.vstep(v.tensor_add(tiny[0:16, 4:5], pays[0:16, 32:33], pasb[0:16, 32:33]))
    em.vstep(v.reciprocal(tiny[0:16, 5:6], tiny[0:16, 4:5]))
    em.vstep(v.tensor_scalar_mul(tiny[0:16, 6:7], tiny[0:16, 5:6], float(S)))
    em.vstep(v.tensor_scalar(trow(2, 0, TLOC), trow(2, 0, TLOC), tiny[0:16, 6:7],
                             None, OP.mult))
    em.barrier()

    load_w(cfg["wv"])
    for ti in range(NT):
        tc0 = ti * TT
        em.wait(sy, "PE")
        em.inc(sy.dma_start(out=xnb[:, :].rearrange("p (c l) -> p c l", c=NCH),
                            in_=xn_dram.ap().rearrange("c p l -> p c l")[:, :, tc0:tc0 + TT]),
               "LD", 16)
        em.wait(t, "LD")
        em.wait(t, "LW")
        for t4 in range(4):
            tc = ti * 4 + t4
            for nh in range(2):
                em.wait(t, "AC")
                em.wait(t, "DV")
                for kc in range(NCH):
                    t.matmul(bank(2 + nh), xnb[:, kc * TT + t4 * 128:kc * TT + (t4 + 1) * 128],
                             wsb[:, kc * D + nh * 512:kc * D + (nh + 1) * 512],
                             start=(kc == 0), stop=False)
                em.inc(t.matmul(bank(2 + nh), onesb[0:1, :],
                                brow16[0:1, 2 * D + nh * 512:2 * D + (nh + 1) * 512],
                                start=False, stop=True), "PE")
            em.wait(t, "DV")
            for dc in range(NCH):
                em.inc(t.transpose(PSb[:, dc * 128:(dc + 1) * 128],
                                   kf[:, dc * TLOC + tc * 128:dc * TLOC + (tc + 1) * 128],
                                   idb[:, :]), "PE")
            em.wait(v, "PE")
            em.vstep(v.tensor_copy(ktok[:, :], PSb[:, :]))
            em.wait(t, "DV")
            em.inc(t.transpose(PSb[:, 0:16],
                               trow(2, tc * 128, (tc + 1) * 128),
                               idb[0:16, 0:16]), "PE")
            em.wait(v, "PE")
            em.vstep(v.tensor_copy(sctk[:, (t4 % 2) * 16:(t4 % 2) * 16 + 16],
                                   PSb[:, 0:16]))
            for h in range(16):
                em.vstep(v.tensor_scalar(vsc[:, h * 64:(h + 1) * 64],
                                         bank(2 + h // 8, 0, 128, (h % 8) * 64, 64),
                                         sctk[:, (t4 % 2) * 16 + h:(t4 % 2) * 16 + h + 1],
                                         None, OP.mult))
            em.wait(t, "DV")
            for h in range(16):
                em.inc(t.matmul(bank(5, 64 * (h % 2), 64, (h // 2) * 64, 64),
                                ktok[:, h * 64:(h + 1) * 64], vsc[:, h * 64:(h + 1) * 64],
                                start=(tc == 0), stop=(tc == TCH - 1),
                                skip_group_check=True), "PE")
    em.wait(v, "PE")
    em.vstep(v.tensor_copy(pays[:, 48:560], bank(5)))
    emit_pair_exchange(em, 3, pays, 48, 512)
    em.wait(v, "XD")
    em.vstep(v.tensor_add(pays[:, 48:560], pays[:, 48:560], pasb[:, 48:560]))
    em.vstep(v.tensor_copy(kvb[:, :], pays[:, 48:560]))
    em.barrier()

    for ti in range(NT):
        tc0 = ti * TT
        load_qt(ti, ti % 2)
        for dc in range(NCH):
            em.wait(t, "DV")
            em.inc(t.matmul(bank(2), selp[:, dc * 128:(dc + 1) * 128],
                            trow(0, tc0, tc0 + TT), start=True, stop=True), "PE")
            em.wait(v, "PE")
            q_c = qsl(ti % 2, dc)
            em.vstep(v.tensor_mul(q_c, q_c, bank(2)))
            em.wait(t, "DV")
            em.inc(t.matmul(bank(6, 0, 64), kvb[0:64, dc * 64:(dc + 1) * 64],
                            q_c[0:64, :], start=True, stop=True), "PE")
            em.inc(t.matmul(bank(6, 64, 64), kvb[64:128, dc * 64:(dc + 1) * 64],
                            q_c[64:128, :], start=True, stop=True), "PE")
            em.wait(v, "PE")
            em.vstep(v.tensor_copy(kf[:, dc * TLOC + tc0:dc * TLOC + tc0 + TT],
                                   bank(6)))
    em.barrier()

    wfree = {0: 0, 1: 0}
    load_w(cfg["wo"])
    for ti in range(NT):
        tc0 = ti * TT
        em.wait(sy, "DV")
        em.wait(sy, "PE")
        for c in range(NCH):
            em.inc(sy.dma_start(out=xt[:, c * TT:(c + 1) * TT],
                                in_=x_in[c, :, tc0:tc0 + TT]), "LD", 16)
        em.wait(t, "LD")
        em.wait(t, "LW")
        for dc in range(NCH):
            pb = dc % 2
            em.wait(t, "DV", wfree[pb])
            for kc in range(NCH):
                em.inc(t.matmul(bank(pb), wslice(kc, dc),
                                kf[:, kc * TLOC + tc0:kc * TLOC + tc0 + TT],
                                start=(kc == 0), stop=(kc == NCH - 1)), "PE")
            em.wait(v, "PE")
            em.wait(v, "ST")
            sl = dc % 2
            v.tensor_scalar(sq[:, sl * TT:(sl + 1) * TT], bank(pb),
                            bpp[:, 3 * NCH + dc:3 * NCH + dc + 1], None,
                            OP.add)
            em.inc(v.tensor_add(sq[:, sl * TT:(sl + 1) * TT],
                                sq[:, sl * TT:(sl + 1) * TT],
                                xt[:, dc * TT:(dc + 1) * TT]), "DV")
            wfree[pb] = em.val("DV")
            em.wait(sy, "DV")
            em.inc(sy.dma_start(out=x_out[dc, :, tc0:tc0 + TT],
                                in_=sq[:, sl * TT:(sl + 1) * TT]), "ST", 16)
    em.wait(sy, "ST")


def build_attn0(dbg=False):
    nc = bass.Bass("TRN2", target_bir_lowering=False)
    dr = {}
    dr["x_in"] = nc.dram_tensor("x_in", [NCH, 128, TLOC], F32, kind="ExternalInput")
    dr["x_out"] = nc.dram_tensor("x_out", [NCH, 128, TLOC], F32, kind="ExternalOutput")
    for nm in ["wq", "wk", "wv", "wo"]:
        dr[nm] = nc.dram_tensor(nm, [D, D], F32, kind="ExternalInput")
    dr["b_rows"] = nc.dram_tensor("b_rows", [1, 4 * D], F32, kind="ExternalInput")
    dr["b_pp"] = nc.dram_tensor("b_pp", [128, 4 * NCH], F32, kind="ExternalInput")
    dr["c_ones"] = nc.dram_tensor("c_ones", [128, 128], F32, kind="ExternalInput")
    dr["c_id"] = nc.dram_tensor("c_id", [128, 128], F32, kind="ExternalInput")
    dr["c_sel"] = nc.dram_tensor("c_sel", [8, 8 * 128], BF16, kind="ExternalInput")
    dr["c_misc"] = nc.dram_tensor("c_misc", [128, 8], F32, kind="ExternalInput")
    dr["c_selp"] = nc.dram_tensor("c_selp", [16, NCH * 128], BF16, kind="ExternalInput")
    dr["c_onesb"] = nc.dram_tensor("c_onesb", [128, 128], BF16, kind="ExternalInput")
    dr["c_idb"] = nc.dram_tensor("c_idb", [128, 128], BF16, kind="ExternalInput")
    dr["xmask"] = nc.dram_tensor("xmask", [128, 1], F32, kind="ExternalInput")
    dr["meta"] = nc.dram_tensor("meta", [1, 32], I32, kind="ExternalInput")
    dr["nonce"] = nc.dram_tensor("nonce", [1, 8], I32, kind="ExternalInput")
    dr["xn_dram"] = nc.dram_tensor("xn_scr", [NCH, 128, TLOC], BF16,
                                   kind="ExternalOutput" if dbg else "Internal")
    dr["qf_dram"] = nc.dram_tensor("qf_scr", [NCH, 128, TLOC], BF16,
                                   kind="ExternalOutput" if dbg else "Internal")
    dr["PAYC"] = 560
    dr["shm_pay"] = nc.dram_tensor("shm_pay", [2, 128, 560], F32, addr_space="Shared")
    dr["shm_fl"] = nc.dram_tensor("shm_fl", [2, 8], I32, addr_space="Shared")
    with ExitStack() as stack:
        em = Em(nc, stack)
        cfg = dict(dr)
        cfg["tag"] = "a"
        with nc.allow_low_precision(reason="bf16 operand staging is intentional"):
            emit_attn0(em, cfg)
    return nc


def xchg_host_inputs(rng, col0s=None):
    PAYC = 560
    metas, nonces = [], []
    nonce_vals = rng.integers(1 << 20, 1 << 30, size=8).astype(np.int32)
    if col0s is None:
        col0s = [0, 16, 32, 48]
    col0s = list(col0s) + [0] * (8 - len(col0s))
    for c in range(NCORES):
        m = c % 2
        p = 1 - m
        meta = np.zeros((1, 32), np.int32)
        for r in range(8):
            meta[0, r] = p * 128 * PAYC + col0s[r]
            meta[0, 8 + r] = m * 128 * PAYC + col0s[r]
            meta[0, 16 + r] = p * 8 + r
            meta[0, 24 + r] = m * 8 + r
        metas.append(meta)
        nonces.append(nonce_vals[None, :])
    return metas, nonces


def attn0_in_map(inputs, xcore, meta, nonce, core_idx=0):
    c_ones, c_id, c_sel, c_misc = consts_f()
    selp = np.zeros((16, NCH * 128), BF)
    for c in range(NCH):
        selp[2 * c, c * 128:c * 128 + 64] = 1
        selp[2 * c + 1, c * 128 + 64:(c + 1) * 128] = 1
    onesb = np.ones((128, 128), BF)
    idb = np.eye(128, dtype=BF)
    bpp = np.zeros((128, 4 * NCH), np.float32)
    rows = np.zeros((1, 4 * D), np.float32)
    for i, nm in enumerate(["bq", "bk", "bv", "bo"]):
        b = inputs[f"L0_attn_{nm}"]
        rows[0, i * D:(i + 1) * D] = b
        bpp[:, i * NCH:(i + 1) * NCH] = b.reshape(NCH, 128).T
    return dict(
        x_in=xcore,
        wq=inputs["L0_attn_Wq"], wk=inputs["L0_attn_Wk"],
        wv=inputs["L0_attn_Wv"], wo=inputs["L0_attn_Wo"],
        b_rows=rows, b_pp=bpp,
        c_ones=c_ones, c_id=c_id, c_sel=c_sel, c_misc=c_misc,
        c_selp=selp, c_onesb=onesb, c_idb=idb, meta=meta, nonce=nonce,
        xmask=np.full((128, 1), float(core_idx % 2), np.float32),
    )


def run_attn0(inputs, xcores):
    nc = _get_prog("attn0", build_attn0)
    rng = np.random.default_rng()
    metas, nonces = xchg_host_inputs(rng)
    maps = [attn0_in_map(inputs, xcores[c], metas[c], nonces[c], c)
            for c in range(NCORES)]
    res = run_bass_kernel_spmd(nc, maps, core_ids=list(range(NCORES)))
    return [res.results[c]["x_out"] for c in range(NCORES)]



def emit_attn1(em, cfg):
    nc, g, v, sc, t, sy = em.nc, em.g, em.v, em.sc, em.t, em.sy
    c_ones, c_id, c_sel, c_misc = em.load_consts(cfg)
    x_in, x_out, xn_dram = cfg["x_in"], cfg["x_out"], cfg["xn_dram"]
    qcd, kcd, vcd, ogd = cfg["qc_dram"], cfg["kc_dram"], cfg["vc_dram"], cfg["og_dram"]
    Ed, Pd, Hd = cfg["E_dram"], cfg["P_dram"], cfg["H_dram"]
    tag = cfg["tag"]
    dbg = cfg.get("dbg")
    sb = lambda name, shape, dt=F32: em.stack.enter_context(
        nc.sbuf_tensor(tag + name, shape, dt))
    CJ = TLOC // 128
    SEG = TT + 3
    GN_EPS = 1e-5
    xt = sb("xt", [128, NCH * SEG])
    pre = xt
    sq = sb("sq", [128, 2 * TT])
    xn32 = sb("xn32", [128, 2 * TT])
    xnb = sb("xnb", [128, NCH * TT], BF16)
    srow = sb("srow", [1, 4 * TT])
    silb = sb("silb", [128, NCH * TT], BF16)
    tailp = sb("tailp", [128, NCH * 3])
    tailx = sb("tailx", [128, NCH * 3], BF16)
    wsb = sb("wsb", [128, NCH * D], BF16)
    wab = sb("wab", [128, 2 * NCH * 16], BF16)
    cvpp = sb("cvpp", [128, 3 * NCH * 4])
    bpp = sb("bpp", [128, 5 * NCH])
    abb = sb("abb", [16, 2])
    gnpp = sb("gnpp", [128, 2 * NCH])
    ghmc = sb("ghmc", [16, 128])
    abrow = sb("abrow", [16, 2 * TT])
    lab = sb("lab", [48, TT])
    strx = sb("strx", [128, 48])
    gtr = sb("gtr", [128, 16 * CJ])
    btr = sb("btr", [128, 16 * CJ])
    dtr = sb("dtr", [128, 16 * CJ])
    actr = sb("actr", [128, 16 * CJ])
    batr = sb("batr", [128, 16 * CJ])
    csx = sb("csx", [16, 2 * TT])
    outs = sb("outs", [128, NCH * TLOC], BF16)
    etst = sb("etst", [128, NCH * 128], BF16)
    ptst = sb("ptst", [128, NCH * 64], BF16)
    hst = sb("hst", [128, NCH * 64], BF16)
    qct = sb("qct", [128, NCH * 128], BF16)
    kct = sb("kct", [128, NCH * 128], BF16)
    vct = sb("vct", [128, NCH * 128], BF16)
    Dsb = sb("Dsb", [128, 128])
    esb = sb("esb", [128, 2 * 128])
    lams = sb("lams", [128, 128], BF16)
    lamu = sb("lamu", [128, 128], BF16)
    absb = sb("absb", [128, 128], BF16)
    Msb = sb("Msb", [128, 128], BF16)
    Mtsb = sb("Mtsb", [128, 128], BF16)
    wtsb = sb("wtsb", [128, 128], BF16)
    Tb = sb("Tb", [128, 2 * 128], BF16)
    Ttb = sb("Ttb", [128, 2 * 128], BF16)
    Pb = sb("Pb", [128, 2 * 128], BF16)
    Ptb = sb("Ptb", [128, 2 * 128], BF16)
    vbsb = sb("vbsb", [128, 64], BF16)
    kpsb = sb("kpsb", [128, 64], BF16)
    kbasb = sb("kbasb", [128, 64], BF16)
    uvsb = sb("uvsb", [128, 64], BF16)
    zsb = sb("zsb", [128, 64], BF16)
    qasb = sb("qasb", [128, 128], BF16)
    st32 = sb("st32", [128, 512])
    stbb = sb("stbb", [128, 512], BF16)
    eld = sb("eld", [128, NCH * 128], BF16)
    pld = sb("pld", [128, NCH * 64], BF16)
    hld = sb("hld", [128, NCH * 64], BF16)
    gnb = sb("gnb", [128, 4 * NCH])
    v16 = sb("v16", [16, 16])
    pays = sb("pays", [128, 560])
    pasb = sb("pasb", [128, 560])
    em.x_pasb = pasb
    selp = sb("selp", [16, NCH * 128], BF16)
    selpf = sb("selpf", [16, NCH * 128])
    s16f = sb("s16f", [16, 16 * 128])
    cut = sb("cut", [128, 128])
    clt = sb("clt", [128, 128])
    cmsl = sb("cmsl", [128, 128])
    cmsu = sb("cmsu", [128, 128])
    chm = sb("chm", [128, NCH * 16])
    cpat = sb("cpat", [16, 8])
    idb = sb("idb", [128, 128], BF16)
    xmk = sb("xmk", [128, 1])
    PS = em.stack.enter_context(nc.psum_tensor(tag + "PS", [128, 7 * 512], F32))
    PSb = em.stack.enter_context(nc.psum_tensor(tag + "PSb", [128, 512], BF16))

    def bank(b, p0=0, pn=128, c0=0, cn=512):
        return PS[p0:p0 + pn, b * 512 + c0:b * 512 + c0 + cn]

    emit_xchg_setup(em, cfg)
    for nm, dst in [("c_selp", selp), ("c_selpf", selpf), ("c_s16f", s16f),
                    ("c_ut", cut), ("c_lt", clt), ("c_msl", cmsl),
                    ("c_msu", cmsu), ("c_hm", chm), ("c_pat", cpat),
                    ("c_idb", idb), ("xmask", xmk), ("b_pp", bpp),
                    ("ab_bias", abb), ("gn_pp", gnpp), ("cv_pp", cvpp)]:
        em.inc(sy.dma_start(out=dst[0:dst.shape[0], :], in_=cfg[nm][:, :]), "LD", 16)
    for kc in range(NCH):
        em.inc(g.dma_start(out=wab[:, kc * 16:(kc + 1) * 16],
                           in_=cfg["wa"][kc * 128:(kc + 1) * 128, :]), "LW", 16)
        em.inc(g.dma_start(out=wab[:, 128 + kc * 16:128 + (kc + 1) * 16],
                           in_=cfg["wb_"][kc * 128:(kc + 1) * 128, :]), "LW", 16)

    def load_w(w):
        em.wait(g, "PE")
        for kc in range(NCH):
            em.inc(g.dma_start(out=wsb[:, kc * D:(kc + 1) * D],
                               in_=w[kc * 128:(kc + 1) * 128, :]), "LW", 16)

    def wslice(kc, dc):
        return wsb[:, kc * D + dc * 128:kc * D + (dc + 1) * 128]

    def load_xn(ti):
        em.wait(sy, "PE")
        em.inc(sy.dma_start(out=xnb[:, :].rearrange("p (c l) -> p c l", c=NCH),
                            in_=xn_dram.ap().rearrange("c p l -> p c l")[:, :, ti * TT:(ti + 1) * TT]),
               "LD", 16)
        em.wait(t, "LD")

    emit_ln_to_dram(em, cfg, xt, sq, xn32, xnb, srow, bank, c_ones, c_misc)
    em.wait(v, "ST")
    for c in range(NCH):
        em.vstep(v.tensor_copy(pays[:, c * 3:(c + 1) * 3],
                               xnb[:, c * TT + TT - 3:c * TT + TT]))
    emit_pair_exchange(em, 0, pays, 0, 24)
    em.wait(v, "XD")
    em.vstep(v.tensor_scalar(pays[:, 24:48], pasb[:, 0:24], xmk[:, 0:1], None,
                             OP.mult))
    em.vstep(v.tensor_copy(tailx[:, :], pays[:, 24:48]))
    em.barrier()

    for ti in range(NT):
        tc0 = ti * TT
        load_xn(ti)
        em.wait(t, "LW")
        em.wait(t, "AC")
        em.wait(t, "DV")
        for half in range(2):
            for kc in range(NCH):
                em.inc(t.matmul(bank(3 + half, 0, 16),
                                wab[:, half * 128 + kc * 16:half * 128 + (kc + 1) * 16],
                                xnb[:, kc * TT:(kc + 1) * TT],
                                start=(kc == 0), stop=(kc == NCH - 1)), "PE")
        em.wait(sc, "PE")
        em.inc(sc.activation(abrow[0:16, 0:TT], bank(3, 0, 16), AF.Sigmoid,
                             bias=abb[0:16, 0:1]), "AC")
        em.inc(sc.activation(abrow[0:16, TT:2 * TT], bank(4, 0, 16), AF.Sigmoid,
                             bias=abb[0:16, 1:2]), "AC")
        if dbg:
            em.wait(sy, "AC")
            em.inc(sy.dma_start(out=cfg["dbg_ab"][0, :, tc0:tc0 + TT],
                                in_=abrow[0:16, 0:TT]), "ST", 16)
            em.inc(sy.dma_start(out=cfg["dbg_ab"][1, :, tc0:tc0 + TT],
                                in_=abrow[0:16, TT:2 * TT]), "ST", 16)
        em.wait(sc, "AC")
        em.inc(sc.activation(lab[0:16, :], abrow[0:16, 0:TT], AF.Ln), "AC")
        em.wait(v, "AC")
        em.vstep(v.tensor_copy(lab[32:48, :], abrow[0:16, TT:2 * TT]))
        for j4 in range(4):
            j = ti * 4 + j4
            em.wait(t, "DV")
            em.wait(t, "AC")
            em.inc(t.transpose(bank(5, 0, 128, 0, 48), lab[0:48, j4 * 128:(j4 + 1) * 128],
                               c_id[0:48, 0:48]), "PE")
            em.wait(v, "PE")
            em.vstep(v.tensor_copy(strx[:, :], bank(5, 0, 128, 0, 48)))
            em.vstep(v.tensor_copy(btr[:, j * 16:(j + 1) * 16], strx[:, 32:48]))
            em.wait(t, "DV")
            em.inc(t.matmul(bank(5, 0, 128, 64, 16), cut[:, :], strx[:, 0:16],
                            start=True, stop=True), "PE")
            em.inc(t.matmul(bank(5, 0, 128, 96, 16), clt[:, :], strx[:, 0:16],
                            start=True, stop=True), "PE")
            em.wait(v, "PE")
            em.vstep(v.tensor_copy(gtr[:, j * 16:(j + 1) * 16],
                                   bank(5, 0, 128, 64, 16)))
            em.wait(sc, "DV")
            em.inc(sc.activation(dtr[:, j * 16:(j + 1) * 16],
                                 bank(5, 0, 128, 96, 16), AF.Exp), "AC")
            em.vstep(v.tensor_add(strx[:, 16:32], gtr[:, j * 16:(j + 1) * 16],
                                  bank(5, 0, 128, 96, 16)))
            em.wait(sc, "DV")
            em.wait(sc, "PE")
            em.inc(sc.activation(actr[:, j * 16:(j + 1) * 16], strx[:, 16:32],
                                 AF.Exp), "AC")
            em.inc(sc.activation(strx[:, 0:16], gtr[:, j * 16:(j + 1) * 16],
                                 AF.Exp), "AC")
            em.wait(v, "AC")
            em.vstep(v.tensor_mul(batr[:, j * 16:(j + 1) * 16],
                                  btr[:, j * 16:(j + 1) * 16], strx[:, 0:16]))
    em.barrier()

    for which, wname, dramt in ((0, "wq", qcd), (1, "wk", kcd), (2, "wv", vcd)):
        load_w(cfg[wname])
        em.wait(t, "LW")
        em.wait(t, "DV")
        for dc in range(NCH):
            for kc in range(NCH):
                em.inc(t.matmul(bank(4, 0, 128, dc * 3, 3), wslice(kc, dc),
                                tailx[:, kc * 3:(kc + 1) * 3],
                                start=(kc == 0), stop=(kc == NCH - 1)), "PE")
        em.wait(v, "PE")
        for dc in range(NCH):
            em.vstep(v.tensor_scalar(tailp[:, dc * 3:(dc + 1) * 3],
                                     bank(4, 0, 128, dc * 3, 3),
                                     bpp[:, which * NCH + dc:which * NCH + dc + 1],
                                     None, OP.add))
        em.vstep(v.tensor_scalar(tailp[:, :], tailp[:, :], xmk[:, 0:1], None,
                                 OP.mult))
        for c in range(NCH):
            em.vstep(v.tensor_copy(pre[:, c * SEG:c * SEG + 3],
                                   tailp[:, c * 3:(c + 1) * 3]))
        for ti in range(NT):
            tc0 = ti * TT
            load_xn(ti)
            em.wait(sc, "ST")
            em.wait(v, "ST")
            for dc in range(NCH):
                pb = dc % 2
                em.wait(t, "DV")
                for kc in range(NCH):
                    em.inc(t.matmul(bank(pb), wslice(kc, dc),
                                    xnb[:, kc * TT:(kc + 1) * TT],
                                    start=(kc == 0), stop=(kc == NCH - 1)), "PE")
                em.wait(v, "PE")
                em.vstep(v.tensor_scalar(pre[:, dc * SEG + 3:dc * SEG + SEG],
                                         bank(pb),
                                         bpp[:, which * NCH + dc:which * NCH + dc + 1],
                                         None, OP.add))
            for c in range(NCH):
                for j in range(4):
                    cwcol = cvpp[:, which * 32 + c * 4 + j:which * 32 + c * 4 + j + 1]
                    if j == 0:
                        em.vstep(v.tensor_scalar(sq[:, 0:TT],
                                                 pre[:, c * SEG + j:c * SEG + j + TT],
                                                 cwcol, None, OP.mult))
                    else:
                        em.vstep(v.scalar_tensor_tensor(sq[:, 0:TT],
                                                        pre[:, c * SEG + j:c * SEG + j + TT],
                                                        cwcol, sq[:, 0:TT],
                                                        OP.mult, OP.add))
                em.wait(sc, "DV")
                em.inc(sc.activation(silb[:, c * TT:(c + 1) * TT], sq[:, 0:TT],
                                     AF.Silu), "AC")
                em.wait(v, "AC")
            for c in range(NCH):
                em.vstep(v.tensor_copy(pre[:, c * SEG:c * SEG + 3],
                                       pre[:, c * SEG + SEG - 3:c * SEG + SEG]))
            if which < 2:
                em.wait(t, "AC")
                em.wait(t, "DV")
                for c in range(NCH):
                    em.wait(sc, "PE")
                    em.inc(sc.activation(sq[:, (c % 2) * TT:(c % 2) * TT + TT],
                                         silb[:, c * TT:(c + 1) * TT], AF.Square),
                           "AC")
                    em.wait(t, "AC")
                    em.inc(t.matmul(bank(2, 0, 16), chm[:, c * 16:(c + 1) * 16],
                                    sq[:, (c % 2) * TT:(c % 2) * TT + TT],
                                    start=(c == 0), stop=(c == NCH - 1)), "PE")
                em.wait(sc, "PE")
                em.inc(sc.activation(csx[0:16, 0:TT], bank(2, 0, 16), AF.Sqrt), "AC")
                em.wait(v, "AC")
                em.vstep(v.tensor_scalar(csx[0:16, TT:2 * TT], csx[0:16, 0:TT],
                                         1e-6, None, OP.add))
                em.vstep(v.reciprocal(csx[0:16, 0:TT], csx[0:16, TT:2 * TT]))
                em.vstep(v.tensor_copy(abrow[0:16, 0:TT], csx[0:16, 0:TT]))
                for c in range(NCH):
                    em.wait(t, "DV")
                    em.inc(t.matmul(bank(2), selpf[:, c * 128:(c + 1) * 128],
                                    abrow[0:16, 0:TT], start=True, stop=True), "PE")
                    em.wait(v, "PE")
                    em.vstep(v.tensor_mul(silb[:, c * TT:(c + 1) * TT],
                                          silb[:, c * TT:(c + 1) * TT], bank(2)))
            em.wait(sy, "DV")
            em.inc(sy.dma_start(
                out=dramt.ap().rearrange("c p l -> p c l")[:, :, tc0:tc0 + TT],
                in_=silb[:, :].rearrange("p (c l) -> p c l", c=NCH)), "ST", 16)
    em.barrier()

    em.vstep(v.memset(st32[:, :], 0.0))
    em.vstep(v.memset(stbb[:, :], 0.0))
    for j in range(CJ):
        em.wait(sy, "PE")
        em.wait(sy, "DV")
        for dramt, dst in ((qcd, qct), (kcd, kct), (vcd, vct)):
            em.inc(sy.dma_start(
                out=dst[:, :].rearrange("p (c l) -> p c l", c=NCH),
                in_=dramt.ap().rearrange("c p l -> p c l")[:, :, j * 128:(j + 1) * 128]),
                "LD", 16)
        em.wait(t, "LD")
        em.wait(v, "LD")
        em.wait(t, "DV")
        em.inc(t.transpose(bank(0, 0, 16, 384, 128), gtr[:, j * 16:(j + 1) * 16],
                           c_id[:, :]), "PE")
        em.wait(v, "PE")
        em.vstep(v.tensor_copy(ghmc[0:16, :], bank(0, 0, 16, 384, 128)))
        for h in range(16):
            p0 = 64 * (h % 2)
            cc = (h // 2) * 128
            QT = qct[p0:p0 + 64, cc:cc + 128]
            KT = kct[p0:p0 + 64, cc:cc + 128]
            VT = vct[p0:p0 + 64, cc:cc + 128]
            gcol = gtr[:, j * 16 + h:j * 16 + h + 1]
            bcol = btr[:, j * 16 + h:j * 16 + h + 1]
            dcol = dtr[:, j * 16 + h:j * 16 + h + 1]
            bacol = batr[:, j * 16 + h:j * 16 + h + 1]
            accol = actr[0:64, j * 16 + h:j * 16 + h + 1]
            em.wait(t, "DV")
            em.wait(t, "AC")
            em.inc(t.matmul(bank(0, 0, 128, 0, 128), KT, KT, start=True, stop=True), "PE")
            em.inc(t.matmul(bank(0, 0, 128, 128, 128), KT, QT, start=True, stop=True), "PE")
            em.inc(t.matmul(bank(0, 0, 128, 256, 128), s16f[:, h * 128:(h + 1) * 128],
                            ghmc[0:16, :], start=True, stop=True), "PE")
            em.wait(v, "PE")
            em.vstep(v.tensor_scalar(Dsb[:, :], bank(0, 0, 128, 256, 128), gcol,
                                     None, OP.subtract))
            em.vstep(v.tensor_sub(esb[:, 0:128], cmsl[:, :], Dsb[:, :]))
            em.vstep(v.tensor_add(esb[:, 128:256], cmsu[:, :], Dsb[:, :]))
            em.wait(sc, "DV")
            em.inc(sc.activation(lams[:, :], esb[:, 0:128], AF.Exp), "AC")
            em.inc(sc.activation(lamu[:, :], esb[:, 128:256], AF.Exp), "AC")
            em.inc(sc.activation(absb[:, :], bank(0, 0, 128, 256, 128), AF.Exp), "AC")
            em.wait(v, "AC")
            em.vstep(v.scalar_tensor_tensor(Msb[:, :], bank(0, 0, 128, 0, 128),
                                            bcol, lams[:, :], OP.mult, OP.mult))
            em.vstep(v.tensor_mul(wtsb[:, :], lamu[:, :], bank(0, 0, 128, 128, 128)))
            em.vstep(v.tensor_mul(qasb[:, 0:128], QT, absb[p0:p0 + 64, :])
                     if False else
                     v.tensor_mul(qasb[p0:p0 + 64, 0:128], QT, absb[p0:p0 + 64, :]))
            em.wait(t, "DV")
            em.inc(t.transpose(PSb[:, 0:128], Msb[:, :], idb[:, :]), "PE")
            em.inc(t.transpose(PSb[:, 128:192], VT, idb[p0:p0 + 64, p0:p0 + 64]), "PE")
            em.inc(t.transpose(PSb[:, 192:256], KT, idb[p0:p0 + 64, p0:p0 + 64]), "PE")
            em.wait(v, "PE")
            em.vstep(v.tensor_copy(Mtsb[:, :], PSb[:, 0:128]))
            em.vstep(v.tensor_scalar(vbsb[:, :], PSb[:, 128:192], bcol, None, OP.mult))
            em.vstep(v.tensor_scalar(kpsb[:, :], PSb[:, 192:256], dcol, None, OP.mult))
            em.vstep(v.tensor_scalar(kbasb[:, :], PSb[:, 192:256], bacol, None, OP.mult))
            em.vstep(v.tensor_sub(Tb[:, 0:128], idb[:, :], Msb[:, :]))
            em.vstep(v.tensor_sub(Ttb[:, 0:128], idb[:, :], Mtsb[:, :]))
            em.wait(t, "DV")
            em.inc(t.matmul(bank(1, 0, 128, 0, 128), Mtsb[:, :], Msb[:, :],
                            start=True, stop=True), "PE")
            em.inc(t.matmul(bank(1, 0, 128, 128, 128), Msb[:, :], Mtsb[:, :],
                            start=True, stop=True), "PE")
            em.wait(v, "PE")
            em.vstep(v.tensor_copy(Pb[:, 0:128], bank(1, 0, 128, 0, 128)))
            em.vstep(v.tensor_copy(Ptb[:, 0:128], bank(1, 0, 128, 128, 128)))
            for lev in range(1, 7):
                s0 = ((lev - 1) % 2) * 128
                s1 = (lev % 2) * 128
                em.wait(t, "DV")
                em.inc(t.matmul(bank(1, 0, 128, 0, 128), Ptb[:, s0:s0 + 128],
                                Tb[:, s0:s0 + 128], start=True, stop=True), "PE")
                em.inc(t.matmul(bank(1, 0, 128, 128, 128), Tb[:, s0:s0 + 128],
                                Ptb[:, s0:s0 + 128], start=True, stop=True), "PE")
                if lev < 6:
                    em.inc(t.matmul(bank(2, 0, 128, 0, 128), Ptb[:, s0:s0 + 128],
                                    Pb[:, s0:s0 + 128], start=True, stop=True), "PE")
                    em.inc(t.matmul(bank(2, 0, 128, 128, 128), Pb[:, s0:s0 + 128],
                                    Ptb[:, s0:s0 + 128], start=True, stop=True), "PE")
                em.wait(v, "PE")
                em.vstep(v.tensor_add(Tb[:, s1:s1 + 128], bank(1, 0, 128, 0, 128),
                                      Tb[:, s0:s0 + 128]))
                em.vstep(v.tensor_add(Ttb[:, s1:s1 + 128], bank(1, 0, 128, 128, 128),
                                      Ttb[:, s0:s0 + 128]))
                if lev < 6:
                    em.vstep(v.tensor_copy(Pb[:, s1:s1 + 128], bank(2, 0, 128, 0, 128)))
                    em.vstep(v.tensor_copy(Ptb[:, s1:s1 + 128], bank(2, 0, 128, 128, 128)))
            tts = Ttb[:, (6 % 2) * 128:(6 % 2) * 128 + 128]
            em.wait(t, "DV")
            em.inc(t.matmul(bank(1, 0, 128, 256, 64), tts, vbsb[:, :],
                            start=True, stop=True), "PE")
            em.inc(t.matmul(bank(1, 0, 128, 320, 64), tts, kbasb[:, :],
                            start=True, stop=True), "PE")
            em.wait(v, "PE")
            em.vstep(v.tensor_copy(uvsb[:, :], bank(1, 0, 128, 256, 64)))
            em.vstep(v.tensor_copy(zsb[:, :], bank(1, 0, 128, 320, 64)))
            em.wait(t, "DV")
            em.inc(t.matmul(bank(2, 0, 64, 0, 128), uvsb[:, :], wtsb[:, :],
                            start=True, stop=True), "PE")
            em.inc(t.matmul(bank(2, 0, 64, 128, 128), zsb[:, :], wtsb[:, :],
                            start=True, stop=True), "PE")
            em.inc(t.matmul(bank(2, 0, 64, 256, 64), zsb[:, :], kpsb[:, :],
                            start=True, stop=True), "PE")
            em.inc(t.matmul(bank(2, 0, 64, 320, 64), kpsb[:, :], uvsb[:, :],
                            start=True, stop=True), "PE")
            em.wait(v, "PE")
            ocol = (h // 2) * TLOC + j * 128
            em.vstep(v.tensor_copy(outs[p0:p0 + 64, ocol:ocol + 128],
                                   bank(2, 0, 64, 0, 128)))
            em.vstep(v.tensor_sub(etst[p0:p0 + 64, cc:cc + 128],
                                  qasb[p0:p0 + 64, 0:128],
                                  bank(2, 0, 64, 128, 128)))
            em.vstep(v.scalar_tensor_tensor(ptst[p0:p0 + 64, (h // 2) * 64:(h // 2) * 64 + 64],
                                            idb[0:64, 0:64], accol,
                                            bank(2, 0, 64, 256, 64),
                                            OP.mult, OP.subtract))
            em.vstep(v.tensor_copy(hst[p0:p0 + 64, (h // 2) * 64:(h // 2) * 64 + 64],
                                   bank(2, 0, 64, 320, 64)))
        em.wait(sy, "DV")
        em.inc(sy.dma_start(out=Ed[j, :, :], in_=etst[:, :]), "ST", 16)
        em.inc(sy.dma_start(out=Pd[j, :, :], in_=ptst[:, :]), "ST", 16)
        em.inc(sy.dma_start(out=Hd[j, :, :], in_=hst[:, :]), "ST", 16)
    em.barrier()

    def state_pass(correction):
        for j in range(CJ):
            em.wait(sy, "PE")
            em.inc(sy.dma_start(out=eld[:, :], in_=Ed[j, :, :]), "LD", 16)
            em.inc(sy.dma_start(out=pld[:, :], in_=Pd[j, :, :]), "LD", 16)
            if not correction:
                em.inc(sy.dma_start(out=hld[:, :], in_=Hd[j, :, :]), "LD", 16)
            em.wait(t, "LD")
            for h in range(16):
                p0 = 64 * (h % 2)
                c64 = (h // 2) * 64
                cc = (h // 2) * 128
                stb_h = stbb[p0:p0 + 64, c64:c64 + 64]
                et_h = eld[p0:p0 + 64, cc:cc + 128]
                pt_h = pld[p0:p0 + 64, c64:c64 + 64]
                em.wait(t, "DV")
                em.inc(t.matmul(bank(3, p0, 64, 0, 128), stb_h, et_h,
                                start=True, stop=True), "PE")
                em.inc(t.matmul(bank(3, p0, 64, 128, 64), pt_h, stb_h,
                                start=True, stop=True), "PE")
                em.wait(v, "PE")
                ocol = (h // 2) * TLOC + j * 128
                em.vstep(v.tensor_add(outs[p0:p0 + 64, ocol:ocol + 128],
                                      outs[p0:p0 + 64, ocol:ocol + 128],
                                      bank(3, p0, 64, 0, 128)))
                if not correction:
                    em.vstep(v.tensor_add(st32[p0:p0 + 64, c64:c64 + 64],
                                          bank(3, p0, 64, 128, 64),
                                          hld[p0:p0 + 64, c64:c64 + 64]))
                else:
                    em.vstep(v.tensor_copy(st32[p0:p0 + 64, c64:c64 + 64],
                                           bank(3, p0, 64, 128, 64)))
                em.vstep(v.tensor_copy(stbb[p0:p0 + 64, c64:c64 + 64],
                                       st32[p0:p0 + 64, c64:c64 + 64]))

    state_pass(False)
    em.vstep(v.tensor_copy(pays[:, 48:560], st32[:, 0:512]))
    emit_pair_exchange(em, 1, pays, 48, 512)
    em.wait(v, "XD")
    em.vstep(v.tensor_scalar(st32[:, 0:512], pasb[:, 48:560], xmk[:, 0:1],
                             None, OP.mult))
    em.vstep(v.tensor_copy(stbb[:, 0:512], st32[:, 0:512]))
    state_pass(True)
    em.barrier()
    if dbg:
        em.inc(sy.dma_start(out=cfg["dbg_outs"].ap().rearrange("c p l -> p c l"),
                            in_=outs[:, :].rearrange("p (c l) -> p c l", c=NCH)),
               "ST", 16)
        em.wait(sy, "ST")

    for c in range(NCH):
        em.vstep(v.reduce_sum(gnb[:, c:c + 1], outs[:, c * TLOC:(c + 1) * TLOC],
                              axis=mybir.AxisListType.X))
    for ti in range(NT):
        for c in range(NCH):
            em.wait(sc, "DV")
            em.inc(sc.activation(sq[:, (c % 2) * TT:(c % 2) * TT + TT],
                                 outs[:, c * TLOC + ti * TT:c * TLOC + ti * TT + TT],
                                 AF.Square), "AC")
            em.wait(v, "AC")
            em.vstep(v.reduce_sum(pays[:, 40 + c:40 + c + 1] if ti == 0 else xn32[:, 0:1],
                                  sq[:, (c % 2) * TT:(c % 2) * TT + TT],
                                  axis=mybir.AxisListType.X))
            if ti > 0:
                em.vstep(v.tensor_add(pays[:, 40 + c:40 + c + 1],
                                      pays[:, 40 + c:40 + c + 1], xn32[:, 0:1]))
    em.wait(t, "DV")
    for c in range(NCH):
        em.inc(t.matmul(bank(4, 0, 16, 0, 1), chm[:, c * 16:(c + 1) * 16],
                        gnb[:, c:c + 1], start=(c == 0), stop=(c == NCH - 1)), "PE")
        em.inc(t.matmul(bank(4, 0, 16, 1, 1), chm[:, c * 16:(c + 1) * 16],
                        pays[:, 40 + c:40 + c + 1], start=(c == 0), stop=(c == NCH - 1)),
               "PE")
    em.wait(v, "PE")
    em.vstep(v.tensor_copy(pays[0:16, 24:25], bank(4, 0, 16, 0, 1)))
    em.vstep(v.tensor_copy(pays[0:16, 25:26], bank(4, 0, 16, 1, 1)))
    emit_pair_exchange(em, 2, pays, 24, 16)
    em.wait(v, "XD")
    NT_TOT = float(S * 64)
    em.vstep(v.tensor_add(v16[0:16, 0:2], pays[0:16, 24:26], pasb[0:16, 24:26]))
    em.vstep(v.tensor_scalar_mul(v16[0:16, 2:3], v16[0:16, 0:1], 1.0 / NT_TOT))
    em.vstep(v.tensor_scalar_mul(v16[0:16, 3:4], v16[0:16, 1:2], 1.0 / NT_TOT))
    em.vstep(v.tensor_mul(v16[0:16, 4:5], v16[0:16, 2:3], v16[0:16, 2:3]))
    em.vstep(v.tensor_sub(v16[0:16, 5:6], v16[0:16, 3:4], v16[0:16, 4:5]))
    em.wait(sc, "DV")
    em.inc(sc.activation(v16[0:16, 6:7], v16[0:16, 5:6], AF.Sqrt,
                         bias=c_misc[0:16, 0:1]), "AC")
    em.wait(v, "AC")
    em.vstep(v.reciprocal(v16[0:16, 7:8], v16[0:16, 6:7]))
    for qi, col in ((2, 0), (7, 1)):
        em.vstep(v.tensor_scalar(v16[0:16, 8:16], cpat[0:16, :], v16[0:16, qi:qi + 1],
                                 None, OP.mult))
        em.wait(t, "DV")
        for c in range(NCH):
            em.inc(t.matmul(bank(4, 0, 128, 0, 8), selpf[:, c * 128:(c + 1) * 128],
                            v16[0:16, 8:16], start=(c == 0), stop=(c == NCH - 1)), "PE")
        em.wait(v, "PE")
        em.vstep(v.tensor_copy(gnb[:, 8 + col * 8:16 + col * 8], bank(4, 0, 128, 0, 8)))
    em.vstep(v.tensor_mul(gnb[:, 24:32], gnpp[:, 0:8], gnb[:, 16:24]))
    em.vstep(v.tensor_mul(gnb[:, 0:8], gnb[:, 8:16], gnb[:, 24:32]))
    em.vstep(v.tensor_sub(gnb[:, 8:16], gnpp[:, 8:16], gnb[:, 0:8]))
    em.barrier()

    load_w(cfg["wg"])
    for ti in range(NT):
        tc0 = ti * TT
        load_xn(ti)
        em.wait(t, "LW")
        for dc in range(NCH):
            pb = dc % 2
            em.wait(t, "AC")
            for kc in range(NCH):
                em.inc(t.matmul(bank(pb), wslice(kc, dc),
                                xnb[:, kc * TT:(kc + 1) * TT],
                                start=(kc == 0), stop=(kc == NCH - 1)), "PE")
            em.wait(sc, "PE")
            em.inc(sc.activation(silb[:, dc * TT:(dc + 1) * TT], bank(pb), AF.Silu,
                                 bias=bpp[:, 3 * NCH + dc:3 * NCH + dc + 1]), "AC")
        em.wait(sy, "AC")
        em.inc(sy.dma_start(
            out=ogd.ap().rearrange("c p l -> p c l")[:, :, tc0:tc0 + TT],
            in_=silb[:, :].rearrange("p (c l) -> p c l", c=NCH)), "ST", 16)
        em.wait(sc, "ST")
    em.barrier()

    load_w(cfg["wo"])
    for ti in range(NT):
        tc0 = ti * TT
        em.wait(sy, "PE")
        em.wait(sy, "DV")
        em.inc(sy.dma_start(out=silb[:, :].rearrange("p (c l) -> p c l", c=NCH),
                            in_=ogd.ap().rearrange("c p l -> p c l")[:, :, tc0:tc0 + TT]),
               "LD", 16)
        for c in range(NCH):
            em.inc(sy.dma_start(out=xt[:, c * TT:(c + 1) * TT],
                                in_=x_in[c, :, tc0:tc0 + TT]), "LD", 16)
        em.wait(v, "LD")
        for c in range(NCH):
            em.vstep(v.tensor_scalar(xnb[:, c * TT:(c + 1) * TT],
                                     outs[:, c * TLOC + tc0:c * TLOC + tc0 + TT],
                                     gnb[:, 24 + c:24 + c + 1],
                                     gnb[:, 8 + c:8 + c + 1],
                                     OP.mult, OP.add))
            em.vstep(v.tensor_mul(xnb[:, c * TT:(c + 1) * TT],
                                  xnb[:, c * TT:(c + 1) * TT],
                                  silb[:, c * TT:(c + 1) * TT]))
        em.wait(t, "DV")
        em.wait(t, "LW")
        for dc in range(NCH):
            pb = dc % 2
            em.wait(t, "ST")
            em.wait(t, "DV")
            for kc in range(NCH):
                em.inc(t.matmul(bank(pb), wslice(kc, dc),
                                xnb[:, kc * TT:(kc + 1) * TT],
                                start=(kc == 0), stop=(kc == NCH - 1)), "PE")
            em.wait(v, "PE")
            sl2 = dc % 2
            em.vstep(v.tensor_scalar(sq[:, sl2 * TT:(sl2 + 1) * TT], bank(pb),
                                     bpp[:, 4 * NCH + dc:4 * NCH + dc + 1], None,
                                     OP.add))
            em.vstep(v.tensor_add(sq[:, sl2 * TT:(sl2 + 1) * TT],
                                  sq[:, sl2 * TT:(sl2 + 1) * TT],
                                  xt[:, dc * TT:(dc + 1) * TT]))
            em.wait(sy, "DV")
            em.inc(sy.dma_start(out=x_out[dc, :, tc0:tc0 + TT],
                                in_=sq[:, sl2 * TT:(sl2 + 1) * TT]), "ST", 16)
    em.wait(sy, "ST")


def build_attn1(dbg=False):
    nc = bass.Bass("TRN2", target_bir_lowering=False)
    CJ = TLOC // 128
    dr = {}
    dr["x_in"] = nc.dram_tensor("x_in", [NCH, 128, TLOC], F32, kind="ExternalInput")
    dr["x_out"] = nc.dram_tensor("x_out", [NCH, 128, TLOC], F32, kind="ExternalOutput")
    for nm in ["wq", "wk", "wv", "wg", "wo"]:
        dr[nm] = nc.dram_tensor(nm, [D, D], F32, kind="ExternalInput")
    dr["wa"] = nc.dram_tensor("wa", [D, 16], F32, kind="ExternalInput")
    dr["wb_"] = nc.dram_tensor("wb_", [D, 16], F32, kind="ExternalInput")
    for nm, shape in [("b_pp", [128, 5 * NCH]), ("ab_bias", [16, 2]),
                      ("gn_pp", [128, 2 * NCH]), ("cv_pp", [128, 3 * NCH * 4]),
                      ("c_ones", [128, 128]), ("c_id", [128, 128]),
                      ("c_misc", [128, 8]), ("c_selpf", [16, NCH * 128]),
                      ("c_s16f", [16, 16 * 128]), ("c_ut", [128, 128]),
                      ("c_lt", [128, 128]), ("c_msl", [128, 128]),
                      ("c_msu", [128, 128]), ("c_hm", [128, NCH * 16]),
                      ("c_pat", [16, 8]), ("xmask", [128, 1])]:
        dr[nm] = nc.dram_tensor(nm, shape, F32, kind="ExternalInput")
    dr["c_sel"] = nc.dram_tensor("c_sel", [8, 8 * 128], BF16, kind="ExternalInput")
    dr["c_selp"] = nc.dram_tensor("c_selp", [16, NCH * 128], BF16, kind="ExternalInput")
    dr["c_idb"] = nc.dram_tensor("c_idb", [128, 128], BF16, kind="ExternalInput")
    dr["meta"] = nc.dram_tensor("meta", [1, 32], I32, kind="ExternalInput")
    dr["nonce"] = nc.dram_tensor("nonce", [1, 8], I32, kind="ExternalInput")
    kind = "ExternalOutput" if dbg else "Internal"
    dr["xn_dram"] = nc.dram_tensor("xn_scr", [NCH, 128, TLOC], BF16, kind=kind)
    dr["qc_dram"] = nc.dram_tensor("qc_scr", [NCH, 128, TLOC], BF16, kind=kind)
    dr["kc_dram"] = nc.dram_tensor("kc_scr", [NCH, 128, TLOC], BF16, kind=kind)
    dr["vc_dram"] = nc.dram_tensor("vc_scr", [NCH, 128, TLOC], BF16, kind=kind)
    dr["og_dram"] = nc.dram_tensor("og_scr", [NCH, 128, TLOC], BF16)
    dr["E_dram"] = nc.dram_tensor("E_scr", [CJ, 128, NCH * 128], BF16)
    dr["P_dram"] = nc.dram_tensor("P_scr", [CJ, 128, NCH * 64], BF16)
    dr["H_dram"] = nc.dram_tensor("H_scr", [CJ, 128, NCH * 64], BF16)
    if dbg:
        dr["dbg_ab"] = nc.dram_tensor("dbg_ab", [2, 16, TLOC], F32, kind="ExternalOutput")
        dr["dbg_outs"] = nc.dram_tensor("dbg_outs", [NCH, 128, TLOC], BF16,
                                        kind="ExternalOutput")
    dr["PAYC"] = 560
    dr["shm_pay"] = nc.dram_tensor("shm_pay", [2, 128, 560], F32, addr_space="Shared")
    dr["shm_fl"] = nc.dram_tensor("shm_fl", [2, 8], I32, addr_space="Shared")
    with ExitStack() as stack:
        em = Em(nc, stack)
        cfg = dict(dr)
        cfg["tag"] = "b"
        cfg["dbg"] = dbg
        with nc.allow_low_precision(reason="bf16 operand staging is intentional"):
            emit_attn1(em, cfg)
    return nc


def attn1_in_map(inputs, xcore, meta, nonce, core_idx):
    c_ones, c_id, c_sel, c_misc = consts_f()
    selp = np.zeros((16, NCH * 128), BF)
    selpf = np.zeros((16, NCH * 128), np.float32)
    for c in range(NCH):
        selp[2 * c, c * 128:c * 128 + 64] = 1
        selp[2 * c + 1, c * 128 + 64:(c + 1) * 128] = 1
        selpf[2 * c, c * 128:c * 128 + 64] = 1
        selpf[2 * c + 1, c * 128 + 64:(c + 1) * 128] = 1
    s16f = np.zeros((16, 16 * 128), np.float32)
    for h in range(16):
        s16f[h, h * 128:(h + 1) * 128] = 1
    idx = np.arange(128)
    cut = (idx[:, None] <= idx[None, :]).astype(np.float32)
    clt = (idx[:, None] > idx[None, :]).astype(np.float32)
    cmsl = np.where(idx[None, :] < idx[:, None], 0.0, -1e30).astype(np.float32)
    cmsu = np.where(idx[:, None] <= idx[None, :], 0.0, -1e30).astype(np.float32)
    chm = np.zeros((128, NCH * 16), np.float32)
    for c in range(NCH):
        chm[0:64, c * 16 + 2 * c] = 1
        chm[64:128, c * 16 + 2 * c + 1] = 1
    cpat = np.zeros((16, 8), np.float32)
    for h in range(16):
        cpat[h, h // 2] = 1
    idb = np.eye(128, dtype=BF)
    onesb = np.ones((128, 128), BF)
    xmask = np.full((128, 1), float(core_idx % 2), np.float32)
    bpp = np.zeros((128, 5 * NCH), np.float32)
    for i, nm in enumerate(["bq", "bk", "bv", "bg", "bo"]):
        bpp[:, i * NCH:(i + 1) * NCH] = inputs[f"L1_attn_{nm}"].reshape(NCH, 128).T
    abb = np.stack([inputs["L1_attn_ba"], inputs["L1_attn_bb"]], axis=1).astype(np.float32)
    gnpp = np.zeros((128, 2 * NCH), np.float32)
    gnpp[:, 0:NCH] = inputs["L1_attn_gn_scale"].reshape(NCH, 128).T
    gnpp[:, NCH:2 * NCH] = inputs["L1_attn_gn_bias"].reshape(NCH, 128).T
    cvpp = np.zeros((128, 3 * NCH * 4), np.float32)
    for i, nm in enumerate(["q_conv", "k_conv", "v_conv"]):
        w = inputs[f"L1_attn_{nm}"]
        cvpp[:, i * 32:(i + 1) * 32] = w.reshape(NCH, 128, 4).transpose(1, 0, 2).reshape(128, 32)
    return dict(
        x_in=xcore,
        wq=inputs["L1_attn_Wq"], wk=inputs["L1_attn_Wk"], wv=inputs["L1_attn_Wv"],
        wg=inputs["L1_attn_Wg"], wo=inputs["L1_attn_Wo"],
        wa=inputs["L1_attn_Wa"], wb_=inputs["L1_attn_Wb"],
        b_pp=bpp, ab_bias=abb, gn_pp=gnpp, cv_pp=cvpp,
        c_ones=c_ones, c_id=c_id, c_sel=c_sel, c_misc=c_misc,
        c_selp=selp, c_selpf=selpf, c_s16f=s16f, c_ut=cut, c_lt=clt,
        c_msl=cmsl, c_msu=cmsu, c_hm=chm, c_pat=cpat, c_idb=idb,
        xmask=xmask, meta=meta, nonce=nonce,
    )


def run_attn1(inputs, xcores):
    nc = _get_prog("attn1", build_attn1)
    rng = np.random.default_rng()
    metas, nonces = xchg_host_inputs(rng, col0s=[0, 48, 24, 0])
    maps = [attn1_in_map(inputs, xcores[c], metas[c], nonces[c], c)
            for c in range(NCORES)]
    res = run_bass_kernel_spmd(nc, maps, core_ids=list(range(NCORES)))
    return [res.results[c]["x_out"] for c in range(NCORES)]


def build_ffn(layer, dbg=False):
    nc = bass.Bass("TRN2", target_bir_lowering=False)
    E = 8 if layer == 0 else 7
    I = 2048 if layer == 0 else 512
    NBIAS = 1 if layer == 0 else 2
    dr = {}
    dr["x_in"] = nc.dram_tensor("x_in", [NCH, 128, TLOC], F32, kind="ExternalInput")
    dr["x_out"] = nc.dram_tensor("x_out", [NCH, 128, TLOC], F32, kind="ExternalOutput")
    dr["gw"] = nc.dram_tensor("gw", [E, D, I], F32, kind="ExternalInput")
    dr["uw"] = nc.dram_tensor("uw", [E, D, I], F32, kind="ExternalInput")
    dr["dw"] = nc.dram_tensor("dw", [E, I, D], F32, kind="ExternalInput")
    dr["sg"] = nc.dram_tensor("sg", [D, I], F32, kind="ExternalInput")
    dr["su"] = nc.dram_tensor("su", [D, I], F32, kind="ExternalInput")
    dr["sd"] = nc.dram_tensor("sd", [I, D], F32, kind="ExternalInput")
    dr["rw"] = nc.dram_tensor("rw", [D, E], F32, kind="ExternalInput")
    dr["rbias"] = nc.dram_tensor("rbias", [NBIAS, E], F32, kind="ExternalInput")
    dr["c_ones"] = nc.dram_tensor("c_ones", [128, 128], F32, kind="ExternalInput")
    dr["c_id"] = nc.dram_tensor("c_id", [128, 128], F32, kind="ExternalInput")
    dr["c_sel"] = nc.dram_tensor("c_sel", [8, 8 * 128], BF16, kind="ExternalInput")
    dr["c_misc"] = nc.dram_tensor("c_misc", [128, 8], F32, kind="ExternalInput")
    kind = "ExternalOutput" if dbg else "Internal"
    dr["xn_dram"] = nc.dram_tensor("xn_scr", [NCH, 128, TLOC], BF16, kind=kind)
    dr["yacc"] = nc.dram_tensor("yacc_scr", [NCH, 128, TLOC], F32, kind=kind)
    if dbg:
        dr["dbg_probs"] = nc.dram_tensor("dbg_probs", [128, TCH * 8], F32, kind="ExternalOutput")
        dr["dbg_wtok"] = nc.dram_tensor("dbg_wtok", [128, TCH * 8], F32, kind="ExternalOutput")
    with ExitStack() as stack:
        em = Em(nc, stack)
        cfg = dict(dr)
        cfg["E"], cfg["I"], cfg["tag"] = E, I, "f"
        cfg["dbg"] = dbg
        cfg["use_sel"] = True
        with nc.allow_low_precision(reason="bf16 operand staging is intentional"):
            emit_ffn(em, cfg)
    return nc


def shard_x(x):
    out = []
    for c in range(NCORES):
        b, h = c // 2, c % 2
        xs = x[b, h * TLOC:(h + 1) * TLOC, :]
        xf = np.ascontiguousarray(xs.T)
        out.append(np.ascontiguousarray(xf.reshape(NCH, 128, TLOC)))
    return out


def unshard_x(parts):
    x = np.empty((B, S, D), np.float32)
    for c in range(NCORES):
        b, h = c // 2, c % 2
        xf = parts[c].reshape(D, TLOC)
        x[b, h * TLOC:(h + 1) * TLOC, :] = xf.T
    return x


def consts_f():
    c_ones = np.ones((128, 128), np.float32)
    c_id = np.eye(128, dtype=np.float32)
    c_sel = np.zeros((8, 8 * 128), BF)
    for e in range(8):
        c_sel[e, e * 128:(e + 1) * 128] = 1
    c_misc = np.zeros((128, 8), np.float32)
    c_misc[:, 0] = LN_EPS
    c_misc[:, 1] = 1.0
    c_misc[:, 2] = 1e-6
    return c_ones, c_id, c_sel, c_misc


def ffn_in_map(layer, inputs, xcore):
    c_ones, c_id, c_sel, c_misc = consts_f()
    if layer == 0:
        m = dict(
            gw=inputs["L0_ffn_gate_w"], uw=inputs["L0_ffn_up_w"],
            dw=inputs["L0_ffn_down_w"],
            sg=np.ascontiguousarray(inputs["L0_ffn_shared_gate"].T),
            su=np.ascontiguousarray(inputs["L0_ffn_shared_up"].T),
            sd=np.ascontiguousarray(inputs["L0_ffn_shared_down"].T),
            rw=np.ascontiguousarray(inputs["L0_ffn_router_w"].T),
            rbias=inputs["L0_ffn_routing_bias"][None, :],
        )
    else:
        m = dict(
            gw=inputs["L1_ffn_r_gate"], uw=inputs["L1_ffn_r_up"],
            dw=inputs["L1_ffn_r_down"],
            sg=inputs["L1_ffn_sh_gate"], su=inputs["L1_ffn_sh_up"],
            sd=inputs["L1_ffn_sh_down"],
            rw=inputs["L1_ffn_w_r"],
            rbias=np.stack([inputs["L1_ffn_b_r"], inputs["L1_ffn_routing_bias"]]),
        )
    m["c_ones"], m["c_id"], m["c_sel"], m["c_misc"] = c_ones, c_id, c_sel, c_misc
    m["x_in"] = xcore
    return m


_prog_cache = {}


def _get_prog(name, builder):
    if name not in _prog_cache:
        _prog_cache[name] = builder()
    return _prog_cache[name]


def run_ffn(layer, inputs, xcores):
    nc = _get_prog(f"ffn{layer}", lambda: build_ffn(layer))
    maps = [ffn_in_map(layer, inputs, xcores[c]) for c in range(NCORES)]
    res = run_bass_kernel_spmd(nc, maps, core_ids=list(range(NCORES)))
    return [res.results[c]["x_out"] for c in range(NCORES)]



def build_fused(dbg=False):
    nc = bass.Bass("TRN2", target_bir_lowering=False)
    CJ = TLOC // 128
    dr = {}
    dr["x0"] = nc.dram_tensor("x0", [NCH, 128, TLOC], F32, kind="ExternalInput")
    xs = [dr["x0"]]
    for i in range(1, 6):
        xs.append(nc.dram_tensor(f"xi{i}", [NCH, 128, TLOC], F32))
    xs.append(nc.dram_tensor("x_out", [NCH, 128, TLOC], F32, kind="ExternalOutput"))
    for nm, shape, dt in [("c_ones", [128, 128], F32), ("c_id", [128, 128], F32),
                          ("c_misc", [128, 8], F32), ("c_sel", [8, 8 * 128], BF16),
                          ("c_selp", [16, NCH * 128], BF16),
                          ("c_idb", [128, 128], BF16),
                          ("c_onesb", [128, 128], BF16),
                          ("c_selpf", [16, NCH * 128], F32),
                          ("c_s16f", [16, 16 * 128], F32),
                          ("c_ut", [128, 128], F32), ("c_lt", [128, 128], F32),
                          ("c_msl", [128, 128], F32), ("c_msu", [128, 128], F32),
                          ("c_hm", [128, NCH * 16], F32), ("c_pat", [16, 8], F32),
                          ("xmask", [128, 1], F32),
                          ("meta", [1, 32], I32), ("nonce", [1, 8], I32)]:
        dr[nm] = nc.dram_tensor(nm, shape, dt, kind="ExternalInput")
    dr["shm_pay"] = nc.dram_tensor("shm_pay", [2, 128, 560], F32, addr_space="Shared")
    dr["shm_fl"] = nc.dram_tensor("shm_fl", [2, 8], I32, addr_space="Shared")
    for L, E, I in ((0, 8, 2048), (1, 7, 512)):
        NB = 1 if L == 0 else 2
        dr[f"f{L}_gw"] = nc.dram_tensor(f"f{L}_gw", [E, D, I], F32, kind="ExternalInput")
        dr[f"f{L}_uw"] = nc.dram_tensor(f"f{L}_uw", [E, D, I], F32, kind="ExternalInput")
        dr[f"f{L}_dw"] = nc.dram_tensor(f"f{L}_dw", [E, I, D], F32, kind="ExternalInput")
        dr[f"f{L}_sg"] = nc.dram_tensor(f"f{L}_sg", [D, I], F32, kind="ExternalInput")
        dr[f"f{L}_su"] = nc.dram_tensor(f"f{L}_su", [D, I], F32, kind="ExternalInput")
        dr[f"f{L}_sd"] = nc.dram_tensor(f"f{L}_sd", [I, D], F32, kind="ExternalInput")
        dr[f"f{L}_rw"] = nc.dram_tensor(f"f{L}_rw", [D, E], F32, kind="ExternalInput")
        dr[f"f{L}_rb"] = nc.dram_tensor(f"f{L}_rb", [NB, E], F32, kind="ExternalInput")
    for nm in ["wq", "wk", "wv", "wo"]:
        dr[f"a0_{nm}"] = nc.dram_tensor(f"a0_{nm}", [D, D], F32, kind="ExternalInput")
    dr["a0_b_rows"] = nc.dram_tensor("a0_b_rows", [1, 4 * D], F32, kind="ExternalInput")
    dr["a0_b_pp"] = nc.dram_tensor("a0_b_pp", [128, 4 * NCH], F32, kind="ExternalInput")
    for nm in ["wq", "wk", "wv", "wg", "wo"]:
        dr[f"a1_{nm}"] = nc.dram_tensor(f"a1_{nm}", [D, D], F32, kind="ExternalInput")
    dr["a1_wa"] = nc.dram_tensor("a1_wa", [D, 16], F32, kind="ExternalInput")
    dr["a1_wb_"] = nc.dram_tensor("a1_wb_", [D, 16], F32, kind="ExternalInput")
    dr["a1_b_pp"] = nc.dram_tensor("a1_b_pp", [128, 5 * NCH], F32, kind="ExternalInput")
    dr["a1_ab_bias"] = nc.dram_tensor("a1_ab_bias", [16, 2], F32, kind="ExternalInput")
    dr["a1_gn_pp"] = nc.dram_tensor("a1_gn_pp", [128, 2 * NCH], F32, kind="ExternalInput")
    dr["a1_cv_pp"] = nc.dram_tensor("a1_cv_pp", [128, 3 * NCH * 4], F32, kind="ExternalInput")
    scr = {}
    for i in range(6):
        scr[f"xn{i}"] = nc.dram_tensor(f"xns{i}", [NCH, 128, TLOC], BF16)
    for i in (0, 2, 3, 5):
        scr[f"yacc{i}"] = nc.dram_tensor(f"yacc{i}", [NCH, 128, TLOC], F32)
    scr["qf0"] = nc.dram_tensor("qf0", [NCH, 128, TLOC], BF16)
    for nm in ["qc", "kc", "vc", "og"]:
        scr[nm] = nc.dram_tensor(nm + "s", [NCH, 128, TLOC], BF16)
    scr["E"] = nc.dram_tensor("E_scr", [CJ, 128, NCH * 128], BF16)
    scr["P"] = nc.dram_tensor("P_scr", [CJ, 128, NCH * 64], BF16)
    scr["H"] = nc.dram_tensor("H_scr", [CJ, 128, NCH * 64], BF16)

    with ExitStack() as outer:
        em = Em(nc, outer)
        base = {k: dr[k] for k in dr}
        base["PAYC"] = 560
        base["use_sel"] = True
        em.load_consts(base)
        base["tag"] = ""
        emit_xchg_setup(em, base)

        def sub(i, emitfn, cfg):
            cfg = dict(base, **cfg)
            cfg["x_in"] = xs[i]
            cfg["x_out"] = xs[i + 1]
            cfg["tag"] = f"s{i}_"
            with ExitStack() as inner:
                saved = em.stack
                em.stack = inner
                with nc.allow_low_precision(reason="bf16 staging intentional"):
                    emitfn(em, cfg)
                em.stack = saved
            em.barrier()

        def fcfg(L, i):
            return dict(gw=dr[f"f{L}_gw"], uw=dr[f"f{L}_uw"], dw=dr[f"f{L}_dw"],
                        sg=dr[f"f{L}_sg"], su=dr[f"f{L}_su"], sd=dr[f"f{L}_sd"],
                        rw=dr[f"f{L}_rw"], rbias=dr[f"f{L}_rb"],
                        E=(8 if L == 0 else 7), I=(2048 if L == 0 else 512),
                        xn_dram=scr[f"xn{i}"], yacc=scr[f"yacc{i}"])

        sub(0, emit_ffn, fcfg(0, 0))
        sub(1, emit_attn0, dict(wq=dr["a0_wq"], wk=dr["a0_wk"], wv=dr["a0_wv"],
                                wo=dr["a0_wo"], b_rows=dr["a0_b_rows"],
                                b_pp=dr["a0_b_pp"], xn_dram=scr["xn1"],
                                qf_dram=scr["qf0"], rbase=0))
        sub(2, emit_ffn, fcfg(0, 2))
        sub(3, emit_ffn, fcfg(1, 3))
        sub(4, emit_attn1, dict(wq=dr["a1_wq"], wk=dr["a1_wk"], wv=dr["a1_wv"],
                                wg=dr["a1_wg"], wo=dr["a1_wo"], wa=dr["a1_wa"],
                                wb_=dr["a1_wb_"], b_pp=dr["a1_b_pp"],
                                ab_bias=dr["a1_ab_bias"], gn_pp=dr["a1_gn_pp"],
                                cv_pp=dr["a1_cv_pp"], xn_dram=scr["xn4"],
                                qc_dram=scr["qc"], kc_dram=scr["kc"],
                                vc_dram=scr["vc"], og_dram=scr["og"],
                                E_dram=scr["E"], P_dram=scr["P"], H_dram=scr["H"],
                                rbase=4))
        sub(5, emit_ffn, fcfg(1, 5))
    return nc


def fused_in_maps(inputs):
    c_ones, c_id, c_sel, c_misc = consts_f()
    rng = np.random.default_rng()
    col0s = [0, 16, 32, 48, 0, 48, 24, 0]
    metas, nonces = xchg_host_inputs(rng, col0s=col0s)
    xcores = shard_x(np.asarray(inputs["L0_attn_x"]))
    maps = []
    for c in range(NCORES):
        a0 = attn0_in_map(inputs, None, metas[c], nonces[c], c)
        a1 = attn1_in_map(inputs, None, metas[c], nonces[c], c)
        m = dict(x0=xcores[c], c_ones=c_ones, c_id=c_id, c_misc=c_misc,
                 c_sel=a0["c_sel"], c_selp=a1["c_selp"], c_idb=a1["c_idb"],
                 c_onesb=a0["c_onesb"], c_selpf=a1["c_selpf"],
                 c_s16f=a1["c_s16f"], c_ut=a1["c_ut"], c_lt=a1["c_lt"],
                 c_msl=a1["c_msl"], c_msu=a1["c_msu"], c_hm=a1["c_hm"],
                 c_pat=a1["c_pat"], xmask=a1["xmask"],
                 meta=metas[c], nonce=nonces[c])
        for L in (0, 1):
            fm = ffn_in_map(L, inputs, None)
            for k in ["gw", "uw", "dw", "sg", "su", "sd", "rw", "rbias"]:
                m[f"f{L}_{'rb' if k == 'rbias' else k}"] = fm[k]
        for nm in ["wq", "wk", "wv", "wo"]:
            m[f"a0_{nm}"] = a0[nm]
        m["a0_b_rows"] = a0["b_rows"]
        m["a0_b_pp"] = a0["b_pp"]
        for nm in ["wq", "wk", "wv", "wg", "wo", "wa", "wb_"]:
            m[f"a1_{nm}"] = a1[nm]
        m["a1_b_pp"] = a1["b_pp"]
        m["a1_ab_bias"] = a1["ab_bias"]
        m["a1_gn_pp"] = a1["gn_pp"]
        m["a1_cv_pp"] = a1["cv_pp"]
        maps.append(m)
    return maps


def kernel(**inputs):
    nc = _get_prog("fused", build_fused)
    maps = fused_in_maps(inputs)
    res = run_bass_kernel_spmd(nc, maps, core_ids=list(range(NCORES)))
    return unshard_x([res.results[c]["x_out"] for c in range(NCORES)])


def kernel_unfused(**inputs):
    xcores = shard_x(np.asarray(inputs["L0_attn_x"]))
    xcores = run_ffn(0, inputs, xcores)
    xcores = run_attn0(inputs, xcores)
    xcores = run_ffn(0, inputs, xcores)
    xcores = run_ffn(1, inputs, xcores)
    xcores = run_attn1(inputs, xcores)
    xcores = run_ffn(1, inputs, xcores)
    return unshard_x(xcores)



